# version 03
# baseline (speedup 1.0000x reference)
import numpy as np
import ml_dtypes
import concourse.bass as bass
import concourse.mybir as mybir
from concourse.bass_utils import run_bass_kernel_spmd

F32 = mybir.dt.float32
BF16 = mybir.dt.bfloat16
I16 = mybir.dt.int16
OP = mybir.AluOpType
AF = mybir.ActivationFunctionType
AX = mybir.AxisListType

B = 2
LLOC = 1024
T = 2048
NT = 16
D = 1024
DI = 2048
TC = 32
NCH = 32
NG = 8
NCORES = 8

ENGINES = ("sync", "vector", "scalar", "gpsimd", "tensor")
SEMS = {}
V, S, G, TE, SY = "vector", "scalar", "gpsimd", "tensor", "sync"


def ts_(i, n=128):
    return slice(i * n, (i + 1) * n)


def off_(base, i, n=128):
    return slice(base + i * n, base + (i + 1) * n)


class Prog:
    NW = 6

    def __init__(self):
        self.stream = {e: [] for e in ENGINES}
        self.cnt = {s: 0 for s in ["v", "sc", "g", "t", "cc", "bs"]}
        self.qs = {}
        for q in ("q0", "q1", "q2"):
            self.qs[q] = {"idx": 0, "tot": [0] * self.NW, "open": False}

    def _qsem(self, q, i):
        return q + chr(ord("a") + i)

    def w(self, eng, sem, val=None):
        if sem in self.qs:
            st = self.qs[sem]
            for i in range(self.NW):
                if st["tot"][i] > 0:
                    self.stream[eng].append(
                        lambda e, s=self._qsem(sem, i), vv=st["tot"][i]: e.wait_ge(SEMS[s], vv))
            st["open"] = False
            return
        v = self.cnt[sem] if val is None else val
        if v > 0:
            self.stream[eng].append(lambda e, s=sem, vv=v: e.wait_ge(SEMS[s], vv))

    OWN = {"vector": "v", "scalar": "sc", "gpsimd": "g", "tensor": "t"}

    def r(self, eng, fn, inc=None, k=1):
        own = self.OWN.get(eng)
        if own is not None:
            v = self.cnt[own]
            if v > 0:
                self.stream[eng].append(lambda e, s=own, vv=v: e.wait_ge(SEMS[s], vv))
            if inc is None or inc == own:
                self.cnt[own] += 1
                self.stream[eng].append(lambda e, f=fn, s=own: f(e).then_inc(SEMS[s], 1))
            else:
                self.cnt[inc] += k
                self.stream[eng].append(lambda e, f=fn, s=inc, kk=k: f(e).then_inc(SEMS[s], kk))
        else:
            if inc is not None:
                self.cnt[inc] += k
                self.stream[eng].append(lambda e, f=fn, s=inc, kk=k: f(e).then_inc(SEMS[s], kk))
            else:
                self.stream[eng].append(lambda e, f=fn: f(e)) 

    def d(self, eng, out, in_, q, transpose=False, slow=False):
        st = self.qs[q]
        if not st["open"]:
            st["idx"] = (st["idx"] + 1) % self.NW
            i = st["idx"]
            if st["tot"][i] > 0:
                self.stream[eng].append(
                    lambda e, s=self._qsem(q, i), vv=st["tot"][i]: e.wait_ge(SEMS[s], vv))
            st["open"] = True
        i = st["idx"]
        st["tot"][i] += 16
        sname = self._qsem(q, i)
        if transpose and not slow:
            self.stream[eng].append(
                lambda e, o=out, i_=in_, s=sname: e.dma_start_transpose(o, i_).then_inc(SEMS[s], 16))
        elif transpose and slow:
            def _f(e, o=out, i_=in_, s=sname):
                with e.bass.allow_non_contiguous_dma(reason="tiny transpose"):
                    return e.dma_start_transpose(o, i_).then_inc(SEMS[s], 16)
            self.stream[eng].append(_f)
        else:
            self.stream[eng].append(
                lambda e, o=out, i_=in_, s=sname: e.dma_start(out=o, in_=i_, allow_slow_non_contiguous=slow).then_inc(SEMS[s], 16))

    def snap(self):
        return dict(self.cnt)

    def wall(self, eng, snap=None):
        for q in ("q0", "q1", "q2"):
            self.w(eng, q)
        sn = self.cnt if snap is None else snap
        for s, v in sn.items():
            if v > 0:
                self.w(eng, s, v)

    def barrier(self):
        m = {}
        for eng, sem in [("sync", "bs"), ("vector", "v"), ("scalar", "sc"),
                         ("gpsimd", "g"), ("tensor", "t")]:
            self.cnt[sem] += 1
            m[sem] = self.cnt[sem]
            self.stream[eng].append(lambda e, s=sem: e.sem_inc(SEMS[s], 1))
        for eng in ENGINES:
            for s, v in m.items():
                if s != {"sync": "bs", "vector": "v", "scalar": "sc",
                         "gpsimd": "g", "tensor": "t"}[eng] or True:
                    self.w(eng, s, v)
            for q in ("q0", "q1", "q2"):
                self.w(eng, q)
        if not hasattr(self, "marks"):
            self.marks = []
        self.marks.append({e: len(self.stream[e]) for e in ENGINES})


def build_program():
    nc = bass.Bass("TRN2", target_bir_lowering=False)
    P = Prog()
    ap = {}

    def IN(name, shape, dt=F32):
        ap[name] = nc.dram_tensor(name, list(shape), dt, kind="ExternalInput").ap()

    def OUT(name, shape, dt=F32):
        ap[name] = nc.dram_tensor(name, list(shape), dt, kind="ExternalOutput").ap()

    import os as _os2
    _dbg = set((_os2.environ.get("MK_DBG", "")).split(",")) - {""}

    def INT(name, shape, dt=F32, **kw):
        kind = "ExternalOutput" if name in _dbg else "Internal"
        if kw.get("addr_space"):
            kind = "Internal"
        ap[name] = nc.dram_tensor(name, list(shape), dt, kind=kind, **kw).ap()

    IN("x", (B, LLOC, D)); IN("xh", (B, 3, D))
    IN("W_in", (2 * DI, D)); IN("conv_w", (DI, 4)); IN("conv_b", (DI,))
    IN("W_xproj", (33, DI)); IN("A_log", (16,)); IN("D_param", (DI,))
    IN("W_out", (D, DI))
    IN("f0_wg", (4096, D)); IN("f0_wu", (4096, D)); IN("f0_wd", (D, 4096))
    IN("Wqkvb", (3104, D))
    IN("W_o", (D, D)); IN("ln_g", (D,)); IN("ln_b", (D,))
    IN("f1_wg", (2816, D)); IN("f1_wu", (2816, D)); IN("f1_wd", (D, 2816))
    IN("mask", (128, TC), BF16)
    IN("ident", (128, 128), BF16)
    IN("identf", (128, 128))
    IN("rmask", (NCORES,))
    OUT("out", (B, LLOC, D))

    INT("s_win", (2 * DI, D), BF16)
    INT("s_wxp", (48, DI), BF16)
    INT("s_wout", (D, DI), BF16)
    INT("s_f0wg", (4096, D), BF16); INT("s_f0wu", (4096, D), BF16); INT("s_f0wd", (D, 4096), BF16)
    INT("s_qkvb", (3104, D), BF16); INT("s_wo", (D, D), BF16)
    INT("s_f1wg", (2816, D), BF16); INT("s_f1wu", (2816, D), BF16); INT("s_f1wd", (D, 2816), BF16)
    INT("s_xn", (T, D), BF16)
    INT("s_xq", (T, D), BF16)
    INT("s_xcf", (DI, T), BF16)
    INT("s_sz", (T, DI), BF16)
    INT("s_y", (T, DI), BF16)
    INT("s_yg", (T, DI), BF16)
    INT("s_h2", (T, 4096), F32)
    INT("s_xq2", (T, 4096), BF16)
    INT("s_x1", (T, D), F32)
    INT("s_xn1", (T, D), BF16)
    INT("s_xq1", (T, D), BF16)
    INT("s_qkv", (T, 3104), BF16)
    INT("s_xq3", (T, 2816), BF16)
    INT("s_lay", (T, D), BF16)
    INT("s_s", (T, D), BF16)
    INT("s_g", (T, D), BF16)
    INT("s_g1", (T, D), BF16)
    INT("s_fb", (16, B * LLOC), BF16)
    INT("s_dbg", (128, B * LLOC), F32)
    INT("s_dbg2", (128, 64), F32)
    INT("s_dbg3", (128, 2560), BF16)
    INT("cc_in", (32, DI + 1), F32)
    INT("cc_out", (NCORES, 32, DI + 1), F32, addr_space="Shared")

    ctx = []
    SB = {}

    def sb(name, p, f, dt=F32):
        t = nc.sbuf_tensor(name, [p, f], dt)
        h = t.__enter__(); ctx.append(t); SB[name] = h
        return h

    msk = sb("msk", 128, TC, BF16)
    idn = sb("idn", 128, 128, BF16)
    idnf = sb("idnf", 128, 128)
    rmk = sb("rmk", 32, NCORES)
    Acol = sb("Acol", 16, 1)
    cbw = sb("cbw", 128, 16 * 4)
    cbb = sb("cbb", 128, 16)
    onec = sb("onec", 128, 2)
    sxinv = sb("sxinv", 128, NT)
    sx2i = sb("sx2i", 128, NT)
    sxg = sb("sxg", 128, NT); sxu = sb("sxu", 128, NT); sxd = sb("sxd", 128, NT)
    wsc = sb("wsc", 4, 8)
    mgr = sb("mgr", 128, 4)
    p_a = sb("p_a", 128, 64)
    p_b = sb("p_b", 128, 128)
    w_a = sb("w_a", 128, 4096)
    w_b = sb("w_b", 128, 4096)
    c_a = sb("c_a", 128, 4096, BF16)
    c_b = sb("c_b", 128, 4096, BF16)
    i_a = sb("i_a", 128, 4096, I16)
    xhn = sb("xhn", 8, D, BF16)
    xhh = sb("xhh", 8, D)
    xhf = sb("xhf", 128, 8 * 8, BF16)
    A16 = sb("A16", 128, 49152, BF16)
    A32 = sb("A32", 128, 11264)

    def a16(off, n):
        return A16[:, off:off + n]

    def a32(off, n):
        return A32[:, off:off + n]

    dprow = A16[:, 47104:49152]
    lngrow = A16[:, 16384:17408]
    lnbrow = A16[:, 17408:18432]

    xnfm = a16(0, 16384)
    xpfc = a16(16384, 8256)
    xcfc = a16(24640, 8192)
    winS = a16(32832, 2048)
    zwv = a16(34880, 4096)
    wxpT = a16(41024, 768)
    tails = a16(41856, 64)
    c_c = a16(43008, 4096)
    fC = a16(0, 2048); fB = a16(2048, 2048); fCc = a16(4096, 2048)
    btT = a16(6144, 2048)
    gsb = a16(7168, 16384)
    hstg = a16(23552, 2048)
    hinb = a16(25600, 2048)
    hinb2 = a16(6144, 2048)
    xtm = a16(27648, 4096)
    big1 = A16[:, 0:32768]
    bufA = A16[:, 32768:49152]
    wmov = bufA[:, 0:8192]
    wmov2 = bufA[:, 8192:16384]
    fL = A32[:, 0:2048]; fsc = A32[:, 2048:4096]; f0z = A32[:, 4096:5120]
    xdb = A32[:, 5120:7168]
    hcurb = [A32[:, 7168:9216], A32[:, 9216:11264]]; hin = A32[:, 0:2048]
    xdbB = A32[:, 5120:7168]
    xdbC = A32[:, 7168:9216]
    xdbV = A32[:, 9216:11264]
    fDst = p_b[:, 0:64]; fDch = p_b[:, 64:128]
    fzv = w_a[:, 2048:4096]

    PSG_t = nc.psum_tensor("psg", [128, 2048], F32)
    PSG = PSG_t.__enter__(); ctx.append(PSG_t)
    PSY_t = nc.psum_tensor("psy", [128, 2048], F32)
    PSY = PSY_t.__enter__(); ctx.append(PSY_t)
    PS = [PSG[:, 0:512], PSG[:, 512:1024], PSG[:, 1024:1536], PSG[:, 1536:2048],
          PSY[:, 0:512], PSY[:, 512:1024], PSY[:, 1024:1536], PSY[:, 1536:2048]]

    semnames = ["v", "sc", "g", "t", "cc", "bs"]
    for q in ("q0", "q1", "q2"):
        for i in range(Prog.NW):
            semnames.append(q + chr(ord("a") + i))
    for s in semnames:
        h = nc.semaphore(s)
        SEMS[s] = h.__enter__(); ctx.append(h)

    P.d(SY, msk[:, :], ap["mask"][:, :], "q0")
    P.d(SY, idn[:, :], ap["ident"][:, :], "q0")
    P.d(SY, idnf[:, :], ap["identf"][:, :], "q0")
    P.d(SY, rmk[:, :], ap["rmask"][None, :].broadcast_to([32, NCORES]), "q0")
    P.d(G, dprow[:, :], ap["D_param"][None, :].broadcast_to([128, DI]), "q2")
    P.d(SY, Acol[:, 0:1], ap["A_log"][:, None], "q0")
    P.d(SY, cbw[:, :].rearrange("p (k j) -> p k j", j=4),
        ap["conv_w"].rearrange("(k p) j -> p k j", p=128), "q0", slow=True)
    P.d(SY, cbb[:, :], ap["conv_b"].rearrange("(k p) -> p k", p=128), "q0", slow=True)
    P.wall(V); P.wall(S)
    P.r(V, lambda e: e.memset(onec[:, :], 1.0), inc="v")
    P.r(S, lambda e: e.activation(out=Acol[0:16, 0:1], in_=Acol[0:16, 0:1], func=AF.Exp), inc="sc")
    P.w(V, "sc")
    P.r(V, lambda e: e.tensor_scalar_mul(Acol[0:16, 0:1], Acol[0:16, 0:1], -1.0), inc="v")

    xr = ap["x"].rearrange("b (i p) d -> (b i) p d", p=128)
    for i in range(NT):
        P.d(SY, w_a[:, 0:D], xr[i], "q0")
        P.wall(V)
        mu = p_a[:, 0:1]; vv = p_a[:, 1:2]; rr = p_a[:, 2:3]; rmx = p_a[:, 3:4]
        P.r(V, lambda e, mu=mu: e.reduce_sum(mu, w_a[:, 0:D], AX.X))
        P.r(V, lambda e, mu=mu: e.tensor_scalar_mul(mu, mu, 1.0 / D), inc="v")
        P.w(S, "v")
        P.r(S, lambda e: e.activation(out=w_b[:, 0:D], in_=w_a[:, 0:D], func=AF.Square), inc="sc")
        P.w(V, "sc")
        P.r(V, lambda e, vv=vv: e.reduce_sum(vv, w_b[:, 0:D], AX.X))
        P.r(V, lambda e, vv=vv: e.tensor_scalar_mul(vv, vv, 1.0 / D))
        P.r(V, lambda e, mu=mu, rr=rr: e.tensor_tensor(rr, mu, mu, OP.mult))
        P.r(V, lambda e, vv=vv, rr=rr: e.tensor_tensor(rr, vv, rr, OP.subtract))
        P.r(V, lambda e, rr=rr: e.tensor_scalar_add(rr, rr, 1e-5), inc="v")
        P.w(S, "v")
        P.r(S, lambda e, rr=rr: e.activation(out=rr, in_=rr, func=AF.Sqrt), inc="sc")
        P.w(V, "sc")
        P.r(V, lambda e, rr=rr: e.reciprocal(rr, rr))
        P.r(V, lambda e, mu=mu, rr=rr: e.tensor_scalar(w_a[:, 0:D], w_a[:, 0:D], mu, rr, OP.subtract, OP.mult))
        P.r(V, lambda e: e.tensor_copy(c_a[:, 0:D], w_a[:, 0:D]))
        P.r(V, lambda e, rmx=rmx: e.reduce_max(rmx, w_a[:, 0:D], AX.X, apply_absolute_value=True))
        P.r(V, lambda e, rmx=rmx: e.tensor_scalar_max(rmx, rmx, 1e-5))
        P.r(V, lambda e, i=i, rmx=rmx: e.tensor_scalar_mul(sxinv[:, i:i + 1], rmx, 1.0 / 127.0))
        P.r(V, lambda e, i=i, rr=rr: e.reciprocal(rr, sxinv[:, i:i + 1]), inc="v")
        P.w(S, "v")
        P.r(S, lambda e, rr=rr: e.activation(out=w_b[:, 0:D], in_=w_a[:, 0:D], func=AF.Identity, scale=rr, bias=0.0), inc="sc")
        P.w(V, "sc")
        P.r(V, lambda e: e.tensor_copy(i_a[:, 0:D], w_b[:, 0:D]))
        P.r(V, lambda e: e.tensor_copy(c_b[:, 0:D], i_a[:, 0:D]), inc="v")
        P.w(S, "v")
        P.d(S, ap["s_xn"][ts_(i, 128), :], c_a[:, 0:D], "q1")
        P.d(S, ap["s_xq"][ts_(i, 128), :], c_b[:, 0:D], "q1")
        P.w(SY, "q1"); P.w(V, "q1"); P.w(S, "q1")

    P.r(V, lambda e: e.memset(xhh[:, :], 0.0), inc="v")
    P.wall(SY)
    P.d(SY, xhh[0:3, :], ap["xh"][0], "q0")
    P.d(SY, xhh[3:6, :], ap["xh"][1], "q0")
    P.wall(V); P.wall(S)
    hmu = p_a[0:8, 4:5]; hvv = p_a[0:8, 5:6]; hm2 = p_a[0:8, 6:7]
    P.r(V, lambda e: e.reduce_sum(hmu, xhh[:, :], AX.X))
    P.r(V, lambda e: e.tensor_scalar_mul(hmu, hmu, 1.0 / D), inc="v")
    P.w(S, "v")
    P.r(S, lambda e: e.activation(out=w_b[0:8, 0:D], in_=xhh[:, :], func=AF.Square), inc="sc")
    P.w(V, "sc")
    P.r(V, lambda e: e.reduce_sum(hvv, w_b[0:8, 0:D], AX.X))
    P.r(V, lambda e: e.tensor_scalar_mul(hvv, hvv, 1.0 / D))
    P.r(V, lambda e: e.tensor_tensor(hm2, hmu, hmu, OP.mult))
    P.r(V, lambda e: e.tensor_tensor(hvv, hvv, hm2, OP.subtract))
    P.r(V, lambda e: e.tensor_scalar_add(hvv, hvv, 1e-5), inc="v")
    P.w(S, "v")
    P.r(S, lambda e: e.activation(out=hvv, in_=hvv, func=AF.Sqrt), inc="sc")
    P.w(V, "sc")
    P.r(V, lambda e: e.reciprocal(hvv, hvv))
    P.r(V, lambda e: e.tensor_scalar(xhh[:, :], xhh[:, :], hmu, hvv, OP.subtract, OP.mult))
    P.r(V, lambda e: e.tensor_copy(xhn[:, :], xhh[:, :]), inc="v")
    P.wall(TE)
    for k in range(8):
        P.r(TE, lambda e, k=k: e.transpose(PSY[0:128, k * 8:k * 8 + 8], xhh[0:8, ts_(k, 128)], idnf[0:8, 0:8]), inc="t")
    P.w(V, "t")
    P.r(V, lambda e: e.tensor_copy(xhf[:, 0:64], PSY[:, 0:64]), inc="v")
    P.w(TE, "v")

    def cast_rows(src_ap, dst_ap, nrows, ncols):
        nt = (nrows + 127) // 128
        for r in range(nt):
            p = min(128, nrows - r * 128)
            P.d(SY, w_a[0:p, 0:ncols], src_ap[r * 128:r * 128 + p, :], "q0")
            P.wall(V)
            P.r(V, lambda e, p=p, n=ncols: e.tensor_copy(c_a[0:p, 0:n], w_a[0:p, 0:n]), inc="v")
            P.w(S, "v")
            P.d(S, dst_ap[r * 128:r * 128 + p, :], c_a[0:p, 0:n] if False else c_a[0:p, 0:ncols], "q1")
            P.w(SY, "q1"); P.w(V, "q1"); P.w(S, "q1")
    cast_rows(ap["W_in"], ap["s_win"], 2 * DI, D)
    cast_rows(ap["W_xproj"], ap["s_wxp"], 33, DI)
    P.r(V, lambda e: e.memset(c_a[0:15, 0:DI], 0.0), inc="v")
    P.w(S, "v")
    P.d(S, ap["s_wxp"][33:48, :], c_a[0:15, 0:DI], "q1")
    P.w(SY, "q1"); P.w(V, "q1"); P.w(S, "q1")
    cast_rows(ap["W_out"], ap["s_wout"], D, DI)
    cast_rows(ap["Wqkvb"], ap["s_qkvb"], 3104, D)
    cast_rows(ap["W_o"], ap["s_wo"], D, D)

    def f0_quant(src, dst, nrows, ncols, scidx):
        nt = nrows // 128
        P.r(V, lambda e: e.memset(p_a[:, 8:9], 0.0), inc="v")
        for r in range(nt):
            P.d(SY, w_a[:, 0:ncols], src[ts_(r, 128), :], "q0")
            P.wall(V)
            P.r(V, lambda e, n=ncols: e.reduce_sum(p_a[:, 9:10], w_a[:, 0:n], AX.X, apply_absolute_value=True))
            P.r(V, lambda e: e.tensor_tensor(p_a[:, 8:9], p_a[:, 8:9], p_a[:, 9:10], OP.add), inc="v")
            P.w(SY, "v")
        P.wall(TE)
        P.r(TE, lambda e: e.matmul(PS[7][0:1, 0:1], p_a[:, 8:9], onec[:, 0:1], start=True, stop=True), inc="t")
        P.w(V, "t")
        P.r(V, lambda e, n=nrows * ncols: e.tensor_scalar(wsc[0:1, 0:1], PS[7][0:1, 0:1], 1.0 / n, 1e-5, OP.mult, OP.max))
        P.r(V, lambda e: e.tensor_scalar_mul(wsc[0:1, 1:2], wsc[0:1, 0:1], 0.5), inc="v")
        P.r(V, lambda e: e.memset(w_b[0:1, 0:128], 1.0), inc="v")
        P.w(TE, "v")
        P.r(TE, lambda e: e.matmul(PS[7][:, 4:5], w_b[0:1, 0:128], wsc[0:1, 1:2], start=True, stop=True), inc="t")
        P.r(TE, lambda e, scidx=scidx: e.matmul(PS[7][:, 8:9], w_b[0:1, 0:128], wsc[0:1, 0:1], start=True, stop=True), inc="t")
        P.w(V, "t")
        P.r(V, lambda e: e.tensor_copy(p_a[:, 10:11], PS[7][:, 4:5]))
        P.r(V, lambda e: e.tensor_scalar_mul(p_a[:, 11:12], p_a[:, 10:11], -1.0))
        P.r(V, lambda e, scidx=scidx: e.tensor_copy(mgr[:, scidx:scidx + 1], PS[7][:, 8:9]), inc="v")
        for r in range(nt):
            P.d(SY, w_a[:, 0:ncols], src[ts_(r, 128), :], "q0")
            P.wall(V)
            P.r(V, lambda e, n=ncols: e.tensor_scalar(c_a[:, 0:n], w_a[:, 0:n], p_a[:, 10:11], None, OP.is_gt))
            P.r(V, lambda e, n=ncols: e.tensor_scalar(c_b[:, 0:n], w_a[:, 0:n], p_a[:, 11:12], None, OP.is_lt))
            P.r(V, lambda e, n=ncols: e.tensor_tensor(c_a[:, 0:n], c_a[:, 0:n], c_b[:, 0:n], OP.subtract), inc="v")
            P.w(S, "v")
            P.d(S, dst[ts_(r, 128), :], c_a[:, 0:ncols], "q1")
            P.w(SY, "q1"); P.w(V, "q1"); P.w(S, "q1")
    f0_quant(ap["f0_wg"], ap["s_f0wg"], 4096, D, 0)
    f0_quant(ap["f0_wu"], ap["s_f0wu"], 4096, D, 1)
    f0_quant(ap["f0_wd"], ap["s_f0wd"], D, 4096, 2)

    def fp4_on_tile(p, ncols):
        nb = ncols // 16
        a3 = w_a[0:p, 0:ncols].rearrange("p (n s) -> p n s", s=16)
        bm = w_b[0:p, 3500:3500 + nb]
        rs = w_b[0:p, 3200:3200 + nb]
        P.r(V, lambda e, a3=a3, bm=bm: e.tensor_reduce(bm, a3, AX.X, OP.max, apply_absolute_value=True))
        P.r(V, lambda e, bm=bm: e.tensor_scalar(bm, bm, 1.0 / 6.0, 1e-12, OP.mult, OP.max))
        P.r(V, lambda e, rs=rs, bm=bm: e.reciprocal(rs, bm), inc="v")
        rsb = rs[:, :, None].broadcast_to([p, nb, 16])
        P.r(V, lambda e, a3=a3, rsb=rsb: e.tensor_tensor(a3, a3, rsb, OP.mult), inc="v")
        P.w(S, "v")
        P.r(S, lambda e, p=p, n=ncols: e.activation(out=w_b[0:p, 0:n], in_=w_a[0:p, 0:n], func=AF.Abs, scale=2.0), inc="sc")
        P.w(V, "sc")
        P.r(V, lambda e, p=p, n=ncols: e.tensor_copy(i_a[0:p, 0:n], w_b[0:p, 0:n]))
        P.r(V, lambda e, p=p, n=ncols: e.tensor_copy(c_a[0:p, 0:n], i_a[0:p, 0:n]))
        P.r(V, lambda e, p=p, n=ncols: e.tensor_scalar(c_a[0:p, 0:n], c_a[0:p, 0:n], 4.0, 0.5, OP.min, OP.mult), inc="v")
        for th in (5.0, 7.0):
            P.r(V, lambda e, p=p, n=ncols, th=th: e.tensor_scalar(c_b[0:p, 0:n], w_b[0:p, 0:n], th, None, OP.is_gt))
            P.r(V, lambda e, p=p, n=ncols: e.tensor_tensor(c_a[0:p, 0:n], c_a[0:p, 0:n], c_b[0:p, 0:n], OP.add))
        P.r(V, lambda e, p=p, n=ncols: e.tensor_scalar(c_b[0:p, 0:n], w_b[0:p, 0:n], 10.0, 2.0, OP.is_gt, OP.mult))
        P.r(V, lambda e, p=p, n=ncols: e.tensor_tensor(c_a[0:p, 0:n], c_a[0:p, 0:n], c_b[0:p, 0:n], OP.add), inc="v")
        P.w(S, "v")
        P.r(S, lambda e, p=p, n=ncols: e.activation(out=c_b[0:p, 0:n], in_=w_a[0:p, 0:n], func=AF.Sign), inc="sc")
        P.w(V, "sc")
        P.r(V, lambda e, p=p, n=ncols: e.tensor_tensor(c_a[0:p, 0:n], c_a[0:p, 0:n], c_b[0:p, 0:n], OP.mult))
        scb = bm[:, :, None].broadcast_to([p, nb, 16])
        ca3 = c_a[0:p, 0:ncols].rearrange("p (n s) -> p n s", s=16)
        P.r(V, lambda e, ca3=ca3, scb=scb: e.tensor_tensor(ca3, ca3, scb, OP.mult), inc="v")

    def f1_quant(src, dst, nrows, ncols):
        nt = (nrows + 127) // 128
        for r in range(nt):
            p = min(128, nrows - r * 128)
            P.d(SY, w_a[0:p, 0:ncols], src[r * 128:r * 128 + p, :], "q0")
            P.wall(V)
            fp4_on_tile(p, ncols)
            P.w(S, "v")
            P.d(S, dst[r * 128:r * 128 + p, :], c_a[0:p, 0:ncols], "q1")
            P.w(SY, "q1"); P.w(V, "q1"); P.w(S, "q1")
    f1_quant(ap["f1_wg"], ap["s_f1wg"], 2816, D)
    f1_quant(ap["f1_wu"], ap["s_f1wu"], 2816, D)
    f1_quant(ap["f1_wd"], ap["s_f1wd"], D, 2816)

    P.r(V, lambda e: e.tensor_scalar(sxg[:, :], sxinv[:, :], mgr[:, 0:1], None, OP.mult))
    P.r(V, lambda e: e.tensor_scalar(sxu[:, :], sxinv[:, :], mgr[:, 1:2], None, OP.mult), inc="v")
    if "s_dbg2" in _dbg:
        P.w(S, "v")
        P.d(S, ap["s_dbg2"][:, 0:NT], sxg[:, :], "q1")
        P.d(S, ap["s_dbg2"][:, 16:16 + NT], sxu[:, :], "q1")
        P.d(S, ap["s_dbg2"][:, 32:32 + NT], sxinv[:, :], "q1")
        P.d(S, ap["s_dbg2"][:, 48:52], mgr[:, 0:4], "q1")
        P.w(SY, "q1"); P.w(V, "q1"); P.w(S, "q1")
    P.barrier()

    for k in range(8):
        P.d(SY, xnfm[:, ts_(k, T)], ap["s_xn"][:, ts_(k, 128)], "q0", transpose=True)
    for k in range(16):
        P.d(SY, wxpT[:, k * 48:k * 48 + 48], ap["s_wxp"][:, ts_(k, 128)], "q0", transpose=True)
    for b in range(B):
        for lc in range(2):
            t0 = b * LLOC + lc * 512
            for ot in range(16):
                wbuf = winS[:, (ot % 2) * 1024:(ot % 2) * 1024 + 1024]
                P.w(SY, "t")
                for k in range(8):
                    P.d(SY, wbuf[:, k * 128:(k + 1) * 128], ap["s_win"][ts_(ot, 128), ts_(k, 128)], "q0", transpose=True)
                P.wall(TE)
                for k in range(8):
                    st = wbuf[:, k * 128:(k + 1) * 128]
                    mv = xnfm[:, k * T + t0: k * T + t0 + 512]
                    P.r(TE, lambda e, st=st, mv=mv, k=k:
                        e.matmul(PS[0][:, 0:512], st, mv, start=(k == 0), stop=(k == 7)),
                        inc=("t" if k == 7 else None))
                if lc == 0:
                    for k in range(8):
                        st = wbuf[:, k * 128:(k + 1) * 128]
                        mv = xhf[:, k * 8 + b * 3: k * 8 + b * 3 + 3]
                        P.r(TE, lambda e, st=st, mv=mv, k=k:
                            e.matmul(PS[1][:, 0:3], st, mv, start=(k == 0), stop=(k == 7)),
                            inc=("t" if k == 7 else None))
                P.w(V, "t")
                col0 = ot * 516
                if lc == 0:
                    P.r(V, lambda e, col0=col0: e.tensor_copy(xpfc[:, col0:col0 + 3], PS[1][:, 0:3]))
                else:
                    P.r(V, lambda e, col0=col0, ot=ot: e.tensor_copy(xpfc[:, col0:col0 + 3], tails[:, ot * 4:ot * 4 + 3]))
                P.r(V, lambda e, col0=col0: e.tensor_copy(xpfc[:, col0 + 3:col0 + 515], PS[0][:, 0:512]))
                P.r(V, lambda e, ot=ot: e.tensor_copy(tails[:, ot * 4:ot * 4 + 3], PS[0][:, 509:512]), inc="v")
                P.w(TE, "v")
            for ot in range(16):
                col0 = ot * 516
                P.r(V, lambda e, col0=col0, ot=ot:
                    e.tensor_scalar(w_a[:, 0:512], xpfc[:, col0:col0 + 512], cbw[:, 4 * ot:4 * ot + 1], cbb[:, ot:ot + 1], OP.mult, OP.add))
                for j in (1, 2, 3):
                    P.r(V, lambda e, col0=col0, ot=ot, j=j:
                        e.scalar_tensor_tensor(w_a[:, 0:512], xpfc[:, col0 + j:col0 + j + 512], cbw[:, 4 * ot + j:4 * ot + j + 1], w_a[:, 0:512], OP.mult, OP.add), inc="v")
                P.w(S, "v")
                P.r(S, lambda e, ot=ot: e.activation(out=xcfc[:, ts_(ot, 512)], in_=w_a[:, 0:512], func=AF.Silu), inc="sc")
                P.w(V, "sc")
            P.wall(S)
            for ot in range(16):
                P.d(S, ap["s_xcf"][ts_(ot, 128), t0:t0 + 512], xcfc[:, ts_(ot, 512)], "q1")
            P.w(S, "q1"); P.w(V, "q1")
            P.wall(TE)
            for k in range(16):
                stB = wxpT[:, k * 48:k * 48 + 16]
                stC = wxpT[:, k * 48 + 16:k * 48 + 32]
                stV = wxpT[:, k * 48 + 32:k * 48 + 33]
                mv = xcfc[:, ts_(k, 512)]
                P.r(TE, lambda e, st=stB, mv=mv, k=k:
                    e.matmul(PSY[0:16, 0:512], st, mv, start=(k == 0), stop=(k == 15)))
                P.r(TE, lambda e, st=stC, mv=mv, k=k:
                    e.matmul(PSY[0:16, 512:1024], st, mv, start=(k == 0), stop=(k == 15)))
                P.r(TE, lambda e, st=stV, mv=mv, k=k:
                    e.matmul(PSY[0:1, 1024:1536], st, mv, start=(k == 0), stop=(k == 15)),
                    inc=("t" if k == 15 else None))
            P.w(V, "t")
            P.r(V, lambda e, t0=t0: e.tensor_copy(xdbB[0:16, t0:t0 + 512], PSY[0:16, 0:512]))
            P.r(V, lambda e, t0=t0: e.tensor_copy(xdbC[0:16, t0:t0 + 512], PSY[0:16, 512:1024]))
            P.r(V, lambda e, t0=t0: e.tensor_copy(xdbV[0:1, t0:t0 + 512], PSY[0:1, 1024:1536]), inc="v")
            P.w(TE, "v")
    for zc in range(4):
        P.w(SY, "t")
        for k in range(8):
            P.d(SY, zwv[:, k * 512:(k + 1) * 512], ap["s_win"][2048 + zc * 512:2048 + (zc + 1) * 512, ts_(k, 128)], "q0", transpose=True)
        P.wall(TE)
        for i in range(NT):
            ps = PS[4 + (i % 2)]
            for k in range(8):
                st = xnfm[:, off_(k * T, i, 128)]
                mv = zwv[:, k * 512:(k + 1) * 512]
                P.r(TE, lambda e, ps=ps, st=st, mv=mv, k=k:
                    e.matmul(ps[:, 0:512], st, mv, start=(k == 0), stop=(k == 7)),
                    inc=("t" if k == 7 else None))
            P.w(S, "t")
            P.w(S, "q1")
            P.r(S, lambda e, ps=ps: e.activation(out=c_a[:, 0:512], in_=ps[:, 0:512], func=AF.Silu), inc="sc")
            P.w(S, "sc")
            P.d(S, ap["s_sz"][ts_(i, 128), zc * 512:(zc + 1) * 512], c_a[:, 0:512], "q1")
            P.w(TE, "sc")
    P.barrier()

    P.r(S, lambda e: e.activation(out=fsc[0:1, :], in_=xdbV[0:1, 0:T], func=AF.Exp), inc="sc")
    P.w(S, "sc")
    P.r(S, lambda e: e.activation(out=fsc[0:1, :], in_=fsc[0:1, :], func=AF.Ln, bias=1.0), inc="sc")
    P.w(V, "sc")
    P.r(V, lambda e: e.tensor_scalar(fsc[0:1, :], fsc[0:1, :], 0.001, 0.1, OP.max, OP.min), inc="v")
    P.r(V, lambda e: e.memset(w_b[0:1, 100:116], 1.0), inc="v")
    P.w(TE, "v")
    for half in range(4):
        P.r(TE, lambda e, half=half: e.matmul(PS[0][0:16, 0:512], w_b[0:1, 100:116],
                                              fsc[0:1, half * 512:(half + 1) * 512], start=True, stop=True), inc="t")
        P.w(V, "t")
        P.r(V, lambda e, half=half: e.tensor_copy(fzv[0:16, half * 512:(half + 1) * 512], PS[0][0:16, 0:512]), inc="v")
        P.w(TE, "v")
    P.r(V, lambda e: e.tensor_scalar(fsc[0:16, :], fzv[0:16, :], Acol[0:16, 0:1], None, OP.mult))
    P.r(V, lambda e: e.tensor_scalar(fsc[0:16, :], fsc[0:16, :], -20.0, 0.0, OP.max, OP.min), inc="v")
    P.r(V, lambda e: e.tensor_tensor(fL[0:16, :], fzv[0:16, :], xdbB[0:16, 0:T], OP.mult))
    P.r(V, lambda e: e.tensor_scalar(fL[0:16, :], fL[0:16, :], -10.0, 10.0, OP.max, OP.min))
    P.r(V, lambda e: e.tensor_copy(fzv[0:16, :], fL[0:16, :]), inc="v")
    P.r(V, lambda e: e.memset(f0z[:, :], 0.0), inc="v")
    for b in range(B):
        P.r(V, lambda e, b=b: e.tensor_tensor_scan(
            fL[0:16, b * LLOC:(b + 1) * LLOC], fsc[0:16, b * LLOC:(b + 1) * LLOC],
            f0z[0:16, :], 0.0, OP.add, OP.add), inc="v")
    P.barrier()
    P.r(V, lambda e: e.memset(fDst[:, :], 0.0), inc="v")
    for b in range(B):
        src3 = fL[0:16, b * LLOC:(b + 1) * LLOC].rearrange("p (n c) -> p n c", c=TC)[:, 0:NCH - 1, TC - 1:TC]
        dst3 = fDst[0:16, b * NCH + 1:(b + 1) * NCH][:, :, None]
        P.r(V, lambda e, s=src3, d=dst3: e.tensor_copy(d, s), inc="v")
    for b in range(B):
        P.r(V, lambda e, b=b: e.tensor_tensor(
            fsc[0:16, b * LLOC:(b + 1) * LLOC].rearrange("p (n c) -> p n c", c=TC),
            fL[0:16, b * LLOC:(b + 1) * LLOC].rearrange("p (n c) -> p n c", c=TC),
            fDst[0:16, b * NCH:(b + 1) * NCH][:, :, None].broadcast_to([16, NCH, TC]),
            OP.subtract), inc="v")
    P.w(S, "v")
    P.r(S, lambda e: e.activation(out=w_a[0:16, 0:T], in_=fsc[0:16, :], func=AF.Exp), inc="sc")
    P.r(S, lambda e: e.activation(out=w_b[0:16, 0:T], in_=fsc[0:16, :], func=AF.Exp, scale=-1.0), inc="sc")
    P.w(V, "sc")
    P.r(V, lambda e: e.tensor_tensor(fC[0:16, :], xdbC[0:16, 0:T], w_a[0:16, 0:T], OP.mult))
    P.r(V, lambda e: e.tensor_tensor(fB[0:16, :], fzv[0:16, :], w_b[0:16, 0:T], OP.mult), inc="v")
    for b in range(B):
        src3 = w_a[0:16, b * LLOC:(b + 1) * LLOC].rearrange("p (n c) -> p n c", c=TC)[:, :, TC - 1:TC]
        dst3 = fDch[0:16, b * NCH:(b + 1) * NCH][:, :, None]
        P.r(V, lambda e, s=src3, d=dst3: e.tensor_copy(d, s), inc="v")
    P.w(S, "v")
    P.r(S, lambda e: e.activation(out=w_b[0:16, 3000:3000 + B * NCH], in_=fDst[0:16, :], func=AF.Exp), inc="sc")
    P.w(V, "sc")
    for b in range(B):
        P.r(V, lambda e, b=b: e.tensor_tensor(
            fCc[0:16, b * LLOC:(b + 1) * LLOC].rearrange("p (n c) -> p n c", c=TC),
            fC[0:16, b * LLOC:(b + 1) * LLOC].rearrange("p (n c) -> p n c", c=TC),
            w_b[0:16, 3000 + b * NCH:3000 + (b + 1) * NCH][:, :, None].broadcast_to([16, NCH, TC]),
            OP.mult), inc="v")
    P.barrier()
    if "s_dbg" in _dbg:
        P.w(S, "v"); P.w(S, "sc")
        P.d(S, ap["s_dbg"][0:16, :], fL[0:16, :], "q1")
        P.d(S, ap["s_dbg"][16:32, :], fsc[0:16, :], "q1")
        P.d(S, ap["s_dbg"][32:48, 0:B * NCH], fDst[0:16, :], "q1")
        P.d(S, ap["s_dbg"][48:64, 0:B * NCH], fDch[0:16, :], "q1")
        P.d(S, ap["s_dbg"][64:80, :], xdbB[0:16, 0:T], "q1")
        P.d(S, ap["s_dbg"][80:96, :], xdbC[0:16, 0:T], "q1")
        P.d(S, ap["s_dbg"][96:112, :], fzv[0:16, 0:T], "q1")
        P.d(S, ap["s_dbg"][112:128, :], w_b[0:16, 0:T], "q1")
        P.w(SY, "q1"); P.w(V, "q1"); P.w(S, "q1")
    P.w(S, "v")
    P.d(S, ap["s_fb"][:, :], fB[0:16, :], "q1")
    P.w(SY, "q1"); P.w(V, "q1"); P.w(S, "q1")
    P.barrier()
    for b in range(B):
        for gid in range(NG):
            g_ = b * NG + gid
            dst = btT[:, g_ * 128:g_ * 128 + 16]
            src = ap["s_fb"][0:16, b * LLOC + gid * 128:(b * LLOC) + (gid + 1) * 128]
            P.d(SY, dst, src, "q0", transpose=True)
            P.w(SY, "q0")
            if g_ == 0:
                P.barrier()
    P.barrier()

    P.r(V, lambda e: e.memset(hcurb[0][0:16, :], 0.0))
    P.r(V, lambda e: e.memset(hcurb[1][0:16, :], 0.0), inc="v")
    for b in range(B):
        for gid in range(NG):
            xbuf = xtm[:, (gid % 2) * DI:((gid % 2) + 1) * DI]
            P.w(SY, "t")
            P.d(SY, xbuf, ap["s_xcf"][:, b * LLOC + gid * 128: b * LLOC + (gid + 1) * 128], "q0", transpose=True)
            P.wall(TE)
            for c in range(4):
                n = gid * 4 + c
                blk = (b * NG + gid) * 4 + c
                stm = fB[0:16, b * LLOC + n * TC: b * LLOC + (n + 1) * TC]
                mvm = fC[0:16, b * LLOC + n * TC: b * LLOC + (n + 1) * TC]
                P.r(TE, lambda e, stm=stm, mvm=mvm, c=c:
                    e.matmul(PSG[32 * c:32 * c + 32, 0:32], stm, mvm, start=True, stop=True,
                             tile_position=(0, 32 * c)), inc="t")
                P.w(V, "t")
                mcb = c_b[32 * c:32 * c + 32, (n % 2) * TC:((n % 2) + 1) * TC]
                P.r(V, lambda e, mcb=mcb, c=c: e.tensor_tensor(mcb, PSG[32 * c:32 * c + 32, 0:32], msk[32 * c:32 * c + 32, :], OP.mult), inc="v")
                P.w(TE, "v")
                for dc in range(4):
                    stg = btT[32 * c:32 * c + 32, (b * NG + gid) * 128:(b * NG + gid) * 128 + 16]
                    mvx = xbuf[32 * c:32 * c + 32, dc * 512:(dc + 1) * 512]
                    P.r(TE, lambda e, stg=stg, mvx=mvx, c=c, dc=dc:
                        e.matmul(PSG[0:16, dc * 512:(dc + 1) * 512], stg, mvx, start=True, stop=True,
                                 tile_position=(32 * c, 0)),
                        inc=("t" if dc == 3 else None))
                P.w(V, "t")
                P.r(V, lambda e, b=b: e.tensor_copy(hstg[0:16, :], hcurb[b][0:16, :]), inc="v")
                P.r(V, lambda e, b=b: e.tensor_tensor(w_a[0:16, 0:DI], hcurb[b][0:16, :], PSG[0:16, 0:DI], OP.add))
                P.r(V, lambda e, b=b, n=n: e.tensor_scalar(hcurb[b][0:16, :], w_a[0:16, 0:DI], fDch[0:16, b * NCH + n:b * NCH + n + 1], None, OP.mult), inc="v")
                P.w(TE, "v")
                for dc in range(4):
                    mvx = xbuf[32 * c:32 * c + 32, dc * 512:(dc + 1) * 512]
                    stc = fC[0:16, b * LLOC + n * TC: b * LLOC + (n + 1) * TC]
                    mvh = hstg[0:16, dc * 512:(dc + 1) * 512]
                    P.r(TE, lambda e, mcb=mcb, mvx=mvx, c=c, dc=dc:
                        e.matmul(PSY[32 * c:32 * c + 32, dc * 512:(dc + 1) * 512], mcb, mvx, start=True, stop=False,
                                 tile_position=(32 * c, 32 * c)))
                    P.r(TE, lambda e, stc=stc, mvh=mvh, c=c, dc=dc:
                        e.matmul(PSY[32 * c:32 * c + 32, dc * 512:(dc + 1) * 512], stc, mvh, start=False, stop=True,
                                 tile_position=(0, 32 * c)),
                        inc=("t" if dc == 3 else None))
            P.w(V, "t"); P.w(V, "q1")
            P.r(V, lambda e: e.tensor_copy(c_a[:, 0:DI], PSY[:, 0:DI]), inc="v")
            P.w(S, "v")
            P.d(S, ap["s_y"][ts_(b * NG + gid, 128), :], c_a[:, 0:DI], "q1")
            P.w(TE, "q1"); P.w(V, "q1"); P.w(S, "q1")
    P.w(S, "v")
    P.r(S, lambda e: e.activation(out=p_a[0:16, 16:17], in_=fL[0:16, LLOC - 1:LLOC], func=AF.Exp), inc="sc")
    P.r(S, lambda e: e.activation(out=p_a[0:16, 17:18], in_=fL[0:16, 2 * LLOC - 1:2 * LLOC], func=AF.Exp), inc="sc")
    P.wall(SY)
    P.d(SY, ap["cc_in"][0:16, 0:DI], hcurb[0][0:16, :], "q0")
    P.d(SY, ap["cc_in"][16:32, 0:DI], hcurb[1][0:16, :], "q0")
    P.d(SY, ap["cc_in"][0:16, DI:DI + 1], p_a[0:16, 16:17], "q0", slow=True)
    P.d(SY, ap["cc_in"][16:32, DI:DI + 1], p_a[0:16, 17:18], "q0", slow=True)
    P.w(G, "q0")
    P.r(G, lambda e: e.collective_compute("AllGather", OP.bypass, replica_groups=[list(range(NCORES))],
                                          ins=[ap["cc_in"][:]], outs=[ap["cc_out"][:]]), inc="cc")
    P.w(SY, "cc")
    P.wall(V)
    P.r(V, lambda e: e.memset(hin[0:32, :], 0.0), inc="v")
    for r in range(NCORES - 1):
        P.w(V, "q1")
        P.wall(SY)
        P.d(SY, w_b[0:32, 0:DI + 1], ap["cc_out"][r], "q0")
        P.wall(V)
        Dr = w_b[0:32, DI:DI + 1]
        Gr = w_b[0:32, 0:DI]
        mr = rmk[:, r + 1:r + 2]
        P.r(V, lambda e, Dr=Dr: e.tensor_scalar(p_a[0:32, 20:21], Dr, -1.0, None, OP.add))
        P.r(V, lambda e, mr=mr: e.scalar_tensor_tensor(p_a[0:32, 21:22], p_a[0:32, 20:21], mr, onec[0:32, 0:1], OP.mult, OP.add), inc="v")
        P.r(V, lambda e: e.tensor_scalar(hin[0:32, :], hin[0:32, :], p_a[0:32, 21:22], None, OP.mult), inc="v")
        P.r(V, lambda e, Gr=Gr, mr=mr: e.scalar_tensor_tensor(hin[0:32, :], Gr, mr, hin[0:32, :], OP.mult, OP.add), inc="v")
    P.r(V, lambda e: e.tensor_copy(hinb[0:16, 0:DI], hin[0:16, :]), inc="v")
    P.wall(SY)
    P.d(SY, ap["cc_in"][0:16, 0:DI], hin[16:32, :], "q0")
    P.w(G, "q0")
    P.d(G, hinb2[0:16, 0:DI], ap["cc_in"][0:16, 0:DI], "q2")
    P.barrier()

    for i in range(NT):
        b = i // 8; j = i % 8
        P.wall(TE)
        for dc in range(4):
            stc = fCc[0:16, b * LLOC + j * 128:(b * LLOC + j * 128) + 128]
            hb = hinb if b == 0 else hinb2
            mvh = hb[0:16, dc * 512:(dc + 1) * 512]
            P.r(TE, lambda e, stc=stc, mvh=mvh, dc=dc:
                e.matmul(PS[dc][:, 0:512], stc, mvh, start=True, stop=True), inc="t")
        P.d(SY, c_a[:, 0:DI], ap["s_y"][ts_(i, 128), :], "q0")
        P.d(SY, c_b[:, 0:DI], ap["s_sz"][ts_(i, 128), :], "q0")
        P.d(SY, c_c[:, 0:DI], ap["s_xcf"][:, 0:0 + 0] if False else c_c[:, 0:DI], "q0") if False else None
        P.d(SY, c_c[:, 0:DI], ap["s_xcf"][:, b * LLOC + j * 128: b * LLOC + (j + 1) * 128], "q0", transpose=True)
        P.wall(V)
        for dc in range(4):
            P.r(V, lambda e, dc=dc: e.tensor_tensor(w_a[:, dc * 512:(dc + 1) * 512], c_a[:, dc * 512:(dc + 1) * 512], PS[dc][:, 0:512], OP.add), inc="v")
        P.w(TE, "v")
        P.r(V, lambda e: e.tensor_tensor(w_a[:, 0:DI], w_a[:, 0:DI], c_b[:, 0:DI], OP.mult), inc="v")
        P.r(V, lambda e: e.tensor_tensor(c_c[:, 0:DI], c_c[:, 0:DI], dprow[:, :], OP.mult))
        P.r(V, lambda e: e.tensor_tensor(c_a[:, 0:DI], w_a[:, 0:DI], c_c[:, 0:DI], OP.add), inc="v")
        P.w(S, "v")
        P.d(S, ap["s_yg"][ts_(i, 128), :], c_a[:, 0:DI], "q1")
        P.w(SY, "q1"); P.w(V, "q1"); P.w(S, "q1")
    P.barrier()

    ygfm = big1
    woT = bufA
    for k in range(16):
        P.d(SY, ygfm[:, ts_(k, T)], ap["s_yg"][:, ts_(k, 128)], "q0", transpose=True)
        P.d(SY, woT[:, ts_(k, D)], ap["s_wout"][:, ts_(k, 128)], "q0", transpose=True)
    P.wall(TE)
    for i in range(NT):
        for mc in range(2):
            ps = PS[mc]
            for k in range(16):
                st = ygfm[:, off_(k * T, i, 128)]
                mv = woT[:, k * D + mc * 512: k * D + mc * 512 + 512]
                P.r(TE, lambda e, ps=ps, st=st, mv=mv, k=k:
                    e.matmul(ps[:, 0:512], st, mv, start=(k == 0), stop=(k == 15)),
                    inc=("t" if k == 15 else None))
        P.w(V, "t"); P.w(V, "q1")
        P.r(V, lambda e: e.tensor_copy(c_a[:, 0:512], PS[0][:, 0:512]))
        P.r(V, lambda e: e.tensor_copy(c_a[:, 512:1024], PS[1][:, 0:512]), inc="v")
        P.w(TE, "v"); P.w(S, "v")
        P.d(S, ap["s_s"][ts_(i, 128), :], c_a[:, 0:D], "q1")
    P.barrier()

    xqfm = A16[:, 0:16384]
    for k in range(8):
        P.d(SY, xqfm[:, ts_(k, T)], ap["s_xq"][:, ts_(k, 128)], "q0", transpose=True)
    P.r(V, lambda e: e.memset(p_a[:, 24:25], 0.0), inc="v")
    P.r(V, lambda e: e.memset(sx2i[:, :], 0.0), inc="v")
    for oc in range(8):
        P.w(SY, "t")
        for k in range(8):
            P.d(SY, wmov[:, ts_(k, 512)], ap["s_f0wg"][oc * 512:(oc + 1) * 512, ts_(k, 128)], "q0", transpose=True)
            P.d(SY, wmov2[:, ts_(k, 512)], ap["s_f0wu"][oc * 512:(oc + 1) * 512, ts_(k, 128)], "q0", transpose=True)
        P.wall(TE)
        for i in range(NT):
            for k in range(8):
                st = xqfm[:, off_(k * T, i, 128)]
                P.r(TE, lambda e, st=st, k=k: e.matmul(PS[0][:, 0:512], st, wmov[:, ts_(k, 512)], start=(k == 0), stop=(k == 7)), inc=("t" if k == 7 else None))
            for k in range(8):
                st = xqfm[:, off_(k * T, i, 128)]
                P.r(TE, lambda e, st=st, k=k: e.matmul(PS[1][:, 0:512], st, wmov2[:, ts_(k, 512)], start=(k == 0), stop=(k == 7)), inc=("t" if k == 7 else None))
            P.w(S, "t")
            P.r(S, lambda e, i=i: e.activation(out=w_a[:, 0:512], in_=PS[0][:, 0:512], func=AF.Silu, scale=sxg[:, i:i + 1]), inc="sc")
            P.r(S, lambda e, i=i: e.activation(out=w_b[:, 0:512], in_=PS[1][:, 0:512], func=AF.Identity, scale=sxu[:, i:i + 1], bias=0.0), inc="sc")
            P.w(V, "sc")
            P.w(TE, "sc")
            P.r(V, lambda e: e.tensor_tensor(w_a[:, 0:512], w_a[:, 0:512], w_b[:, 0:512], OP.mult))
            P.r(V, lambda e, i=i: e.reduce_max(p_a[:, 25:26], w_a[:, 0:512], AX.X, apply_absolute_value=True))
            P.r(V, lambda e, i=i: e.tensor_tensor(sx2i[:, i:i + 1], sx2i[:, i:i + 1], p_a[:, 25:26], OP.max), inc="v")
            P.w(S, "v")
            P.d(S, ap["s_h2"][ts_(i, 128), oc * 512:(oc + 1) * 512], w_a[:, 0:512], "q1")
            P.w(V, "q1"); P.w(S, "q1")
    P.r(V, lambda e: e.tensor_scalar(sx2i[:, :], sx2i[:, :], 1e-5, 1.0 / 127.0, OP.max, OP.mult))
    P.r(V, lambda e: e.tensor_scalar(sxd[:, :], sx2i[:, :], mgr[:, 2:3], None, OP.mult), inc="v")
    P.barrier()
    for i in range(NT):
        P.d(SY, w_a[:, 0:4096], ap["s_h2"][ts_(i, 128), :], "q0")
        P.wall(V)
        P.r(V, lambda e, i=i: e.reciprocal(p_a[:, 26:27], sx2i[:, i:i + 1]), inc="v")
        P.w(S, "v")
        P.r(S, lambda e: e.activation(out=w_b[:, 0:4096], in_=w_a[:, 0:4096], func=AF.Identity, scale=p_a[:, 26:27], bias=0.0), inc="sc")
        P.w(V, "sc")
        P.r(V, lambda e: e.tensor_copy(i_a[:, 0:4096], w_b[:, 0:4096]))
        P.r(V, lambda e: e.tensor_copy(c_a[:, 0:4096], i_a[:, 0:4096]), inc="v")
        P.w(S, "v")
        P.d(S, ap["s_xq2"][ts_(i, 128), :], c_a[:, 0:4096], "q1")
        P.w(SY, "q1"); P.w(V, "q1"); P.w(S, "q1")
    xq2fm = big1
    for half in range(2):
        P.w(SY, "t")
        for k in range(16):
            kk = half * 16 + k
            P.d(SY, xq2fm[:, ts_(k, T)], ap["s_xq2"][:, ts_(kk, 128)], "q0", transpose=True)
            P.d(SY, wmov[:, ts_(k, 512)], ap["s_f0wd"][0:512, ts_(kk, 128)], "q0", transpose=True)
            P.d(SY, wmov2[:, ts_(k, 512)], ap["s_f0wd"][512:1024, ts_(kk, 128)], "q0", transpose=True)
        if half == 0 and "s_dbg3" in _dbg:
            P.w(S, "q0")
            P.d(S, ap["s_dbg3"][:, 0:T], xq2fm[:, 0:T], "q1")
            P.d(S, ap["s_dbg3"][:, T:T + 512], wmov[:, 0:512], "q1")
            P.w(SY, "q1"); P.w(V, "q1"); P.w(S, "q1")
        P.wall(TE)
        for i in range(NT):
            for mc in range(2):
                ps = PS[mc]
                wv = wmov if mc == 0 else wmov2
                for k in range(16):
                    st = xq2fm[:, off_(k * T, i, 128)]
                    P.r(TE, lambda e, ps=ps, st=st, wv=wv, k=k:
                        e.matmul(ps[:, 0:512], st, wv[:, ts_(k, 512)], start=(k == 0), stop=(k == 15)),
                        inc=("t" if k == 15 else None))
            if half == 0:
                P.w(V, "t")
                P.r(V, lambda e: e.tensor_copy(w_a[:, 0:512], PS[0][:, 0:512]))
                P.r(V, lambda e: e.tensor_copy(w_a[:, 512:1024], PS[1][:, 0:512]), inc="v")
                P.w(TE, "v"); P.w(S, "v")
                P.d(S, ap["s_h2"][ts_(i, 128), 0:1024], w_a[:, 0:1024], "q1")
                P.w(V, "q1")
            else:
                P.w(SY, "v")
                P.d(SY, w_b[:, 0:1024], ap["s_h2"][ts_(i, 128), 0:1024], "q0")
                P.w(V, "t"); P.wall(V)
                P.r(V, lambda e: e.tensor_tensor(w_a[:, 0:512], PS[0][:, 0:512], w_b[:, 0:512], OP.add))
                P.r(V, lambda e: e.tensor_tensor(w_a[:, 512:1024], PS[1][:, 0:512], w_b[:, 512:1024], OP.add), inc="v")
                P.w(TE, "v"); P.w(S, "v"); P.w(S, "q1")
                P.r(S, lambda e, i=i: e.activation(out=c_b[:, 0:D], in_=w_a[:, 0:D], func=AF.Silu, scale=sxd[:, i:i + 1]), inc="sc")
                P.w(S, "sc")
                P.d(S, ap["s_g"][ts_(i, 128), :], c_b[:, 0:D], "q1")
                P.w(V, "sc")
    P.barrier()

    for i in range(NT):
        P.d(SY, w_b[:, 0:D], xr[i], "q0")
        P.d(SY, c_a[:, 0:D], ap["s_s"][ts_(i, 128), :], "q0")
        P.d(SY, c_b[:, 0:D], ap["s_g"][ts_(i, 128), :], "q0")
        P.wall(V)
        P.r(V, lambda e: e.tensor_tensor(c_a[:, 0:D], c_a[:, 0:D], c_b[:, 0:D], OP.mult), inc="v")
        P.r(V, lambda e: e.tensor_tensor(w_a[:, 0:D], w_b[:, 0:D], c_a[:, 0:D], OP.add), inc="v")
        P.w(S, "v")
        P.d(S, ap["s_x1"][ts_(i, 128), :], w_a[:, 0:D], "q1")
        P.w(V, "q1"); P.w(S, "q1")
        mu = p_a[:, 0:1]; vv = p_a[:, 1:2]; rr = p_a[:, 2:3]
        P.r(V, lambda e, mu=mu: e.reduce_sum(mu, w_a[:, 0:D], AX.X))
        P.r(V, lambda e, mu=mu: e.tensor_scalar_mul(mu, mu, 1.0 / D), inc="v")
        P.w(S, "v")
        P.r(S, lambda e: e.activation(out=w_b[:, 0:D], in_=w_a[:, 0:D], func=AF.Square), inc="sc")
        P.w(V, "sc")
        P.r(V, lambda e, vv=vv: e.reduce_sum(vv, w_b[:, 0:D], AX.X))
        P.r(V, lambda e, vv=vv: e.tensor_scalar_mul(vv, vv, 1.0 / D))
        P.r(V, lambda e, mu=mu, rr=rr: e.tensor_tensor(rr, mu, mu, OP.mult))
        P.r(V, lambda e, vv=vv, rr=rr: e.tensor_tensor(rr, vv, rr, OP.subtract))
        P.r(V, lambda e, rr=rr: e.tensor_scalar_add(rr, rr, 1e-5), inc="v")
        P.w(S, "v")
        P.r(S, lambda e, rr=rr: e.activation(out=rr, in_=rr, func=AF.Sqrt), inc="sc")
        P.w(V, "sc")
        P.r(V, lambda e, rr=rr: e.reciprocal(rr, rr))
        P.r(V, lambda e, mu=mu, rr=rr: e.tensor_scalar(w_a[:, 0:D], w_a[:, 0:D], mu, rr, OP.subtract, OP.mult))
        P.r(V, lambda e: e.tensor_copy(c_b[:, 0:D], w_a[:, 0:D]), inc="v")
        P.w(S, "v")
        P.d(S, ap["s_xn1"][ts_(i, 128), :], c_b[:, 0:D], "q1")
        P.w(V, "q1"); P.w(S, "q1")
        fp4_on_tile(128, D)
        P.w(S, "v")
        P.d(S, ap["s_xq1"][ts_(i, 128), :], c_a[:, 0:D], "q1")
        P.w(SY, "q1"); P.w(V, "q1"); P.w(S, "q1")
    P.barrier()

    xq1fm = A16[:, 0:16384]
    for k in range(8):
        P.d(SY, xq1fm[:, ts_(k, T)], ap["s_xq1"][:, ts_(k, 128)], "q0", transpose=True)
    OCW = [512, 512, 512, 512, 512, 256]
    ocs = [0, 512, 1024, 1536, 2048, 2560]
    for oc in range(6):
        ow = OCW[oc]; o0 = ocs[oc]
        P.w(SY, "t")
        for k in range(8):
            P.d(SY, wmov[:, k * 512:k * 512 + ow], ap["s_f1wg"][o0:o0 + ow, ts_(k, 128)], "q0", transpose=True)
            P.d(SY, wmov2[:, k * 512:k * 512 + ow], ap["s_f1wu"][o0:o0 + ow, ts_(k, 128)], "q0", transpose=True)
        P.wall(TE)
        for i in range(NT):
            for k in range(8):
                st = xq1fm[:, off_(k * T, i, 128)]
                P.r(TE, lambda e, st=st, k=k, ow=ow: e.matmul(PS[0][:, 0:ow], st, wmov[:, k * 512:k * 512 + ow], start=(k == 0), stop=(k == 7)), inc=("t" if k == 7 else None))
            for k in range(8):
                st = xq1fm[:, off_(k * T, i, 128)]
                P.r(TE, lambda e, st=st, k=k, ow=ow: e.matmul(PS[1][:, 0:ow], st, wmov2[:, k * 512:k * 512 + ow], start=(k == 0), stop=(k == 7)), inc=("t" if k == 7 else None))
            P.w(S, "t")
            P.r(S, lambda e, ow=ow: e.activation(out=w_a[:, 0:ow], in_=PS[0][:, 0:ow], func=AF.Silu), inc="sc")
            P.r(S, lambda e, ow=ow: e.activation(out=w_b[:, 3000:3000 + ow] if False else w_b[:, 0:ow], in_=PS[1][:, 0:ow], func=AF.Identity, bias=0.0), inc="sc")
            P.w(V, "sc"); P.w(TE, "sc")
            P.r(V, lambda e, ow=ow: e.tensor_tensor(w_a[:, 0:ow], w_a[:, 0:ow], w_b[:, 0:ow], OP.mult), inc="v")
            fp4_on_tile(128, ow)
            P.w(S, "v")
            P.d(S, ap["s_xq3"][ts_(i, 128), o0:o0 + ow], c_a[:, 0:ow], "q1")
            P.w(V, "q1"); P.w(S, "q1")
    P.barrier()

    xq3fm = big1
    for half in range(2):
        P.w(SY, "t")
        for k in range(11):
            kk = half * 11 + k
            P.d(SY, xq3fm[:, ts_(k, T)], ap["s_xq3"][:, ts_(kk, 128)], "q0", transpose=True)
            P.d(SY, wmov[:, ts_(k, 512)], ap["s_f1wd"][0:512, ts_(kk, 128)], "q0", transpose=True)
            P.d(SY, wmov2[:, ts_(k, 512)], ap["s_f1wd"][512:1024, ts_(kk, 128)], "q0", transpose=True)
        P.wall(TE)
        for i in range(NT):
            for mc in range(2):
                ps = PS[mc]
                wv = wmov if mc == 0 else wmov2
                for k in range(11):
                    st = xq3fm[:, off_(k * T, i, 128)]
                    P.r(TE, lambda e, ps=ps, st=st, wv=wv, k=k:
                        e.matmul(ps[:, 0:512], st, wv[:, ts_(k, 512)], start=(k == 0), stop=(k == 10)),
                        inc=("t" if k == 10 else None))
            if half == 0:
                P.w(V, "t")
                P.r(V, lambda e: e.tensor_copy(w_a[:, 0:512], PS[0][:, 0:512]))
                P.r(V, lambda e: e.tensor_copy(w_a[:, 512:1024], PS[1][:, 0:512]), inc="v")
                P.w(TE, "v"); P.w(S, "v")
                P.d(S, ap["s_h2"][ts_(i, 128), 0:1024], w_a[:, 0:1024], "q1")
                P.w(V, "q1")
            else:
                P.w(SY, "v")
                P.d(SY, w_b[:, 0:1024], ap["s_h2"][ts_(i, 128), 0:1024], "q0")
                P.w(V, "t"); P.wall(V)
                P.r(V, lambda e: e.tensor_tensor(w_a[:, 0:512], PS[0][:, 0:512], w_b[:, 0:512], OP.add))
                P.r(V, lambda e: e.tensor_tensor(w_a[:, 512:1024], PS[1][:, 0:512], w_b[:, 512:1024], OP.add), inc="v")
                P.w(TE, "v"); P.w(S, "v"); P.w(S, "q1")
                P.r(S, lambda e: e.activation(out=c_b[:, 0:D], in_=w_a[:, 0:D], func=AF.Silu), inc="sc")
                P.w(S, "sc")
                P.d(S, ap["s_g1"][ts_(i, 128), :], c_b[:, 0:D], "q1")
                P.w(V, "sc")
    P.barrier()

    xn1fm = big1[:, 0:16384]
    for k in range(8):
        P.d(SY, xn1fm[:, ts_(k, T)], ap["s_xn1"][:, ts_(k, 128)], "q0", transpose=True)
    QCW = [512, 512, 512, 512, 512, 512, 32]
    qcs = [0, 512, 1024, 1536, 2048, 2560, 3072]
    for oc in range(7):
        ow = QCW[oc]; o0 = qcs[oc]
        P.w(SY, "t")
        for k in range(8):
            P.d(SY, wmov[:, k * 512:k * 512 + ow], ap["s_qkvb"][o0:o0 + ow, ts_(k, 128)], "q0", transpose=True)
        P.wall(TE)
        for i in range(NT):
            for k in range(8):
                st = xn1fm[:, off_(k * T, i, 128)]
                P.r(TE, lambda e, st=st, k=k, ow=ow: e.matmul(PS[0][:, 0:ow], st, wmov[:, k * 512:k * 512 + ow], start=(k == 0), stop=(k == 7)), inc=("t" if k == 7 else None))
            P.w(V, "t")
            P.r(V, lambda e, ow=ow: e.tensor_copy(c_a[:, 0:ow], PS[0][:, 0:ow]), inc="v")
            P.w(TE, "v"); P.w(S, "v")
            P.d(S, ap["s_qkv"][ts_(i, 128), o0:o0 + ow], c_a[:, 0:ow], "q1")
            P.w(V, "q1"); P.w(S, "q1")
    P.barrier()

    def elu1_norm(dst_bf, src_bf, tmpf):
        P.r(V, lambda e: e.tensor_scalar(w_a[:, 0:1024], src_bf, 0.0, None, OP.min), inc="v")
        P.w(S, "v")
        P.r(S, lambda e: e.activation(out=w_a[:, 1024:2048], in_=w_a[:, 0:1024], func=AF.Exp), inc="sc")
        P.w(V, "sc")
        P.r(V, lambda e: e.tensor_scalar(w_a[:, 0:1024], src_bf, 0.0, None, OP.max))
        P.r(V, lambda e: e.tensor_tensor(w_a[:, 0:1024], w_a[:, 0:1024], w_a[:, 1024:2048], OP.add), inc="v")
        a3 = w_a[:, 0:1024].rearrange("p (h d) -> p h d", d=64)
        P.r(V, lambda e, a3=a3: e.tensor_reduce(p_a[:, 32:48], a3, AX.X, OP.add))
        P.r(V, lambda e: e.reciprocal(p_a[:, 32:48], p_a[:, 32:48]), inc="v")
        rsb = p_a[:, 32:48][:, :, None].broadcast_to([128, 16, 64])
        P.r(V, lambda e, a3=a3, rsb=rsb, dst_bf=dst_bf: e.tensor_tensor(
            dst_bf.rearrange("p (h d) -> p h d", d=64), a3, rsb, OP.mult), inc="v")

    for j in range(8):
        P.wall(V)
        P.d(SY, c_a[:, 0:3104], ap["s_qkv"][ts_(j, 128), :], "q0")
        P.d(SY, c_b[:, 0:3104], ap["s_qkv"][ts_(8 + j, 128), :], "q0")
        P.wall(V)
        elu1_norm(c_c[:, 0:1024], c_a[:, 0:1024], 0)
        elu1_norm(c_c[:, 1024:2048], c_a[:, 1024:2048], 0)
        elu1_norm(c_c[:, 2048:3072], c_b[:, 0:1024], 0)
        elu1_norm(c_b[:, 0:1024], c_b[:, 1024:2048], 0)
        P.r(V, lambda e: e.tensor_scalar(w_a[:, 0:16], c_a[:, 3072:3088], -1.0, None, OP.mult))
        P.r(V, lambda e: e.tensor_scalar(w_a[:, 16:32], c_b[:, 3072:3088], -1.0, None, OP.mult), inc="v")
        P.w(S, "v")
        P.r(S, lambda e: e.activation(out=w_a[:, 0:32], in_=w_a[:, 0:32], func=AF.Exp), inc="sc")
        P.w(V, "sc")
        P.r(V, lambda e: e.tensor_scalar_add(w_a[:, 0:32], w_a[:, 0:32], 1.0))
        P.r(V, lambda e: e.reciprocal(w_a[:, 0:32], w_a[:, 0:32]), inc="v")
        b0c = w_a[:, 0:16]; b1c = w_a[:, 16:32]
        def dot(dst_cols, aa, bb):
            P.r(V, lambda e, aa=aa, bb=bb: e.tensor_tensor(w_b[:, 0:1024], aa, bb, OP.mult), inc="v")
            a3 = w_b[:, 0:1024].rearrange("p (h d) -> p h d", d=64)
            P.r(V, lambda e, a3=a3, dst_cols=dst_cols: e.tensor_reduce(p_a[:, dst_cols:dst_cols + 16], a3, AX.X, OP.add), inc="v")
        q0n = c_c[:, 0:1024]; k0n = c_c[:, 1024:2048]; q1n = c_c[:, 2048:3072]; k1n = c_b[:, 0:1024]
        dot(32, q0n, k0n)
        dot(48, q1n, k0n)
        dot(16, q1n, k1n)
        dot(0, k1n, k0n)
        v0 = c_a[:, 2048:3072]; v1 = c_b[:, 2048:3072]
        b0b = b0c[:, :, None].broadcast_to([128, 16, 64])
        P.r(V, lambda e, b0b=b0b, v0=v0: e.tensor_tensor(
            c_a[:, 0:1024].rearrange("p (h d) -> p h d", d=64), v0.rearrange("p (h d) -> p h d", d=64), b0b, OP.mult), inc="v")
        vi0 = c_a[:, 0:1024]
        d_q0k0 = p_a[:, 32:48][:, :, None].broadcast_to([128, 16, 64])
        P.r(V, lambda e, d_q0k0=d_q0k0, vi0=vi0: e.tensor_tensor(
            c_a[:, 1024:2048].rearrange("p (h d) -> p h d", d=64), vi0.rearrange("p (h d) -> p h d", d=64), d_q0k0, OP.mult), inc="v")
        d_k1k0 = p_a[:, 0:16][:, :, None].broadcast_to([128, 16, 64])
        P.r(V, lambda e, d_k1k0=d_k1k0, vi0=vi0: e.tensor_tensor(
            w_b[:, 0:1024].rearrange("p (h d) -> p h d", d=64), vi0.rearrange("p (h d) -> p h d", d=64), d_k1k0, OP.mult), inc="v")
        P.r(V, lambda e, v1=v1: e.tensor_tensor(w_b[:, 0:1024], v1, w_b[:, 0:1024], OP.subtract), inc="v")
        b1b = b1c[:, :, None].broadcast_to([128, 16, 64])
        P.r(V, lambda e, b1b=b1b: e.tensor_tensor(
            w_b[:, 0:1024].rearrange("p (h d) -> p h d", d=64), w_b[:, 0:1024].rearrange("p (h d) -> p h d", d=64), b1b, OP.mult), inc="v")
        vi1 = w_b[:, 0:1024]
        d_q1k0 = p_a[:, 48:64][:, :, None].broadcast_to([128, 16, 64])
        d_q1k1 = p_a[:, 16:32][:, :, None].broadcast_to([128, 16, 64])
        P.r(V, lambda e, d_q1k0=d_q1k0, vi0=vi0: e.tensor_tensor(
            w_b[:, 1024:2048].rearrange("p (h d) -> p h d", d=64), vi0.rearrange("p (h d) -> p h d", d=64), d_q1k0, OP.mult), inc="v")
        P.r(V, lambda e, d_q1k1=d_q1k1, vi1=vi1: e.tensor_tensor(
            w_b[:, 2048:3072].rearrange("p (h d) -> p h d", d=64), vi1.rearrange("p (h d) -> p h d", d=64), d_q1k1, OP.mult), inc="v")
        P.r(V, lambda e: e.tensor_tensor(w_b[:, 1024:2048], w_b[:, 1024:2048], w_b[:, 2048:3072], OP.add), inc="v")
        P.r(V, lambda e: e.tensor_scalar(c_a[:, 1024:2048], c_a[:, 1024:2048], 0.125, None, OP.mult))
        P.r(V, lambda e: e.tensor_scalar(c_b[:, 1024:2048], w_b[:, 1024:2048], 0.125, None, OP.mult), inc="v")
        P.w(S, "v")
        P.d(S, ap["s_lay"][ts_(j, 128), :], c_a[:, 1024:2048], "q1")
        P.d(S, ap["s_lay"][ts_(8 + j, 128), :], c_b[:, 1024:2048], "q1")
        P.w(SY, "q1"); P.w(V, "q1"); P.w(S, "q1")
    P.barrier()

    layfm = big1[:, 0:16384]
    P.d(G, lngrow[:, :], ap["ln_g"][None, :].broadcast_to([128, D]), "q2")
    P.d(G, lnbrow[:, :], ap["ln_b"][None, :].broadcast_to([128, D]), "q2")
    for k in range(8):
        P.d(SY, layfm[:, ts_(k, T)], ap["s_lay"][:, ts_(k, 128)], "q0", transpose=True)
    for k in range(8):
        P.d(SY, wmov[:, k * 512:(k + 1) * 512], ap["s_wo"][0:512, ts_(k, 128)], "q0", transpose=True)
        P.d(SY, wmov2[:, k * 512:(k + 1) * 512], ap["s_wo"][512:1024, ts_(k, 128)], "q0", transpose=True)
    P.barrier()
    P.wall(TE)
    for i in range(NT):
        for mc in range(2):
            ps = PS[mc]
            wv = wmov if mc == 0 else wmov2
            for k in range(8):
                st = layfm[:, off_(k * T, i, 128)]
                P.r(TE, lambda e, ps=ps, st=st, wv=wv, k=k:
                    e.matmul(ps[:, 0:512], st, wv[:, ts_(k, 512)], start=(k == 0), stop=(k == 7)),
                    inc=("t" if k == 7 else None))
        if i == 1:
            P.barrier()
        P.d(SY, c_a[:, 0:D], ap["s_xn1"][ts_(i, 128), :], "q0")
        P.d(SY, w_b[:, 0:D], ap["s_x1"][ts_(i, 128), :], "q0")
        P.w(V, "t"); P.wall(V)
        P.r(V, lambda e: e.tensor_tensor(w_a[:, 0:512], PS[0][:, 0:512], c_a[:, 0:512], OP.add))
        P.r(V, lambda e: e.tensor_tensor(w_a[:, 512:1024], PS[1][:, 0:512], c_a[:, 512:1024], OP.add), inc="v")
        P.w(TE, "v")
        mu = p_a[:, 0:1]; vv = p_a[:, 1:2]; rr = p_a[:, 2:3]
        P.r(V, lambda e, mu=mu: e.reduce_sum(mu, w_a[:, 0:D], AX.X))
        P.r(V, lambda e, mu=mu: e.tensor_scalar_mul(mu, mu, 1.0 / D), inc="v")
        P.w(S, "v")
        P.r(S, lambda e: e.activation(out=c_b[:, 0:D], in_=w_a[:, 0:D], func=AF.Square), inc="sc")
        P.w(V, "sc")
        P.r(V, lambda e, vv=vv: e.reduce_sum(vv, c_b[:, 0:D], AX.X))
        P.r(V, lambda e, vv=vv: e.tensor_scalar_mul(vv, vv, 1.0 / D))
        P.r(V, lambda e, mu=mu, rr=rr: e.tensor_tensor(rr, mu, mu, OP.mult))
        P.r(V, lambda e, vv=vv, rr=rr: e.tensor_tensor(rr, vv, rr, OP.subtract))
        P.r(V, lambda e, rr=rr: e.tensor_scalar_add(rr, rr, 1e-5), inc="v")
        P.w(S, "v")
        P.r(S, lambda e, rr=rr: e.activation(out=rr, in_=rr, func=AF.Sqrt), inc="sc")
        P.w(V, "sc")
        P.r(V, lambda e, rr=rr: e.reciprocal(rr, rr))
        P.r(V, lambda e, mu=mu, rr=rr: e.tensor_scalar(w_a[:, 0:D], w_a[:, 0:D], mu, rr, OP.subtract, OP.mult))
        P.r(V, lambda e: e.tensor_tensor(w_a[:, 0:D], w_a[:, 0:D], lngrow[:, :], OP.mult))
        P.r(V, lambda e: e.tensor_tensor(w_a[:, 0:D], w_a[:, 0:D], lnbrow[:, :], OP.add), inc="v")
        P.w(SY, "v")
        P.d(SY, c_b[:, 0:D], ap["s_g1"][ts_(i, 128), :], "q0")
        P.wall(V)
        P.r(V, lambda e: e.tensor_tensor(w_a[:, 0:D], w_a[:, 0:D], c_b[:, 0:D], OP.mult), inc="v")
        P.r(V, lambda e: e.tensor_tensor(w_a[:, 0:D], w_a[:, 0:D], w_b[:, 0:D], OP.add), inc="v")
        P.w(S, "v")
        b = i // 8; j = i % 8
        P.d(S, ap["out"][b, ts_(j, 128), :], w_a[:, 0:D], "q1")
        P.w(SY, "q1"); P.w(V, "q1"); P.w(S, "q1"); P.w(G, "q1")
        if i == 0:
            P.barrier()
    P.wall(SY); P.wall(G); P.wall(V)

    import os as _os
    _cut = int(_os.environ.get("MK_CUT", "-1"))
    if _cut >= 0 and hasattr(P, "marks") and _cut < len(P.marks):
        for e in ENGINES:
            P.stream[e] = P.stream[e][:P.marks[_cut][e]]
    blkc = nc.Block(); blk = blkc.__enter__()

    def emit(eng_name):
        def _f(e):
            for fn in P.stream[eng_name]:
                fn(e)
        return _f
    blk.sync(emit("sync"))
    blk.vector(emit("vector"))
    blk.scalar(emit("scalar"))
    blk.gpsimd(emit("gpsimd"))
    blk.tensor(emit("tensor"))
    blkc.__exit__(None, None, None)
    for t in reversed(ctx):
        t.__exit__(None, None, None)
    return nc


_NC_CACHE = None


def kernel(L0_attn_x, L0_attn_W_in, L0_attn_conv_w, L0_attn_conv_b, L0_attn_W_xproj,
           L0_attn_A_log, L0_attn_D_param, L0_attn_W_out, L0_ffn_w_gate, L0_ffn_w_up,
           L0_ffn_w_down, L1_attn_W_qkvb, L1_attn_W_o, L1_attn_ln_g, L1_attn_ln_b,
           L1_ffn_w_gate, L1_ffn_w_up, L1_ffn_w_down):
    global _NC_CACHE
    if _NC_CACHE is None:
        _NC_CACHE = build_program()
    nc_ = _NC_CACHE
    x = np.ascontiguousarray(L0_attn_x, dtype=np.float32)
    Wq = np.ascontiguousarray(L1_attn_W_qkvb, dtype=np.float32).reshape(16, 193, 1024)
    Wq_perm = np.concatenate([
        Wq[:, 0:64].reshape(1024, 1024),
        Wq[:, 64:128].reshape(1024, 1024),
        Wq[:, 128:192].reshape(1024, 1024),
        Wq[:, 192:193].reshape(16, 1024),
        np.zeros((16, 1024), np.float32)], axis=0)
    maskc = (np.arange(128)[:, None] % TC <= np.arange(TC)[None, :]).astype(ml_dtypes.bfloat16)
    identc = np.eye(128, dtype=ml_dtypes.bfloat16)
    in_maps = []
    for c in range(NCORES):
        lo = c * LLOC
        halo = x[:, lo - 3:lo].copy() if c > 0 else np.zeros((B, 3, D), np.float32)
        rm = (np.arange(1, NCORES + 1) <= c).astype(np.float32)
        rm = np.zeros(NCORES, np.float32)
        rm[1:] = (np.arange(0, NCORES - 1) < c).astype(np.float32)
        in_maps.append({
            "x": x[:, lo:lo + LLOC].copy(),
            "xh": halo,
            "W_in": L0_attn_W_in, "conv_w": L0_attn_conv_w, "conv_b": L0_attn_conv_b,
            "W_xproj": L0_attn_W_xproj, "A_log": L0_attn_A_log, "D_param": L0_attn_D_param,
            "W_out": L0_attn_W_out,
            "f0_wg": L0_ffn_w_gate, "f0_wu": L0_ffn_w_up, "f0_wd": L0_ffn_w_down,
            "Wqkvb": Wq_perm, "W_o": L1_attn_W_o, "ln_g": L1_attn_ln_g, "ln_b": L1_attn_ln_b,
            "f1_wg": L1_ffn_w_gate, "f1_wu": L1_ffn_w_up, "f1_wd": L1_ffn_w_down,
            "mask": maskc, "ident": identc, "identf": identc.astype(np.float32), "rmask": rm,
        })
    res = run_bass_kernel_spmd(nc_, in_maps, core_ids=list(range(NCORES)))
    out = np.concatenate([np.asarray(r["out"], np.float32) for r in res.results], axis=1)
    return out




# version 04
# speedup vs baseline: 1.0462x; 1.0462x over previous
import numpy as np
import ml_dtypes
import concourse.bass as bass
import concourse.mybir as mybir
from concourse.bass_utils import run_bass_kernel_spmd

F32 = mybir.dt.float32
BF16 = mybir.dt.bfloat16
I16 = mybir.dt.int16
OP = mybir.AluOpType
AF = mybir.ActivationFunctionType
AX = mybir.AxisListType

B = 2
LLOC = 1024
T = 2048
NT = 16
D = 1024
DI = 2048
TC = 32
NCH = 32
NG = 8
NCORES = 8

ENGINES = ("sync", "vector", "scalar", "gpsimd", "tensor")
SEMS = {}
V, S, G, TE, SY = "vector", "scalar", "gpsimd", "tensor", "sync"


def ts_(i, n=128):
    return slice(i * n, (i + 1) * n)


def off_(base, i, n=128):
    return slice(base + i * n, base + (i + 1) * n)


class Prog:
    NW = 6

    def __init__(self):
        self.stream = {e: [] for e in ENGINES}
        self.cnt = {s: 0 for s in ["v", "sc", "g", "t", "cc", "bs"]}
        self.qs = {}
        for q in ("q0", "q1", "q2"):
            self.qs[q] = {"idx": 0, "tot": [0] * self.NW, "open": False}

    def _qsem(self, q, i):
        return q + chr(ord("a") + i)

    def w(self, eng, sem, val=None):
        if sem in self.qs:
            st = self.qs[sem]
            for i in range(self.NW):
                if st["tot"][i] > 0:
                    self.stream[eng].append(
                        lambda e, s=self._qsem(sem, i), vv=st["tot"][i]: e.wait_ge(SEMS[s], vv))
            st["open"] = False
            return
        v = self.cnt[sem] if val is None else val
        if v > 0:
            self.stream[eng].append(lambda e, s=sem, vv=v: e.wait_ge(SEMS[s], vv))

    OWN = {"vector": "v", "scalar": "sc", "gpsimd": "g", "tensor": "t"}

    def r(self, eng, fn, inc=None, k=1):
        own = self.OWN.get(eng)
        if own is not None:
            v = self.cnt[own]
            if v > 0:
                self.stream[eng].append(lambda e, s=own, vv=v: e.wait_ge(SEMS[s], vv))
            if inc is None or inc == own:
                self.cnt[own] += 1
                self.stream[eng].append(lambda e, f=fn, s=own: f(e).then_inc(SEMS[s], 1))
            else:
                self.cnt[inc] += k
                self.stream[eng].append(lambda e, f=fn, s=inc, kk=k: f(e).then_inc(SEMS[s], kk))
        else:
            if inc is not None:
                self.cnt[inc] += k
                self.stream[eng].append(lambda e, f=fn, s=inc, kk=k: f(e).then_inc(SEMS[s], kk))
            else:
                self.stream[eng].append(lambda e, f=fn: f(e)) 

    def d(self, eng, out, in_, q, transpose=False, slow=False):
        st = self.qs[q]
        if not st["open"]:
            st["idx"] = (st["idx"] + 1) % self.NW
            i = st["idx"]
            if st["tot"][i] > 0:
                self.stream[eng].append(
                    lambda e, s=self._qsem(q, i), vv=st["tot"][i]: e.wait_ge(SEMS[s], vv))
            st["open"] = True
        i = st["idx"]
        st["tot"][i] += 16
        sname = self._qsem(q, i)
        if transpose and not slow:
            self.stream[eng].append(
                lambda e, o=out, i_=in_, s=sname: e.dma_start_transpose(o, i_).then_inc(SEMS[s], 16))
        elif transpose and slow:
            def _f(e, o=out, i_=in_, s=sname):
                with e.bass.allow_non_contiguous_dma(reason="tiny transpose"):
                    return e.dma_start_transpose(o, i_).then_inc(SEMS[s], 16)
            self.stream[eng].append(_f)
        else:
            self.stream[eng].append(
                lambda e, o=out, i_=in_, s=sname: e.dma_start(out=o, in_=i_, allow_slow_non_contiguous=slow).then_inc(SEMS[s], 16))

    def snap(self):
        return dict(self.cnt)

    def wall(self, eng, snap=None):
        for q in ("q0", "q1", "q2"):
            self.w(eng, q)
        sn = self.cnt if snap is None else snap
        for s, v in sn.items():
            if v > 0:
                self.w(eng, s, v)

    def barrier(self):
        m = {}
        for eng, sem in [("sync", "bs"), ("vector", "v"), ("scalar", "sc"),
                         ("gpsimd", "g"), ("tensor", "t")]:
            self.cnt[sem] += 1
            m[sem] = self.cnt[sem]
            self.stream[eng].append(lambda e, s=sem: e.sem_inc(SEMS[s], 1))
        for eng in ENGINES:
            for s, v in m.items():
                if s != {"sync": "bs", "vector": "v", "scalar": "sc",
                         "gpsimd": "g", "tensor": "t"}[eng] or True:
                    self.w(eng, s, v)
            for q in ("q0", "q1", "q2"):
                self.w(eng, q)
        if not hasattr(self, "marks"):
            self.marks = []
        self.marks.append({e: len(self.stream[e]) for e in ENGINES})


def build_program():
    nc = bass.Bass("TRN2", target_bir_lowering=False)
    P = Prog()
    ap = {}

    def IN(name, shape, dt=F32):
        ap[name] = nc.dram_tensor(name, list(shape), dt, kind="ExternalInput").ap()

    def OUT(name, shape, dt=F32):
        ap[name] = nc.dram_tensor(name, list(shape), dt, kind="ExternalOutput").ap()

    import os as _os2
    _dbg = set((_os2.environ.get("MK_DBG", "")).split(",")) - {""}

    def INT(name, shape, dt=F32, **kw):
        kind = "ExternalOutput" if name in _dbg else "Internal"
        if kw.get("addr_space"):
            kind = "Internal"
        ap[name] = nc.dram_tensor(name, list(shape), dt, kind=kind, **kw).ap()

    IN("x", (B, LLOC, D)); IN("xh", (B, 3, D))
    IN("W_in", (2 * DI, D)); IN("conv_w", (DI, 4)); IN("conv_b", (DI,))
    IN("W_xproj", (33, DI)); IN("A_log", (16,)); IN("D_param", (DI,))
    IN("W_out", (D, DI))
    IN("f0_wg", (4096, D)); IN("f0_wu", (4096, D)); IN("f0_wd", (D, 4096))
    IN("Wqkvb", (3104, D))
    IN("W_o", (D, D)); IN("ln_g", (D,)); IN("ln_b", (D,))
    IN("f1_wg", (2816, D)); IN("f1_wu", (2816, D)); IN("f1_wd", (D, 2816))
    IN("mask", (128, TC), BF16)
    IN("ident", (128, 128), BF16)
    IN("identf", (128, 128))
    IN("rmask", (NCORES,))
    OUT("out", (B, LLOC, D))

    INT("s_win", (2 * DI, D), BF16)
    INT("s_wxp", (48, DI), BF16)
    INT("s_wout", (D, DI), BF16)
    INT("s_f0wg", (4096, D), BF16); INT("s_f0wu", (4096, D), BF16); INT("s_f0wd", (D, 4096), BF16)
    INT("s_qkvb", (3104, D), BF16); INT("s_wo", (D, D), BF16)
    INT("s_f1wg", (2816, D), BF16); INT("s_f1wu", (2816, D), BF16); INT("s_f1wd", (D, 2816), BF16)
    INT("s_xn", (T, D), BF16)
    INT("s_xq", (T, D), BF16)
    INT("s_xcf", (DI, T), BF16)
    INT("s_sz", (T, DI), BF16)
    INT("s_y", (T, DI), BF16)
    INT("s_yg", (T, DI), BF16)
    INT("s_h2", (T, 4096), F32)
    INT("s_xq2", (T, 4096), BF16)
    INT("s_x1", (T, D), F32)
    INT("s_xn1", (T, D), BF16)
    INT("s_xq1", (T, D), BF16)
    INT("s_qkv", (T, 3104), BF16)
    INT("s_xq3", (T, 2816), BF16)
    INT("s_lay", (T, D), BF16)
    INT("s_s", (T, D), BF16)
    INT("s_g", (T, D), BF16)
    INT("s_g1", (T, D), BF16)
    INT("s_fb", (16, B * LLOC), BF16)
    INT("s_dbg", (128, B * LLOC), F32)
    INT("s_dbg2", (128, 64), F32)
    INT("s_dbg3", (128, 2560), BF16)
    INT("cc_in", (32, DI + 1), F32)
    INT("cc_out", (NCORES, 32, DI + 1), F32, addr_space="Shared")

    ctx = []
    SB = {}

    def sb(name, p, f, dt=F32):
        t = nc.sbuf_tensor(name, [p, f], dt)
        h = t.__enter__(); ctx.append(t); SB[name] = h
        return h

    msk = sb("msk", 128, TC, BF16)
    idn = sb("idn", 128, 128, BF16)
    idnf = sb("idnf", 128, 128)
    rmk = sb("rmk", 32, NCORES)
    Acol = sb("Acol", 16, 1)
    cbw = sb("cbw", 128, 16 * 4)
    cbb = sb("cbb", 128, 16)
    onec = sb("onec", 128, 2)
    sxinv = sb("sxinv", 128, NT)
    sx2i = sb("sx2i", 128, NT)
    sxg = sb("sxg", 128, NT); sxu = sb("sxu", 128, NT); sxd = sb("sxd", 128, NT)
    wsc = sb("wsc", 4, 8)
    mgr = sb("mgr", 128, 4)
    p_a = sb("p_a", 128, 64)
    p_b = sb("p_b", 128, 128)
    w_a = sb("w_a", 128, 4096)
    w_b = sb("w_b", 128, 4096)
    c_a = sb("c_a", 128, 4096, BF16)
    c_b = sb("c_b", 128, 4096, BF16)
    i_a = sb("i_a", 128, 4096, I16)
    xhn = sb("xhn", 8, D, BF16)
    xhh = sb("xhh", 8, D)
    xhf = sb("xhf", 128, 8 * 8, BF16)
    A16 = sb("A16", 128, 49152, BF16)
    A32 = sb("A32", 128, 11264)

    def a16(off, n):
        return A16[:, off:off + n]

    def a32(off, n):
        return A32[:, off:off + n]

    dprow = A16[:, 47104:49152]
    lngrow = A16[:, 16384:17408]
    lnbrow = A16[:, 17408:18432]

    xnfm = a16(0, 16384)
    xpfc = a16(16384, 8256)
    xcfc = a16(24640, 8192)
    winS = a16(32832, 2048)
    zwv = a16(34880, 4096)
    wxpT = a16(41024, 768)
    tails = a16(41856, 64)
    c_c = a16(43008, 4096)
    fC = a16(0, 2048); fB = a16(2048, 2048); fCc = a16(4096, 2048)
    btT = a16(6144, 2048)
    gsb = a16(7168, 16384)
    hstg = a16(23552, 2048)
    hinb = a16(25600, 2048)
    hinb2 = a16(6144, 2048)
    xtm = a16(27648, 4096)
    big1 = A16[:, 0:32768]
    bufA = A16[:, 32768:49152]
    wmov = bufA[:, 0:8192]
    wmov2 = bufA[:, 8192:16384]
    fL = A32[:, 0:2048]; fsc = A32[:, 2048:4096]; f0z = A32[:, 4096:5120]
    xdb = A32[:, 5120:7168]
    hcurb = [A32[:, 7168:9216], A32[:, 9216:11264]]; hin = A32[:, 0:2048]
    xdbB = A32[:, 5120:7168]
    xdbC = A32[:, 7168:9216]
    xdbV = A32[:, 9216:11264]
    fDst = p_b[:, 0:64]; fDch = p_b[:, 64:128]
    fzv = w_a[:, 2048:4096]

    PSG_t = nc.psum_tensor("psg", [128, 2048], F32)
    PSG = PSG_t.__enter__(); ctx.append(PSG_t)
    PSY_t = nc.psum_tensor("psy", [128, 2048], F32)
    PSY = PSY_t.__enter__(); ctx.append(PSY_t)
    PS = [PSG[:, 0:512], PSG[:, 512:1024], PSG[:, 1024:1536], PSG[:, 1536:2048],
          PSY[:, 0:512], PSY[:, 512:1024], PSY[:, 1024:1536], PSY[:, 1536:2048]]

    semnames = ["v", "sc", "g", "t", "cc", "bs"]
    for q in ("q0", "q1", "q2"):
        for i in range(Prog.NW):
            semnames.append(q + chr(ord("a") + i))
    for s in semnames:
        h = nc.semaphore(s)
        SEMS[s] = h.__enter__(); ctx.append(h)

    P.d(SY, msk[:, :], ap["mask"][:, :], "q0")
    P.d(SY, idn[:, :], ap["ident"][:, :], "q0")
    P.d(SY, idnf[:, :], ap["identf"][:, :], "q0")
    P.d(SY, rmk[:, :], ap["rmask"][None, :].broadcast_to([32, NCORES]), "q0")
    P.d(G, dprow[:, :], ap["D_param"][None, :].broadcast_to([128, DI]), "q2")
    P.d(SY, Acol[:, 0:1], ap["A_log"][:, None], "q0")
    P.d(SY, cbw[:, :].rearrange("p (k j) -> p k j", j=4),
        ap["conv_w"].rearrange("(k p) j -> p k j", p=128), "q0", slow=True)
    P.d(SY, cbb[:, :], ap["conv_b"].rearrange("(k p) -> p k", p=128), "q0", slow=True)
    P.wall(V); P.wall(S)
    P.r(V, lambda e: e.memset(onec[:, :], 1.0), inc="v")
    P.r(S, lambda e: e.activation(out=Acol[0:16, 0:1], in_=Acol[0:16, 0:1], func=AF.Exp), inc="sc")
    P.w(V, "sc")
    P.r(V, lambda e: e.tensor_scalar_mul(Acol[0:16, 0:1], Acol[0:16, 0:1], -1.0), inc="v")

    xr = ap["x"].rearrange("b (i p) d -> (b i) p d", p=128)
    for i in range(NT):
        P.d(SY, w_a[:, 0:D], xr[i], "q0")
        P.wall(V)
        mu = p_a[:, 0:1]; vv = p_a[:, 1:2]; rr = p_a[:, 2:3]; rmx = p_a[:, 3:4]
        P.r(V, lambda e, mu=mu: e.reduce_sum(mu, w_a[:, 0:D], AX.X))
        P.r(V, lambda e, mu=mu: e.tensor_scalar_mul(mu, mu, 1.0 / D), inc="v")
        P.w(S, "v")
        P.r(S, lambda e: e.activation(out=w_b[:, 0:D], in_=w_a[:, 0:D], func=AF.Square), inc="sc")
        P.w(V, "sc")
        P.r(V, lambda e, vv=vv: e.reduce_sum(vv, w_b[:, 0:D], AX.X))
        P.r(V, lambda e, vv=vv: e.tensor_scalar_mul(vv, vv, 1.0 / D))
        P.r(V, lambda e, mu=mu, rr=rr: e.tensor_tensor(rr, mu, mu, OP.mult))
        P.r(V, lambda e, vv=vv, rr=rr: e.tensor_tensor(rr, vv, rr, OP.subtract))
        P.r(V, lambda e, rr=rr: e.tensor_scalar_add(rr, rr, 1e-5), inc="v")
        P.w(S, "v")
        P.r(S, lambda e, rr=rr: e.activation(out=rr, in_=rr, func=AF.Sqrt), inc="sc")
        P.w(V, "sc")
        P.r(V, lambda e, rr=rr: e.reciprocal(rr, rr))
        P.r(V, lambda e, mu=mu, rr=rr: e.tensor_scalar(w_a[:, 0:D], w_a[:, 0:D], mu, rr, OP.subtract, OP.mult))
        P.r(V, lambda e: e.tensor_copy(c_a[:, 0:D], w_a[:, 0:D]))
        P.r(V, lambda e, rmx=rmx: e.reduce_max(rmx, w_a[:, 0:D], AX.X, apply_absolute_value=True))
        P.r(V, lambda e, rmx=rmx: e.tensor_scalar_max(rmx, rmx, 1e-5))
        P.r(V, lambda e, i=i, rmx=rmx: e.tensor_scalar_mul(sxinv[:, i:i + 1], rmx, 1.0 / 127.0))
        P.r(V, lambda e, i=i, rr=rr: e.reciprocal(rr, sxinv[:, i:i + 1]), inc="v")
        P.w(S, "v")
        P.r(S, lambda e, rr=rr: e.activation(out=w_b[:, 0:D], in_=w_a[:, 0:D], func=AF.Identity, scale=rr, bias=0.0), inc="sc")
        P.w(V, "sc")
        P.r(V, lambda e: e.tensor_copy(i_a[:, 0:D], w_b[:, 0:D]))
        P.r(V, lambda e: e.tensor_copy(c_b[:, 0:D], i_a[:, 0:D]), inc="v")
        P.w(S, "v")
        P.d(S, ap["s_xn"][ts_(i, 128), :], c_a[:, 0:D], "q1")
        P.d(S, ap["s_xq"][ts_(i, 128), :], c_b[:, 0:D], "q1")
        P.w(SY, "q1"); P.w(V, "q1"); P.w(S, "q1")

    P.r(V, lambda e: e.memset(xhh[:, :], 0.0), inc="v")
    P.wall(SY)
    P.d(SY, xhh[0:3, :], ap["xh"][0], "q0")
    P.d(SY, xhh[3:6, :], ap["xh"][1], "q0")
    P.wall(V); P.wall(S)
    hmu = p_a[0:8, 4:5]; hvv = p_a[0:8, 5:6]; hm2 = p_a[0:8, 6:7]
    P.r(V, lambda e: e.reduce_sum(hmu, xhh[:, :], AX.X))
    P.r(V, lambda e: e.tensor_scalar_mul(hmu, hmu, 1.0 / D), inc="v")
    P.w(S, "v")
    P.r(S, lambda e: e.activation(out=w_b[0:8, 0:D], in_=xhh[:, :], func=AF.Square), inc="sc")
    P.w(V, "sc")
    P.r(V, lambda e: e.reduce_sum(hvv, w_b[0:8, 0:D], AX.X))
    P.r(V, lambda e: e.tensor_scalar_mul(hvv, hvv, 1.0 / D))
    P.r(V, lambda e: e.tensor_tensor(hm2, hmu, hmu, OP.mult))
    P.r(V, lambda e: e.tensor_tensor(hvv, hvv, hm2, OP.subtract))
    P.r(V, lambda e: e.tensor_scalar_add(hvv, hvv, 1e-5), inc="v")
    P.w(S, "v")
    P.r(S, lambda e: e.activation(out=hvv, in_=hvv, func=AF.Sqrt), inc="sc")
    P.w(V, "sc")
    P.r(V, lambda e: e.reciprocal(hvv, hvv))
    P.r(V, lambda e: e.tensor_scalar(xhh[:, :], xhh[:, :], hmu, hvv, OP.subtract, OP.mult))
    P.r(V, lambda e: e.tensor_copy(xhn[:, :], xhh[:, :]), inc="v")
    P.wall(TE)
    for k in range(8):
        P.r(TE, lambda e, k=k: e.transpose(PSY[0:128, k * 8:k * 8 + 8], xhh[0:8, ts_(k, 128)], idnf[0:8, 0:8]), inc="t")
    P.w(V, "t")
    P.r(V, lambda e: e.tensor_copy(xhf[:, 0:64], PSY[:, 0:64]), inc="v")
    P.w(TE, "v")

    def cast_rows(src_ap, dst_ap, nrows, ncols):
        nt = (nrows + 127) // 128
        vmark = {}
        qmark = {}
        for r in range(nt):
            p = min(128, nrows - r * 128)
            wsrc = w_a if r % 2 == 0 else w_b
            cdst = c_a if r % 2 == 0 else c_b
            if r >= 2:
                P.w(SY, "v", vmark[r - 2])
            else:
                P.w(SY, "v"); P.w(SY, "sc")
            P.d(SY, wsrc[0:p, 0:ncols], src_ap[r * 128:r * 128 + p, :], "q0")
            P.w(V, "q0")
            P.w(V, "q1")
            P.r(V, lambda e, p=p, n=ncols, a=wsrc, b=cdst: e.tensor_copy(b[0:p, 0:n], a[0:p, 0:n]), inc="v")
            vmark[r] = P.cnt["v"]
            P.w(S, "v", vmark[r])
            P.d(S, dst_ap[r * 128:r * 128 + p, :], cdst[0:p, 0:ncols], "q1")
        P.w(SY, "q1"); P.w(V, "q1"); P.w(S, "q1")
    cast_rows(ap["W_in"], ap["s_win"], 2 * DI, D)
    cast_rows(ap["W_xproj"], ap["s_wxp"], 33, DI)
    P.r(V, lambda e: e.memset(c_a[0:15, 0:DI], 0.0), inc="v")
    P.w(S, "v")
    P.d(S, ap["s_wxp"][33:48, :], c_a[0:15, 0:DI], "q1")
    P.w(SY, "q1"); P.w(V, "q1"); P.w(S, "q1")
    cast_rows(ap["W_out"], ap["s_wout"], D, DI)
    cast_rows(ap["Wqkvb"], ap["s_qkvb"], 3104, D)
    cast_rows(ap["W_o"], ap["s_wo"], D, D)

    def f0_quant(src, dst, nrows, ncols, scidx):
        nt = nrows // 128
        P.r(V, lambda e: e.memset(p_a[:, 8:9], 0.0), inc="v")
        for r in range(nt):
            wsrc = w_a if r % 2 == 0 else w_b
            if r >= 2:
                P.w(SY, "v")
            else:
                P.w(SY, "v"); P.w(SY, "sc")
            P.d(SY, wsrc[:, 0:ncols], src[ts_(r, 128), :], "q0")
            P.w(V, "q0")
            P.r(V, lambda e, n=ncols, a=wsrc: e.reduce_sum(p_a[:, 9:10], a[:, 0:n], AX.X, apply_absolute_value=True))
            P.r(V, lambda e: e.tensor_tensor(p_a[:, 8:9], p_a[:, 8:9], p_a[:, 9:10], OP.add), inc="v")
        P.wall(TE)
        P.r(TE, lambda e: e.matmul(PS[7][0:1, 0:1], p_a[:, 8:9], onec[:, 0:1], start=True, stop=True), inc="t")
        P.w(V, "t")
        P.r(V, lambda e, n=nrows * ncols: e.tensor_scalar(wsc[0:1, 0:1], PS[7][0:1, 0:1], 1.0 / n, 1e-5, OP.mult, OP.max))
        P.r(V, lambda e: e.tensor_scalar_mul(wsc[0:1, 1:2], wsc[0:1, 0:1], 0.5), inc="v")
        P.r(V, lambda e: e.memset(w_b[0:1, 0:128], 1.0), inc="v")
        P.w(TE, "v")
        P.r(TE, lambda e: e.matmul(PS[7][:, 4:5], w_b[0:1, 0:128], wsc[0:1, 1:2], start=True, stop=True), inc="t")
        P.r(TE, lambda e, scidx=scidx: e.matmul(PS[7][:, 8:9], w_b[0:1, 0:128], wsc[0:1, 0:1], start=True, stop=True), inc="t")
        P.w(V, "t")
        P.r(V, lambda e: e.tensor_copy(p_a[:, 10:11], PS[7][:, 4:5]))
        P.r(V, lambda e: e.tensor_scalar_mul(p_a[:, 11:12], p_a[:, 10:11], -1.0))
        P.r(V, lambda e, scidx=scidx: e.tensor_copy(mgr[:, scidx:scidx + 1], PS[7][:, 8:9]), inc="v")
        for r in range(nt):
            wsrc = w_a if r % 2 == 0 else w_b
            P.w(SY, "v")
            P.d(SY, wsrc[:, 0:ncols], src[ts_(r, 128), :], "q0")
            P.w(V, "q0"); P.w(V, "q1")
            P.r(V, lambda e, n=ncols, a=wsrc: e.tensor_scalar(c_a[:, 0:n], a[:, 0:n], p_a[:, 10:11], None, OP.is_gt))
            P.r(V, lambda e, n=ncols, a=wsrc: e.tensor_scalar(c_b[:, 0:n], a[:, 0:n], p_a[:, 11:12], None, OP.is_lt))
            P.r(V, lambda e, n=ncols: e.tensor_tensor(c_a[:, 0:n], c_a[:, 0:n], c_b[:, 0:n], OP.subtract), inc="v")
            P.w(S, "v")
            P.d(S, dst[ts_(r, 128), :], c_a[:, 0:ncols], "q1")
        P.w(SY, "q1"); P.w(V, "q1"); P.w(S, "q1")
    f0_quant(ap["f0_wg"], ap["s_f0wg"], 4096, D, 0)
    f0_quant(ap["f0_wu"], ap["s_f0wu"], 4096, D, 1)
    f0_quant(ap["f0_wd"], ap["s_f0wd"], D, 4096, 2)

    def fp4_on_tile(p, ncols):
        nb = ncols // 16
        a3 = w_a[0:p, 0:ncols].rearrange("p (n s) -> p n s", s=16)
        bm = w_b[0:p, 3500:3500 + nb]
        rs = w_b[0:p, 3200:3200 + nb]
        P.r(V, lambda e, a3=a3, bm=bm: e.tensor_reduce(bm, a3, AX.X, OP.max, apply_absolute_value=True))
        P.r(V, lambda e, bm=bm: e.tensor_scalar(bm, bm, 1.0 / 6.0, 1e-12, OP.mult, OP.max))
        P.r(V, lambda e, rs=rs, bm=bm: e.reciprocal(rs, bm), inc="v")
        rsb = rs[:, :, None].broadcast_to([p, nb, 16])
        P.r(V, lambda e, a3=a3, rsb=rsb: e.tensor_tensor(a3, a3, rsb, OP.mult), inc="v")
        P.w(S, "v")
        P.r(S, lambda e, p=p, n=ncols: e.activation(out=w_b[0:p, 0:n], in_=w_a[0:p, 0:n], func=AF.Abs, scale=2.0), inc="sc")
        P.w(V, "sc")
        P.r(V, lambda e, p=p, n=ncols: e.tensor_copy(i_a[0:p, 0:n], w_b[0:p, 0:n]))
        P.r(V, lambda e, p=p, n=ncols: e.tensor_copy(c_a[0:p, 0:n], i_a[0:p, 0:n]))
        P.r(V, lambda e, p=p, n=ncols: e.tensor_scalar(c_a[0:p, 0:n], c_a[0:p, 0:n], 4.0, 0.5, OP.min, OP.mult), inc="v")
        for th in (5.0, 7.0):
            P.r(V, lambda e, p=p, n=ncols, th=th: e.tensor_scalar(c_b[0:p, 0:n], w_b[0:p, 0:n], th, None, OP.is_gt))
            P.r(V, lambda e, p=p, n=ncols: e.tensor_tensor(c_a[0:p, 0:n], c_a[0:p, 0:n], c_b[0:p, 0:n], OP.add))
        P.r(V, lambda e, p=p, n=ncols: e.tensor_scalar(c_b[0:p, 0:n], w_b[0:p, 0:n], 10.0, 2.0, OP.is_gt, OP.mult))
        P.r(V, lambda e, p=p, n=ncols: e.tensor_tensor(c_a[0:p, 0:n], c_a[0:p, 0:n], c_b[0:p, 0:n], OP.add), inc="v")
        P.w(S, "v")
        P.r(S, lambda e, p=p, n=ncols: e.activation(out=c_b[0:p, 0:n], in_=w_a[0:p, 0:n], func=AF.Sign), inc="sc")
        P.w(V, "sc")
        P.r(V, lambda e, p=p, n=ncols: e.tensor_tensor(c_a[0:p, 0:n], c_a[0:p, 0:n], c_b[0:p, 0:n], OP.mult))
        scb = bm[:, :, None].broadcast_to([p, nb, 16])
        ca3 = c_a[0:p, 0:ncols].rearrange("p (n s) -> p n s", s=16)
        P.r(V, lambda e, ca3=ca3, scb=scb: e.tensor_tensor(ca3, ca3, scb, OP.mult), inc="v")

    def f1_quant(src, dst, nrows, ncols):
        nt = (nrows + 127) // 128
        for r in range(nt):
            p = min(128, nrows - r * 128)
            if r >= 1:
                P.w(SY, "v")
            P.d(SY, w_a[0:p, 0:ncols], src[r * 128:r * 128 + p, :], "q0")
            P.w(V, "q0"); P.w(V, "q1")
            fp4_on_tile(p, ncols)
            P.w(S, "v")
            P.d(S, dst[r * 128:r * 128 + p, :], c_a[0:p, 0:ncols], "q1")
            P.w(SY, "q1"); P.w(V, "q1"); P.w(S, "q1")
    f1_quant(ap["f1_wg"], ap["s_f1wg"], 2816, D)
    f1_quant(ap["f1_wu"], ap["s_f1wu"], 2816, D)
    f1_quant(ap["f1_wd"], ap["s_f1wd"], D, 2816)

    P.r(V, lambda e: e.tensor_scalar(sxg[:, :], sxinv[:, :], mgr[:, 0:1], None, OP.mult))
    P.r(V, lambda e: e.tensor_scalar(sxu[:, :], sxinv[:, :], mgr[:, 1:2], None, OP.mult), inc="v")
    if "s_dbg2" in _dbg:
        P.w(S, "v")
        P.d(S, ap["s_dbg2"][:, 0:NT], sxg[:, :], "q1")
        P.d(S, ap["s_dbg2"][:, 16:16 + NT], sxu[:, :], "q1")
        P.d(S, ap["s_dbg2"][:, 32:32 + NT], sxinv[:, :], "q1")
        P.d(S, ap["s_dbg2"][:, 48:52], mgr[:, 0:4], "q1")
        P.w(SY, "q1"); P.w(V, "q1"); P.w(S, "q1")
    P.barrier()

    for k in range(8):
        P.d(SY, xnfm[:, ts_(k, T)], ap["s_xn"][:, ts_(k, 128)], "q0", transpose=True)
    for k in range(16):
        P.d(SY, wxpT[:, k * 48:k * 48 + 48], ap["s_wxp"][:, ts_(k, 128)], "q0", transpose=True)
    for b in range(B):
        for lc in range(2):
            t0 = b * LLOC + lc * 512
            for ot in range(16):
                wbuf = winS[:, (ot % 2) * 1024:(ot % 2) * 1024 + 1024]
                P.w(SY, "t")
                for k in range(8):
                    P.d(SY, wbuf[:, k * 128:(k + 1) * 128], ap["s_win"][ts_(ot, 128), ts_(k, 128)], "q0", transpose=True)
                P.wall(TE)
                for k in range(8):
                    st = wbuf[:, k * 128:(k + 1) * 128]
                    mv = xnfm[:, k * T + t0: k * T + t0 + 512]
                    P.r(TE, lambda e, st=st, mv=mv, k=k:
                        e.matmul(PS[0][:, 0:512], st, mv, start=(k == 0), stop=(k == 7)),
                        inc=("t" if k == 7 else None))
                if lc == 0:
                    for k in range(8):
                        st = wbuf[:, k * 128:(k + 1) * 128]
                        mv = xhf[:, k * 8 + b * 3: k * 8 + b * 3 + 3]
                        P.r(TE, lambda e, st=st, mv=mv, k=k:
                            e.matmul(PS[1][:, 0:3], st, mv, start=(k == 0), stop=(k == 7)),
                            inc=("t" if k == 7 else None))
                P.w(V, "t")
                col0 = ot * 516
                if lc == 0:
                    P.r(V, lambda e, col0=col0: e.tensor_copy(xpfc[:, col0:col0 + 3], PS[1][:, 0:3]))
                else:
                    P.r(V, lambda e, col0=col0, ot=ot: e.tensor_copy(xpfc[:, col0:col0 + 3], tails[:, ot * 4:ot * 4 + 3]))
                P.r(V, lambda e, col0=col0: e.tensor_copy(xpfc[:, col0 + 3:col0 + 515], PS[0][:, 0:512]))
                P.r(V, lambda e, ot=ot: e.tensor_copy(tails[:, ot * 4:ot * 4 + 3], PS[0][:, 509:512]), inc="v")
                P.w(TE, "v")
            for ot in range(16):
                col0 = ot * 516
                P.r(V, lambda e, col0=col0, ot=ot:
                    e.tensor_scalar(w_a[:, 0:512], xpfc[:, col0:col0 + 512], cbw[:, 4 * ot:4 * ot + 1], cbb[:, ot:ot + 1], OP.mult, OP.add))
                for j in (1, 2, 3):
                    P.r(V, lambda e, col0=col0, ot=ot, j=j:
                        e.scalar_tensor_tensor(w_a[:, 0:512], xpfc[:, col0 + j:col0 + j + 512], cbw[:, 4 * ot + j:4 * ot + j + 1], w_a[:, 0:512], OP.mult, OP.add), inc="v")
                P.w(S, "v")
                P.r(S, lambda e, ot=ot: e.activation(out=xcfc[:, ts_(ot, 512)], in_=w_a[:, 0:512], func=AF.Silu), inc="sc")
                P.w(V, "sc")
            P.wall(S)
            for ot in range(16):
                P.d(S, ap["s_xcf"][ts_(ot, 128), t0:t0 + 512], xcfc[:, ts_(ot, 512)], "q1")
            P.w(S, "q1"); P.w(V, "q1")
            P.wall(TE)
            for k in range(16):
                stB = wxpT[:, k * 48:k * 48 + 16]
                stC = wxpT[:, k * 48 + 16:k * 48 + 32]
                stV = wxpT[:, k * 48 + 32:k * 48 + 33]
                mv = xcfc[:, ts_(k, 512)]
                P.r(TE, lambda e, st=stB, mv=mv, k=k:
                    e.matmul(PSY[0:16, 0:512], st, mv, start=(k == 0), stop=(k == 15)))
                P.r(TE, lambda e, st=stC, mv=mv, k=k:
                    e.matmul(PSY[0:16, 512:1024], st, mv, start=(k == 0), stop=(k == 15)))
                P.r(TE, lambda e, st=stV, mv=mv, k=k:
                    e.matmul(PSY[0:1, 1024:1536], st, mv, start=(k == 0), stop=(k == 15)),
                    inc=("t" if k == 15 else None))
            P.w(V, "t")
            P.r(V, lambda e, t0=t0: e.tensor_copy(xdbB[0:16, t0:t0 + 512], PSY[0:16, 0:512]))
            P.r(V, lambda e, t0=t0: e.tensor_copy(xdbC[0:16, t0:t0 + 512], PSY[0:16, 512:1024]))
            P.r(V, lambda e, t0=t0: e.tensor_copy(xdbV[0:1, t0:t0 + 512], PSY[0:1, 1024:1536]), inc="v")
            P.w(TE, "v")
    for zc in range(4):
        P.w(SY, "t")
        for k in range(8):
            P.d(SY, zwv[:, k * 512:(k + 1) * 512], ap["s_win"][2048 + zc * 512:2048 + (zc + 1) * 512, ts_(k, 128)], "q0", transpose=True)
        P.wall(TE)
        for i in range(NT):
            ps = PS[4 + (i % 2)]
            for k in range(8):
                st = xnfm[:, off_(k * T, i, 128)]
                mv = zwv[:, k * 512:(k + 1) * 512]
                P.r(TE, lambda e, ps=ps, st=st, mv=mv, k=k:
                    e.matmul(ps[:, 0:512], st, mv, start=(k == 0), stop=(k == 7)),
                    inc=("t" if k == 7 else None))
            P.w(S, "t")
            P.w(S, "q1")
            P.r(S, lambda e, ps=ps: e.activation(out=c_a[:, 0:512], in_=ps[:, 0:512], func=AF.Silu), inc="sc")
            P.w(S, "sc")
            P.d(S, ap["s_sz"][ts_(i, 128), zc * 512:(zc + 1) * 512], c_a[:, 0:512], "q1")
            P.w(TE, "sc")
    P.barrier()

    P.r(S, lambda e: e.activation(out=fsc[0:1, :], in_=xdbV[0:1, 0:T], func=AF.Exp), inc="sc")
    P.w(S, "sc")
    P.r(S, lambda e: e.activation(out=fsc[0:1, :], in_=fsc[0:1, :], func=AF.Ln, bias=1.0), inc="sc")
    P.w(V, "sc")
    P.r(V, lambda e: e.tensor_scalar(fsc[0:1, :], fsc[0:1, :], 0.001, 0.1, OP.max, OP.min), inc="v")
    P.r(V, lambda e: e.memset(w_b[0:1, 100:116], 1.0), inc="v")
    P.w(TE, "v")
    for half in range(4):
        P.r(TE, lambda e, half=half: e.matmul(PS[0][0:16, 0:512], w_b[0:1, 100:116],
                                              fsc[0:1, half * 512:(half + 1) * 512], start=True, stop=True), inc="t")
        P.w(V, "t")
        P.r(V, lambda e, half=half: e.tensor_copy(fzv[0:16, half * 512:(half + 1) * 512], PS[0][0:16, 0:512]), inc="v")
        P.w(TE, "v")
    P.r(V, lambda e: e.tensor_scalar(fsc[0:16, :], fzv[0:16, :], Acol[0:16, 0:1], None, OP.mult))
    P.r(V, lambda e: e.tensor_scalar(fsc[0:16, :], fsc[0:16, :], -20.0, 0.0, OP.max, OP.min), inc="v")
    P.r(V, lambda e: e.tensor_tensor(fL[0:16, :], fzv[0:16, :], xdbB[0:16, 0:T], OP.mult))
    P.r(V, lambda e: e.tensor_scalar(fL[0:16, :], fL[0:16, :], -10.0, 10.0, OP.max, OP.min))
    P.r(V, lambda e: e.tensor_copy(fzv[0:16, :], fL[0:16, :]), inc="v")
    P.r(V, lambda e: e.memset(f0z[:, :], 0.0), inc="v")
    for b in range(B):
        P.r(V, lambda e, b=b: e.tensor_tensor_scan(
            fL[0:16, b * LLOC:(b + 1) * LLOC], fsc[0:16, b * LLOC:(b + 1) * LLOC],
            f0z[0:16, :], 0.0, OP.add, OP.add), inc="v")
    P.barrier()
    P.r(V, lambda e: e.memset(fDst[:, :], 0.0), inc="v")
    for b in range(B):
        src3 = fL[0:16, b * LLOC:(b + 1) * LLOC].rearrange("p (n c) -> p n c", c=TC)[:, 0:NCH - 1, TC - 1:TC]
        dst3 = fDst[0:16, b * NCH + 1:(b + 1) * NCH][:, :, None]
        P.r(V, lambda e, s=src3, d=dst3: e.tensor_copy(d, s), inc="v")
    for b in range(B):
        P.r(V, lambda e, b=b: e.tensor_tensor(
            fsc[0:16, b * LLOC:(b + 1) * LLOC].rearrange("p (n c) -> p n c", c=TC),
            fL[0:16, b * LLOC:(b + 1) * LLOC].rearrange("p (n c) -> p n c", c=TC),
            fDst[0:16, b * NCH:(b + 1) * NCH][:, :, None].broadcast_to([16, NCH, TC]),
            OP.subtract), inc="v")
    P.w(S, "v")
    P.r(S, lambda e: e.activation(out=w_a[0:16, 0:T], in_=fsc[0:16, :], func=AF.Exp), inc="sc")
    P.r(S, lambda e: e.activation(out=w_b[0:16, 0:T], in_=fsc[0:16, :], func=AF.Exp, scale=-1.0), inc="sc")
    P.w(V, "sc")
    P.r(V, lambda e: e.tensor_tensor(fC[0:16, :], xdbC[0:16, 0:T], w_a[0:16, 0:T], OP.mult))
    P.r(V, lambda e: e.tensor_tensor(fB[0:16, :], fzv[0:16, :], w_b[0:16, 0:T], OP.mult), inc="v")
    for b in range(B):
        src3 = w_a[0:16, b * LLOC:(b + 1) * LLOC].rearrange("p (n c) -> p n c", c=TC)[:, :, TC - 1:TC]
        dst3 = fDch[0:16, b * NCH:(b + 1) * NCH][:, :, None]
        P.r(V, lambda e, s=src3, d=dst3: e.tensor_copy(d, s), inc="v")
    P.w(S, "v")
    P.r(S, lambda e: e.activation(out=w_b[0:16, 3000:3000 + B * NCH], in_=fDst[0:16, :], func=AF.Exp), inc="sc")
    P.w(V, "sc")
    for b in range(B):
        P.r(V, lambda e, b=b: e.tensor_tensor(
            fCc[0:16, b * LLOC:(b + 1) * LLOC].rearrange("p (n c) -> p n c", c=TC),
            fC[0:16, b * LLOC:(b + 1) * LLOC].rearrange("p (n c) -> p n c", c=TC),
            w_b[0:16, 3000 + b * NCH:3000 + (b + 1) * NCH][:, :, None].broadcast_to([16, NCH, TC]),
            OP.mult), inc="v")
    P.barrier()
    if "s_dbg" in _dbg:
        P.w(S, "v"); P.w(S, "sc")
        P.d(S, ap["s_dbg"][0:16, :], fL[0:16, :], "q1")
        P.d(S, ap["s_dbg"][16:32, :], fsc[0:16, :], "q1")
        P.d(S, ap["s_dbg"][32:48, 0:B * NCH], fDst[0:16, :], "q1")
        P.d(S, ap["s_dbg"][48:64, 0:B * NCH], fDch[0:16, :], "q1")
        P.d(S, ap["s_dbg"][64:80, :], xdbB[0:16, 0:T], "q1")
        P.d(S, ap["s_dbg"][80:96, :], xdbC[0:16, 0:T], "q1")
        P.d(S, ap["s_dbg"][96:112, :], fzv[0:16, 0:T], "q1")
        P.d(S, ap["s_dbg"][112:128, :], w_b[0:16, 0:T], "q1")
        P.w(SY, "q1"); P.w(V, "q1"); P.w(S, "q1")
    P.w(S, "v")
    P.d(S, ap["s_fb"][:, :], fB[0:16, :], "q1")
    P.w(SY, "q1"); P.w(V, "q1"); P.w(S, "q1")
    P.barrier()
    for b in range(B):
        for gid in range(NG):
            g_ = b * NG + gid
            dst = btT[:, g_ * 128:g_ * 128 + 16]
            src = ap["s_fb"][0:16, b * LLOC + gid * 128:(b * LLOC) + (gid + 1) * 128]
            P.d(SY, dst, src, "q0", transpose=True)
            P.w(SY, "q0")
            if g_ == 0:
                P.barrier()
    P.barrier()

    P.r(V, lambda e: e.memset(hcurb[0][0:16, :], 0.0))
    P.r(V, lambda e: e.memset(hcurb[1][0:16, :], 0.0), inc="v")
    for b in range(B):
        for gid in range(NG):
            xbuf = xtm[:, (gid % 2) * DI:((gid % 2) + 1) * DI]
            P.w(SY, "t")
            P.d(SY, xbuf, ap["s_xcf"][:, b * LLOC + gid * 128: b * LLOC + (gid + 1) * 128], "q0", transpose=True)
            P.wall(TE)
            for c in range(4):
                n = gid * 4 + c
                blk = (b * NG + gid) * 4 + c
                stm = fB[0:16, b * LLOC + n * TC: b * LLOC + (n + 1) * TC]
                mvm = fC[0:16, b * LLOC + n * TC: b * LLOC + (n + 1) * TC]
                P.r(TE, lambda e, stm=stm, mvm=mvm, c=c:
                    e.matmul(PSG[32 * c:32 * c + 32, 0:32], stm, mvm, start=True, stop=True,
                             tile_position=(0, 32 * c)), inc="t")
                P.w(V, "t")
                mcb = c_b[32 * c:32 * c + 32, (n % 2) * TC:((n % 2) + 1) * TC]
                P.r(V, lambda e, mcb=mcb, c=c: e.tensor_tensor(mcb, PSG[32 * c:32 * c + 32, 0:32], msk[32 * c:32 * c + 32, :], OP.mult), inc="v")
                P.w(TE, "v")
                for dc in range(4):
                    stg = btT[32 * c:32 * c + 32, (b * NG + gid) * 128:(b * NG + gid) * 128 + 16]
                    mvx = xbuf[32 * c:32 * c + 32, dc * 512:(dc + 1) * 512]
                    P.r(TE, lambda e, stg=stg, mvx=mvx, c=c, dc=dc:
                        e.matmul(PSG[0:16, dc * 512:(dc + 1) * 512], stg, mvx, start=True, stop=True,
                                 tile_position=(32 * c, 0)),
                        inc=("t" if dc == 3 else None))
                P.w(V, "t")
                P.r(V, lambda e, b=b: e.tensor_copy(hstg[0:16, :], hcurb[b][0:16, :]), inc="v")
                P.r(V, lambda e, b=b: e.tensor_tensor(w_a[0:16, 0:DI], hcurb[b][0:16, :], PSG[0:16, 0:DI], OP.add))
                P.r(V, lambda e, b=b, n=n: e.tensor_scalar(hcurb[b][0:16, :], w_a[0:16, 0:DI], fDch[0:16, b * NCH + n:b * NCH + n + 1], None, OP.mult), inc="v")
                P.w(TE, "v")
                for dc in range(4):
                    mvx = xbuf[32 * c:32 * c + 32, dc * 512:(dc + 1) * 512]
                    stc = fC[0:16, b * LLOC + n * TC: b * LLOC + (n + 1) * TC]
                    mvh = hstg[0:16, dc * 512:(dc + 1) * 512]
                    P.r(TE, lambda e, mcb=mcb, mvx=mvx, c=c, dc=dc:
                        e.matmul(PSY[32 * c:32 * c + 32, dc * 512:(dc + 1) * 512], mcb, mvx, start=True, stop=False,
                                 tile_position=(32 * c, 32 * c)))
                    P.r(TE, lambda e, stc=stc, mvh=mvh, c=c, dc=dc:
                        e.matmul(PSY[32 * c:32 * c + 32, dc * 512:(dc + 1) * 512], stc, mvh, start=False, stop=True,
                                 tile_position=(0, 32 * c)),
                        inc=("t" if dc == 3 else None))
            P.w(V, "t"); P.w(V, "q1")
            P.r(V, lambda e: e.tensor_copy(c_a[:, 0:DI], PSY[:, 0:DI]), inc="v")
            P.w(S, "v")
            P.d(S, ap["s_y"][ts_(b * NG + gid, 128), :], c_a[:, 0:DI], "q1")
            P.w(TE, "q1"); P.w(V, "q1"); P.w(S, "q1")
    P.w(S, "v")
    P.r(S, lambda e: e.activation(out=p_a[0:16, 16:17], in_=fL[0:16, LLOC - 1:LLOC], func=AF.Exp), inc="sc")
    P.r(S, lambda e: e.activation(out=p_a[0:16, 17:18], in_=fL[0:16, 2 * LLOC - 1:2 * LLOC], func=AF.Exp), inc="sc")
    P.wall(SY)
    P.d(SY, ap["cc_in"][0:16, 0:DI], hcurb[0][0:16, :], "q0")
    P.d(SY, ap["cc_in"][16:32, 0:DI], hcurb[1][0:16, :], "q0")
    P.d(SY, ap["cc_in"][0:16, DI:DI + 1], p_a[0:16, 16:17], "q0", slow=True)
    P.d(SY, ap["cc_in"][16:32, DI:DI + 1], p_a[0:16, 17:18], "q0", slow=True)
    P.w(G, "q0")
    P.r(G, lambda e: e.collective_compute("AllGather", OP.bypass, replica_groups=[list(range(NCORES))],
                                          ins=[ap["cc_in"][:]], outs=[ap["cc_out"][:]]), inc="cc")
    P.w(SY, "cc")
    P.wall(V)
    P.r(V, lambda e: e.memset(hin[0:32, :], 0.0), inc="v")
    for r in range(NCORES - 1):
        P.w(V, "q1")
        P.wall(SY)
        P.d(SY, w_b[0:32, 0:DI + 1], ap["cc_out"][r], "q0")
        P.wall(V)
        Dr = w_b[0:32, DI:DI + 1]
        Gr = w_b[0:32, 0:DI]
        mr = rmk[:, r + 1:r + 2]
        P.r(V, lambda e, Dr=Dr: e.tensor_scalar(p_a[0:32, 20:21], Dr, -1.0, None, OP.add))
        P.r(V, lambda e, mr=mr: e.scalar_tensor_tensor(p_a[0:32, 21:22], p_a[0:32, 20:21], mr, onec[0:32, 0:1], OP.mult, OP.add), inc="v")
        P.r(V, lambda e: e.tensor_scalar(hin[0:32, :], hin[0:32, :], p_a[0:32, 21:22], None, OP.mult), inc="v")
        P.r(V, lambda e, Gr=Gr, mr=mr: e.scalar_tensor_tensor(hin[0:32, :], Gr, mr, hin[0:32, :], OP.mult, OP.add), inc="v")
    P.r(V, lambda e: e.tensor_copy(hinb[0:16, 0:DI], hin[0:16, :]), inc="v")
    P.wall(SY)
    P.d(SY, ap["cc_in"][0:16, 0:DI], hin[16:32, :], "q0")
    P.w(G, "q0")
    P.d(G, hinb2[0:16, 0:DI], ap["cc_in"][0:16, 0:DI], "q2")
    P.barrier()

    for i in range(NT):
        b = i // 8; j = i % 8
        P.wall(TE)
        for dc in range(4):
            stc = fCc[0:16, b * LLOC + j * 128:(b * LLOC + j * 128) + 128]
            hb = hinb if b == 0 else hinb2
            mvh = hb[0:16, dc * 512:(dc + 1) * 512]
            P.r(TE, lambda e, stc=stc, mvh=mvh, dc=dc:
                e.matmul(PS[dc][:, 0:512], stc, mvh, start=True, stop=True), inc="t")
        P.d(SY, c_a[:, 0:DI], ap["s_y"][ts_(i, 128), :], "q0")
        P.d(SY, c_b[:, 0:DI], ap["s_sz"][ts_(i, 128), :], "q0")
        P.d(SY, c_c[:, 0:DI], ap["s_xcf"][:, 0:0 + 0] if False else c_c[:, 0:DI], "q0") if False else None
        P.d(SY, c_c[:, 0:DI], ap["s_xcf"][:, b * LLOC + j * 128: b * LLOC + (j + 1) * 128], "q0", transpose=True)
        P.wall(V)
        for dc in range(4):
            P.r(V, lambda e, dc=dc: e.tensor_tensor(w_a[:, dc * 512:(dc + 1) * 512], c_a[:, dc * 512:(dc + 1) * 512], PS[dc][:, 0:512], OP.add), inc="v")
        P.w(TE, "v")
        P.r(V, lambda e: e.tensor_tensor(w_a[:, 0:DI], w_a[:, 0:DI], c_b[:, 0:DI], OP.mult), inc="v")
        P.r(V, lambda e: e.tensor_tensor(c_c[:, 0:DI], c_c[:, 0:DI], dprow[:, :], OP.mult))
        P.r(V, lambda e: e.tensor_tensor(c_a[:, 0:DI], w_a[:, 0:DI], c_c[:, 0:DI], OP.add), inc="v")
        P.w(S, "v")
        P.d(S, ap["s_yg"][ts_(i, 128), :], c_a[:, 0:DI], "q1")
        P.w(SY, "q1"); P.w(V, "q1"); P.w(S, "q1")
    P.barrier()

    ygfm = big1
    woT = bufA
    for k in range(16):
        P.d(SY, ygfm[:, ts_(k, T)], ap["s_yg"][:, ts_(k, 128)], "q0", transpose=True)
        P.d(SY, woT[:, ts_(k, D)], ap["s_wout"][:, ts_(k, 128)], "q0", transpose=True)
    P.wall(TE)
    for i in range(NT):
        for mc in range(2):
            ps = PS[mc]
            for k in range(16):
                st = ygfm[:, off_(k * T, i, 128)]
                mv = woT[:, k * D + mc * 512: k * D + mc * 512 + 512]
                P.r(TE, lambda e, ps=ps, st=st, mv=mv, k=k:
                    e.matmul(ps[:, 0:512], st, mv, start=(k == 0), stop=(k == 15)),
                    inc=("t" if k == 15 else None))
        P.w(V, "t"); P.w(V, "q1")
        P.r(V, lambda e: e.tensor_copy(c_a[:, 0:512], PS[0][:, 0:512]))
        P.r(V, lambda e: e.tensor_copy(c_a[:, 512:1024], PS[1][:, 0:512]), inc="v")
        P.w(TE, "v"); P.w(S, "v")
        P.d(S, ap["s_s"][ts_(i, 128), :], c_a[:, 0:D], "q1")
    P.barrier()

    xqfm = A16[:, 0:16384]
    for k in range(8):
        P.d(SY, xqfm[:, ts_(k, T)], ap["s_xq"][:, ts_(k, 128)], "q0", transpose=True)
    P.r(V, lambda e: e.memset(p_a[:, 24:25], 0.0), inc="v")
    P.r(V, lambda e: e.memset(sx2i[:, :], 0.0), inc="v")
    wmovB = A16[:, 16384:24576]
    wmov2B = A16[:, 24576:32768]
    tmark = {}
    for oc in range(8):
        wg_ = wmov if oc % 2 == 0 else wmovB
        wu_ = wmov2 if oc % 2 == 0 else wmov2B
        if oc >= 2:
            P.w(SY, "t", tmark[oc - 2])
        for k in range(8):
            P.d(SY, wg_[:, ts_(k, 512)], ap["s_f0wg"][oc * 512:(oc + 1) * 512, ts_(k, 128)], "q0", transpose=True)
            P.d(SY, wu_[:, ts_(k, 512)], ap["s_f0wu"][oc * 512:(oc + 1) * 512, ts_(k, 128)], "q0", transpose=True)
        P.w(TE, "q0")
        for i in range(NT):
            for k in range(8):
                st = xqfm[:, off_(k * T, i, 128)]
                P.r(TE, lambda e, st=st, k=k, w=wg_: e.matmul(PS[0][:, 0:512], st, w[:, ts_(k, 512)], start=(k == 0), stop=(k == 7)), inc=("t" if k == 7 else None))
            for k in range(8):
                st = xqfm[:, off_(k * T, i, 128)]
                P.r(TE, lambda e, st=st, k=k, w=wu_: e.matmul(PS[1][:, 0:512], st, w[:, ts_(k, 512)], start=(k == 0), stop=(k == 7)), inc=("t" if k == 7 else None))
            if i == NT - 1:
                tmark[oc] = P.cnt["t"]
            P.w(S, "t")
            P.r(S, lambda e, i=i: e.activation(out=w_a[:, 0:512], in_=PS[0][:, 0:512], func=AF.Silu, scale=sxg[:, i:i + 1]), inc="sc")
            P.r(S, lambda e, i=i: e.activation(out=w_b[:, 0:512], in_=PS[1][:, 0:512], func=AF.Identity, scale=sxu[:, i:i + 1], bias=0.0), inc="sc")
            P.w(V, "sc")
            P.w(TE, "sc")
            P.r(V, lambda e: e.tensor_tensor(w_a[:, 0:512], w_a[:, 0:512], w_b[:, 0:512], OP.mult))
            P.r(V, lambda e, i=i: e.reduce_max(p_a[:, 25:26], w_a[:, 0:512], AX.X, apply_absolute_value=True))
            P.r(V, lambda e, i=i: e.tensor_tensor(sx2i[:, i:i + 1], sx2i[:, i:i + 1], p_a[:, 25:26], OP.max), inc="v")
            P.w(S, "v")
            P.d(S, ap["s_h2"][ts_(i, 128), oc * 512:(oc + 1) * 512], w_a[:, 0:512], "q1")
            P.w(V, "q1"); P.w(S, "q1")
    P.r(V, lambda e: e.tensor_scalar(sx2i[:, :], sx2i[:, :], 1e-5, 1.0 / 127.0, OP.max, OP.mult))
    P.r(V, lambda e: e.tensor_scalar(sxd[:, :], sx2i[:, :], mgr[:, 2:3], None, OP.mult), inc="v")
    P.barrier()
    for i in range(NT):
        P.d(SY, w_a[:, 0:4096], ap["s_h2"][ts_(i, 128), :], "q0")
        P.wall(V)
        P.r(V, lambda e, i=i: e.reciprocal(p_a[:, 26:27], sx2i[:, i:i + 1]), inc="v")
        P.w(S, "v")
        P.r(S, lambda e: e.activation(out=w_b[:, 0:4096], in_=w_a[:, 0:4096], func=AF.Identity, scale=p_a[:, 26:27], bias=0.0), inc="sc")
        P.w(V, "sc")
        P.r(V, lambda e: e.tensor_copy(i_a[:, 0:4096], w_b[:, 0:4096]))
        P.r(V, lambda e: e.tensor_copy(c_a[:, 0:4096], i_a[:, 0:4096]), inc="v")
        P.w(S, "v")
        P.d(S, ap["s_xq2"][ts_(i, 128), :], c_a[:, 0:4096], "q1")
        P.w(SY, "q1"); P.w(V, "q1"); P.w(S, "q1")
    xq2fm = big1
    for half in range(2):
        P.w(SY, "t")
        for k in range(16):
            kk = half * 16 + k
            P.d(SY, xq2fm[:, ts_(k, T)], ap["s_xq2"][:, ts_(kk, 128)], "q0", transpose=True)
            P.d(SY, wmov[:, ts_(k, 512)], ap["s_f0wd"][0:512, ts_(kk, 128)], "q0", transpose=True)
            P.d(SY, wmov2[:, ts_(k, 512)], ap["s_f0wd"][512:1024, ts_(kk, 128)], "q0", transpose=True)
        if half == 0 and "s_dbg3" in _dbg:
            P.w(S, "q0")
            P.d(S, ap["s_dbg3"][:, 0:T], xq2fm[:, 0:T], "q1")
            P.d(S, ap["s_dbg3"][:, T:T + 512], wmov[:, 0:512], "q1")
            P.w(SY, "q1"); P.w(V, "q1"); P.w(S, "q1")
        P.wall(TE)
        for i in range(NT):
            for mc in range(2):
                ps = PS[mc]
                wv = wmov if mc == 0 else wmov2
                for k in range(16):
                    st = xq2fm[:, off_(k * T, i, 128)]
                    P.r(TE, lambda e, ps=ps, st=st, wv=wv, k=k:
                        e.matmul(ps[:, 0:512], st, wv[:, ts_(k, 512)], start=(k == 0), stop=(k == 15)),
                        inc=("t" if k == 15 else None))
            if half == 0:
                P.w(V, "t")
                P.r(V, lambda e: e.tensor_copy(w_a[:, 0:512], PS[0][:, 0:512]))
                P.r(V, lambda e: e.tensor_copy(w_a[:, 512:1024], PS[1][:, 0:512]), inc="v")
                P.w(TE, "v"); P.w(S, "v")
                P.d(S, ap["s_h2"][ts_(i, 128), 0:1024], w_a[:, 0:1024], "q1")
                P.w(V, "q1")
            else:
                P.w(SY, "v")
                P.d(SY, w_b[:, 0:1024], ap["s_h2"][ts_(i, 128), 0:1024], "q0")
                P.w(V, "t"); P.wall(V)
                P.r(V, lambda e: e.tensor_tensor(w_a[:, 0:512], PS[0][:, 0:512], w_b[:, 0:512], OP.add))
                P.r(V, lambda e: e.tensor_tensor(w_a[:, 512:1024], PS[1][:, 0:512], w_b[:, 512:1024], OP.add), inc="v")
                P.w(TE, "v"); P.w(S, "v"); P.w(S, "q1")
                P.r(S, lambda e, i=i: e.activation(out=c_b[:, 0:D], in_=w_a[:, 0:D], func=AF.Silu, scale=sxd[:, i:i + 1]), inc="sc")
                P.w(S, "sc")
                P.d(S, ap["s_g"][ts_(i, 128), :], c_b[:, 0:D], "q1")
                P.w(V, "sc")
    P.barrier()

    for i in range(NT):
        P.d(SY, w_b[:, 0:D], xr[i], "q0")
        P.d(SY, c_a[:, 0:D], ap["s_s"][ts_(i, 128), :], "q0")
        P.d(SY, c_b[:, 0:D], ap["s_g"][ts_(i, 128), :], "q0")
        P.wall(V)
        P.r(V, lambda e: e.tensor_tensor(c_a[:, 0:D], c_a[:, 0:D], c_b[:, 0:D], OP.mult), inc="v")
        P.r(V, lambda e: e.tensor_tensor(w_a[:, 0:D], w_b[:, 0:D], c_a[:, 0:D], OP.add), inc="v")
        P.w(S, "v")
        P.d(S, ap["s_x1"][ts_(i, 128), :], w_a[:, 0:D], "q1")
        P.w(V, "q1"); P.w(S, "q1")
        mu = p_a[:, 0:1]; vv = p_a[:, 1:2]; rr = p_a[:, 2:3]
        P.r(V, lambda e, mu=mu: e.reduce_sum(mu, w_a[:, 0:D], AX.X))
        P.r(V, lambda e, mu=mu: e.tensor_scalar_mul(mu, mu, 1.0 / D), inc="v")
        P.w(S, "v")
        P.r(S, lambda e: e.activation(out=w_b[:, 0:D], in_=w_a[:, 0:D], func=AF.Square), inc="sc")
        P.w(V, "sc")
        P.r(V, lambda e, vv=vv: e.reduce_sum(vv, w_b[:, 0:D], AX.X))
        P.r(V, lambda e, vv=vv: e.tensor_scalar_mul(vv, vv, 1.0 / D))
        P.r(V, lambda e, mu=mu, rr=rr: e.tensor_tensor(rr, mu, mu, OP.mult))
        P.r(V, lambda e, vv=vv, rr=rr: e.tensor_tensor(rr, vv, rr, OP.subtract))
        P.r(V, lambda e, rr=rr: e.tensor_scalar_add(rr, rr, 1e-5), inc="v")
        P.w(S, "v")
        P.r(S, lambda e, rr=rr: e.activation(out=rr, in_=rr, func=AF.Sqrt), inc="sc")
        P.w(V, "sc")
        P.r(V, lambda e, rr=rr: e.reciprocal(rr, rr))
        P.r(V, lambda e, mu=mu, rr=rr: e.tensor_scalar(w_a[:, 0:D], w_a[:, 0:D], mu, rr, OP.subtract, OP.mult))
        P.r(V, lambda e: e.tensor_copy(c_b[:, 0:D], w_a[:, 0:D]), inc="v")
        P.w(S, "v")
        P.d(S, ap["s_xn1"][ts_(i, 128), :], c_b[:, 0:D], "q1")
        P.w(V, "q1"); P.w(S, "q1")
        fp4_on_tile(128, D)
        P.w(S, "v")
        P.d(S, ap["s_xq1"][ts_(i, 128), :], c_a[:, 0:D], "q1")
        P.w(SY, "q1"); P.w(V, "q1"); P.w(S, "q1")
    P.barrier()

    xq1fm = A16[:, 0:16384]
    for k in range(8):
        P.d(SY, xq1fm[:, ts_(k, T)], ap["s_xq1"][:, ts_(k, 128)], "q0", transpose=True)
    OCW = [512, 512, 512, 512, 512, 256]
    ocs = [0, 512, 1024, 1536, 2048, 2560]
    wmovB = A16[:, 16384:24576]
    wmov2B = A16[:, 24576:32768]
    tmark = {}
    for oc in range(6):
        ow = OCW[oc]; o0 = ocs[oc]
        wg_ = wmov if oc % 2 == 0 else wmovB
        wu_ = wmov2 if oc % 2 == 0 else wmov2B
        if oc >= 2:
            P.w(SY, "t", tmark[oc - 2])
        for k in range(8):
            P.d(SY, wg_[:, k * 512:k * 512 + ow], ap["s_f1wg"][o0:o0 + ow, ts_(k, 128)], "q0", transpose=True)
            P.d(SY, wu_[:, k * 512:k * 512 + ow], ap["s_f1wu"][o0:o0 + ow, ts_(k, 128)], "q0", transpose=True)
        P.w(TE, "q0")
        for i in range(NT):
            for k in range(8):
                st = xq1fm[:, off_(k * T, i, 128)]
                P.r(TE, lambda e, st=st, k=k, ow=ow, w=wg_: e.matmul(PS[0][:, 0:ow], st, w[:, k * 512:k * 512 + ow], start=(k == 0), stop=(k == 7)), inc=("t" if k == 7 else None))
            for k in range(8):
                st = xq1fm[:, off_(k * T, i, 128)]
                P.r(TE, lambda e, st=st, k=k, ow=ow, w=wu_: e.matmul(PS[1][:, 0:ow], st, w[:, k * 512:k * 512 + ow], start=(k == 0), stop=(k == 7)), inc=("t" if k == 7 else None))
            if i == NT - 1:
                tmark[oc] = P.cnt["t"]
            P.w(S, "t")
            P.r(S, lambda e, ow=ow: e.activation(out=w_a[:, 0:ow], in_=PS[0][:, 0:ow], func=AF.Silu), inc="sc")
            P.r(S, lambda e, ow=ow: e.activation(out=w_b[:, 3000:3000 + ow] if False else w_b[:, 0:ow], in_=PS[1][:, 0:ow], func=AF.Identity, bias=0.0), inc="sc")
            P.w(V, "sc"); P.w(TE, "sc")
            P.r(V, lambda e, ow=ow: e.tensor_tensor(w_a[:, 0:ow], w_a[:, 0:ow], w_b[:, 0:ow], OP.mult), inc="v")
            fp4_on_tile(128, ow)
            P.w(S, "v")
            P.d(S, ap["s_xq3"][ts_(i, 128), o0:o0 + ow], c_a[:, 0:ow], "q1")
            P.w(V, "q1"); P.w(S, "q1")
    P.barrier()

    xq3fm = big1
    for half in range(2):
        P.w(SY, "t")
        for k in range(11):
            kk = half * 11 + k
            P.d(SY, xq3fm[:, ts_(k, T)], ap["s_xq3"][:, ts_(kk, 128)], "q0", transpose=True)
            P.d(SY, wmov[:, ts_(k, 512)], ap["s_f1wd"][0:512, ts_(kk, 128)], "q0", transpose=True)
            P.d(SY, wmov2[:, ts_(k, 512)], ap["s_f1wd"][512:1024, ts_(kk, 128)], "q0", transpose=True)
        P.wall(TE)
        for i in range(NT):
            for mc in range(2):
                ps = PS[mc]
                wv = wmov if mc == 0 else wmov2
                for k in range(11):
                    st = xq3fm[:, off_(k * T, i, 128)]
                    P.r(TE, lambda e, ps=ps, st=st, wv=wv, k=k:
                        e.matmul(ps[:, 0:512], st, wv[:, ts_(k, 512)], start=(k == 0), stop=(k == 10)),
                        inc=("t" if k == 10 else None))
            if half == 0:
                P.w(V, "t")
                P.r(V, lambda e: e.tensor_copy(w_a[:, 0:512], PS[0][:, 0:512]))
                P.r(V, lambda e: e.tensor_copy(w_a[:, 512:1024], PS[1][:, 0:512]), inc="v")
                P.w(TE, "v"); P.w(S, "v")
                P.d(S, ap["s_h2"][ts_(i, 128), 0:1024], w_a[:, 0:1024], "q1")
                P.w(V, "q1")
            else:
                P.w(SY, "v")
                P.d(SY, w_b[:, 0:1024], ap["s_h2"][ts_(i, 128), 0:1024], "q0")
                P.w(V, "t"); P.wall(V)
                P.r(V, lambda e: e.tensor_tensor(w_a[:, 0:512], PS[0][:, 0:512], w_b[:, 0:512], OP.add))
                P.r(V, lambda e: e.tensor_tensor(w_a[:, 512:1024], PS[1][:, 0:512], w_b[:, 512:1024], OP.add), inc="v")
                P.w(TE, "v"); P.w(S, "v"); P.w(S, "q1")
                P.r(S, lambda e: e.activation(out=c_b[:, 0:D], in_=w_a[:, 0:D], func=AF.Silu), inc="sc")
                P.w(S, "sc")
                P.d(S, ap["s_g1"][ts_(i, 128), :], c_b[:, 0:D], "q1")
                P.w(V, "sc")
    P.barrier()

    xn1fm = big1[:, 0:16384]
    for k in range(8):
        P.d(SY, xn1fm[:, ts_(k, T)], ap["s_xn1"][:, ts_(k, 128)], "q0", transpose=True)
    QCW = [512, 512, 512, 512, 512, 512, 32]
    qcs = [0, 512, 1024, 1536, 2048, 2560, 3072]
    wmovB = A16[:, 16384:24576]
    tmark = {}
    for oc in range(7):
        tmark[oc] = P.cnt["t"]
        ow = QCW[oc]; o0 = qcs[oc]
        wg_ = wmov if oc % 2 == 0 else wmovB
        if oc >= 2:
            P.w(SY, "t", tmark[oc - 1])
        for k in range(8):
            P.d(SY, wg_[:, k * 512:k * 512 + ow], ap["s_qkvb"][o0:o0 + ow, ts_(k, 128)], "q0", transpose=True)
        P.w(TE, "q0")
        for i in range(NT):
            for k in range(8):
                st = xn1fm[:, off_(k * T, i, 128)]
                P.r(TE, lambda e, st=st, k=k, ow=ow, w=wg_: e.matmul(PS[0][:, 0:ow], st, w[:, k * 512:k * 512 + ow], start=(k == 0), stop=(k == 7)), inc=("t" if k == 7 else None))
            P.w(V, "t")
            P.r(V, lambda e, ow=ow: e.tensor_copy(c_a[:, 0:ow], PS[0][:, 0:ow]), inc="v")
            P.w(TE, "v"); P.w(S, "v")
            P.d(S, ap["s_qkv"][ts_(i, 128), o0:o0 + ow], c_a[:, 0:ow], "q1")
            P.w(V, "q1"); P.w(S, "q1")
    P.barrier()

    def elu1_norm(dst_bf, src_bf, tmpf):
        P.r(V, lambda e: e.tensor_scalar(w_a[:, 0:1024], src_bf, 0.0, None, OP.min), inc="v")
        P.w(S, "v")
        P.r(S, lambda e: e.activation(out=w_a[:, 1024:2048], in_=w_a[:, 0:1024], func=AF.Exp), inc="sc")
        P.w(V, "sc")
        P.r(V, lambda e: e.tensor_scalar(w_a[:, 0:1024], src_bf, 0.0, None, OP.max))
        P.r(V, lambda e: e.tensor_tensor(w_a[:, 0:1024], w_a[:, 0:1024], w_a[:, 1024:2048], OP.add), inc="v")
        a3 = w_a[:, 0:1024].rearrange("p (h d) -> p h d", d=64)
        P.r(V, lambda e, a3=a3: e.tensor_reduce(p_a[:, 32:48], a3, AX.X, OP.add))
        P.r(V, lambda e: e.reciprocal(p_a[:, 32:48], p_a[:, 32:48]), inc="v")
        rsb = p_a[:, 32:48][:, :, None].broadcast_to([128, 16, 64])
        P.r(V, lambda e, a3=a3, rsb=rsb, dst_bf=dst_bf: e.tensor_tensor(
            dst_bf.rearrange("p (h d) -> p h d", d=64), a3, rsb, OP.mult), inc="v")

    for j in range(8):
        P.wall(V)
        P.d(SY, c_a[:, 0:3104], ap["s_qkv"][ts_(j, 128), :], "q0")
        P.d(SY, c_b[:, 0:3104], ap["s_qkv"][ts_(8 + j, 128), :], "q0")
        P.wall(V)
        elu1_norm(c_c[:, 0:1024], c_a[:, 0:1024], 0)
        elu1_norm(c_c[:, 1024:2048], c_a[:, 1024:2048], 0)
        elu1_norm(c_c[:, 2048:3072], c_b[:, 0:1024], 0)
        elu1_norm(c_b[:, 0:1024], c_b[:, 1024:2048], 0)
        P.r(V, lambda e: e.tensor_scalar(w_a[:, 0:16], c_a[:, 3072:3088], -1.0, None, OP.mult))
        P.r(V, lambda e: e.tensor_scalar(w_a[:, 16:32], c_b[:, 3072:3088], -1.0, None, OP.mult), inc="v")
        P.w(S, "v")
        P.r(S, lambda e: e.activation(out=w_a[:, 0:32], in_=w_a[:, 0:32], func=AF.Exp), inc="sc")
        P.w(V, "sc")
        P.r(V, lambda e: e.tensor_scalar_add(w_a[:, 0:32], w_a[:, 0:32], 1.0))
        P.r(V, lambda e: e.reciprocal(w_a[:, 0:32], w_a[:, 0:32]), inc="v")
        b0c = w_a[:, 0:16]; b1c = w_a[:, 16:32]
        def dot(dst_cols, aa, bb):
            P.r(V, lambda e, aa=aa, bb=bb: e.tensor_tensor(w_b[:, 0:1024], aa, bb, OP.mult), inc="v")
            a3 = w_b[:, 0:1024].rearrange("p (h d) -> p h d", d=64)
            P.r(V, lambda e, a3=a3, dst_cols=dst_cols: e.tensor_reduce(p_a[:, dst_cols:dst_cols + 16], a3, AX.X, OP.add), inc="v")
        q0n = c_c[:, 0:1024]; k0n = c_c[:, 1024:2048]; q1n = c_c[:, 2048:3072]; k1n = c_b[:, 0:1024]
        dot(32, q0n, k0n)
        dot(48, q1n, k0n)
        dot(16, q1n, k1n)
        dot(0, k1n, k0n)
        v0 = c_a[:, 2048:3072]; v1 = c_b[:, 2048:3072]
        b0b = b0c[:, :, None].broadcast_to([128, 16, 64])
        P.r(V, lambda e, b0b=b0b, v0=v0: e.tensor_tensor(
            c_a[:, 0:1024].rearrange("p (h d) -> p h d", d=64), v0.rearrange("p (h d) -> p h d", d=64), b0b, OP.mult), inc="v")
        vi0 = c_a[:, 0:1024]
        d_q0k0 = p_a[:, 32:48][:, :, None].broadcast_to([128, 16, 64])
        P.r(V, lambda e, d_q0k0=d_q0k0, vi0=vi0: e.tensor_tensor(
            c_a[:, 1024:2048].rearrange("p (h d) -> p h d", d=64), vi0.rearrange("p (h d) -> p h d", d=64), d_q0k0, OP.mult), inc="v")
        d_k1k0 = p_a[:, 0:16][:, :, None].broadcast_to([128, 16, 64])
        P.r(V, lambda e, d_k1k0=d_k1k0, vi0=vi0: e.tensor_tensor(
            w_b[:, 0:1024].rearrange("p (h d) -> p h d", d=64), vi0.rearrange("p (h d) -> p h d", d=64), d_k1k0, OP.mult), inc="v")
        P.r(V, lambda e, v1=v1: e.tensor_tensor(w_b[:, 0:1024], v1, w_b[:, 0:1024], OP.subtract), inc="v")
        b1b = b1c[:, :, None].broadcast_to([128, 16, 64])
        P.r(V, lambda e, b1b=b1b: e.tensor_tensor(
            w_b[:, 0:1024].rearrange("p (h d) -> p h d", d=64), w_b[:, 0:1024].rearrange("p (h d) -> p h d", d=64), b1b, OP.mult), inc="v")
        vi1 = w_b[:, 0:1024]
        d_q1k0 = p_a[:, 48:64][:, :, None].broadcast_to([128, 16, 64])
        d_q1k1 = p_a[:, 16:32][:, :, None].broadcast_to([128, 16, 64])
        P.r(V, lambda e, d_q1k0=d_q1k0, vi0=vi0: e.tensor_tensor(
            w_b[:, 1024:2048].rearrange("p (h d) -> p h d", d=64), vi0.rearrange("p (h d) -> p h d", d=64), d_q1k0, OP.mult), inc="v")
        P.r(V, lambda e, d_q1k1=d_q1k1, vi1=vi1: e.tensor_tensor(
            w_b[:, 2048:3072].rearrange("p (h d) -> p h d", d=64), vi1.rearrange("p (h d) -> p h d", d=64), d_q1k1, OP.mult), inc="v")
        P.r(V, lambda e: e.tensor_tensor(w_b[:, 1024:2048], w_b[:, 1024:2048], w_b[:, 2048:3072], OP.add), inc="v")
        P.r(V, lambda e: e.tensor_scalar(c_a[:, 1024:2048], c_a[:, 1024:2048], 0.125, None, OP.mult))
        P.r(V, lambda e: e.tensor_scalar(c_b[:, 1024:2048], w_b[:, 1024:2048], 0.125, None, OP.mult), inc="v")
        P.w(S, "v")
        P.d(S, ap["s_lay"][ts_(j, 128), :], c_a[:, 1024:2048], "q1")
        P.d(S, ap["s_lay"][ts_(8 + j, 128), :], c_b[:, 1024:2048], "q1")
        P.w(SY, "q1"); P.w(V, "q1"); P.w(S, "q1")
    P.barrier()

    layfm = big1[:, 0:16384]
    P.d(G, lngrow[:, :], ap["ln_g"][None, :].broadcast_to([128, D]), "q2")
    P.d(G, lnbrow[:, :], ap["ln_b"][None, :].broadcast_to([128, D]), "q2")
    for k in range(8):
        P.d(SY, layfm[:, ts_(k, T)], ap["s_lay"][:, ts_(k, 128)], "q0", transpose=True)
    for k in range(8):
        P.d(SY, wmov[:, k * 512:(k + 1) * 512], ap["s_wo"][0:512, ts_(k, 128)], "q0", transpose=True)
        P.d(SY, wmov2[:, k * 512:(k + 1) * 512], ap["s_wo"][512:1024, ts_(k, 128)], "q0", transpose=True)
    P.barrier()
    P.wall(TE)
    for i in range(NT):
        for mc in range(2):
            ps = PS[mc]
            wv = wmov if mc == 0 else wmov2
            for k in range(8):
                st = layfm[:, off_(k * T, i, 128)]
                P.r(TE, lambda e, ps=ps, st=st, wv=wv, k=k:
                    e.matmul(ps[:, 0:512], st, wv[:, ts_(k, 512)], start=(k == 0), stop=(k == 7)),
                    inc=("t" if k == 7 else None))
        if i == 1:
            P.barrier()
        P.d(SY, c_a[:, 0:D], ap["s_xn1"][ts_(i, 128), :], "q0")
        P.d(SY, w_b[:, 0:D], ap["s_x1"][ts_(i, 128), :], "q0")
        P.w(V, "t"); P.wall(V)
        P.r(V, lambda e: e.tensor_tensor(w_a[:, 0:512], PS[0][:, 0:512], c_a[:, 0:512], OP.add))
        P.r(V, lambda e: e.tensor_tensor(w_a[:, 512:1024], PS[1][:, 0:512], c_a[:, 512:1024], OP.add), inc="v")
        P.w(TE, "v")
        mu = p_a[:, 0:1]; vv = p_a[:, 1:2]; rr = p_a[:, 2:3]
        P.r(V, lambda e, mu=mu: e.reduce_sum(mu, w_a[:, 0:D], AX.X))
        P.r(V, lambda e, mu=mu: e.tensor_scalar_mul(mu, mu, 1.0 / D), inc="v")
        P.w(S, "v")
        P.r(S, lambda e: e.activation(out=c_b[:, 0:D], in_=w_a[:, 0:D], func=AF.Square), inc="sc")
        P.w(V, "sc")
        P.r(V, lambda e, vv=vv: e.reduce_sum(vv, c_b[:, 0:D], AX.X))
        P.r(V, lambda e, vv=vv: e.tensor_scalar_mul(vv, vv, 1.0 / D))
        P.r(V, lambda e, mu=mu, rr=rr: e.tensor_tensor(rr, mu, mu, OP.mult))
        P.r(V, lambda e, vv=vv, rr=rr: e.tensor_tensor(rr, vv, rr, OP.subtract))
        P.r(V, lambda e, rr=rr: e.tensor_scalar_add(rr, rr, 1e-5), inc="v")
        P.w(S, "v")
        P.r(S, lambda e, rr=rr: e.activation(out=rr, in_=rr, func=AF.Sqrt), inc="sc")
        P.w(V, "sc")
        P.r(V, lambda e, rr=rr: e.reciprocal(rr, rr))
        P.r(V, lambda e, mu=mu, rr=rr: e.tensor_scalar(w_a[:, 0:D], w_a[:, 0:D], mu, rr, OP.subtract, OP.mult))
        P.r(V, lambda e: e.tensor_tensor(w_a[:, 0:D], w_a[:, 0:D], lngrow[:, :], OP.mult))
        P.r(V, lambda e: e.tensor_tensor(w_a[:, 0:D], w_a[:, 0:D], lnbrow[:, :], OP.add), inc="v")
        P.w(SY, "v")
        P.d(SY, c_b[:, 0:D], ap["s_g1"][ts_(i, 128), :], "q0")
        P.wall(V)
        P.r(V, lambda e: e.tensor_tensor(w_a[:, 0:D], w_a[:, 0:D], c_b[:, 0:D], OP.mult), inc="v")
        P.r(V, lambda e: e.tensor_tensor(w_a[:, 0:D], w_a[:, 0:D], w_b[:, 0:D], OP.add), inc="v")
        P.w(S, "v")
        b = i // 8; j = i % 8
        P.d(S, ap["out"][b, ts_(j, 128), :], w_a[:, 0:D], "q1")
        P.w(SY, "q1"); P.w(V, "q1"); P.w(S, "q1"); P.w(G, "q1")
        if i == 0:
            P.barrier()
    P.wall(SY); P.wall(G); P.wall(V)

    import os as _os
    _cut = int(_os.environ.get("MK_CUT", "-1"))
    if _cut >= 0 and hasattr(P, "marks") and _cut < len(P.marks):
        for e in ENGINES:
            P.stream[e] = P.stream[e][:P.marks[_cut][e]]
    blkc = nc.Block(); blk = blkc.__enter__()

    def emit(eng_name):
        def _f(e):
            for fn in P.stream[eng_name]:
                fn(e)
        return _f
    blk.sync(emit("sync"))
    blk.vector(emit("vector"))
    blk.scalar(emit("scalar"))
    blk.gpsimd(emit("gpsimd"))
    blk.tensor(emit("tensor"))
    blkc.__exit__(None, None, None)
    for t in reversed(ctx):
        t.__exit__(None, None, None)
    return nc


_NC_CACHE = None


def kernel(L0_attn_x, L0_attn_W_in, L0_attn_conv_w, L0_attn_conv_b, L0_attn_W_xproj,
           L0_attn_A_log, L0_attn_D_param, L0_attn_W_out, L0_ffn_w_gate, L0_ffn_w_up,
           L0_ffn_w_down, L1_attn_W_qkvb, L1_attn_W_o, L1_attn_ln_g, L1_attn_ln_b,
           L1_ffn_w_gate, L1_ffn_w_up, L1_ffn_w_down):
    global _NC_CACHE
    if _NC_CACHE is None:
        _NC_CACHE = build_program()
    nc_ = _NC_CACHE
    x = np.ascontiguousarray(L0_attn_x, dtype=np.float32)
    Wq = np.ascontiguousarray(L1_attn_W_qkvb, dtype=np.float32).reshape(16, 193, 1024)
    Wq_perm = np.concatenate([
        Wq[:, 0:64].reshape(1024, 1024),
        Wq[:, 64:128].reshape(1024, 1024),
        Wq[:, 128:192].reshape(1024, 1024),
        Wq[:, 192:193].reshape(16, 1024),
        np.zeros((16, 1024), np.float32)], axis=0)
    maskc = (np.arange(128)[:, None] % TC <= np.arange(TC)[None, :]).astype(ml_dtypes.bfloat16)
    identc = np.eye(128, dtype=ml_dtypes.bfloat16)
    in_maps = []
    for c in range(NCORES):
        lo = c * LLOC
        halo = x[:, lo - 3:lo].copy() if c > 0 else np.zeros((B, 3, D), np.float32)
        rm = (np.arange(1, NCORES + 1) <= c).astype(np.float32)
        rm = np.zeros(NCORES, np.float32)
        rm[1:] = (np.arange(0, NCORES - 1) < c).astype(np.float32)
        in_maps.append({
            "x": x[:, lo:lo + LLOC].copy(),
            "xh": halo,
            "W_in": L0_attn_W_in, "conv_w": L0_attn_conv_w, "conv_b": L0_attn_conv_b,
            "W_xproj": L0_attn_W_xproj, "A_log": L0_attn_A_log, "D_param": L0_attn_D_param,
            "W_out": L0_attn_W_out,
            "f0_wg": L0_ffn_w_gate, "f0_wu": L0_ffn_w_up, "f0_wd": L0_ffn_w_down,
            "Wqkvb": Wq_perm, "W_o": L1_attn_W_o, "ln_g": L1_attn_ln_g, "ln_b": L1_attn_ln_b,
            "f1_wg": L1_ffn_w_gate, "f1_wu": L1_ffn_w_up, "f1_wd": L1_ffn_w_down,
            "mask": maskc, "ident": identc, "identf": identc.astype(np.float32), "rmask": rm,
        })
    res = run_bass_kernel_spmd(nc_, in_maps, core_ids=list(range(NCORES)))
    out = np.concatenate([np.asarray(r["out"], np.float32) for r in res.results], axis=1)
    return out




# version 05
# speedup vs baseline: 1.0827x; 1.0349x over previous
import numpy as np
import ml_dtypes
import concourse.bass as bass
import concourse.mybir as mybir
from concourse.bass_utils import run_bass_kernel_spmd

F32 = mybir.dt.float32
BF16 = mybir.dt.bfloat16
I16 = mybir.dt.int16
OP = mybir.AluOpType
AF = mybir.ActivationFunctionType
AX = mybir.AxisListType

B = 2
LLOC = 1024
T = 2048
NT = 16
D = 1024
DI = 2048
TC = 32
NCH = 32
NG = 8
NCORES = 8

ENGINES = ("sync", "vector", "scalar", "gpsimd", "tensor")
SEMS = {}
V, S, G, TE, SY = "vector", "scalar", "gpsimd", "tensor", "sync"


def ts_(i, n=128):
    return slice(i * n, (i + 1) * n)


def off_(base, i, n=128):
    return slice(base + i * n, base + (i + 1) * n)


class Prog:
    NW = 6

    def __init__(self):
        self.stream = {e: [] for e in ENGINES}
        self.cnt = {s: 0 for s in ["v", "sc", "g", "t", "cc", "bs"]}
        self.qs = {}
        for q in ("q0", "q1", "q2"):
            self.qs[q] = {"idx": 0, "tot": [0] * self.NW, "open": False}

    def _qsem(self, q, i):
        return q + chr(ord("a") + i)

    def w(self, eng, sem, val=None):
        if sem in self.qs:
            st = self.qs[sem]
            for i in range(self.NW):
                if st["tot"][i] > 0:
                    self.stream[eng].append(
                        lambda e, s=self._qsem(sem, i), vv=st["tot"][i]: e.wait_ge(SEMS[s], vv))
            st["open"] = False
            return
        v = self.cnt[sem] if val is None else val
        if v > 0:
            self.stream[eng].append(lambda e, s=sem, vv=v: e.wait_ge(SEMS[s], vv))

    OWN = {"vector": "v", "scalar": "sc", "gpsimd": "g", "tensor": "t"}

    def r(self, eng, fn, inc=None, k=1):
        own = self.OWN.get(eng)
        if own is not None:
            v = self.cnt[own]
            if v > 0:
                self.stream[eng].append(lambda e, s=own, vv=v: e.wait_ge(SEMS[s], vv))
            if inc is None or inc == own:
                self.cnt[own] += 1
                self.stream[eng].append(lambda e, f=fn, s=own: f(e).then_inc(SEMS[s], 1))
            else:
                self.cnt[inc] += k
                self.stream[eng].append(lambda e, f=fn, s=inc, kk=k: f(e).then_inc(SEMS[s], kk))
        else:
            if inc is not None:
                self.cnt[inc] += k
                self.stream[eng].append(lambda e, f=fn, s=inc, kk=k: f(e).then_inc(SEMS[s], kk))
            else:
                self.stream[eng].append(lambda e, f=fn: f(e)) 

    def d(self, eng, out, in_, q, transpose=False, slow=False):
        st = self.qs[q]
        if not st["open"]:
            st["idx"] = (st["idx"] + 1) % self.NW
            i = st["idx"]
            if st["tot"][i] > 0:
                self.stream[eng].append(
                    lambda e, s=self._qsem(q, i), vv=st["tot"][i]: e.wait_ge(SEMS[s], vv))
            st["open"] = True
        i = st["idx"]
        st["tot"][i] += 16
        sname = self._qsem(q, i)
        if transpose and not slow:
            self.stream[eng].append(
                lambda e, o=out, i_=in_, s=sname: e.dma_start_transpose(o, i_).then_inc(SEMS[s], 16))
        elif transpose and slow:
            def _f(e, o=out, i_=in_, s=sname):
                with e.bass.allow_non_contiguous_dma(reason="tiny transpose"):
                    return e.dma_start_transpose(o, i_).then_inc(SEMS[s], 16)
            self.stream[eng].append(_f)
        else:
            self.stream[eng].append(
                lambda e, o=out, i_=in_, s=sname: e.dma_start(out=o, in_=i_, allow_slow_non_contiguous=slow).then_inc(SEMS[s], 16))

    def snap(self):
        return dict(self.cnt)

    def wall(self, eng, snap=None):
        for q in ("q0", "q1", "q2"):
            self.w(eng, q)
        sn = self.cnt if snap is None else snap
        for s, v in sn.items():
            if v > 0:
                self.w(eng, s, v)

    def barrier(self):
        m = {}
        for eng, sem in [("sync", "bs"), ("vector", "v"), ("scalar", "sc"),
                         ("gpsimd", "g"), ("tensor", "t")]:
            self.cnt[sem] += 1
            m[sem] = self.cnt[sem]
            self.stream[eng].append(lambda e, s=sem: e.sem_inc(SEMS[s], 1))
        for eng in ENGINES:
            for s, v in m.items():
                if s != {"sync": "bs", "vector": "v", "scalar": "sc",
                         "gpsimd": "g", "tensor": "t"}[eng] or True:
                    self.w(eng, s, v)
            for q in ("q0", "q1", "q2"):
                self.w(eng, q)
        if not hasattr(self, "marks"):
            self.marks = []
        self.marks.append({e: len(self.stream[e]) for e in ENGINES})


def build_program():
    nc = bass.Bass("TRN2", target_bir_lowering=False)
    P = Prog()
    ap = {}

    def IN(name, shape, dt=F32):
        ap[name] = nc.dram_tensor(name, list(shape), dt, kind="ExternalInput").ap()

    def OUT(name, shape, dt=F32):
        ap[name] = nc.dram_tensor(name, list(shape), dt, kind="ExternalOutput").ap()

    import os as _os2
    _dbg = set((_os2.environ.get("MK_DBG", "")).split(",")) - {""}

    def INT(name, shape, dt=F32, **kw):
        kind = "ExternalOutput" if name in _dbg else "Internal"
        if kw.get("addr_space"):
            kind = "Internal"
        ap[name] = nc.dram_tensor(name, list(shape), dt, kind=kind, **kw).ap()

    IN("x", (B, LLOC, D)); IN("xh", (B, 3, D))
    IN("W_in", (2 * DI, D)); IN("conv_w", (DI, 4)); IN("conv_b", (DI,))
    IN("W_xproj", (33, DI)); IN("A_log", (16,)); IN("D_param", (DI,))
    IN("W_out", (D, DI))
    IN("f0_wg", (4096, D)); IN("f0_wu", (4096, D)); IN("f0_wd", (D, 4096))
    IN("Wqkvb", (3104, D))
    IN("W_o", (D, D)); IN("ln_g", (D,)); IN("ln_b", (D,))
    IN("f1_wg", (2816, D)); IN("f1_wu", (2816, D)); IN("f1_wd", (D, 2816))
    IN("mask", (128, TC), BF16)
    IN("ident", (128, 128), BF16)
    IN("identf", (128, 128))
    IN("rmask", (NCORES,))
    OUT("out", (B, LLOC, D))

    INT("s_win", (2 * DI, D), BF16)
    INT("s_wxp", (48, DI), BF16)
    INT("s_wout", (D, DI), BF16)
    INT("s_f0wg", (4096, D), BF16); INT("s_f0wu", (4096, D), BF16); INT("s_f0wd", (D, 4096), BF16)
    INT("s_qkvb", (3104, D), BF16); INT("s_wo", (D, D), BF16)
    INT("s_f1wg", (2816, D), BF16); INT("s_f1wu", (2816, D), BF16); INT("s_f1wd", (D, 2816), BF16)
    INT("s_xn", (T, D), BF16)
    INT("s_xq", (T, D), BF16)
    INT("s_xcf", (DI, T), BF16)
    INT("s_sz", (T, DI), BF16)
    INT("s_y", (T, DI), BF16)
    INT("s_yg", (T, DI), BF16)
    INT("s_h2", (T, 4096), F32)
    INT("s_xq2", (T, 4096), BF16)
    INT("s_x1", (T, D), F32)
    INT("s_xn1", (T, D), BF16)
    INT("s_xq1", (T, D), BF16)
    INT("s_qkv", (T, 3104), BF16)
    INT("s_xq3", (T, 2816), BF16)
    INT("s_lay", (T, D), BF16)
    INT("s_s", (T, D), BF16)
    INT("s_g", (T, D), BF16)
    INT("s_g1", (T, D), BF16)
    INT("s_fb", (16, B * LLOC), BF16)
    INT("s_dbg", (128, B * LLOC), F32)
    INT("s_dbg2", (128, 64), F32)
    INT("s_dbg3", (128, 2560), BF16)
    INT("cc_in", (32, DI + 1), F32)
    INT("cc_out", (NCORES, 32, DI + 1), F32, addr_space="Shared")

    ctx = []
    SB = {}

    def sb(name, p, f, dt=F32):
        t = nc.sbuf_tensor(name, [p, f], dt)
        h = t.__enter__(); ctx.append(t); SB[name] = h
        return h

    msk = sb("msk", 128, TC, BF16)
    idn = sb("idn", 128, 128, BF16)
    idnf = sb("idnf", 128, 128)
    rmk = sb("rmk", 32, NCORES)
    Acol = sb("Acol", 16, 1)
    cbw = sb("cbw", 128, 16 * 4)
    cbb = sb("cbb", 128, 16)
    onec = sb("onec", 128, 2)
    sxinv = sb("sxinv", 128, NT)
    sx2i = sb("sx2i", 128, NT)
    sxg = sb("sxg", 128, NT); sxu = sb("sxu", 128, NT); sxd = sb("sxd", 128, NT)
    wsc = sb("wsc", 4, 8)
    mgr = sb("mgr", 128, 4)
    p_a = sb("p_a", 128, 64)
    p_b = sb("p_b", 128, 128)
    w_a = sb("w_a", 128, 4096)
    w_b = sb("w_b", 128, 4096)
    c_a = sb("c_a", 128, 4096, BF16)
    c_b = sb("c_b", 128, 4096, BF16)
    i_a = sb("i_a", 128, 4096, I16)
    xhn = sb("xhn", 8, D, BF16)
    xhh = sb("xhh", 8, D)
    xhf = sb("xhf", 128, 8 * 8, BF16)
    A16 = sb("A16", 128, 49152, BF16)
    A32 = sb("A32", 128, 11264)

    def a16(off, n):
        return A16[:, off:off + n]

    def a32(off, n):
        return A32[:, off:off + n]

    dprow = A16[:, 47104:49152]
    lngrow = A16[:, 16384:17408]
    lnbrow = A16[:, 17408:18432]

    xnfm = a16(0, 16384)
    xpfc = a16(16384, 8256)
    xcfc = a16(24640, 8192)
    winS = a16(32832, 2048)
    zwv = a16(34880, 4096)
    wxpT = a16(41024, 768)
    tails = a16(41856, 64)
    c_c = a16(43008, 4096)
    fC = a16(0, 2048); fB = a16(2048, 2048); fCc = a16(4096, 2048)
    btT = a16(6144, 2048)
    gsb = a16(7168, 16384)
    hstg = a16(23552, 2048)
    hinb = a16(25600, 2048)
    hinb2 = a16(6144, 2048)
    xtm = a16(27648, 4096)
    big1 = A16[:, 0:32768]
    bufA = A16[:, 32768:49152]
    wmov = bufA[:, 0:8192]
    wmov2 = bufA[:, 8192:16384]
    fL = A32[:, 0:2048]; fsc = A32[:, 2048:4096]; f0z = A32[:, 4096:5120]
    xdb = A32[:, 5120:7168]
    hcurb = [A32[:, 7168:9216], A32[:, 9216:11264]]; hin = A32[:, 0:2048]
    xdbB = A32[:, 5120:7168]
    xdbC = A32[:, 7168:9216]
    xdbV = A32[:, 9216:11264]
    fDst = p_b[:, 0:64]; fDch = p_b[:, 64:128]
    fzv = w_a[:, 2048:4096]

    PSG_t = nc.psum_tensor("psg", [128, 2048], F32)
    PSG = PSG_t.__enter__(); ctx.append(PSG_t)
    PSY_t = nc.psum_tensor("psy", [128, 2048], F32)
    PSY = PSY_t.__enter__(); ctx.append(PSY_t)
    PS = [PSG[:, 0:512], PSG[:, 512:1024], PSG[:, 1024:1536], PSG[:, 1536:2048],
          PSY[:, 0:512], PSY[:, 512:1024], PSY[:, 1024:1536], PSY[:, 1536:2048]]

    semnames = ["v", "sc", "g", "t", "cc", "bs"]
    for q in ("q0", "q1", "q2"):
        for i in range(Prog.NW):
            semnames.append(q + chr(ord("a") + i))
    for s in semnames:
        h = nc.semaphore(s)
        SEMS[s] = h.__enter__(); ctx.append(h)

    P.d(SY, msk[:, :], ap["mask"][:, :], "q0")
    P.d(SY, idn[:, :], ap["ident"][:, :], "q0")
    P.d(SY, idnf[:, :], ap["identf"][:, :], "q0")
    P.d(SY, rmk[:, :], ap["rmask"][None, :].broadcast_to([32, NCORES]), "q0")
    P.d(G, dprow[:, :], ap["D_param"][None, :].broadcast_to([128, DI]), "q2")
    P.d(SY, Acol[:, 0:1], ap["A_log"][:, None], "q0")
    P.d(SY, cbw[:, :].rearrange("p (k j) -> p k j", j=4),
        ap["conv_w"].rearrange("(k p) j -> p k j", p=128), "q0", slow=True)
    P.d(SY, cbb[:, :], ap["conv_b"].rearrange("(k p) -> p k", p=128), "q0", slow=True)
    P.wall(V); P.wall(S)
    P.r(V, lambda e: e.memset(onec[:, :], 1.0), inc="v")
    P.r(S, lambda e: e.activation(out=Acol[0:16, 0:1], in_=Acol[0:16, 0:1], func=AF.Exp), inc="sc")
    P.w(V, "sc")
    P.r(V, lambda e: e.tensor_scalar_mul(Acol[0:16, 0:1], Acol[0:16, 0:1], -1.0), inc="v")

    xr = ap["x"].rearrange("b (i p) d -> (b i) p d", p=128)
    for i in range(NT):
        P.d(SY, w_a[:, 0:D], xr[i], "q0")
        P.wall(V)
        mu = p_a[:, 0:1]; vv = p_a[:, 1:2]; rr = p_a[:, 2:3]; rmx = p_a[:, 3:4]
        P.r(V, lambda e, mu=mu: e.reduce_sum(mu, w_a[:, 0:D], AX.X))
        P.r(V, lambda e, mu=mu: e.tensor_scalar_mul(mu, mu, 1.0 / D), inc="v")
        P.w(S, "v")
        P.r(S, lambda e: e.activation(out=w_b[:, 0:D], in_=w_a[:, 0:D], func=AF.Square), inc="sc")
        P.w(V, "sc")
        P.r(V, lambda e, vv=vv: e.reduce_sum(vv, w_b[:, 0:D], AX.X))
        P.r(V, lambda e, vv=vv: e.tensor_scalar_mul(vv, vv, 1.0 / D))
        P.r(V, lambda e, mu=mu, rr=rr: e.tensor_tensor(rr, mu, mu, OP.mult))
        P.r(V, lambda e, vv=vv, rr=rr: e.tensor_tensor(rr, vv, rr, OP.subtract))
        P.r(V, lambda e, rr=rr: e.tensor_scalar_add(rr, rr, 1e-5), inc="v")
        P.w(S, "v")
        P.r(S, lambda e, rr=rr: e.activation(out=rr, in_=rr, func=AF.Sqrt), inc="sc")
        P.w(V, "sc")
        P.r(V, lambda e, rr=rr: e.reciprocal(rr, rr))
        P.r(V, lambda e, mu=mu, rr=rr: e.tensor_scalar(w_a[:, 0:D], w_a[:, 0:D], mu, rr, OP.subtract, OP.mult))
        P.r(V, lambda e: e.tensor_copy(c_a[:, 0:D], w_a[:, 0:D]))
        P.r(V, lambda e, rmx=rmx: e.reduce_max(rmx, w_a[:, 0:D], AX.X, apply_absolute_value=True))
        P.r(V, lambda e, rmx=rmx: e.tensor_scalar_max(rmx, rmx, 1e-5))
        P.r(V, lambda e, i=i, rmx=rmx: e.tensor_scalar_mul(sxinv[:, i:i + 1], rmx, 1.0 / 127.0))
        P.r(V, lambda e, i=i, rr=rr: e.reciprocal(rr, sxinv[:, i:i + 1]), inc="v")
        P.w(S, "v")
        P.r(S, lambda e, rr=rr: e.activation(out=w_b[:, 0:D], in_=w_a[:, 0:D], func=AF.Identity, scale=rr, bias=0.0), inc="sc")
        P.w(V, "sc")
        P.r(V, lambda e: e.tensor_copy(i_a[:, 0:D], w_b[:, 0:D]))
        P.r(V, lambda e: e.tensor_copy(c_b[:, 0:D], i_a[:, 0:D]), inc="v")
        P.w(S, "v")
        P.d(S, ap["s_xn"][ts_(i, 128), :], c_a[:, 0:D], "q1")
        P.d(S, ap["s_xq"][ts_(i, 128), :], c_b[:, 0:D], "q1")
        P.w(SY, "q1"); P.w(V, "q1"); P.w(S, "q1")

    P.r(V, lambda e: e.memset(xhh[:, :], 0.0), inc="v")
    P.wall(SY)
    P.d(SY, xhh[0:3, :], ap["xh"][0], "q0")
    P.d(SY, xhh[3:6, :], ap["xh"][1], "q0")
    P.wall(V); P.wall(S)
    hmu = p_a[0:8, 4:5]; hvv = p_a[0:8, 5:6]; hm2 = p_a[0:8, 6:7]
    P.r(V, lambda e: e.reduce_sum(hmu, xhh[:, :], AX.X))
    P.r(V, lambda e: e.tensor_scalar_mul(hmu, hmu, 1.0 / D), inc="v")
    P.w(S, "v")
    P.r(S, lambda e: e.activation(out=w_b[0:8, 0:D], in_=xhh[:, :], func=AF.Square), inc="sc")
    P.w(V, "sc")
    P.r(V, lambda e: e.reduce_sum(hvv, w_b[0:8, 0:D], AX.X))
    P.r(V, lambda e: e.tensor_scalar_mul(hvv, hvv, 1.0 / D))
    P.r(V, lambda e: e.tensor_tensor(hm2, hmu, hmu, OP.mult))
    P.r(V, lambda e: e.tensor_tensor(hvv, hvv, hm2, OP.subtract))
    P.r(V, lambda e: e.tensor_scalar_add(hvv, hvv, 1e-5), inc="v")
    P.w(S, "v")
    P.r(S, lambda e: e.activation(out=hvv, in_=hvv, func=AF.Sqrt), inc="sc")
    P.w(V, "sc")
    P.r(V, lambda e: e.reciprocal(hvv, hvv))
    P.r(V, lambda e: e.tensor_scalar(xhh[:, :], xhh[:, :], hmu, hvv, OP.subtract, OP.mult))
    P.r(V, lambda e: e.tensor_copy(xhn[:, :], xhh[:, :]), inc="v")
    P.wall(TE)
    for k in range(8):
        P.r(TE, lambda e, k=k: e.transpose(PSY[0:128, k * 8:k * 8 + 8], xhh[0:8, ts_(k, 128)], idnf[0:8, 0:8]), inc="t")
    P.w(V, "t")
    P.r(V, lambda e: e.tensor_copy(xhf[:, 0:64], PSY[:, 0:64]), inc="v")
    P.w(TE, "v")

    def cast_rows(src_ap, dst_ap, nrows, ncols):
        nt = (nrows + 127) // 128
        vmark = {}
        for r in range(nt):
            p = min(128, nrows - r * 128)
            wsrc = w_a if r % 2 == 0 else w_b
            cdst = c_a if r % 2 == 0 else c_b
            sq = "q1" if r % 2 == 0 else "q2"
            if r >= 2:
                P.w(SY, "v", vmark[r - 2])
            else:
                P.w(SY, "v"); P.w(SY, "sc")
            P.d(SY, wsrc[0:p, 0:ncols], src_ap[r * 128:r * 128 + p, :], "q0")
            P.w(V, "q0")
            if r < 2:
                P.w(V, "q1")
            P.w(V, sq)
            P.r(V, lambda e, p=p, n=ncols, a=wsrc, b=cdst: e.tensor_copy(b[0:p, 0:n], a[0:p, 0:n]), inc="v")
            vmark[r] = P.cnt["v"]
            P.w(S, "v", vmark[r])
            P.d(S, dst_ap[r * 128:r * 128 + p, :], cdst[0:p, 0:ncols], sq)
        P.w(SY, "q1"); P.w(V, "q1"); P.w(S, "q1")
        P.w(SY, "q2"); P.w(V, "q2"); P.w(S, "q2")
    cast_rows(ap["W_in"], ap["s_win"], 2 * DI, D)
    cast_rows(ap["W_xproj"], ap["s_wxp"], 33, DI)
    P.r(V, lambda e: e.memset(c_a[0:15, 0:DI], 0.0), inc="v")
    P.w(S, "v")
    P.d(S, ap["s_wxp"][33:48, :], c_a[0:15, 0:DI], "q1")
    P.w(SY, "q1"); P.w(V, "q1"); P.w(S, "q1")
    cast_rows(ap["W_out"], ap["s_wout"], D, DI)
    cast_rows(ap["Wqkvb"], ap["s_qkvb"], 3104, D)
    cast_rows(ap["W_o"], ap["s_wo"], D, D)

    def f0_quant(src, dst, nrows, ncols, scidx):
        nt = nrows // 128
        P.r(V, lambda e: e.memset(p_a[:, 8:9], 0.0), inc="v")
        amark = {}
        for r in range(nt):
            wsrc = w_a if r % 2 == 0 else w_b
            if r >= 2:
                P.w(SY, "v", amark[r - 2])
            else:
                P.w(SY, "v"); P.w(SY, "sc")
            P.d(SY, wsrc[:, 0:ncols], src[ts_(r, 128), :], "q0")
            P.w(V, "q0")
            P.r(V, lambda e, n=ncols, a=wsrc: e.reduce_sum(p_a[:, 9:10], a[:, 0:n], AX.X, apply_absolute_value=True))
            P.r(V, lambda e: e.tensor_tensor(p_a[:, 8:9], p_a[:, 8:9], p_a[:, 9:10], OP.add), inc="v")
            amark[r] = P.cnt["v"]
        P.wall(TE)
        P.r(TE, lambda e: e.matmul(PS[7][0:1, 0:1], p_a[:, 8:9], onec[:, 0:1], start=True, stop=True), inc="t")
        P.w(V, "t")
        P.r(V, lambda e, n=nrows * ncols: e.tensor_scalar(wsc[0:1, 0:1], PS[7][0:1, 0:1], 1.0 / n, 1e-5, OP.mult, OP.max))
        P.r(V, lambda e: e.tensor_scalar_mul(wsc[0:1, 1:2], wsc[0:1, 0:1], 0.5), inc="v")
        P.r(V, lambda e: e.memset(w_b[0:1, 0:128], 1.0), inc="v")
        P.w(TE, "v")
        P.r(TE, lambda e: e.matmul(PS[7][:, 4:5], w_b[0:1, 0:128], wsc[0:1, 1:2], start=True, stop=True), inc="t")
        P.r(TE, lambda e, scidx=scidx: e.matmul(PS[7][:, 8:9], w_b[0:1, 0:128], wsc[0:1, 0:1], start=True, stop=True), inc="t")
        P.w(V, "t")
        P.r(V, lambda e: e.tensor_copy(p_a[:, 10:11], PS[7][:, 4:5]))
        P.r(V, lambda e: e.tensor_scalar_mul(p_a[:, 11:12], p_a[:, 10:11], -1.0))
        P.r(V, lambda e, scidx=scidx: e.tensor_copy(mgr[:, scidx:scidx + 1], PS[7][:, 8:9]), inc="v")
        vmark = {}
        gq0 = A16[:, 7168:11264]; gq1 = A16[:, 11264:15360]
        for r in range(nt):
            wsrc = w_a if r % 2 == 0 else w_b
            ga = c_a if r % 2 == 0 else gq0
            gb = c_b if r % 2 == 0 else gq1
            sq = "q1" if r % 2 == 0 else "q2"
            if r >= 2:
                P.w(SY, "v", vmark[r - 2])
            else:
                P.w(SY, "v")
            P.d(SY, wsrc[:, 0:ncols], src[ts_(r, 128), :], "q0")
            P.w(V, "q0"); P.w(V, sq)
            P.r(V, lambda e, n=ncols, a=wsrc, g=ga: e.tensor_scalar(g[:, 0:n], a[:, 0:n], p_a[:, 10:11], None, OP.is_gt))
            P.r(V, lambda e, n=ncols, a=wsrc, g=gb: e.tensor_scalar(g[:, 0:n], a[:, 0:n], p_a[:, 11:12], None, OP.is_lt))
            P.r(V, lambda e, n=ncols, g=ga, h=gb: e.tensor_tensor(g[:, 0:n], g[:, 0:n], h[:, 0:n], OP.subtract), inc="v")
            vmark[r] = P.cnt["v"]
            P.w(S, "v", vmark[r])
            P.d(S, dst[ts_(r, 128), :], ga[:, 0:ncols], sq)
        P.w(SY, "q1"); P.w(V, "q1"); P.w(S, "q1")
        P.w(SY, "q2"); P.w(V, "q2"); P.w(S, "q2")
    f0_quant(ap["f0_wg"], ap["s_f0wg"], 4096, D, 0)
    f0_quant(ap["f0_wu"], ap["s_f0wu"], 4096, D, 1)
    f0_quant(ap["f0_wd"], ap["s_f0wd"], D, 4096, 2)

    def fp4_on_tile(p, ncols):
        nb = ncols // 16
        a3 = w_a[0:p, 0:ncols].rearrange("p (n s) -> p n s", s=16)
        bm = w_b[0:p, 3500:3500 + nb]
        rs = w_b[0:p, 3200:3200 + nb]
        P.r(V, lambda e, a3=a3, bm=bm: e.tensor_reduce(bm, a3, AX.X, OP.max, apply_absolute_value=True))
        P.r(V, lambda e, bm=bm: e.tensor_scalar(bm, bm, 1.0 / 6.0, 1e-12, OP.mult, OP.max))
        P.r(V, lambda e, rs=rs, bm=bm: e.reciprocal(rs, bm), inc="v")
        rsb = rs[:, :, None].broadcast_to([p, nb, 16])
        P.r(V, lambda e, a3=a3, rsb=rsb: e.tensor_tensor(a3, a3, rsb, OP.mult), inc="v")
        P.w(S, "v")
        P.r(S, lambda e, p=p, n=ncols: e.activation(out=w_b[0:p, 0:n], in_=w_a[0:p, 0:n], func=AF.Abs, scale=2.0), inc="sc")
        P.w(V, "sc")
        P.r(V, lambda e, p=p, n=ncols: e.tensor_copy(i_a[0:p, 0:n], w_b[0:p, 0:n]))
        P.r(V, lambda e, p=p, n=ncols: e.tensor_copy(c_a[0:p, 0:n], i_a[0:p, 0:n]))
        P.r(V, lambda e, p=p, n=ncols: e.tensor_scalar(c_a[0:p, 0:n], c_a[0:p, 0:n], 4.0, 0.5, OP.min, OP.mult), inc="v")
        for th in (5.0, 7.0):
            P.r(V, lambda e, p=p, n=ncols, th=th: e.tensor_scalar(c_b[0:p, 0:n], w_b[0:p, 0:n], th, None, OP.is_gt))
            P.r(V, lambda e, p=p, n=ncols: e.tensor_tensor(c_a[0:p, 0:n], c_a[0:p, 0:n], c_b[0:p, 0:n], OP.add))
        P.r(V, lambda e, p=p, n=ncols: e.tensor_scalar(c_b[0:p, 0:n], w_b[0:p, 0:n], 10.0, 2.0, OP.is_gt, OP.mult))
        P.r(V, lambda e, p=p, n=ncols: e.tensor_tensor(c_a[0:p, 0:n], c_a[0:p, 0:n], c_b[0:p, 0:n], OP.add), inc="v")
        P.w(S, "v")
        P.r(S, lambda e, p=p, n=ncols: e.activation(out=c_b[0:p, 0:n], in_=w_a[0:p, 0:n], func=AF.Sign), inc="sc")
        P.w(V, "sc")
        P.r(V, lambda e, p=p, n=ncols: e.tensor_tensor(c_a[0:p, 0:n], c_a[0:p, 0:n], c_b[0:p, 0:n], OP.mult))
        scb = bm[:, :, None].broadcast_to([p, nb, 16])
        ca3 = c_a[0:p, 0:ncols].rearrange("p (n s) -> p n s", s=16)
        P.r(V, lambda e, ca3=ca3, scb=scb: e.tensor_tensor(ca3, ca3, scb, OP.mult), inc="v")

    def f1_quant(src, dst, nrows, ncols):
        nt = (nrows + 127) // 128
        for r in range(nt):
            p = min(128, nrows - r * 128)
            if r >= 1:
                P.w(SY, "v")
            P.d(SY, w_a[0:p, 0:ncols], src[r * 128:r * 128 + p, :], "q0")
            P.w(V, "q0"); P.w(V, "q1")
            fp4_on_tile(p, ncols)
            P.w(S, "v")
            P.d(S, dst[r * 128:r * 128 + p, :], c_a[0:p, 0:ncols], "q1")
            P.w(SY, "q1"); P.w(V, "q1"); P.w(S, "q1")
    f1_quant(ap["f1_wg"], ap["s_f1wg"], 2816, D)
    f1_quant(ap["f1_wu"], ap["s_f1wu"], 2816, D)
    f1_quant(ap["f1_wd"], ap["s_f1wd"], D, 2816)

    P.r(V, lambda e: e.tensor_scalar(sxg[:, :], sxinv[:, :], mgr[:, 0:1], None, OP.mult))
    P.r(V, lambda e: e.tensor_scalar(sxu[:, :], sxinv[:, :], mgr[:, 1:2], None, OP.mult), inc="v")
    if "s_dbg2" in _dbg:
        P.w(S, "v")
        P.d(S, ap["s_dbg2"][:, 0:NT], sxg[:, :], "q1")
        P.d(S, ap["s_dbg2"][:, 16:16 + NT], sxu[:, :], "q1")
        P.d(S, ap["s_dbg2"][:, 32:32 + NT], sxinv[:, :], "q1")
        P.d(S, ap["s_dbg2"][:, 48:52], mgr[:, 0:4], "q1")
        P.w(SY, "q1"); P.w(V, "q1"); P.w(S, "q1")
    P.barrier()

    for k in range(8):
        P.d(SY, xnfm[:, ts_(k, T)], ap["s_xn"][:, ts_(k, 128)], "q0", transpose=True)
    for k in range(16):
        P.d(SY, wxpT[:, k * 48:k * 48 + 48], ap["s_wxp"][:, ts_(k, 128)], "q0", transpose=True)
    for b in range(B):
        for lc in range(2):
            t0 = b * LLOC + lc * 512
            for ot in range(16):
                wbuf = winS[:, (ot % 2) * 1024:(ot % 2) * 1024 + 1024]
                P.w(SY, "t")
                for k in range(8):
                    P.d(SY, wbuf[:, k * 128:(k + 1) * 128], ap["s_win"][ts_(ot, 128), ts_(k, 128)], "q0", transpose=True)
                P.wall(TE)
                for k in range(8):
                    st = wbuf[:, k * 128:(k + 1) * 128]
                    mv = xnfm[:, k * T + t0: k * T + t0 + 512]
                    P.r(TE, lambda e, st=st, mv=mv, k=k:
                        e.matmul(PS[0][:, 0:512], st, mv, start=(k == 0), stop=(k == 7)),
                        inc=("t" if k == 7 else None))
                if lc == 0:
                    for k in range(8):
                        st = wbuf[:, k * 128:(k + 1) * 128]
                        mv = xhf[:, k * 8 + b * 3: k * 8 + b * 3 + 3]
                        P.r(TE, lambda e, st=st, mv=mv, k=k:
                            e.matmul(PS[1][:, 0:3], st, mv, start=(k == 0), stop=(k == 7)),
                            inc=("t" if k == 7 else None))
                P.w(V, "t")
                col0 = ot * 516
                if lc == 0:
                    P.r(V, lambda e, col0=col0: e.tensor_copy(xpfc[:, col0:col0 + 3], PS[1][:, 0:3]))
                else:
                    P.r(V, lambda e, col0=col0, ot=ot: e.tensor_copy(xpfc[:, col0:col0 + 3], tails[:, ot * 4:ot * 4 + 3]))
                P.r(V, lambda e, col0=col0: e.tensor_copy(xpfc[:, col0 + 3:col0 + 515], PS[0][:, 0:512]))
                P.r(V, lambda e, ot=ot: e.tensor_copy(tails[:, ot * 4:ot * 4 + 3], PS[0][:, 509:512]), inc="v")
                P.w(TE, "v")
            for ot in range(16):
                col0 = ot * 516
                P.r(V, lambda e, col0=col0, ot=ot:
                    e.tensor_scalar(w_a[:, 0:512], xpfc[:, col0:col0 + 512], cbw[:, 4 * ot:4 * ot + 1], cbb[:, ot:ot + 1], OP.mult, OP.add))
                for j in (1, 2, 3):
                    P.r(V, lambda e, col0=col0, ot=ot, j=j:
                        e.scalar_tensor_tensor(w_a[:, 0:512], xpfc[:, col0 + j:col0 + j + 512], cbw[:, 4 * ot + j:4 * ot + j + 1], w_a[:, 0:512], OP.mult, OP.add), inc="v")
                P.w(S, "v")
                P.r(S, lambda e, ot=ot: e.activation(out=xcfc[:, ts_(ot, 512)], in_=w_a[:, 0:512], func=AF.Silu), inc="sc")
                P.w(V, "sc")
            P.wall(S)
            for ot in range(16):
                P.d(S, ap["s_xcf"][ts_(ot, 128), t0:t0 + 512], xcfc[:, ts_(ot, 512)], "q1")
            P.w(S, "q1"); P.w(V, "q1")
            P.wall(TE)
            for k in range(16):
                stB = wxpT[:, k * 48:k * 48 + 16]
                stC = wxpT[:, k * 48 + 16:k * 48 + 32]
                stV = wxpT[:, k * 48 + 32:k * 48 + 33]
                mv = xcfc[:, ts_(k, 512)]
                P.r(TE, lambda e, st=stB, mv=mv, k=k:
                    e.matmul(PSY[0:16, 0:512], st, mv, start=(k == 0), stop=(k == 15)))
                P.r(TE, lambda e, st=stC, mv=mv, k=k:
                    e.matmul(PSY[0:16, 512:1024], st, mv, start=(k == 0), stop=(k == 15)))
                P.r(TE, lambda e, st=stV, mv=mv, k=k:
                    e.matmul(PSY[0:1, 1024:1536], st, mv, start=(k == 0), stop=(k == 15)),
                    inc=("t" if k == 15 else None))
            P.w(V, "t")
            P.r(V, lambda e, t0=t0: e.tensor_copy(xdbB[0:16, t0:t0 + 512], PSY[0:16, 0:512]))
            P.r(V, lambda e, t0=t0: e.tensor_copy(xdbC[0:16, t0:t0 + 512], PSY[0:16, 512:1024]))
            P.r(V, lambda e, t0=t0: e.tensor_copy(xdbV[0:1, t0:t0 + 512], PSY[0:1, 1024:1536]), inc="v")
            P.w(TE, "v")
    for zc in range(4):
        P.w(SY, "t")
        for k in range(8):
            P.d(SY, zwv[:, k * 512:(k + 1) * 512], ap["s_win"][2048 + zc * 512:2048 + (zc + 1) * 512, ts_(k, 128)], "q0", transpose=True)
        P.wall(TE)
        for i in range(NT):
            ps = PS[4 + (i % 2)]
            for k in range(8):
                st = xnfm[:, off_(k * T, i, 128)]
                mv = zwv[:, k * 512:(k + 1) * 512]
                P.r(TE, lambda e, ps=ps, st=st, mv=mv, k=k:
                    e.matmul(ps[:, 0:512], st, mv, start=(k == 0), stop=(k == 7)),
                    inc=("t" if k == 7 else None))
            P.w(S, "t")
            P.w(S, "q1")
            P.r(S, lambda e, ps=ps: e.activation(out=c_a[:, 0:512], in_=ps[:, 0:512], func=AF.Silu), inc="sc")
            P.w(S, "sc")
            P.d(S, ap["s_sz"][ts_(i, 128), zc * 512:(zc + 1) * 512], c_a[:, 0:512], "q1")
            P.w(TE, "sc")
    P.barrier()

    P.r(S, lambda e: e.activation(out=fsc[0:1, :], in_=xdbV[0:1, 0:T], func=AF.Exp), inc="sc")
    P.w(S, "sc")
    P.r(S, lambda e: e.activation(out=fsc[0:1, :], in_=fsc[0:1, :], func=AF.Ln, bias=1.0), inc="sc")
    P.w(V, "sc")
    P.r(V, lambda e: e.tensor_scalar(fsc[0:1, :], fsc[0:1, :], 0.001, 0.1, OP.max, OP.min), inc="v")
    P.r(V, lambda e: e.memset(w_b[0:1, 100:116], 1.0), inc="v")
    P.w(TE, "v")
    for half in range(4):
        P.r(TE, lambda e, half=half: e.matmul(PS[0][0:16, 0:512], w_b[0:1, 100:116],
                                              fsc[0:1, half * 512:(half + 1) * 512], start=True, stop=True), inc="t")
        P.w(V, "t")
        P.r(V, lambda e, half=half: e.tensor_copy(fzv[0:16, half * 512:(half + 1) * 512], PS[0][0:16, 0:512]), inc="v")
        P.w(TE, "v")
    P.r(V, lambda e: e.tensor_scalar(fsc[0:16, :], fzv[0:16, :], Acol[0:16, 0:1], None, OP.mult))
    P.r(V, lambda e: e.tensor_scalar(fsc[0:16, :], fsc[0:16, :], -20.0, 0.0, OP.max, OP.min), inc="v")
    P.r(V, lambda e: e.tensor_tensor(fL[0:16, :], fzv[0:16, :], xdbB[0:16, 0:T], OP.mult))
    P.r(V, lambda e: e.tensor_scalar(fL[0:16, :], fL[0:16, :], -10.0, 10.0, OP.max, OP.min))
    P.r(V, lambda e: e.tensor_copy(fzv[0:16, :], fL[0:16, :]), inc="v")
    P.r(V, lambda e: e.memset(f0z[:, :], 0.0), inc="v")
    for b in range(B):
        P.r(V, lambda e, b=b: e.tensor_tensor_scan(
            fL[0:16, b * LLOC:(b + 1) * LLOC], fsc[0:16, b * LLOC:(b + 1) * LLOC],
            f0z[0:16, :], 0.0, OP.add, OP.add), inc="v")
    P.barrier()
    P.r(V, lambda e: e.memset(fDst[:, :], 0.0), inc="v")
    for b in range(B):
        src3 = fL[0:16, b * LLOC:(b + 1) * LLOC].rearrange("p (n c) -> p n c", c=TC)[:, 0:NCH - 1, TC - 1:TC]
        dst3 = fDst[0:16, b * NCH + 1:(b + 1) * NCH][:, :, None]
        P.r(V, lambda e, s=src3, d=dst3: e.tensor_copy(d, s), inc="v")
    for b in range(B):
        P.r(V, lambda e, b=b: e.tensor_tensor(
            fsc[0:16, b * LLOC:(b + 1) * LLOC].rearrange("p (n c) -> p n c", c=TC),
            fL[0:16, b * LLOC:(b + 1) * LLOC].rearrange("p (n c) -> p n c", c=TC),
            fDst[0:16, b * NCH:(b + 1) * NCH][:, :, None].broadcast_to([16, NCH, TC]),
            OP.subtract), inc="v")
    P.w(S, "v")
    P.r(S, lambda e: e.activation(out=w_a[0:16, 0:T], in_=fsc[0:16, :], func=AF.Exp), inc="sc")
    P.r(S, lambda e: e.activation(out=w_b[0:16, 0:T], in_=fsc[0:16, :], func=AF.Exp, scale=-1.0), inc="sc")
    P.w(V, "sc")
    P.r(V, lambda e: e.tensor_tensor(fC[0:16, :], xdbC[0:16, 0:T], w_a[0:16, 0:T], OP.mult))
    P.r(V, lambda e: e.tensor_tensor(fB[0:16, :], fzv[0:16, :], w_b[0:16, 0:T], OP.mult), inc="v")
    for b in range(B):
        src3 = w_a[0:16, b * LLOC:(b + 1) * LLOC].rearrange("p (n c) -> p n c", c=TC)[:, :, TC - 1:TC]
        dst3 = fDch[0:16, b * NCH:(b + 1) * NCH][:, :, None]
        P.r(V, lambda e, s=src3, d=dst3: e.tensor_copy(d, s), inc="v")
    P.w(S, "v")
    P.r(S, lambda e: e.activation(out=w_b[0:16, 3000:3000 + B * NCH], in_=fDst[0:16, :], func=AF.Exp), inc="sc")
    P.w(V, "sc")
    for b in range(B):
        P.r(V, lambda e, b=b: e.tensor_tensor(
            fCc[0:16, b * LLOC:(b + 1) * LLOC].rearrange("p (n c) -> p n c", c=TC),
            fC[0:16, b * LLOC:(b + 1) * LLOC].rearrange("p (n c) -> p n c", c=TC),
            w_b[0:16, 3000 + b * NCH:3000 + (b + 1) * NCH][:, :, None].broadcast_to([16, NCH, TC]),
            OP.mult), inc="v")
    P.barrier()
    if "s_dbg" in _dbg:
        P.w(S, "v"); P.w(S, "sc")
        P.d(S, ap["s_dbg"][0:16, :], fL[0:16, :], "q1")
        P.d(S, ap["s_dbg"][16:32, :], fsc[0:16, :], "q1")
        P.d(S, ap["s_dbg"][32:48, 0:B * NCH], fDst[0:16, :], "q1")
        P.d(S, ap["s_dbg"][48:64, 0:B * NCH], fDch[0:16, :], "q1")
        P.d(S, ap["s_dbg"][64:80, :], xdbB[0:16, 0:T], "q1")
        P.d(S, ap["s_dbg"][80:96, :], xdbC[0:16, 0:T], "q1")
        P.d(S, ap["s_dbg"][96:112, :], fzv[0:16, 0:T], "q1")
        P.d(S, ap["s_dbg"][112:128, :], w_b[0:16, 0:T], "q1")
        P.w(SY, "q1"); P.w(V, "q1"); P.w(S, "q1")
    P.w(S, "v")
    P.d(S, ap["s_fb"][:, :], fB[0:16, :], "q1")
    P.w(SY, "q1"); P.w(V, "q1"); P.w(S, "q1")
    P.barrier()
    for b in range(B):
        for gid in range(NG):
            g_ = b * NG + gid
            dst = btT[:, g_ * 128:g_ * 128 + 16]
            src = ap["s_fb"][0:16, b * LLOC + gid * 128:(b * LLOC) + (gid + 1) * 128]
            P.d(SY, dst, src, "q0", transpose=True)
            P.w(SY, "q0")
            if g_ == 0:
                P.barrier()
    P.barrier()

    P.r(V, lambda e: e.memset(hcurb[0][0:16, :], 0.0))
    P.r(V, lambda e: e.memset(hcurb[1][0:16, :], 0.0), inc="v")
    for b in range(B):
        for gid in range(NG):
            xbuf = xtm[:, (gid % 2) * DI:((gid % 2) + 1) * DI]
            P.w(SY, "t")
            P.d(SY, xbuf, ap["s_xcf"][:, b * LLOC + gid * 128: b * LLOC + (gid + 1) * 128], "q0", transpose=True)
            P.wall(TE)
            for c in range(4):
                n = gid * 4 + c
                blk = (b * NG + gid) * 4 + c
                stm = fB[0:16, b * LLOC + n * TC: b * LLOC + (n + 1) * TC]
                mvm = fC[0:16, b * LLOC + n * TC: b * LLOC + (n + 1) * TC]
                P.r(TE, lambda e, stm=stm, mvm=mvm, c=c:
                    e.matmul(PSG[32 * c:32 * c + 32, 0:32], stm, mvm, start=True, stop=True,
                             tile_position=(0, 32 * c)), inc="t")
                P.w(V, "t")
                mcb = c_b[32 * c:32 * c + 32, (n % 2) * TC:((n % 2) + 1) * TC]
                P.r(V, lambda e, mcb=mcb, c=c: e.tensor_tensor(mcb, PSG[32 * c:32 * c + 32, 0:32], msk[32 * c:32 * c + 32, :], OP.mult), inc="v")
                P.w(TE, "v")
                for dc in range(4):
                    stg = btT[32 * c:32 * c + 32, (b * NG + gid) * 128:(b * NG + gid) * 128 + 16]
                    mvx = xbuf[32 * c:32 * c + 32, dc * 512:(dc + 1) * 512]
                    P.r(TE, lambda e, stg=stg, mvx=mvx, c=c, dc=dc:
                        e.matmul(PSG[0:16, dc * 512:(dc + 1) * 512], stg, mvx, start=True, stop=True,
                                 tile_position=(32 * c, 0)),
                        inc=("t" if dc == 3 else None))
                P.w(V, "t")
                P.r(V, lambda e, b=b: e.tensor_copy(hstg[0:16, :], hcurb[b][0:16, :]), inc="v")
                P.r(V, lambda e, b=b: e.tensor_tensor(w_a[0:16, 0:DI], hcurb[b][0:16, :], PSG[0:16, 0:DI], OP.add))
                P.r(V, lambda e, b=b, n=n: e.tensor_scalar(hcurb[b][0:16, :], w_a[0:16, 0:DI], fDch[0:16, b * NCH + n:b * NCH + n + 1], None, OP.mult), inc="v")
                P.w(TE, "v")
                for dc in range(4):
                    mvx = xbuf[32 * c:32 * c + 32, dc * 512:(dc + 1) * 512]
                    stc = fC[0:16, b * LLOC + n * TC: b * LLOC + (n + 1) * TC]
                    mvh = hstg[0:16, dc * 512:(dc + 1) * 512]
                    P.r(TE, lambda e, mcb=mcb, mvx=mvx, c=c, dc=dc:
                        e.matmul(PSY[32 * c:32 * c + 32, dc * 512:(dc + 1) * 512], mcb, mvx, start=True, stop=False,
                                 tile_position=(32 * c, 32 * c)))
                    P.r(TE, lambda e, stc=stc, mvh=mvh, c=c, dc=dc:
                        e.matmul(PSY[32 * c:32 * c + 32, dc * 512:(dc + 1) * 512], stc, mvh, start=False, stop=True,
                                 tile_position=(0, 32 * c)),
                        inc=("t" if dc == 3 else None))
            P.w(V, "t"); P.w(V, "q1")
            P.r(V, lambda e: e.tensor_copy(c_a[:, 0:DI], PSY[:, 0:DI]), inc="v")
            P.w(S, "v")
            P.d(S, ap["s_y"][ts_(b * NG + gid, 128), :], c_a[:, 0:DI], "q1")
            P.w(TE, "q1"); P.w(V, "q1"); P.w(S, "q1")
    P.w(S, "v")
    P.r(S, lambda e: e.activation(out=p_a[0:16, 16:17], in_=fL[0:16, LLOC - 1:LLOC], func=AF.Exp), inc="sc")
    P.r(S, lambda e: e.activation(out=p_a[0:16, 17:18], in_=fL[0:16, 2 * LLOC - 1:2 * LLOC], func=AF.Exp), inc="sc")
    P.wall(SY)
    P.d(SY, ap["cc_in"][0:16, 0:DI], hcurb[0][0:16, :], "q0")
    P.d(SY, ap["cc_in"][16:32, 0:DI], hcurb[1][0:16, :], "q0")
    P.d(SY, ap["cc_in"][0:16, DI:DI + 1], p_a[0:16, 16:17], "q0", slow=True)
    P.d(SY, ap["cc_in"][16:32, DI:DI + 1], p_a[0:16, 17:18], "q0", slow=True)
    P.w(G, "q0")
    P.r(G, lambda e: e.collective_compute("AllGather", OP.bypass, replica_groups=[list(range(NCORES))],
                                          ins=[ap["cc_in"][:]], outs=[ap["cc_out"][:]]), inc="cc")
    P.w(SY, "cc")
    P.wall(V)
    P.r(V, lambda e: e.memset(hin[0:32, :], 0.0), inc="v")
    for r in range(NCORES - 1):
        P.w(V, "q1")
        P.wall(SY)
        P.d(SY, w_b[0:32, 0:DI + 1], ap["cc_out"][r], "q0")
        P.wall(V)
        Dr = w_b[0:32, DI:DI + 1]
        Gr = w_b[0:32, 0:DI]
        mr = rmk[:, r + 1:r + 2]
        P.r(V, lambda e, Dr=Dr: e.tensor_scalar(p_a[0:32, 20:21], Dr, -1.0, None, OP.add))
        P.r(V, lambda e, mr=mr: e.scalar_tensor_tensor(p_a[0:32, 21:22], p_a[0:32, 20:21], mr, onec[0:32, 0:1], OP.mult, OP.add), inc="v")
        P.r(V, lambda e: e.tensor_scalar(hin[0:32, :], hin[0:32, :], p_a[0:32, 21:22], None, OP.mult), inc="v")
        P.r(V, lambda e, Gr=Gr, mr=mr: e.scalar_tensor_tensor(hin[0:32, :], Gr, mr, hin[0:32, :], OP.mult, OP.add), inc="v")
    P.r(V, lambda e: e.tensor_copy(hinb[0:16, 0:DI], hin[0:16, :]), inc="v")
    P.wall(SY)
    P.d(SY, ap["cc_in"][0:16, 0:DI], hin[16:32, :], "q0")
    P.w(G, "q0")
    P.d(G, hinb2[0:16, 0:DI], ap["cc_in"][0:16, 0:DI], "q2")
    P.barrier()

    for i in range(NT):
        b = i // 8; j = i % 8
        P.wall(TE)
        for dc in range(4):
            stc = fCc[0:16, b * LLOC + j * 128:(b * LLOC + j * 128) + 128]
            hb = hinb if b == 0 else hinb2
            mvh = hb[0:16, dc * 512:(dc + 1) * 512]
            P.r(TE, lambda e, stc=stc, mvh=mvh, dc=dc:
                e.matmul(PS[dc][:, 0:512], stc, mvh, start=True, stop=True), inc="t")
        P.d(SY, c_a[:, 0:DI], ap["s_y"][ts_(i, 128), :], "q0")
        P.d(SY, c_b[:, 0:DI], ap["s_sz"][ts_(i, 128), :], "q0")
        P.d(SY, c_c[:, 0:DI], ap["s_xcf"][:, 0:0 + 0] if False else c_c[:, 0:DI], "q0") if False else None
        P.d(SY, c_c[:, 0:DI], ap["s_xcf"][:, b * LLOC + j * 128: b * LLOC + (j + 1) * 128], "q0", transpose=True)
        P.wall(V)
        for dc in range(4):
            P.r(V, lambda e, dc=dc: e.tensor_tensor(w_a[:, dc * 512:(dc + 1) * 512], c_a[:, dc * 512:(dc + 1) * 512], PS[dc][:, 0:512], OP.add), inc="v")
        P.w(TE, "v")
        P.r(V, lambda e: e.tensor_tensor(w_a[:, 0:DI], w_a[:, 0:DI], c_b[:, 0:DI], OP.mult), inc="v")
        P.r(V, lambda e: e.tensor_tensor(c_c[:, 0:DI], c_c[:, 0:DI], dprow[:, :], OP.mult))
        P.r(V, lambda e: e.tensor_tensor(c_a[:, 0:DI], w_a[:, 0:DI], c_c[:, 0:DI], OP.add), inc="v")
        P.w(S, "v")
        P.d(S, ap["s_yg"][ts_(i, 128), :], c_a[:, 0:DI], "q1")
        P.w(SY, "q1"); P.w(V, "q1"); P.w(S, "q1")
    P.barrier()

    ygfm = big1
    woT = bufA
    for k in range(16):
        P.d(SY, ygfm[:, ts_(k, T)], ap["s_yg"][:, ts_(k, 128)], "q0", transpose=True)
        P.d(SY, woT[:, ts_(k, D)], ap["s_wout"][:, ts_(k, 128)], "q0", transpose=True)
    P.wall(TE)
    for i in range(NT):
        for mc in range(2):
            ps = PS[mc]
            for k in range(16):
                st = ygfm[:, off_(k * T, i, 128)]
                mv = woT[:, k * D + mc * 512: k * D + mc * 512 + 512]
                P.r(TE, lambda e, ps=ps, st=st, mv=mv, k=k:
                    e.matmul(ps[:, 0:512], st, mv, start=(k == 0), stop=(k == 15)),
                    inc=("t" if k == 15 else None))
        P.w(V, "t"); P.w(V, "q1")
        P.r(V, lambda e: e.tensor_copy(c_a[:, 0:512], PS[0][:, 0:512]))
        P.r(V, lambda e: e.tensor_copy(c_a[:, 512:1024], PS[1][:, 0:512]), inc="v")
        P.w(TE, "v"); P.w(S, "v")
        P.d(S, ap["s_s"][ts_(i, 128), :], c_a[:, 0:D], "q1")
    P.barrier()

    xqfm = A16[:, 0:16384]
    for k in range(8):
        P.d(SY, xqfm[:, ts_(k, T)], ap["s_xq"][:, ts_(k, 128)], "q0", transpose=True)
    P.r(V, lambda e: e.memset(p_a[:, 24:25], 0.0), inc="v")
    P.r(V, lambda e: e.memset(sx2i[:, :], 0.0), inc="v")
    wmovB = A16[:, 16384:24576]
    wmov2B = A16[:, 24576:32768]
    tmark = {}
    for oc in range(8):
        wg_ = wmov if oc % 2 == 0 else wmovB
        wu_ = wmov2 if oc % 2 == 0 else wmov2B
        if oc >= 2:
            P.w(SY, "t", tmark[oc - 2])
        for k in range(8):
            P.d(SY, wg_[:, ts_(k, 512)], ap["s_f0wg"][oc * 512:(oc + 1) * 512, ts_(k, 128)], "q0", transpose=True)
            P.d(SY, wu_[:, ts_(k, 512)], ap["s_f0wu"][oc * 512:(oc + 1) * 512, ts_(k, 128)], "q0", transpose=True)
        P.w(TE, "q0")
        for i in range(NT):
            for k in range(8):
                st = xqfm[:, off_(k * T, i, 128)]
                P.r(TE, lambda e, st=st, k=k, w=wg_: e.matmul(PS[0][:, 0:512], st, w[:, ts_(k, 512)], start=(k == 0), stop=(k == 7)), inc=("t" if k == 7 else None))
            for k in range(8):
                st = xqfm[:, off_(k * T, i, 128)]
                P.r(TE, lambda e, st=st, k=k, w=wu_: e.matmul(PS[1][:, 0:512], st, w[:, ts_(k, 512)], start=(k == 0), stop=(k == 7)), inc=("t" if k == 7 else None))
            if i == NT - 1:
                tmark[oc] = P.cnt["t"]
            P.w(S, "t")
            P.r(S, lambda e, i=i: e.activation(out=w_a[:, 0:512], in_=PS[0][:, 0:512], func=AF.Silu, scale=sxg[:, i:i + 1]), inc="sc")
            P.r(S, lambda e, i=i: e.activation(out=w_b[:, 0:512], in_=PS[1][:, 0:512], func=AF.Identity, scale=sxu[:, i:i + 1], bias=0.0), inc="sc")
            P.w(V, "sc")
            P.w(TE, "sc")
            P.r(V, lambda e: e.tensor_tensor(w_a[:, 0:512], w_a[:, 0:512], w_b[:, 0:512], OP.mult))
            P.r(V, lambda e, i=i: e.reduce_max(p_a[:, 25:26], w_a[:, 0:512], AX.X, apply_absolute_value=True))
            P.r(V, lambda e, i=i: e.tensor_tensor(sx2i[:, i:i + 1], sx2i[:, i:i + 1], p_a[:, 25:26], OP.max), inc="v")
            P.w(S, "v")
            P.d(S, ap["s_h2"][ts_(i, 128), oc * 512:(oc + 1) * 512], w_a[:, 0:512], "q1")
            P.w(V, "q1"); P.w(S, "q1")
    P.r(V, lambda e: e.tensor_scalar(sx2i[:, :], sx2i[:, :], 1e-5, 1.0 / 127.0, OP.max, OP.mult))
    P.r(V, lambda e: e.tensor_scalar(sxd[:, :], sx2i[:, :], mgr[:, 2:3], None, OP.mult), inc="v")
    P.barrier()
    for i in range(NT):
        P.d(SY, w_a[:, 0:4096], ap["s_h2"][ts_(i, 128), :], "q0")
        P.wall(V)
        P.r(V, lambda e, i=i: e.reciprocal(p_a[:, 26:27], sx2i[:, i:i + 1]), inc="v")
        P.w(S, "v")
        P.r(S, lambda e: e.activation(out=w_b[:, 0:4096], in_=w_a[:, 0:4096], func=AF.Identity, scale=p_a[:, 26:27], bias=0.0), inc="sc")
        P.w(V, "sc")
        P.r(V, lambda e: e.tensor_copy(i_a[:, 0:4096], w_b[:, 0:4096]))
        P.r(V, lambda e: e.tensor_copy(c_a[:, 0:4096], i_a[:, 0:4096]), inc="v")
        P.w(S, "v")
        P.d(S, ap["s_xq2"][ts_(i, 128), :], c_a[:, 0:4096], "q1")
        P.w(SY, "q1"); P.w(V, "q1"); P.w(S, "q1")
    xq2fm = big1
    for half in range(2):
        P.w(SY, "t")
        for k in range(16):
            kk = half * 16 + k
            P.d(SY, xq2fm[:, ts_(k, T)], ap["s_xq2"][:, ts_(kk, 128)], "q0", transpose=True)
            P.d(SY, wmov[:, ts_(k, 512)], ap["s_f0wd"][0:512, ts_(kk, 128)], "q0", transpose=True)
            P.d(SY, wmov2[:, ts_(k, 512)], ap["s_f0wd"][512:1024, ts_(kk, 128)], "q0", transpose=True)
        if half == 0 and "s_dbg3" in _dbg:
            P.w(S, "q0")
            P.d(S, ap["s_dbg3"][:, 0:T], xq2fm[:, 0:T], "q1")
            P.d(S, ap["s_dbg3"][:, T:T + 512], wmov[:, 0:512], "q1")
            P.w(SY, "q1"); P.w(V, "q1"); P.w(S, "q1")
        P.wall(TE)
        for i in range(NT):
            for mc in range(2):
                ps = PS[mc]
                wv = wmov if mc == 0 else wmov2
                for k in range(16):
                    st = xq2fm[:, off_(k * T, i, 128)]
                    P.r(TE, lambda e, ps=ps, st=st, wv=wv, k=k:
                        e.matmul(ps[:, 0:512], st, wv[:, ts_(k, 512)], start=(k == 0), stop=(k == 15)),
                        inc=("t" if k == 15 else None))
            if half == 0:
                P.w(V, "t")
                P.r(V, lambda e: e.tensor_copy(w_a[:, 0:512], PS[0][:, 0:512]))
                P.r(V, lambda e: e.tensor_copy(w_a[:, 512:1024], PS[1][:, 0:512]), inc="v")
                P.w(TE, "v"); P.w(S, "v")
                P.d(S, ap["s_h2"][ts_(i, 128), 0:1024], w_a[:, 0:1024], "q1")
                P.w(V, "q1")
            else:
                P.w(SY, "v")
                P.d(SY, w_b[:, 0:1024], ap["s_h2"][ts_(i, 128), 0:1024], "q0")
                P.w(V, "t"); P.wall(V)
                P.r(V, lambda e: e.tensor_tensor(w_a[:, 0:512], PS[0][:, 0:512], w_b[:, 0:512], OP.add))
                P.r(V, lambda e: e.tensor_tensor(w_a[:, 512:1024], PS[1][:, 0:512], w_b[:, 512:1024], OP.add), inc="v")
                P.w(TE, "v"); P.w(S, "v"); P.w(S, "q1")
                P.r(S, lambda e, i=i: e.activation(out=c_b[:, 0:D], in_=w_a[:, 0:D], func=AF.Silu, scale=sxd[:, i:i + 1]), inc="sc")
                P.w(S, "sc")
                P.d(S, ap["s_g"][ts_(i, 128), :], c_b[:, 0:D], "q1")
                P.w(V, "sc")
    P.barrier()

    for i in range(NT):
        P.d(SY, w_b[:, 0:D], xr[i], "q0")
        P.d(SY, c_a[:, 0:D], ap["s_s"][ts_(i, 128), :], "q0")
        P.d(SY, c_b[:, 0:D], ap["s_g"][ts_(i, 128), :], "q0")
        P.wall(V)
        P.r(V, lambda e: e.tensor_tensor(c_a[:, 0:D], c_a[:, 0:D], c_b[:, 0:D], OP.mult), inc="v")
        P.r(V, lambda e: e.tensor_tensor(w_a[:, 0:D], w_b[:, 0:D], c_a[:, 0:D], OP.add), inc="v")
        P.w(S, "v")
        P.d(S, ap["s_x1"][ts_(i, 128), :], w_a[:, 0:D], "q1")
        P.w(V, "q1"); P.w(S, "q1")
        mu = p_a[:, 0:1]; vv = p_a[:, 1:2]; rr = p_a[:, 2:3]
        P.r(V, lambda e, mu=mu: e.reduce_sum(mu, w_a[:, 0:D], AX.X))
        P.r(V, lambda e, mu=mu: e.tensor_scalar_mul(mu, mu, 1.0 / D), inc="v")
        P.w(S, "v")
        P.r(S, lambda e: e.activation(out=w_b[:, 0:D], in_=w_a[:, 0:D], func=AF.Square), inc="sc")
        P.w(V, "sc")
        P.r(V, lambda e, vv=vv: e.reduce_sum(vv, w_b[:, 0:D], AX.X))
        P.r(V, lambda e, vv=vv: e.tensor_scalar_mul(vv, vv, 1.0 / D))
        P.r(V, lambda e, mu=mu, rr=rr: e.tensor_tensor(rr, mu, mu, OP.mult))
        P.r(V, lambda e, vv=vv, rr=rr: e.tensor_tensor(rr, vv, rr, OP.subtract))
        P.r(V, lambda e, rr=rr: e.tensor_scalar_add(rr, rr, 1e-5), inc="v")
        P.w(S, "v")
        P.r(S, lambda e, rr=rr: e.activation(out=rr, in_=rr, func=AF.Sqrt), inc="sc")
        P.w(V, "sc")
        P.r(V, lambda e, rr=rr: e.reciprocal(rr, rr))
        P.r(V, lambda e, mu=mu, rr=rr: e.tensor_scalar(w_a[:, 0:D], w_a[:, 0:D], mu, rr, OP.subtract, OP.mult))
        P.r(V, lambda e: e.tensor_copy(c_b[:, 0:D], w_a[:, 0:D]), inc="v")
        P.w(S, "v")
        P.d(S, ap["s_xn1"][ts_(i, 128), :], c_b[:, 0:D], "q1")
        P.w(V, "q1"); P.w(S, "q1")
        fp4_on_tile(128, D)
        P.w(S, "v")
        P.d(S, ap["s_xq1"][ts_(i, 128), :], c_a[:, 0:D], "q1")
        P.w(SY, "q1"); P.w(V, "q1"); P.w(S, "q1")
    P.barrier()

    xq1fm = A16[:, 0:16384]
    for k in range(8):
        P.d(SY, xq1fm[:, ts_(k, T)], ap["s_xq1"][:, ts_(k, 128)], "q0", transpose=True)
    OCW = [512, 512, 512, 512, 512, 256]
    ocs = [0, 512, 1024, 1536, 2048, 2560]
    wmovB = A16[:, 16384:24576]
    wmov2B = A16[:, 24576:32768]
    tmark = {}
    for oc in range(6):
        ow = OCW[oc]; o0 = ocs[oc]
        wg_ = wmov if oc % 2 == 0 else wmovB
        wu_ = wmov2 if oc % 2 == 0 else wmov2B
        if oc >= 2:
            P.w(SY, "t", tmark[oc - 2])
        for k in range(8):
            P.d(SY, wg_[:, k * 512:k * 512 + ow], ap["s_f1wg"][o0:o0 + ow, ts_(k, 128)], "q0", transpose=True)
            P.d(SY, wu_[:, k * 512:k * 512 + ow], ap["s_f1wu"][o0:o0 + ow, ts_(k, 128)], "q0", transpose=True)
        P.w(TE, "q0")
        for i in range(NT):
            for k in range(8):
                st = xq1fm[:, off_(k * T, i, 128)]
                P.r(TE, lambda e, st=st, k=k, ow=ow, w=wg_: e.matmul(PS[0][:, 0:ow], st, w[:, k * 512:k * 512 + ow], start=(k == 0), stop=(k == 7)), inc=("t" if k == 7 else None))
            for k in range(8):
                st = xq1fm[:, off_(k * T, i, 128)]
                P.r(TE, lambda e, st=st, k=k, ow=ow, w=wu_: e.matmul(PS[1][:, 0:ow], st, w[:, k * 512:k * 512 + ow], start=(k == 0), stop=(k == 7)), inc=("t" if k == 7 else None))
            if i == NT - 1:
                tmark[oc] = P.cnt["t"]
            P.w(S, "t")
            P.r(S, lambda e, ow=ow: e.activation(out=w_a[:, 0:ow], in_=PS[0][:, 0:ow], func=AF.Silu), inc="sc")
            P.r(S, lambda e, ow=ow: e.activation(out=w_b[:, 3000:3000 + ow] if False else w_b[:, 0:ow], in_=PS[1][:, 0:ow], func=AF.Identity, bias=0.0), inc="sc")
            P.w(V, "sc"); P.w(TE, "sc")
            P.r(V, lambda e, ow=ow: e.tensor_tensor(w_a[:, 0:ow], w_a[:, 0:ow], w_b[:, 0:ow], OP.mult), inc="v")
            fp4_on_tile(128, ow)
            P.w(S, "v")
            P.d(S, ap["s_xq3"][ts_(i, 128), o0:o0 + ow], c_a[:, 0:ow], "q1")
            P.w(V, "q1"); P.w(S, "q1")
    P.barrier()

    xq3fm = big1
    for half in range(2):
        P.w(SY, "t")
        for k in range(11):
            kk = half * 11 + k
            P.d(SY, xq3fm[:, ts_(k, T)], ap["s_xq3"][:, ts_(kk, 128)], "q0", transpose=True)
            P.d(SY, wmov[:, ts_(k, 512)], ap["s_f1wd"][0:512, ts_(kk, 128)], "q0", transpose=True)
            P.d(SY, wmov2[:, ts_(k, 512)], ap["s_f1wd"][512:1024, ts_(kk, 128)], "q0", transpose=True)
        P.wall(TE)
        for i in range(NT):
            for mc in range(2):
                ps = PS[mc]
                wv = wmov if mc == 0 else wmov2
                for k in range(11):
                    st = xq3fm[:, off_(k * T, i, 128)]
                    P.r(TE, lambda e, ps=ps, st=st, wv=wv, k=k:
                        e.matmul(ps[:, 0:512], st, wv[:, ts_(k, 512)], start=(k == 0), stop=(k == 10)),
                        inc=("t" if k == 10 else None))
            if half == 0:
                P.w(V, "t")
                P.r(V, lambda e: e.tensor_copy(w_a[:, 0:512], PS[0][:, 0:512]))
                P.r(V, lambda e: e.tensor_copy(w_a[:, 512:1024], PS[1][:, 0:512]), inc="v")
                P.w(TE, "v"); P.w(S, "v")
                P.d(S, ap["s_h2"][ts_(i, 128), 0:1024], w_a[:, 0:1024], "q1")
                P.w(V, "q1")
            else:
                P.w(SY, "v")
                P.d(SY, w_b[:, 0:1024], ap["s_h2"][ts_(i, 128), 0:1024], "q0")
                P.w(V, "t"); P.wall(V)
                P.r(V, lambda e: e.tensor_tensor(w_a[:, 0:512], PS[0][:, 0:512], w_b[:, 0:512], OP.add))
                P.r(V, lambda e: e.tensor_tensor(w_a[:, 512:1024], PS[1][:, 0:512], w_b[:, 512:1024], OP.add), inc="v")
                P.w(TE, "v"); P.w(S, "v"); P.w(S, "q1")
                P.r(S, lambda e: e.activation(out=c_b[:, 0:D], in_=w_a[:, 0:D], func=AF.Silu), inc="sc")
                P.w(S, "sc")
                P.d(S, ap["s_g1"][ts_(i, 128), :], c_b[:, 0:D], "q1")
                P.w(V, "sc")
    P.barrier()

    xn1fm = big1[:, 0:16384]
    for k in range(8):
        P.d(SY, xn1fm[:, ts_(k, T)], ap["s_xn1"][:, ts_(k, 128)], "q0", transpose=True)
    QCW = [512, 512, 512, 512, 512, 512, 32]
    qcs = [0, 512, 1024, 1536, 2048, 2560, 3072]
    wmovB = A16[:, 16384:24576]
    tmark = {}
    for oc in range(7):
        tmark[oc] = P.cnt["t"]
        ow = QCW[oc]; o0 = qcs[oc]
        wg_ = wmov if oc % 2 == 0 else wmovB
        if oc >= 2:
            P.w(SY, "t", tmark[oc - 1])
        for k in range(8):
            P.d(SY, wg_[:, k * 512:k * 512 + ow], ap["s_qkvb"][o0:o0 + ow, ts_(k, 128)], "q0", transpose=True)
        P.w(TE, "q0")
        for i in range(NT):
            for k in range(8):
                st = xn1fm[:, off_(k * T, i, 128)]
                P.r(TE, lambda e, st=st, k=k, ow=ow, w=wg_: e.matmul(PS[0][:, 0:ow], st, w[:, k * 512:k * 512 + ow], start=(k == 0), stop=(k == 7)), inc=("t" if k == 7 else None))
            P.w(V, "t")
            P.r(V, lambda e, ow=ow: e.tensor_copy(c_a[:, 0:ow], PS[0][:, 0:ow]), inc="v")
            P.w(TE, "v"); P.w(S, "v")
            P.d(S, ap["s_qkv"][ts_(i, 128), o0:o0 + ow], c_a[:, 0:ow], "q1")
            P.w(V, "q1"); P.w(S, "q1")
    P.barrier()

    def elu1_norm(dst_bf, src_bf, tmpf):
        P.r(V, lambda e: e.tensor_scalar(w_a[:, 0:1024], src_bf, 0.0, None, OP.min), inc="v")
        P.w(S, "v")
        P.r(S, lambda e: e.activation(out=w_a[:, 1024:2048], in_=w_a[:, 0:1024], func=AF.Exp), inc="sc")
        P.w(V, "sc")
        P.r(V, lambda e: e.tensor_scalar(w_a[:, 0:1024], src_bf, 0.0, None, OP.max))
        P.r(V, lambda e: e.tensor_tensor(w_a[:, 0:1024], w_a[:, 0:1024], w_a[:, 1024:2048], OP.add), inc="v")
        a3 = w_a[:, 0:1024].rearrange("p (h d) -> p h d", d=64)
        P.r(V, lambda e, a3=a3: e.tensor_reduce(p_a[:, 32:48], a3, AX.X, OP.add))
        P.r(V, lambda e: e.reciprocal(p_a[:, 32:48], p_a[:, 32:48]), inc="v")
        rsb = p_a[:, 32:48][:, :, None].broadcast_to([128, 16, 64])
        P.r(V, lambda e, a3=a3, rsb=rsb, dst_bf=dst_bf: e.tensor_tensor(
            dst_bf.rearrange("p (h d) -> p h d", d=64), a3, rsb, OP.mult), inc="v")

    for j in range(8):
        P.wall(V)
        P.d(SY, c_a[:, 0:3104], ap["s_qkv"][ts_(j, 128), :], "q0")
        P.d(SY, c_b[:, 0:3104], ap["s_qkv"][ts_(8 + j, 128), :], "q0")
        P.wall(V)
        elu1_norm(c_c[:, 0:1024], c_a[:, 0:1024], 0)
        elu1_norm(c_c[:, 1024:2048], c_a[:, 1024:2048], 0)
        elu1_norm(c_c[:, 2048:3072], c_b[:, 0:1024], 0)
        elu1_norm(c_b[:, 0:1024], c_b[:, 1024:2048], 0)
        P.r(V, lambda e: e.tensor_scalar(w_a[:, 0:16], c_a[:, 3072:3088], -1.0, None, OP.mult))
        P.r(V, lambda e: e.tensor_scalar(w_a[:, 16:32], c_b[:, 3072:3088], -1.0, None, OP.mult), inc="v")
        P.w(S, "v")
        P.r(S, lambda e: e.activation(out=w_a[:, 0:32], in_=w_a[:, 0:32], func=AF.Exp), inc="sc")
        P.w(V, "sc")
        P.r(V, lambda e: e.tensor_scalar_add(w_a[:, 0:32], w_a[:, 0:32], 1.0))
        P.r(V, lambda e: e.reciprocal(w_a[:, 0:32], w_a[:, 0:32]), inc="v")
        b0c = w_a[:, 0:16]; b1c = w_a[:, 16:32]
        def dot(dst_cols, aa, bb):
            P.r(V, lambda e, aa=aa, bb=bb: e.tensor_tensor(w_b[:, 0:1024], aa, bb, OP.mult), inc="v")
            a3 = w_b[:, 0:1024].rearrange("p (h d) -> p h d", d=64)
            P.r(V, lambda e, a3=a3, dst_cols=dst_cols: e.tensor_reduce(p_a[:, dst_cols:dst_cols + 16], a3, AX.X, OP.add), inc="v")
        q0n = c_c[:, 0:1024]; k0n = c_c[:, 1024:2048]; q1n = c_c[:, 2048:3072]; k1n = c_b[:, 0:1024]
        dot(32, q0n, k0n)
        dot(48, q1n, k0n)
        dot(16, q1n, k1n)
        dot(0, k1n, k0n)
        v0 = c_a[:, 2048:3072]; v1 = c_b[:, 2048:3072]
        b0b = b0c[:, :, None].broadcast_to([128, 16, 64])
        P.r(V, lambda e, b0b=b0b, v0=v0: e.tensor_tensor(
            c_a[:, 0:1024].rearrange("p (h d) -> p h d", d=64), v0.rearrange("p (h d) -> p h d", d=64), b0b, OP.mult), inc="v")
        vi0 = c_a[:, 0:1024]
        d_q0k0 = p_a[:, 32:48][:, :, None].broadcast_to([128, 16, 64])
        P.r(V, lambda e, d_q0k0=d_q0k0, vi0=vi0: e.tensor_tensor(
            c_a[:, 1024:2048].rearrange("p (h d) -> p h d", d=64), vi0.rearrange("p (h d) -> p h d", d=64), d_q0k0, OP.mult), inc="v")
        d_k1k0 = p_a[:, 0:16][:, :, None].broadcast_to([128, 16, 64])
        P.r(V, lambda e, d_k1k0=d_k1k0, vi0=vi0: e.tensor_tensor(
            w_b[:, 0:1024].rearrange("p (h d) -> p h d", d=64), vi0.rearrange("p (h d) -> p h d", d=64), d_k1k0, OP.mult), inc="v")
        P.r(V, lambda e, v1=v1: e.tensor_tensor(w_b[:, 0:1024], v1, w_b[:, 0:1024], OP.subtract), inc="v")
        b1b = b1c[:, :, None].broadcast_to([128, 16, 64])
        P.r(V, lambda e, b1b=b1b: e.tensor_tensor(
            w_b[:, 0:1024].rearrange("p (h d) -> p h d", d=64), w_b[:, 0:1024].rearrange("p (h d) -> p h d", d=64), b1b, OP.mult), inc="v")
        vi1 = w_b[:, 0:1024]
        d_q1k0 = p_a[:, 48:64][:, :, None].broadcast_to([128, 16, 64])
        d_q1k1 = p_a[:, 16:32][:, :, None].broadcast_to([128, 16, 64])
        P.r(V, lambda e, d_q1k0=d_q1k0, vi0=vi0: e.tensor_tensor(
            w_b[:, 1024:2048].rearrange("p (h d) -> p h d", d=64), vi0.rearrange("p (h d) -> p h d", d=64), d_q1k0, OP.mult), inc="v")
        P.r(V, lambda e, d_q1k1=d_q1k1, vi1=vi1: e.tensor_tensor(
            w_b[:, 2048:3072].rearrange("p (h d) -> p h d", d=64), vi1.rearrange("p (h d) -> p h d", d=64), d_q1k1, OP.mult), inc="v")
        P.r(V, lambda e: e.tensor_tensor(w_b[:, 1024:2048], w_b[:, 1024:2048], w_b[:, 2048:3072], OP.add), inc="v")
        P.r(V, lambda e: e.tensor_scalar(c_a[:, 1024:2048], c_a[:, 1024:2048], 0.125, None, OP.mult))
        P.r(V, lambda e: e.tensor_scalar(c_b[:, 1024:2048], w_b[:, 1024:2048], 0.125, None, OP.mult), inc="v")
        P.w(S, "v")
        P.d(S, ap["s_lay"][ts_(j, 128), :], c_a[:, 1024:2048], "q1")
        P.d(S, ap["s_lay"][ts_(8 + j, 128), :], c_b[:, 1024:2048], "q1")
        P.w(SY, "q1"); P.w(V, "q1"); P.w(S, "q1")
    P.barrier()

    layfm = big1[:, 0:16384]
    P.d(G, lngrow[:, :], ap["ln_g"][None, :].broadcast_to([128, D]), "q2")
    P.d(G, lnbrow[:, :], ap["ln_b"][None, :].broadcast_to([128, D]), "q2")
    for k in range(8):
        P.d(SY, layfm[:, ts_(k, T)], ap["s_lay"][:, ts_(k, 128)], "q0", transpose=True)
    for k in range(8):
        P.d(SY, wmov[:, k * 512:(k + 1) * 512], ap["s_wo"][0:512, ts_(k, 128)], "q0", transpose=True)
        P.d(SY, wmov2[:, k * 512:(k + 1) * 512], ap["s_wo"][512:1024, ts_(k, 128)], "q0", transpose=True)
    P.barrier()
    P.wall(TE)
    for i in range(NT):
        for mc in range(2):
            ps = PS[mc]
            wv = wmov if mc == 0 else wmov2
            for k in range(8):
                st = layfm[:, off_(k * T, i, 128)]
                P.r(TE, lambda e, ps=ps, st=st, wv=wv, k=k:
                    e.matmul(ps[:, 0:512], st, wv[:, ts_(k, 512)], start=(k == 0), stop=(k == 7)),
                    inc=("t" if k == 7 else None))
        if i == 1:
            P.barrier()
        P.d(SY, c_a[:, 0:D], ap["s_xn1"][ts_(i, 128), :], "q0")
        P.d(SY, w_b[:, 0:D], ap["s_x1"][ts_(i, 128), :], "q0")
        P.w(V, "t"); P.wall(V)
        P.r(V, lambda e: e.tensor_tensor(w_a[:, 0:512], PS[0][:, 0:512], c_a[:, 0:512], OP.add))
        P.r(V, lambda e: e.tensor_tensor(w_a[:, 512:1024], PS[1][:, 0:512], c_a[:, 512:1024], OP.add), inc="v")
        P.w(TE, "v")
        mu = p_a[:, 0:1]; vv = p_a[:, 1:2]; rr = p_a[:, 2:3]
        P.r(V, lambda e, mu=mu: e.reduce_sum(mu, w_a[:, 0:D], AX.X))
        P.r(V, lambda e, mu=mu: e.tensor_scalar_mul(mu, mu, 1.0 / D), inc="v")
        P.w(S, "v")
        P.r(S, lambda e: e.activation(out=c_b[:, 0:D], in_=w_a[:, 0:D], func=AF.Square), inc="sc")
        P.w(V, "sc")
        P.r(V, lambda e, vv=vv: e.reduce_sum(vv, c_b[:, 0:D], AX.X))
        P.r(V, lambda e, vv=vv: e.tensor_scalar_mul(vv, vv, 1.0 / D))
        P.r(V, lambda e, mu=mu, rr=rr: e.tensor_tensor(rr, mu, mu, OP.mult))
        P.r(V, lambda e, vv=vv, rr=rr: e.tensor_tensor(rr, vv, rr, OP.subtract))
        P.r(V, lambda e, rr=rr: e.tensor_scalar_add(rr, rr, 1e-5), inc="v")
        P.w(S, "v")
        P.r(S, lambda e, rr=rr: e.activation(out=rr, in_=rr, func=AF.Sqrt), inc="sc")
        P.w(V, "sc")
        P.r(V, lambda e, rr=rr: e.reciprocal(rr, rr))
        P.r(V, lambda e, mu=mu, rr=rr: e.tensor_scalar(w_a[:, 0:D], w_a[:, 0:D], mu, rr, OP.subtract, OP.mult))
        P.r(V, lambda e: e.tensor_tensor(w_a[:, 0:D], w_a[:, 0:D], lngrow[:, :], OP.mult))
        P.r(V, lambda e: e.tensor_tensor(w_a[:, 0:D], w_a[:, 0:D], lnbrow[:, :], OP.add), inc="v")
        P.w(SY, "v")
        P.d(SY, c_b[:, 0:D], ap["s_g1"][ts_(i, 128), :], "q0")
        P.wall(V)
        P.r(V, lambda e: e.tensor_tensor(w_a[:, 0:D], w_a[:, 0:D], c_b[:, 0:D], OP.mult), inc="v")
        P.r(V, lambda e: e.tensor_tensor(w_a[:, 0:D], w_a[:, 0:D], w_b[:, 0:D], OP.add), inc="v")
        P.w(S, "v")
        b = i // 8; j = i % 8
        P.d(S, ap["out"][b, ts_(j, 128), :], w_a[:, 0:D], "q1")
        P.w(SY, "q1"); P.w(V, "q1"); P.w(S, "q1"); P.w(G, "q1")
        if i == 0:
            P.barrier()
    P.wall(SY); P.wall(G); P.wall(V)

    import os as _os
    _cut = int(_os.environ.get("MK_CUT", "-1"))
    if _cut >= 0 and hasattr(P, "marks") and _cut < len(P.marks):
        for e in ENGINES:
            P.stream[e] = P.stream[e][:P.marks[_cut][e]]
    blkc = nc.Block(); blk = blkc.__enter__()

    def emit(eng_name):
        def _f(e):
            for fn in P.stream[eng_name]:
                fn(e)
        return _f
    blk.sync(emit("sync"))
    blk.vector(emit("vector"))
    blk.scalar(emit("scalar"))
    blk.gpsimd(emit("gpsimd"))
    blk.tensor(emit("tensor"))
    blkc.__exit__(None, None, None)
    for t in reversed(ctx):
        t.__exit__(None, None, None)
    return nc


_NC_CACHE = None


def kernel(L0_attn_x, L0_attn_W_in, L0_attn_conv_w, L0_attn_conv_b, L0_attn_W_xproj,
           L0_attn_A_log, L0_attn_D_param, L0_attn_W_out, L0_ffn_w_gate, L0_ffn_w_up,
           L0_ffn_w_down, L1_attn_W_qkvb, L1_attn_W_o, L1_attn_ln_g, L1_attn_ln_b,
           L1_ffn_w_gate, L1_ffn_w_up, L1_ffn_w_down):
    global _NC_CACHE
    if _NC_CACHE is None:
        _NC_CACHE = build_program()
    nc_ = _NC_CACHE
    x = np.ascontiguousarray(L0_attn_x, dtype=np.float32)
    Wq = np.ascontiguousarray(L1_attn_W_qkvb, dtype=np.float32).reshape(16, 193, 1024)
    Wq_perm = np.concatenate([
        Wq[:, 0:64].reshape(1024, 1024),
        Wq[:, 64:128].reshape(1024, 1024),
        Wq[:, 128:192].reshape(1024, 1024),
        Wq[:, 192:193].reshape(16, 1024),
        np.zeros((16, 1024), np.float32)], axis=0)
    maskc = (np.arange(128)[:, None] % TC <= np.arange(TC)[None, :]).astype(ml_dtypes.bfloat16)
    identc = np.eye(128, dtype=ml_dtypes.bfloat16)
    in_maps = []
    for c in range(NCORES):
        lo = c * LLOC
        halo = x[:, lo - 3:lo].copy() if c > 0 else np.zeros((B, 3, D), np.float32)
        rm = (np.arange(1, NCORES + 1) <= c).astype(np.float32)
        rm = np.zeros(NCORES, np.float32)
        rm[1:] = (np.arange(0, NCORES - 1) < c).astype(np.float32)
        in_maps.append({
            "x": x[:, lo:lo + LLOC].copy(),
            "xh": halo,
            "W_in": L0_attn_W_in, "conv_w": L0_attn_conv_w, "conv_b": L0_attn_conv_b,
            "W_xproj": L0_attn_W_xproj, "A_log": L0_attn_A_log, "D_param": L0_attn_D_param,
            "W_out": L0_attn_W_out,
            "f0_wg": L0_ffn_w_gate, "f0_wu": L0_ffn_w_up, "f0_wd": L0_ffn_w_down,
            "Wqkvb": Wq_perm, "W_o": L1_attn_W_o, "ln_g": L1_attn_ln_g, "ln_b": L1_attn_ln_b,
            "f1_wg": L1_ffn_w_gate, "f1_wu": L1_ffn_w_up, "f1_wd": L1_ffn_w_down,
            "mask": maskc, "ident": identc, "identf": identc.astype(np.float32), "rmask": rm,
        })
    res = run_bass_kernel_spmd(nc_, in_maps, core_ids=list(range(NCORES)))
    out = np.concatenate([np.asarray(r["out"], np.float32) for r in res.results], axis=1)
    return out




# version 06
# speedup vs baseline: 1.0927x; 1.0093x over previous
import numpy as np
import ml_dtypes
import concourse.bass as bass
import concourse.mybir as mybir
from concourse.bass_utils import run_bass_kernel_spmd

F32 = mybir.dt.float32
BF16 = mybir.dt.bfloat16
I16 = mybir.dt.int16
OP = mybir.AluOpType
AF = mybir.ActivationFunctionType
AX = mybir.AxisListType

B = 2
LLOC = 1024
T = 2048
NT = 16
D = 1024
DI = 2048
TC = 32
NCH = 32
NG = 8
NCORES = 8

ENGINES = ("sync", "vector", "scalar", "gpsimd", "tensor")
SEMS = {}
V, S, G, TE, SY = "vector", "scalar", "gpsimd", "tensor", "sync"


def ts_(i, n=128):
    return slice(i * n, (i + 1) * n)


def off_(base, i, n=128):
    return slice(base + i * n, base + (i + 1) * n)


class Prog:
    NW = 6

    def __init__(self):
        self.stream = {e: [] for e in ENGINES}
        self.cnt = {s: 0 for s in ["v", "sc", "g", "t", "cc", "bs"]}
        self.qs = {}
        for q in ("q0", "q1", "q2"):
            self.qs[q] = {"idx": 0, "tot": [0] * self.NW, "open": False}

    def _qsem(self, q, i):
        return q + chr(ord("a") + i)

    def w(self, eng, sem, val=None):
        if sem in self.qs:
            st = self.qs[sem]
            for i in range(self.NW):
                if st["tot"][i] > 0:
                    self.stream[eng].append(
                        lambda e, s=self._qsem(sem, i), vv=st["tot"][i]: e.wait_ge(SEMS[s], vv))
            st["open"] = False
            return
        v = self.cnt[sem] if val is None else val
        if v > 0:
            self.stream[eng].append(lambda e, s=sem, vv=v: e.wait_ge(SEMS[s], vv))

    OWN = {"vector": "v", "scalar": "sc", "gpsimd": "g", "tensor": "t"}

    def r(self, eng, fn, inc=None, k=1):
        own = self.OWN.get(eng)
        if own is not None:
            v = self.cnt[own]
            if v > 0:
                self.stream[eng].append(lambda e, s=own, vv=v: e.wait_ge(SEMS[s], vv))
            if inc is None or inc == own:
                self.cnt[own] += 1
                self.stream[eng].append(lambda e, f=fn, s=own: f(e).then_inc(SEMS[s], 1))
            else:
                self.cnt[inc] += k
                self.stream[eng].append(lambda e, f=fn, s=inc, kk=k: f(e).then_inc(SEMS[s], kk))
        else:
            if inc is not None:
                self.cnt[inc] += k
                self.stream[eng].append(lambda e, f=fn, s=inc, kk=k: f(e).then_inc(SEMS[s], kk))
            else:
                self.stream[eng].append(lambda e, f=fn: f(e)) 

    def d(self, eng, out, in_, q, transpose=False, slow=False):
        st = self.qs[q]
        if not st["open"]:
            st["idx"] = (st["idx"] + 1) % self.NW
            i = st["idx"]
            if st["tot"][i] > 0:
                self.stream[eng].append(
                    lambda e, s=self._qsem(q, i), vv=st["tot"][i]: e.wait_ge(SEMS[s], vv))
            st["open"] = True
        i = st["idx"]
        st["tot"][i] += 16
        sname = self._qsem(q, i)
        if transpose and not slow:
            self.stream[eng].append(
                lambda e, o=out, i_=in_, s=sname: e.dma_start_transpose(o, i_).then_inc(SEMS[s], 16))
        elif transpose and slow:
            def _f(e, o=out, i_=in_, s=sname):
                with e.bass.allow_non_contiguous_dma(reason="tiny transpose"):
                    return e.dma_start_transpose(o, i_).then_inc(SEMS[s], 16)
            self.stream[eng].append(_f)
        else:
            self.stream[eng].append(
                lambda e, o=out, i_=in_, s=sname: e.dma_start(out=o, in_=i_, allow_slow_non_contiguous=slow).then_inc(SEMS[s], 16))

    def snap(self):
        return dict(self.cnt)

    def wall(self, eng, snap=None):
        for q in ("q0", "q1", "q2"):
            self.w(eng, q)
        sn = self.cnt if snap is None else snap
        for s, v in sn.items():
            if v > 0:
                self.w(eng, s, v)

    def barrier(self):
        m = {}
        for eng, sem in [("sync", "bs"), ("vector", "v"), ("scalar", "sc"),
                         ("gpsimd", "g"), ("tensor", "t")]:
            self.cnt[sem] += 1
            m[sem] = self.cnt[sem]
            self.stream[eng].append(lambda e, s=sem: e.sem_inc(SEMS[s], 1))
        for eng in ENGINES:
            for s, v in m.items():
                if s != {"sync": "bs", "vector": "v", "scalar": "sc",
                         "gpsimd": "g", "tensor": "t"}[eng] or True:
                    self.w(eng, s, v)
            for q in ("q0", "q1", "q2"):
                self.w(eng, q)
        if not hasattr(self, "marks"):
            self.marks = []
        self.marks.append({e: len(self.stream[e]) for e in ENGINES})


def build_program():
    nc = bass.Bass("TRN2", target_bir_lowering=False)
    P = Prog()
    ap = {}

    def IN(name, shape, dt=F32):
        ap[name] = nc.dram_tensor(name, list(shape), dt, kind="ExternalInput").ap()

    def OUT(name, shape, dt=F32):
        ap[name] = nc.dram_tensor(name, list(shape), dt, kind="ExternalOutput").ap()

    import os as _os2
    _dbg = set((_os2.environ.get("MK_DBG", "")).split(",")) - {""}

    def INT(name, shape, dt=F32, **kw):
        kind = "ExternalOutput" if name in _dbg else "Internal"
        if kw.get("addr_space"):
            kind = "Internal"
        ap[name] = nc.dram_tensor(name, list(shape), dt, kind=kind, **kw).ap()

    IN("x", (B, LLOC, D)); IN("xh", (B, 3, D))
    IN("W_in", (2 * DI, D)); IN("conv_w", (DI, 4)); IN("conv_b", (DI,))
    IN("W_xproj", (33, DI)); IN("A_log", (16,)); IN("D_param", (DI,))
    IN("W_out", (D, DI))
    IN("f0_wg", (4096, D)); IN("f0_wu", (4096, D)); IN("f0_wd", (D, 4096))
    IN("Wqkvb", (3104, D))
    IN("W_o", (D, D)); IN("ln_g", (D,)); IN("ln_b", (D,))
    IN("f1_wg", (2816, D)); IN("f1_wu", (2816, D)); IN("f1_wd", (D, 2816))
    IN("mask", (128, TC), BF16)
    IN("ident", (128, 128), BF16)
    IN("identf", (128, 128))
    IN("rmask", (NCORES,))
    OUT("out", (B, LLOC, D))

    INT("s_win", (2 * DI, D), BF16)
    INT("s_wxp", (48, DI), BF16)
    INT("s_wout", (D, DI), BF16)
    INT("s_f0wg", (4096, D), BF16); INT("s_f0wu", (4096, D), BF16); INT("s_f0wd", (D, 4096), BF16)
    INT("s_qkvb", (3104, D), BF16); INT("s_wo", (D, D), BF16)
    INT("s_f1wg", (2816, D), BF16); INT("s_f1wu", (2816, D), BF16); INT("s_f1wd", (D, 2816), BF16)
    INT("s_xn", (T, D), BF16)
    INT("s_xq", (T, D), BF16)
    INT("s_xcf", (DI, T), BF16)
    INT("s_sz", (T, DI), BF16)
    INT("s_y", (T, DI), BF16)
    INT("s_yg", (T, DI), BF16)
    INT("s_h2", (T, 4096), F32)
    INT("s_xq2", (T, 4096), BF16)
    INT("s_x1", (T, D), F32)
    INT("s_xn1", (T, D), BF16)
    INT("s_xq1", (T, D), BF16)
    INT("s_qkv", (T, 3104), BF16)
    INT("s_xq3", (T, 2816), BF16)
    INT("s_lay", (T, D), BF16)
    INT("s_s", (T, D), BF16)
    INT("s_g", (T, D), BF16)
    INT("s_g1", (T, D), BF16)
    INT("s_fb", (16, B * LLOC), BF16)
    INT("s_dbg", (128, B * LLOC), F32)
    INT("s_dbg2", (128, 64), F32)
    INT("s_dbg3", (128, 2560), BF16)
    INT("cc_in", (32, DI + 1), F32)
    INT("cc_out", (NCORES, 32, DI + 1), F32, addr_space="Shared")

    ctx = []
    SB = {}

    def sb(name, p, f, dt=F32):
        t = nc.sbuf_tensor(name, [p, f], dt)
        h = t.__enter__(); ctx.append(t); SB[name] = h
        return h

    msk = sb("msk", 128, TC, BF16)
    idn = sb("idn", 128, 128, BF16)
    idnf = sb("idnf", 128, 128)
    rmk = sb("rmk", 32, NCORES)
    Acol = sb("Acol", 16, 1)
    cbw = sb("cbw", 128, 16 * 4)
    cbb = sb("cbb", 128, 16)
    onec = sb("onec", 128, 2)
    sxinv = sb("sxinv", 128, NT)
    sx2i = sb("sx2i", 128, NT)
    sxg = sb("sxg", 128, NT); sxu = sb("sxu", 128, NT); sxd = sb("sxd", 128, NT)
    wsc = sb("wsc", 4, 8)
    mgr = sb("mgr", 128, 4)
    p_a = sb("p_a", 128, 64)
    p_b = sb("p_b", 128, 128)
    w_a = sb("w_a", 128, 4096)
    w_b = sb("w_b", 128, 4096)
    c_a = sb("c_a", 128, 4096, BF16)
    c_b = sb("c_b", 128, 4096, BF16)
    i_a = sb("i_a", 128, 4096, I16)
    xhn = sb("xhn", 8, D, BF16)
    xhh = sb("xhh", 8, D)
    xhf = sb("xhf", 128, 8 * 8, BF16)
    A16 = sb("A16", 128, 49152, BF16)
    A32 = sb("A32", 128, 11264)

    def a16(off, n):
        return A16[:, off:off + n]

    def a32(off, n):
        return A32[:, off:off + n]

    dprow = A16[:, 47104:49152]
    lngrow = A16[:, 16384:17408]
    lnbrow = A16[:, 17408:18432]

    xnfm = a16(0, 16384)
    xpfc = a16(16384, 8256)
    xcfc = a16(24640, 8192)
    winS = a16(32832, 2048)
    zwv = a16(34880, 4096)
    wxpT = a16(41024, 768)
    tails = a16(41856, 64)
    c_c = a16(43008, 4096)
    fC = a16(0, 2048); fB = a16(2048, 2048); fCc = a16(4096, 2048)
    btT = a16(6144, 2048)
    gsb = a16(7168, 16384)
    hstg = a16(23552, 2048)
    hinb = a16(25600, 2048)
    hinb2 = a16(6144, 2048)
    xtm = a16(27648, 4096)
    big1 = A16[:, 0:32768]
    bufA = A16[:, 32768:49152]
    wmov = bufA[:, 0:8192]
    wmov2 = bufA[:, 8192:16384]
    fL = A32[:, 0:2048]; fsc = A32[:, 2048:4096]; f0z = A32[:, 4096:5120]
    xdb = A32[:, 5120:7168]
    hcurb = [A32[:, 7168:9216], A32[:, 9216:11264]]; hin = A32[:, 0:2048]
    xdbB = A32[:, 5120:7168]
    xdbC = A32[:, 7168:9216]
    xdbV = A32[:, 9216:11264]
    fDst = p_b[:, 0:64]; fDch = p_b[:, 64:128]
    fzv = w_a[:, 2048:4096]

    PSG_t = nc.psum_tensor("psg", [128, 2048], F32)
    PSG = PSG_t.__enter__(); ctx.append(PSG_t)
    PSY_t = nc.psum_tensor("psy", [128, 2048], F32)
    PSY = PSY_t.__enter__(); ctx.append(PSY_t)
    PS = [PSG[:, 0:512], PSG[:, 512:1024], PSG[:, 1024:1536], PSG[:, 1536:2048],
          PSY[:, 0:512], PSY[:, 512:1024], PSY[:, 1024:1536], PSY[:, 1536:2048]]

    semnames = ["v", "sc", "g", "t", "cc", "bs"]
    for q in ("q0", "q1", "q2"):
        for i in range(Prog.NW):
            semnames.append(q + chr(ord("a") + i))
    for s in semnames:
        h = nc.semaphore(s)
        SEMS[s] = h.__enter__(); ctx.append(h)

    P.d(SY, msk[:, :], ap["mask"][:, :], "q0")
    P.d(SY, idn[:, :], ap["ident"][:, :], "q0")
    P.d(SY, idnf[:, :], ap["identf"][:, :], "q0")
    P.d(SY, rmk[:, :], ap["rmask"][None, :].broadcast_to([32, NCORES]), "q0")
    P.d(G, dprow[:, :], ap["D_param"][None, :].broadcast_to([128, DI]), "q2")
    P.d(SY, Acol[:, 0:1], ap["A_log"][:, None], "q0")
    P.d(SY, cbw[:, :].rearrange("p (k j) -> p k j", j=4),
        ap["conv_w"].rearrange("(k p) j -> p k j", p=128), "q0", slow=True)
    P.d(SY, cbb[:, :], ap["conv_b"].rearrange("(k p) -> p k", p=128), "q0", slow=True)
    P.wall(V); P.wall(S)
    P.r(V, lambda e: e.memset(onec[:, :], 1.0), inc="v")
    P.r(S, lambda e: e.activation(out=Acol[0:16, 0:1], in_=Acol[0:16, 0:1], func=AF.Exp), inc="sc")
    P.w(V, "sc")
    P.r(V, lambda e: e.tensor_scalar_mul(Acol[0:16, 0:1], Acol[0:16, 0:1], -1.0), inc="v")

    xr = ap["x"].rearrange("b (i p) d -> (b i) p d", p=128)
    scmark = {}
    for i in range(NT):
        o = (i % 2) * D
        sq = "q1" if i % 2 == 0 else "q2"
        pb = 4 * (i % 2)
        if i >= 2:
            P.w(SY, "sc", scmark[i - 2])
        else:
            P.w(SY, "v"); P.w(SY, "sc")
        P.d(SY, w_a[:, o:o + D], xr[i], "q0")
        mu = p_a[:, pb:pb + 1]; vv = p_a[:, pb + 1:pb + 2]; rr = p_a[:, pb + 2:pb + 3]; rmx = p_a[:, pb + 3:pb + 4]
        P.w(V, "q0"); P.w(V, sq)
        P.r(V, lambda e, mu=mu, o=o: e.reduce_sum(mu, w_a[:, o:o + D], AX.X))
        P.r(V, lambda e, mu=mu: e.tensor_scalar_mul(mu, mu, 1.0 / D), inc="v")
        P.w(S, "v")
        P.r(S, lambda e, o=o: e.activation(out=w_b[:, o:o + D], in_=w_a[:, o:o + D], func=AF.Square), inc="sc")
        P.w(V, "sc")
        P.r(V, lambda e, vv=vv, o=o: e.reduce_sum(vv, w_b[:, o:o + D], AX.X))
        P.r(V, lambda e, vv=vv: e.tensor_scalar_mul(vv, vv, 1.0 / D))
        P.r(V, lambda e, mu=mu, rr=rr: e.tensor_tensor(rr, mu, mu, OP.mult))
        P.r(V, lambda e, vv=vv, rr=rr: e.tensor_tensor(rr, vv, rr, OP.subtract))
        P.r(V, lambda e, rr=rr: e.tensor_scalar_add(rr, rr, 1e-5), inc="v")
        P.w(S, "v")
        P.r(S, lambda e, rr=rr: e.activation(out=rr, in_=rr, func=AF.Sqrt), inc="sc")
        P.w(V, "sc")
        P.r(V, lambda e, rr=rr: e.reciprocal(rr, rr))
        P.r(V, lambda e, mu=mu, rr=rr, o=o: e.tensor_scalar(w_a[:, o:o + D], w_a[:, o:o + D], mu, rr, OP.subtract, OP.mult))
        P.r(V, lambda e, o=o: e.tensor_copy(c_a[:, o:o + D], w_a[:, o:o + D]))
        P.r(V, lambda e, rmx=rmx, o=o: e.reduce_max(rmx, w_a[:, o:o + D], AX.X, apply_absolute_value=True))
        P.r(V, lambda e, rmx=rmx: e.tensor_scalar_max(rmx, rmx, 1e-5))
        P.r(V, lambda e, i=i, rmx=rmx: e.tensor_scalar_mul(sxinv[:, i:i + 1], rmx, 1.0 / 127.0))
        P.r(V, lambda e, i=i, rr=rr: e.reciprocal(rr, sxinv[:, i:i + 1]), inc="v")
        P.w(S, "v")
        P.r(S, lambda e, rr=rr, o=o: e.activation(out=w_b[:, o:o + D], in_=w_a[:, o:o + D], func=AF.Identity, scale=rr, bias=0.0), inc="sc")
        scmark[i] = P.cnt["sc"]
        P.w(V, "sc")
        P.r(V, lambda e, o=o: e.tensor_copy(i_a[:, o:o + D], w_b[:, o:o + D]))
        P.r(V, lambda e, o=o: e.tensor_copy(c_b[:, o:o + D], i_a[:, o:o + D]), inc="v")
        P.w(S, "v")
        P.d(S, ap["s_xn"][ts_(i, 128), :], c_a[:, o:o + D], sq)
        P.d(S, ap["s_xq"][ts_(i, 128), :], c_b[:, o:o + D], sq)
    P.w(SY, "q1"); P.w(V, "q1"); P.w(S, "q1")
    P.w(SY, "q2"); P.w(V, "q2"); P.w(S, "q2")

    P.r(V, lambda e: e.memset(xhh[:, :], 0.0), inc="v")
    P.wall(SY)
    P.d(SY, xhh[0:3, :], ap["xh"][0], "q0")
    P.d(SY, xhh[3:6, :], ap["xh"][1], "q0")
    P.wall(V); P.wall(S)
    hmu = p_a[0:8, 4:5]; hvv = p_a[0:8, 5:6]; hm2 = p_a[0:8, 6:7]
    P.r(V, lambda e: e.reduce_sum(hmu, xhh[:, :], AX.X))
    P.r(V, lambda e: e.tensor_scalar_mul(hmu, hmu, 1.0 / D), inc="v")
    P.w(S, "v")
    P.r(S, lambda e: e.activation(out=w_b[0:8, 0:D], in_=xhh[:, :], func=AF.Square), inc="sc")
    P.w(V, "sc")
    P.r(V, lambda e: e.reduce_sum(hvv, w_b[0:8, 0:D], AX.X))
    P.r(V, lambda e: e.tensor_scalar_mul(hvv, hvv, 1.0 / D))
    P.r(V, lambda e: e.tensor_tensor(hm2, hmu, hmu, OP.mult))
    P.r(V, lambda e: e.tensor_tensor(hvv, hvv, hm2, OP.subtract))
    P.r(V, lambda e: e.tensor_scalar_add(hvv, hvv, 1e-5), inc="v")
    P.w(S, "v")
    P.r(S, lambda e: e.activation(out=hvv, in_=hvv, func=AF.Sqrt), inc="sc")
    P.w(V, "sc")
    P.r(V, lambda e: e.reciprocal(hvv, hvv))
    P.r(V, lambda e: e.tensor_scalar(xhh[:, :], xhh[:, :], hmu, hvv, OP.subtract, OP.mult))
    P.r(V, lambda e: e.tensor_copy(xhn[:, :], xhh[:, :]), inc="v")
    P.wall(TE)
    for k in range(8):
        P.r(TE, lambda e, k=k: e.transpose(PSY[0:128, k * 8:k * 8 + 8], xhh[0:8, ts_(k, 128)], idnf[0:8, 0:8]), inc="t")
    P.w(V, "t")
    P.r(V, lambda e: e.tensor_copy(xhf[:, 0:64], PSY[:, 0:64]), inc="v")
    P.w(TE, "v")

    def cast_rows(src_ap, dst_ap, nrows, ncols):
        nt = (nrows + 127) // 128
        vmark = {}
        for r in range(nt):
            p = min(128, nrows - r * 128)
            wsrc = w_a if r % 2 == 0 else w_b
            cdst = c_a if r % 2 == 0 else c_b
            sq = "q1" if r % 2 == 0 else "q2"
            if r >= 2:
                P.w(SY, "v", vmark[r - 2])
            else:
                P.w(SY, "v"); P.w(SY, "sc")
            P.d(SY, wsrc[0:p, 0:ncols], src_ap[r * 128:r * 128 + p, :], "q0")
            P.w(V, "q0")
            if r < 2:
                P.w(V, "q1")
            P.w(V, sq)
            P.r(V, lambda e, p=p, n=ncols, a=wsrc, b=cdst: e.tensor_copy(b[0:p, 0:n], a[0:p, 0:n]), inc="v")
            vmark[r] = P.cnt["v"]
            P.w(S, "v", vmark[r])
            P.d(S, dst_ap[r * 128:r * 128 + p, :], cdst[0:p, 0:ncols], sq)
        P.w(SY, "q1"); P.w(V, "q1"); P.w(S, "q1")
        P.w(SY, "q2"); P.w(V, "q2"); P.w(S, "q2")
    cast_rows(ap["W_in"], ap["s_win"], 2 * DI, D)
    cast_rows(ap["W_xproj"], ap["s_wxp"], 33, DI)
    P.r(V, lambda e: e.memset(c_a[0:15, 0:DI], 0.0), inc="v")
    P.w(S, "v")
    P.d(S, ap["s_wxp"][33:48, :], c_a[0:15, 0:DI], "q1")
    P.w(SY, "q1"); P.w(V, "q1"); P.w(S, "q1")
    cast_rows(ap["W_out"], ap["s_wout"], D, DI)
    cast_rows(ap["Wqkvb"], ap["s_qkvb"], 3104, D)
    cast_rows(ap["W_o"], ap["s_wo"], D, D)

    def f0_quant(src, dst, nrows, ncols, scidx):
        nt = nrows // 128
        P.r(V, lambda e: e.memset(p_a[:, 8:9], 0.0), inc="v")
        amark = {}
        for r in range(nt):
            wsrc = w_a if r % 2 == 0 else w_b
            if r >= 2:
                P.w(SY, "v", amark[r - 2])
            else:
                P.w(SY, "v"); P.w(SY, "sc")
            P.d(SY, wsrc[:, 0:ncols], src[ts_(r, 128), :], "q0")
            P.w(V, "q0")
            P.r(V, lambda e, n=ncols, a=wsrc: e.reduce_sum(p_a[:, 9:10], a[:, 0:n], AX.X, apply_absolute_value=True))
            P.r(V, lambda e: e.tensor_tensor(p_a[:, 8:9], p_a[:, 8:9], p_a[:, 9:10], OP.add), inc="v")
            amark[r] = P.cnt["v"]
        P.wall(TE)
        P.r(TE, lambda e: e.matmul(PS[7][0:1, 0:1], p_a[:, 8:9], onec[:, 0:1], start=True, stop=True), inc="t")
        P.w(V, "t")
        P.r(V, lambda e, n=nrows * ncols: e.tensor_scalar(wsc[0:1, 0:1], PS[7][0:1, 0:1], 1.0 / n, 1e-5, OP.mult, OP.max))
        P.r(V, lambda e: e.tensor_scalar_mul(wsc[0:1, 1:2], wsc[0:1, 0:1], 0.5), inc="v")
        P.r(V, lambda e: e.memset(w_b[0:1, 0:128], 1.0), inc="v")
        P.w(TE, "v")
        P.r(TE, lambda e: e.matmul(PS[7][:, 4:5], w_b[0:1, 0:128], wsc[0:1, 1:2], start=True, stop=True), inc="t")
        P.r(TE, lambda e, scidx=scidx: e.matmul(PS[7][:, 8:9], w_b[0:1, 0:128], wsc[0:1, 0:1], start=True, stop=True), inc="t")
        P.w(V, "t")
        P.r(V, lambda e: e.tensor_copy(p_a[:, 10:11], PS[7][:, 4:5]))
        P.r(V, lambda e: e.tensor_scalar_mul(p_a[:, 11:12], p_a[:, 10:11], -1.0))
        P.r(V, lambda e, scidx=scidx: e.tensor_copy(mgr[:, scidx:scidx + 1], PS[7][:, 8:9]), inc="v")
        vmark = {}
        gq0 = A16[:, 7168:11264]; gq1 = A16[:, 11264:15360]
        for r in range(nt):
            wsrc = w_a if r % 2 == 0 else w_b
            ga = c_a if r % 2 == 0 else gq0
            gb = c_b if r % 2 == 0 else gq1
            sq = "q1" if r % 2 == 0 else "q2"
            if r >= 2:
                P.w(SY, "v", vmark[r - 2])
            else:
                P.w(SY, "v")
            P.d(SY, wsrc[:, 0:ncols], src[ts_(r, 128), :], "q0")
            P.w(V, "q0"); P.w(V, sq)
            P.r(V, lambda e, n=ncols, a=wsrc, g=ga: e.tensor_scalar(g[:, 0:n], a[:, 0:n], p_a[:, 10:11], None, OP.is_gt))
            P.r(V, lambda e, n=ncols, a=wsrc, g=gb: e.tensor_scalar(g[:, 0:n], a[:, 0:n], p_a[:, 11:12], None, OP.is_lt))
            P.r(V, lambda e, n=ncols, g=ga, h=gb: e.tensor_tensor(g[:, 0:n], g[:, 0:n], h[:, 0:n], OP.subtract), inc="v")
            vmark[r] = P.cnt["v"]
            P.w(S, "v", vmark[r])
            P.d(S, dst[ts_(r, 128), :], ga[:, 0:ncols], sq)
        P.w(SY, "q1"); P.w(V, "q1"); P.w(S, "q1")
        P.w(SY, "q2"); P.w(V, "q2"); P.w(S, "q2")
    f0_quant(ap["f0_wg"], ap["s_f0wg"], 4096, D, 0)
    f0_quant(ap["f0_wu"], ap["s_f0wu"], 4096, D, 1)
    f0_quant(ap["f0_wd"], ap["s_f0wd"], D, 4096, 2)

    def fp4_on_tile(p, ncols):
        nb = ncols // 16
        a3 = w_a[0:p, 0:ncols].rearrange("p (n s) -> p n s", s=16)
        bm = w_b[0:p, 3500:3500 + nb]
        rs = w_b[0:p, 3200:3200 + nb]
        P.r(V, lambda e, a3=a3, bm=bm: e.tensor_reduce(bm, a3, AX.X, OP.max, apply_absolute_value=True))
        P.r(V, lambda e, bm=bm: e.tensor_scalar(bm, bm, 1.0 / 6.0, 1e-12, OP.mult, OP.max))
        P.r(V, lambda e, rs=rs, bm=bm: e.reciprocal(rs, bm), inc="v")
        rsb = rs[:, :, None].broadcast_to([p, nb, 16])
        P.r(V, lambda e, a3=a3, rsb=rsb: e.tensor_tensor(a3, a3, rsb, OP.mult), inc="v")
        P.w(S, "v")
        P.r(S, lambda e, p=p, n=ncols: e.activation(out=w_b[0:p, 0:n], in_=w_a[0:p, 0:n], func=AF.Abs, scale=2.0), inc="sc")
        P.w(V, "sc")
        P.r(V, lambda e, p=p, n=ncols: e.tensor_copy(i_a[0:p, 0:n], w_b[0:p, 0:n]))
        P.r(V, lambda e, p=p, n=ncols: e.tensor_copy(c_a[0:p, 0:n], i_a[0:p, 0:n]))
        P.r(V, lambda e, p=p, n=ncols: e.tensor_scalar(c_a[0:p, 0:n], c_a[0:p, 0:n], 4.0, 0.5, OP.min, OP.mult), inc="v")
        for th in (5.0, 7.0):
            P.r(V, lambda e, p=p, n=ncols, th=th: e.tensor_scalar(c_b[0:p, 0:n], w_b[0:p, 0:n], th, None, OP.is_gt))
            P.r(V, lambda e, p=p, n=ncols: e.tensor_tensor(c_a[0:p, 0:n], c_a[0:p, 0:n], c_b[0:p, 0:n], OP.add))
        P.r(V, lambda e, p=p, n=ncols: e.tensor_scalar(c_b[0:p, 0:n], w_b[0:p, 0:n], 10.0, 2.0, OP.is_gt, OP.mult))
        P.r(V, lambda e, p=p, n=ncols: e.tensor_tensor(c_a[0:p, 0:n], c_a[0:p, 0:n], c_b[0:p, 0:n], OP.add), inc="v")
        P.w(S, "v")
        P.r(S, lambda e, p=p, n=ncols: e.activation(out=c_b[0:p, 0:n], in_=w_a[0:p, 0:n], func=AF.Sign), inc="sc")
        P.w(V, "sc")
        P.r(V, lambda e, p=p, n=ncols: e.tensor_tensor(c_a[0:p, 0:n], c_a[0:p, 0:n], c_b[0:p, 0:n], OP.mult))
        scb = bm[:, :, None].broadcast_to([p, nb, 16])
        ca3 = c_a[0:p, 0:ncols].rearrange("p (n s) -> p n s", s=16)
        P.r(V, lambda e, ca3=ca3, scb=scb: e.tensor_tensor(ca3, ca3, scb, OP.mult), inc="v")

    def f1_quant(src, dst, nrows, ncols):
        nt = (nrows + 127) // 128
        for r in range(nt):
            p = min(128, nrows - r * 128)
            if r >= 1:
                P.w(SY, "v")
            P.d(SY, w_a[0:p, 0:ncols], src[r * 128:r * 128 + p, :], "q0")
            P.w(V, "q0"); P.w(V, "q1")
            fp4_on_tile(p, ncols)
            P.w(S, "v")
            P.d(S, dst[r * 128:r * 128 + p, :], c_a[0:p, 0:ncols], "q1")
            P.w(SY, "q1"); P.w(V, "q1"); P.w(S, "q1")
    f1_quant(ap["f1_wg"], ap["s_f1wg"], 2816, D)
    f1_quant(ap["f1_wu"], ap["s_f1wu"], 2816, D)
    f1_quant(ap["f1_wd"], ap["s_f1wd"], D, 2816)

    P.r(V, lambda e: e.tensor_scalar(sxg[:, :], sxinv[:, :], mgr[:, 0:1], None, OP.mult))
    P.r(V, lambda e: e.tensor_scalar(sxu[:, :], sxinv[:, :], mgr[:, 1:2], None, OP.mult), inc="v")
    if "s_dbg2" in _dbg:
        P.w(S, "v")
        P.d(S, ap["s_dbg2"][:, 0:NT], sxg[:, :], "q1")
        P.d(S, ap["s_dbg2"][:, 16:16 + NT], sxu[:, :], "q1")
        P.d(S, ap["s_dbg2"][:, 32:32 + NT], sxinv[:, :], "q1")
        P.d(S, ap["s_dbg2"][:, 48:52], mgr[:, 0:4], "q1")
        P.w(SY, "q1"); P.w(V, "q1"); P.w(S, "q1")
    P.barrier()

    for k in range(8):
        P.d(SY, xnfm[:, ts_(k, T)], ap["s_xn"][:, ts_(k, 128)], "q0", transpose=True)
    for k in range(16):
        P.d(SY, wxpT[:, k * 48:k * 48 + 48], ap["s_wxp"][:, ts_(k, 128)], "q0", transpose=True)
    for b in range(B):
        for lc in range(2):
            t0 = b * LLOC + lc * 512
            for ot in range(16):
                wbuf = winS[:, (ot % 2) * 1024:(ot % 2) * 1024 + 1024]
                P.w(SY, "t")
                for k in range(8):
                    P.d(SY, wbuf[:, k * 128:(k + 1) * 128], ap["s_win"][ts_(ot, 128), ts_(k, 128)], "q0", transpose=True)
                P.wall(TE)
                for k in range(8):
                    st = wbuf[:, k * 128:(k + 1) * 128]
                    mv = xnfm[:, k * T + t0: k * T + t0 + 512]
                    P.r(TE, lambda e, st=st, mv=mv, k=k:
                        e.matmul(PS[0][:, 0:512], st, mv, start=(k == 0), stop=(k == 7)),
                        inc=("t" if k == 7 else None))
                if lc == 0:
                    for k in range(8):
                        st = wbuf[:, k * 128:(k + 1) * 128]
                        mv = xhf[:, k * 8 + b * 3: k * 8 + b * 3 + 3]
                        P.r(TE, lambda e, st=st, mv=mv, k=k:
                            e.matmul(PS[1][:, 0:3], st, mv, start=(k == 0), stop=(k == 7)),
                            inc=("t" if k == 7 else None))
                P.w(V, "t")
                col0 = ot * 516
                if lc == 0:
                    P.r(V, lambda e, col0=col0: e.tensor_copy(xpfc[:, col0:col0 + 3], PS[1][:, 0:3]))
                else:
                    P.r(V, lambda e, col0=col0, ot=ot: e.tensor_copy(xpfc[:, col0:col0 + 3], tails[:, ot * 4:ot * 4 + 3]))
                P.r(V, lambda e, col0=col0: e.tensor_copy(xpfc[:, col0 + 3:col0 + 515], PS[0][:, 0:512]))
                P.r(V, lambda e, ot=ot: e.tensor_copy(tails[:, ot * 4:ot * 4 + 3], PS[0][:, 509:512]), inc="v")
                P.w(TE, "v")
            for ot in range(16):
                col0 = ot * 516
                P.r(V, lambda e, col0=col0, ot=ot:
                    e.tensor_scalar(w_a[:, 0:512], xpfc[:, col0:col0 + 512], cbw[:, 4 * ot:4 * ot + 1], cbb[:, ot:ot + 1], OP.mult, OP.add))
                for j in (1, 2, 3):
                    P.r(V, lambda e, col0=col0, ot=ot, j=j:
                        e.scalar_tensor_tensor(w_a[:, 0:512], xpfc[:, col0 + j:col0 + j + 512], cbw[:, 4 * ot + j:4 * ot + j + 1], w_a[:, 0:512], OP.mult, OP.add), inc="v")
                P.w(S, "v")
                P.r(S, lambda e, ot=ot: e.activation(out=xcfc[:, ts_(ot, 512)], in_=w_a[:, 0:512], func=AF.Silu), inc="sc")
                P.w(V, "sc")
            P.wall(S)
            for ot in range(16):
                P.d(S, ap["s_xcf"][ts_(ot, 128), t0:t0 + 512], xcfc[:, ts_(ot, 512)], "q1")
            P.w(S, "q1"); P.w(V, "q1")
            P.wall(TE)
            for k in range(16):
                stB = wxpT[:, k * 48:k * 48 + 16]
                stC = wxpT[:, k * 48 + 16:k * 48 + 32]
                stV = wxpT[:, k * 48 + 32:k * 48 + 33]
                mv = xcfc[:, ts_(k, 512)]
                P.r(TE, lambda e, st=stB, mv=mv, k=k:
                    e.matmul(PSY[0:16, 0:512], st, mv, start=(k == 0), stop=(k == 15)))
                P.r(TE, lambda e, st=stC, mv=mv, k=k:
                    e.matmul(PSY[0:16, 512:1024], st, mv, start=(k == 0), stop=(k == 15)))
                P.r(TE, lambda e, st=stV, mv=mv, k=k:
                    e.matmul(PSY[0:1, 1024:1536], st, mv, start=(k == 0), stop=(k == 15)),
                    inc=("t" if k == 15 else None))
            P.w(V, "t")
            P.r(V, lambda e, t0=t0: e.tensor_copy(xdbB[0:16, t0:t0 + 512], PSY[0:16, 0:512]))
            P.r(V, lambda e, t0=t0: e.tensor_copy(xdbC[0:16, t0:t0 + 512], PSY[0:16, 512:1024]))
            P.r(V, lambda e, t0=t0: e.tensor_copy(xdbV[0:1, t0:t0 + 512], PSY[0:1, 1024:1536]), inc="v")
            P.w(TE, "v")
    for zc in range(4):
        P.w(SY, "t")
        for k in range(8):
            P.d(SY, zwv[:, k * 512:(k + 1) * 512], ap["s_win"][2048 + zc * 512:2048 + (zc + 1) * 512, ts_(k, 128)], "q0", transpose=True)
        P.wall(TE)
        for i in range(NT):
            ps = PS[4 + (i % 2)]
            for k in range(8):
                st = xnfm[:, off_(k * T, i, 128)]
                mv = zwv[:, k * 512:(k + 1) * 512]
                P.r(TE, lambda e, ps=ps, st=st, mv=mv, k=k:
                    e.matmul(ps[:, 0:512], st, mv, start=(k == 0), stop=(k == 7)),
                    inc=("t" if k == 7 else None))
            P.w(S, "t")
            P.w(S, "q1")
            P.r(S, lambda e, ps=ps: e.activation(out=c_a[:, 0:512], in_=ps[:, 0:512], func=AF.Silu), inc="sc")
            P.w(S, "sc")
            P.d(S, ap["s_sz"][ts_(i, 128), zc * 512:(zc + 1) * 512], c_a[:, 0:512], "q1")
            P.w(TE, "sc")
    P.barrier()

    P.r(S, lambda e: e.activation(out=fsc[0:1, :], in_=xdbV[0:1, 0:T], func=AF.Exp), inc="sc")
    P.w(S, "sc")
    P.r(S, lambda e: e.activation(out=fsc[0:1, :], in_=fsc[0:1, :], func=AF.Ln, bias=1.0), inc="sc")
    P.w(V, "sc")
    P.r(V, lambda e: e.tensor_scalar(fsc[0:1, :], fsc[0:1, :], 0.001, 0.1, OP.max, OP.min), inc="v")
    P.r(V, lambda e: e.memset(w_b[0:1, 100:116], 1.0), inc="v")
    P.w(TE, "v")
    for half in range(4):
        P.r(TE, lambda e, half=half: e.matmul(PS[0][0:16, 0:512], w_b[0:1, 100:116],
                                              fsc[0:1, half * 512:(half + 1) * 512], start=True, stop=True), inc="t")
        P.w(V, "t")
        P.r(V, lambda e, half=half: e.tensor_copy(fzv[0:16, half * 512:(half + 1) * 512], PS[0][0:16, 0:512]), inc="v")
        P.w(TE, "v")
    P.r(V, lambda e: e.tensor_scalar(fsc[0:16, :], fzv[0:16, :], Acol[0:16, 0:1], None, OP.mult))
    P.r(V, lambda e: e.tensor_scalar(fsc[0:16, :], fsc[0:16, :], -20.0, 0.0, OP.max, OP.min), inc="v")
    P.r(V, lambda e: e.tensor_tensor(fL[0:16, :], fzv[0:16, :], xdbB[0:16, 0:T], OP.mult))
    P.r(V, lambda e: e.tensor_scalar(fL[0:16, :], fL[0:16, :], -10.0, 10.0, OP.max, OP.min))
    P.r(V, lambda e: e.tensor_copy(fzv[0:16, :], fL[0:16, :]), inc="v")
    P.r(V, lambda e: e.memset(f0z[:, :], 0.0), inc="v")
    for b in range(B):
        P.r(V, lambda e, b=b: e.tensor_tensor_scan(
            fL[0:16, b * LLOC:(b + 1) * LLOC], fsc[0:16, b * LLOC:(b + 1) * LLOC],
            f0z[0:16, :], 0.0, OP.add, OP.add), inc="v")
    P.barrier()
    P.r(V, lambda e: e.memset(fDst[:, :], 0.0), inc="v")
    for b in range(B):
        src3 = fL[0:16, b * LLOC:(b + 1) * LLOC].rearrange("p (n c) -> p n c", c=TC)[:, 0:NCH - 1, TC - 1:TC]
        dst3 = fDst[0:16, b * NCH + 1:(b + 1) * NCH][:, :, None]
        P.r(V, lambda e, s=src3, d=dst3: e.tensor_copy(d, s), inc="v")
    for b in range(B):
        P.r(V, lambda e, b=b: e.tensor_tensor(
            fsc[0:16, b * LLOC:(b + 1) * LLOC].rearrange("p (n c) -> p n c", c=TC),
            fL[0:16, b * LLOC:(b + 1) * LLOC].rearrange("p (n c) -> p n c", c=TC),
            fDst[0:16, b * NCH:(b + 1) * NCH][:, :, None].broadcast_to([16, NCH, TC]),
            OP.subtract), inc="v")
    P.w(S, "v")
    P.r(S, lambda e: e.activation(out=w_a[0:16, 0:T], in_=fsc[0:16, :], func=AF.Exp), inc="sc")
    P.r(S, lambda e: e.activation(out=w_b[0:16, 0:T], in_=fsc[0:16, :], func=AF.Exp, scale=-1.0), inc="sc")
    P.w(V, "sc")
    P.r(V, lambda e: e.tensor_tensor(fC[0:16, :], xdbC[0:16, 0:T], w_a[0:16, 0:T], OP.mult))
    P.r(V, lambda e: e.tensor_tensor(fB[0:16, :], fzv[0:16, :], w_b[0:16, 0:T], OP.mult), inc="v")
    for b in range(B):
        src3 = w_a[0:16, b * LLOC:(b + 1) * LLOC].rearrange("p (n c) -> p n c", c=TC)[:, :, TC - 1:TC]
        dst3 = fDch[0:16, b * NCH:(b + 1) * NCH][:, :, None]
        P.r(V, lambda e, s=src3, d=dst3: e.tensor_copy(d, s), inc="v")
    P.w(S, "v")
    P.r(S, lambda e: e.activation(out=w_b[0:16, 3000:3000 + B * NCH], in_=fDst[0:16, :], func=AF.Exp), inc="sc")
    P.w(V, "sc")
    for b in range(B):
        P.r(V, lambda e, b=b: e.tensor_tensor(
            fCc[0:16, b * LLOC:(b + 1) * LLOC].rearrange("p (n c) -> p n c", c=TC),
            fC[0:16, b * LLOC:(b + 1) * LLOC].rearrange("p (n c) -> p n c", c=TC),
            w_b[0:16, 3000 + b * NCH:3000 + (b + 1) * NCH][:, :, None].broadcast_to([16, NCH, TC]),
            OP.mult), inc="v")
    P.barrier()
    if "s_dbg" in _dbg:
        P.w(S, "v"); P.w(S, "sc")
        P.d(S, ap["s_dbg"][0:16, :], fL[0:16, :], "q1")
        P.d(S, ap["s_dbg"][16:32, :], fsc[0:16, :], "q1")
        P.d(S, ap["s_dbg"][32:48, 0:B * NCH], fDst[0:16, :], "q1")
        P.d(S, ap["s_dbg"][48:64, 0:B * NCH], fDch[0:16, :], "q1")
        P.d(S, ap["s_dbg"][64:80, :], xdbB[0:16, 0:T], "q1")
        P.d(S, ap["s_dbg"][80:96, :], xdbC[0:16, 0:T], "q1")
        P.d(S, ap["s_dbg"][96:112, :], fzv[0:16, 0:T], "q1")
        P.d(S, ap["s_dbg"][112:128, :], w_b[0:16, 0:T], "q1")
        P.w(SY, "q1"); P.w(V, "q1"); P.w(S, "q1")
    P.w(S, "v")
    P.d(S, ap["s_fb"][:, :], fB[0:16, :], "q1")
    P.w(SY, "q1"); P.w(V, "q1"); P.w(S, "q1")
    P.barrier()
    for b in range(B):
        for gid in range(NG):
            g_ = b * NG + gid
            dst = btT[:, g_ * 128:g_ * 128 + 16]
            src = ap["s_fb"][0:16, b * LLOC + gid * 128:(b * LLOC) + (gid + 1) * 128]
            P.d(SY, dst, src, "q0", transpose=True)
            P.w(SY, "q0")
            if g_ == 0:
                P.barrier()
    P.barrier()

    P.r(V, lambda e: e.memset(hcurb[0][0:16, :], 0.0))
    P.r(V, lambda e: e.memset(hcurb[1][0:16, :], 0.0), inc="v")
    for b in range(B):
        for gid in range(NG):
            xbuf = xtm[:, (gid % 2) * DI:((gid % 2) + 1) * DI]
            P.w(SY, "t")
            P.d(SY, xbuf, ap["s_xcf"][:, b * LLOC + gid * 128: b * LLOC + (gid + 1) * 128], "q0", transpose=True)
            P.wall(TE)
            for c in range(4):
                n = gid * 4 + c
                blk = (b * NG + gid) * 4 + c
                stm = fB[0:16, b * LLOC + n * TC: b * LLOC + (n + 1) * TC]
                mvm = fC[0:16, b * LLOC + n * TC: b * LLOC + (n + 1) * TC]
                P.r(TE, lambda e, stm=stm, mvm=mvm, c=c:
                    e.matmul(PSG[32 * c:32 * c + 32, 0:32], stm, mvm, start=True, stop=True,
                             tile_position=(0, 32 * c)), inc="t")
                P.w(V, "t")
                mcb = c_b[32 * c:32 * c + 32, (n % 2) * TC:((n % 2) + 1) * TC]
                P.r(V, lambda e, mcb=mcb, c=c: e.tensor_tensor(mcb, PSG[32 * c:32 * c + 32, 0:32], msk[32 * c:32 * c + 32, :], OP.mult), inc="v")
                P.w(TE, "v")
                for dc in range(4):
                    stg = btT[32 * c:32 * c + 32, (b * NG + gid) * 128:(b * NG + gid) * 128 + 16]
                    mvx = xbuf[32 * c:32 * c + 32, dc * 512:(dc + 1) * 512]
                    P.r(TE, lambda e, stg=stg, mvx=mvx, c=c, dc=dc:
                        e.matmul(PSG[0:16, dc * 512:(dc + 1) * 512], stg, mvx, start=True, stop=True,
                                 tile_position=(32 * c, 0)),
                        inc=("t" if dc == 3 else None))
                P.w(V, "t")
                P.r(V, lambda e, b=b: e.tensor_copy(hstg[0:16, :], hcurb[b][0:16, :]), inc="v")
                P.r(V, lambda e, b=b: e.tensor_tensor(w_a[0:16, 0:DI], hcurb[b][0:16, :], PSG[0:16, 0:DI], OP.add))
                P.r(V, lambda e, b=b, n=n: e.tensor_scalar(hcurb[b][0:16, :], w_a[0:16, 0:DI], fDch[0:16, b * NCH + n:b * NCH + n + 1], None, OP.mult), inc="v")
                P.w(TE, "v")
                for dc in range(4):
                    mvx = xbuf[32 * c:32 * c + 32, dc * 512:(dc + 1) * 512]
                    stc = fC[0:16, b * LLOC + n * TC: b * LLOC + (n + 1) * TC]
                    mvh = hstg[0:16, dc * 512:(dc + 1) * 512]
                    P.r(TE, lambda e, mcb=mcb, mvx=mvx, c=c, dc=dc:
                        e.matmul(PSY[32 * c:32 * c + 32, dc * 512:(dc + 1) * 512], mcb, mvx, start=True, stop=False,
                                 tile_position=(32 * c, 32 * c)))
                    P.r(TE, lambda e, stc=stc, mvh=mvh, c=c, dc=dc:
                        e.matmul(PSY[32 * c:32 * c + 32, dc * 512:(dc + 1) * 512], stc, mvh, start=False, stop=True,
                                 tile_position=(0, 32 * c)),
                        inc=("t" if dc == 3 else None))
            P.w(V, "t"); P.w(V, "q1")
            P.r(V, lambda e: e.tensor_copy(c_a[:, 0:DI], PSY[:, 0:DI]), inc="v")
            P.w(S, "v")
            P.d(S, ap["s_y"][ts_(b * NG + gid, 128), :], c_a[:, 0:DI], "q1")
            P.w(TE, "q1"); P.w(V, "q1"); P.w(S, "q1")
    P.w(S, "v")
    P.r(S, lambda e: e.activation(out=p_a[0:16, 16:17], in_=fL[0:16, LLOC - 1:LLOC], func=AF.Exp), inc="sc")
    P.r(S, lambda e: e.activation(out=p_a[0:16, 17:18], in_=fL[0:16, 2 * LLOC - 1:2 * LLOC], func=AF.Exp), inc="sc")
    P.wall(SY)
    P.d(SY, ap["cc_in"][0:16, 0:DI], hcurb[0][0:16, :], "q0")
    P.d(SY, ap["cc_in"][16:32, 0:DI], hcurb[1][0:16, :], "q0")
    P.d(SY, ap["cc_in"][0:16, DI:DI + 1], p_a[0:16, 16:17], "q0", slow=True)
    P.d(SY, ap["cc_in"][16:32, DI:DI + 1], p_a[0:16, 17:18], "q0", slow=True)
    P.w(G, "q0")
    P.r(G, lambda e: e.collective_compute("AllGather", OP.bypass, replica_groups=[list(range(NCORES))],
                                          ins=[ap["cc_in"][:]], outs=[ap["cc_out"][:]]), inc="cc")
    P.w(SY, "cc")
    P.wall(V)
    P.r(V, lambda e: e.memset(hin[0:32, :], 0.0), inc="v")
    for r in range(NCORES - 1):
        P.w(V, "q1")
        P.wall(SY)
        P.d(SY, w_b[0:32, 0:DI + 1], ap["cc_out"][r], "q0")
        P.wall(V)
        Dr = w_b[0:32, DI:DI + 1]
        Gr = w_b[0:32, 0:DI]
        mr = rmk[:, r + 1:r + 2]
        P.r(V, lambda e, Dr=Dr: e.tensor_scalar(p_a[0:32, 20:21], Dr, -1.0, None, OP.add))
        P.r(V, lambda e, mr=mr: e.scalar_tensor_tensor(p_a[0:32, 21:22], p_a[0:32, 20:21], mr, onec[0:32, 0:1], OP.mult, OP.add), inc="v")
        P.r(V, lambda e: e.tensor_scalar(hin[0:32, :], hin[0:32, :], p_a[0:32, 21:22], None, OP.mult), inc="v")
        P.r(V, lambda e, Gr=Gr, mr=mr: e.scalar_tensor_tensor(hin[0:32, :], Gr, mr, hin[0:32, :], OP.mult, OP.add), inc="v")
    P.r(V, lambda e: e.tensor_copy(hinb[0:16, 0:DI], hin[0:16, :]), inc="v")
    P.wall(SY)
    P.d(SY, ap["cc_in"][0:16, 0:DI], hin[16:32, :], "q0")
    P.w(G, "q0")
    P.d(G, hinb2[0:16, 0:DI], ap["cc_in"][0:16, 0:DI], "q2")
    P.barrier()

    for i in range(NT):
        b = i // 8; j = i % 8
        P.wall(TE)
        for dc in range(4):
            stc = fCc[0:16, b * LLOC + j * 128:(b * LLOC + j * 128) + 128]
            hb = hinb if b == 0 else hinb2
            mvh = hb[0:16, dc * 512:(dc + 1) * 512]
            P.r(TE, lambda e, stc=stc, mvh=mvh, dc=dc:
                e.matmul(PS[dc][:, 0:512], stc, mvh, start=True, stop=True), inc="t")
        P.d(SY, c_a[:, 0:DI], ap["s_y"][ts_(i, 128), :], "q0")
        P.d(SY, c_b[:, 0:DI], ap["s_sz"][ts_(i, 128), :], "q0")
        P.d(SY, c_c[:, 0:DI], ap["s_xcf"][:, 0:0 + 0] if False else c_c[:, 0:DI], "q0") if False else None
        P.d(SY, c_c[:, 0:DI], ap["s_xcf"][:, b * LLOC + j * 128: b * LLOC + (j + 1) * 128], "q0", transpose=True)
        P.wall(V)
        for dc in range(4):
            P.r(V, lambda e, dc=dc: e.tensor_tensor(w_a[:, dc * 512:(dc + 1) * 512], c_a[:, dc * 512:(dc + 1) * 512], PS[dc][:, 0:512], OP.add), inc="v")
        P.w(TE, "v")
        P.r(V, lambda e: e.tensor_tensor(w_a[:, 0:DI], w_a[:, 0:DI], c_b[:, 0:DI], OP.mult), inc="v")
        P.r(V, lambda e: e.tensor_tensor(c_c[:, 0:DI], c_c[:, 0:DI], dprow[:, :], OP.mult))
        P.r(V, lambda e: e.tensor_tensor(c_a[:, 0:DI], w_a[:, 0:DI], c_c[:, 0:DI], OP.add), inc="v")
        P.w(S, "v")
        P.d(S, ap["s_yg"][ts_(i, 128), :], c_a[:, 0:DI], "q1")
        P.w(SY, "q1"); P.w(V, "q1"); P.w(S, "q1")
    P.barrier()

    ygfm = big1
    woT = bufA
    for k in range(16):
        P.d(SY, ygfm[:, ts_(k, T)], ap["s_yg"][:, ts_(k, 128)], "q0", transpose=True)
        P.d(SY, woT[:, ts_(k, D)], ap["s_wout"][:, ts_(k, 128)], "q0", transpose=True)
    P.wall(TE)
    for i in range(NT):
        for mc in range(2):
            ps = PS[mc]
            for k in range(16):
                st = ygfm[:, off_(k * T, i, 128)]
                mv = woT[:, k * D + mc * 512: k * D + mc * 512 + 512]
                P.r(TE, lambda e, ps=ps, st=st, mv=mv, k=k:
                    e.matmul(ps[:, 0:512], st, mv, start=(k == 0), stop=(k == 15)),
                    inc=("t" if k == 15 else None))
        P.w(V, "t"); P.w(V, "q1")
        P.r(V, lambda e: e.tensor_copy(c_a[:, 0:512], PS[0][:, 0:512]))
        P.r(V, lambda e: e.tensor_copy(c_a[:, 512:1024], PS[1][:, 0:512]), inc="v")
        P.w(TE, "v"); P.w(S, "v")
        P.d(S, ap["s_s"][ts_(i, 128), :], c_a[:, 0:D], "q1")
    P.barrier()

    xqfm = A16[:, 0:16384]
    for k in range(8):
        P.d(SY, xqfm[:, ts_(k, T)], ap["s_xq"][:, ts_(k, 128)], "q0", transpose=True)
    P.r(V, lambda e: e.memset(p_a[:, 24:25], 0.0), inc="v")
    P.r(V, lambda e: e.memset(sx2i[:, :], 0.0), inc="v")
    wmovB = A16[:, 16384:24576]
    wmov2B = A16[:, 24576:32768]
    tmark = {}
    for oc in range(8):
        wg_ = wmov if oc % 2 == 0 else wmovB
        wu_ = wmov2 if oc % 2 == 0 else wmov2B
        if oc >= 2:
            P.w(SY, "t", tmark[oc - 2])
        for k in range(8):
            P.d(SY, wg_[:, ts_(k, 512)], ap["s_f0wg"][oc * 512:(oc + 1) * 512, ts_(k, 128)], "q0", transpose=True)
            P.d(SY, wu_[:, ts_(k, 512)], ap["s_f0wu"][oc * 512:(oc + 1) * 512, ts_(k, 128)], "q0", transpose=True)
        P.w(TE, "q0")
        for i in range(NT):
            for k in range(8):
                st = xqfm[:, off_(k * T, i, 128)]
                P.r(TE, lambda e, st=st, k=k, w=wg_: e.matmul(PS[0][:, 0:512], st, w[:, ts_(k, 512)], start=(k == 0), stop=(k == 7)), inc=("t" if k == 7 else None))
            for k in range(8):
                st = xqfm[:, off_(k * T, i, 128)]
                P.r(TE, lambda e, st=st, k=k, w=wu_: e.matmul(PS[1][:, 0:512], st, w[:, ts_(k, 512)], start=(k == 0), stop=(k == 7)), inc=("t" if k == 7 else None))
            if i == NT - 1:
                tmark[oc] = P.cnt["t"]
            P.w(S, "t")
            P.r(S, lambda e, i=i: e.activation(out=w_a[:, 0:512], in_=PS[0][:, 0:512], func=AF.Silu, scale=sxg[:, i:i + 1]), inc="sc")
            P.r(S, lambda e, i=i: e.activation(out=w_b[:, 0:512], in_=PS[1][:, 0:512], func=AF.Identity, scale=sxu[:, i:i + 1], bias=0.0), inc="sc")
            P.w(V, "sc")
            P.w(TE, "sc")
            P.r(V, lambda e: e.tensor_tensor(w_a[:, 0:512], w_a[:, 0:512], w_b[:, 0:512], OP.mult))
            P.r(V, lambda e, i=i: e.reduce_max(p_a[:, 25:26], w_a[:, 0:512], AX.X, apply_absolute_value=True))
            P.r(V, lambda e, i=i: e.tensor_tensor(sx2i[:, i:i + 1], sx2i[:, i:i + 1], p_a[:, 25:26], OP.max), inc="v")
            P.w(S, "v")
            P.d(S, ap["s_h2"][ts_(i, 128), oc * 512:(oc + 1) * 512], w_a[:, 0:512], "q1")
            P.w(V, "q1"); P.w(S, "q1")
    P.r(V, lambda e: e.tensor_scalar(sx2i[:, :], sx2i[:, :], 1e-5, 1.0 / 127.0, OP.max, OP.mult))
    P.r(V, lambda e: e.tensor_scalar(sxd[:, :], sx2i[:, :], mgr[:, 2:3], None, OP.mult), inc="v")
    P.barrier()
    for i in range(NT):
        P.d(SY, w_a[:, 0:4096], ap["s_h2"][ts_(i, 128), :], "q0")
        P.wall(V)
        P.r(V, lambda e, i=i: e.reciprocal(p_a[:, 26:27], sx2i[:, i:i + 1]), inc="v")
        P.w(S, "v")
        P.r(S, lambda e: e.activation(out=w_b[:, 0:4096], in_=w_a[:, 0:4096], func=AF.Identity, scale=p_a[:, 26:27], bias=0.0), inc="sc")
        P.w(V, "sc")
        P.r(V, lambda e: e.tensor_copy(i_a[:, 0:4096], w_b[:, 0:4096]))
        P.r(V, lambda e: e.tensor_copy(c_a[:, 0:4096], i_a[:, 0:4096]), inc="v")
        P.w(S, "v")
        P.d(S, ap["s_xq2"][ts_(i, 128), :], c_a[:, 0:4096], "q1")
        P.w(SY, "q1"); P.w(V, "q1"); P.w(S, "q1")
    xq2fm = big1
    for half in range(2):
        P.w(SY, "t")
        for k in range(16):
            kk = half * 16 + k
            P.d(SY, xq2fm[:, ts_(k, T)], ap["s_xq2"][:, ts_(kk, 128)], "q0", transpose=True)
            P.d(SY, wmov[:, ts_(k, 512)], ap["s_f0wd"][0:512, ts_(kk, 128)], "q0", transpose=True)
            P.d(SY, wmov2[:, ts_(k, 512)], ap["s_f0wd"][512:1024, ts_(kk, 128)], "q0", transpose=True)
        if half == 0 and "s_dbg3" in _dbg:
            P.w(S, "q0")
            P.d(S, ap["s_dbg3"][:, 0:T], xq2fm[:, 0:T], "q1")
            P.d(S, ap["s_dbg3"][:, T:T + 512], wmov[:, 0:512], "q1")
            P.w(SY, "q1"); P.w(V, "q1"); P.w(S, "q1")
        P.wall(TE)
        for i in range(NT):
            for mc in range(2):
                ps = PS[mc]
                wv = wmov if mc == 0 else wmov2
                for k in range(16):
                    st = xq2fm[:, off_(k * T, i, 128)]
                    P.r(TE, lambda e, ps=ps, st=st, wv=wv, k=k:
                        e.matmul(ps[:, 0:512], st, wv[:, ts_(k, 512)], start=(k == 0), stop=(k == 15)),
                        inc=("t" if k == 15 else None))
            if half == 0:
                P.w(V, "t")
                P.r(V, lambda e: e.tensor_copy(w_a[:, 0:512], PS[0][:, 0:512]))
                P.r(V, lambda e: e.tensor_copy(w_a[:, 512:1024], PS[1][:, 0:512]), inc="v")
                P.w(TE, "v"); P.w(S, "v")
                P.d(S, ap["s_h2"][ts_(i, 128), 0:1024], w_a[:, 0:1024], "q1")
                P.w(V, "q1")
            else:
                P.w(SY, "v")
                P.d(SY, w_b[:, 0:1024], ap["s_h2"][ts_(i, 128), 0:1024], "q0")
                P.w(V, "t"); P.wall(V)
                P.r(V, lambda e: e.tensor_tensor(w_a[:, 0:512], PS[0][:, 0:512], w_b[:, 0:512], OP.add))
                P.r(V, lambda e: e.tensor_tensor(w_a[:, 512:1024], PS[1][:, 0:512], w_b[:, 512:1024], OP.add), inc="v")
                P.w(TE, "v"); P.w(S, "v"); P.w(S, "q1")
                P.r(S, lambda e, i=i: e.activation(out=c_b[:, 0:D], in_=w_a[:, 0:D], func=AF.Silu, scale=sxd[:, i:i + 1]), inc="sc")
                P.w(S, "sc")
                P.d(S, ap["s_g"][ts_(i, 128), :], c_b[:, 0:D], "q1")
                P.w(V, "sc")
    P.barrier()

    for i in range(NT):
        P.d(SY, w_b[:, 0:D], xr[i], "q0")
        P.d(SY, c_a[:, 0:D], ap["s_s"][ts_(i, 128), :], "q0")
        P.d(SY, c_b[:, 0:D], ap["s_g"][ts_(i, 128), :], "q0")
        P.wall(V)
        P.r(V, lambda e: e.tensor_tensor(c_a[:, 0:D], c_a[:, 0:D], c_b[:, 0:D], OP.mult), inc="v")
        P.r(V, lambda e: e.tensor_tensor(w_a[:, 0:D], w_b[:, 0:D], c_a[:, 0:D], OP.add), inc="v")
        P.w(S, "v")
        P.d(S, ap["s_x1"][ts_(i, 128), :], w_a[:, 0:D], "q1")
        P.w(V, "q1"); P.w(S, "q1")
        mu = p_a[:, 0:1]; vv = p_a[:, 1:2]; rr = p_a[:, 2:3]
        P.r(V, lambda e, mu=mu: e.reduce_sum(mu, w_a[:, 0:D], AX.X))
        P.r(V, lambda e, mu=mu: e.tensor_scalar_mul(mu, mu, 1.0 / D), inc="v")
        P.w(S, "v")
        P.r(S, lambda e: e.activation(out=w_b[:, 0:D], in_=w_a[:, 0:D], func=AF.Square), inc="sc")
        P.w(V, "sc")
        P.r(V, lambda e, vv=vv: e.reduce_sum(vv, w_b[:, 0:D], AX.X))
        P.r(V, lambda e, vv=vv: e.tensor_scalar_mul(vv, vv, 1.0 / D))
        P.r(V, lambda e, mu=mu, rr=rr: e.tensor_tensor(rr, mu, mu, OP.mult))
        P.r(V, lambda e, vv=vv, rr=rr: e.tensor_tensor(rr, vv, rr, OP.subtract))
        P.r(V, lambda e, rr=rr: e.tensor_scalar_add(rr, rr, 1e-5), inc="v")
        P.w(S, "v")
        P.r(S, lambda e, rr=rr: e.activation(out=rr, in_=rr, func=AF.Sqrt), inc="sc")
        P.w(V, "sc")
        P.r(V, lambda e, rr=rr: e.reciprocal(rr, rr))
        P.r(V, lambda e, mu=mu, rr=rr: e.tensor_scalar(w_a[:, 0:D], w_a[:, 0:D], mu, rr, OP.subtract, OP.mult))
        P.r(V, lambda e: e.tensor_copy(c_b[:, 0:D], w_a[:, 0:D]), inc="v")
        P.w(S, "v")
        P.d(S, ap["s_xn1"][ts_(i, 128), :], c_b[:, 0:D], "q1")
        P.w(V, "q1"); P.w(S, "q1")
        fp4_on_tile(128, D)
        P.w(S, "v")
        P.d(S, ap["s_xq1"][ts_(i, 128), :], c_a[:, 0:D], "q1")
        P.w(SY, "q1"); P.w(V, "q1"); P.w(S, "q1")
    P.barrier()

    xq1fm = A16[:, 0:16384]
    for k in range(8):
        P.d(SY, xq1fm[:, ts_(k, T)], ap["s_xq1"][:, ts_(k, 128)], "q0", transpose=True)
    OCW = [512, 512, 512, 512, 512, 256]
    ocs = [0, 512, 1024, 1536, 2048, 2560]
    wmovB = A16[:, 16384:24576]
    wmov2B = A16[:, 24576:32768]
    tmark = {}
    for oc in range(6):
        ow = OCW[oc]; o0 = ocs[oc]
        wg_ = wmov if oc % 2 == 0 else wmovB
        wu_ = wmov2 if oc % 2 == 0 else wmov2B
        if oc >= 2:
            P.w(SY, "t", tmark[oc - 2])
        for k in range(8):
            P.d(SY, wg_[:, k * 512:k * 512 + ow], ap["s_f1wg"][o0:o0 + ow, ts_(k, 128)], "q0", transpose=True)
            P.d(SY, wu_[:, k * 512:k * 512 + ow], ap["s_f1wu"][o0:o0 + ow, ts_(k, 128)], "q0", transpose=True)
        P.w(TE, "q0")
        for i in range(NT):
            for k in range(8):
                st = xq1fm[:, off_(k * T, i, 128)]
                P.r(TE, lambda e, st=st, k=k, ow=ow, w=wg_: e.matmul(PS[0][:, 0:ow], st, w[:, k * 512:k * 512 + ow], start=(k == 0), stop=(k == 7)), inc=("t" if k == 7 else None))
            for k in range(8):
                st = xq1fm[:, off_(k * T, i, 128)]
                P.r(TE, lambda e, st=st, k=k, ow=ow, w=wu_: e.matmul(PS[1][:, 0:ow], st, w[:, k * 512:k * 512 + ow], start=(k == 0), stop=(k == 7)), inc=("t" if k == 7 else None))
            if i == NT - 1:
                tmark[oc] = P.cnt["t"]
            P.w(S, "t")
            P.r(S, lambda e, ow=ow: e.activation(out=w_a[:, 0:ow], in_=PS[0][:, 0:ow], func=AF.Silu), inc="sc")
            P.r(S, lambda e, ow=ow: e.activation(out=w_b[:, 3000:3000 + ow] if False else w_b[:, 0:ow], in_=PS[1][:, 0:ow], func=AF.Identity, bias=0.0), inc="sc")
            P.w(V, "sc"); P.w(TE, "sc")
            P.r(V, lambda e, ow=ow: e.tensor_tensor(w_a[:, 0:ow], w_a[:, 0:ow], w_b[:, 0:ow], OP.mult), inc="v")
            fp4_on_tile(128, ow)
            P.w(S, "v")
            P.d(S, ap["s_xq3"][ts_(i, 128), o0:o0 + ow], c_a[:, 0:ow], "q1")
            P.w(V, "q1"); P.w(S, "q1")
    P.barrier()

    xq3fm = big1
    for half in range(2):
        P.w(SY, "t")
        for k in range(11):
            kk = half * 11 + k
            P.d(SY, xq3fm[:, ts_(k, T)], ap["s_xq3"][:, ts_(kk, 128)], "q0", transpose=True)
            P.d(SY, wmov[:, ts_(k, 512)], ap["s_f1wd"][0:512, ts_(kk, 128)], "q0", transpose=True)
            P.d(SY, wmov2[:, ts_(k, 512)], ap["s_f1wd"][512:1024, ts_(kk, 128)], "q0", transpose=True)
        P.wall(TE)
        for i in range(NT):
            for mc in range(2):
                ps = PS[mc]
                wv = wmov if mc == 0 else wmov2
                for k in range(11):
                    st = xq3fm[:, off_(k * T, i, 128)]
                    P.r(TE, lambda e, ps=ps, st=st, wv=wv, k=k:
                        e.matmul(ps[:, 0:512], st, wv[:, ts_(k, 512)], start=(k == 0), stop=(k == 10)),
                        inc=("t" if k == 10 else None))
            if half == 0:
                P.w(V, "t")
                P.r(V, lambda e: e.tensor_copy(w_a[:, 0:512], PS[0][:, 0:512]))
                P.r(V, lambda e: e.tensor_copy(w_a[:, 512:1024], PS[1][:, 0:512]), inc="v")
                P.w(TE, "v"); P.w(S, "v")
                P.d(S, ap["s_h2"][ts_(i, 128), 0:1024], w_a[:, 0:1024], "q1")
                P.w(V, "q1")
            else:
                P.w(SY, "v")
                P.d(SY, w_b[:, 0:1024], ap["s_h2"][ts_(i, 128), 0:1024], "q0")
                P.w(V, "t"); P.wall(V)
                P.r(V, lambda e: e.tensor_tensor(w_a[:, 0:512], PS[0][:, 0:512], w_b[:, 0:512], OP.add))
                P.r(V, lambda e: e.tensor_tensor(w_a[:, 512:1024], PS[1][:, 0:512], w_b[:, 512:1024], OP.add), inc="v")
                P.w(TE, "v"); P.w(S, "v"); P.w(S, "q1")
                P.r(S, lambda e: e.activation(out=c_b[:, 0:D], in_=w_a[:, 0:D], func=AF.Silu), inc="sc")
                P.w(S, "sc")
                P.d(S, ap["s_g1"][ts_(i, 128), :], c_b[:, 0:D], "q1")
                P.w(V, "sc")
    P.barrier()

    xn1fm = big1[:, 0:16384]
    for k in range(8):
        P.d(SY, xn1fm[:, ts_(k, T)], ap["s_xn1"][:, ts_(k, 128)], "q0", transpose=True)
    QCW = [512, 512, 512, 512, 512, 512, 32]
    qcs = [0, 512, 1024, 1536, 2048, 2560, 3072]
    wmovB = A16[:, 16384:24576]
    tmark = {}
    for oc in range(7):
        tmark[oc] = P.cnt["t"]
        ow = QCW[oc]; o0 = qcs[oc]
        wg_ = wmov if oc % 2 == 0 else wmovB
        if oc >= 2:
            P.w(SY, "t", tmark[oc - 1])
        for k in range(8):
            P.d(SY, wg_[:, k * 512:k * 512 + ow], ap["s_qkvb"][o0:o0 + ow, ts_(k, 128)], "q0", transpose=True)
        P.w(TE, "q0")
        for i in range(NT):
            for k in range(8):
                st = xn1fm[:, off_(k * T, i, 128)]
                P.r(TE, lambda e, st=st, k=k, ow=ow, w=wg_: e.matmul(PS[0][:, 0:ow], st, w[:, k * 512:k * 512 + ow], start=(k == 0), stop=(k == 7)), inc=("t" if k == 7 else None))
            P.w(V, "t")
            P.r(V, lambda e, ow=ow: e.tensor_copy(c_a[:, 0:ow], PS[0][:, 0:ow]), inc="v")
            P.w(TE, "v"); P.w(S, "v")
            P.d(S, ap["s_qkv"][ts_(i, 128), o0:o0 + ow], c_a[:, 0:ow], "q1")
            P.w(V, "q1"); P.w(S, "q1")
    P.barrier()

    def elu1_norm(dst_bf, src_bf, tmpf):
        P.r(V, lambda e: e.tensor_scalar(w_a[:, 0:1024], src_bf, 0.0, None, OP.min), inc="v")
        P.w(S, "v")
        P.r(S, lambda e: e.activation(out=w_a[:, 1024:2048], in_=w_a[:, 0:1024], func=AF.Exp), inc="sc")
        P.w(V, "sc")
        P.r(V, lambda e: e.tensor_scalar(w_a[:, 0:1024], src_bf, 0.0, None, OP.max))
        P.r(V, lambda e: e.tensor_tensor(w_a[:, 0:1024], w_a[:, 0:1024], w_a[:, 1024:2048], OP.add), inc="v")
        a3 = w_a[:, 0:1024].rearrange("p (h d) -> p h d", d=64)
        P.r(V, lambda e, a3=a3: e.tensor_reduce(p_a[:, 32:48], a3, AX.X, OP.add))
        P.r(V, lambda e: e.reciprocal(p_a[:, 32:48], p_a[:, 32:48]), inc="v")
        rsb = p_a[:, 32:48][:, :, None].broadcast_to([128, 16, 64])
        P.r(V, lambda e, a3=a3, rsb=rsb, dst_bf=dst_bf: e.tensor_tensor(
            dst_bf.rearrange("p (h d) -> p h d", d=64), a3, rsb, OP.mult), inc="v")

    for j in range(8):
        P.wall(V)
        P.d(SY, c_a[:, 0:3104], ap["s_qkv"][ts_(j, 128), :], "q0")
        P.d(SY, c_b[:, 0:3104], ap["s_qkv"][ts_(8 + j, 128), :], "q0")
        P.wall(V)
        elu1_norm(c_c[:, 0:1024], c_a[:, 0:1024], 0)
        elu1_norm(c_c[:, 1024:2048], c_a[:, 1024:2048], 0)
        elu1_norm(c_c[:, 2048:3072], c_b[:, 0:1024], 0)
        elu1_norm(c_b[:, 0:1024], c_b[:, 1024:2048], 0)
        P.r(V, lambda e: e.tensor_scalar(w_a[:, 0:16], c_a[:, 3072:3088], -1.0, None, OP.mult))
        P.r(V, lambda e: e.tensor_scalar(w_a[:, 16:32], c_b[:, 3072:3088], -1.0, None, OP.mult), inc="v")
        P.w(S, "v")
        P.r(S, lambda e: e.activation(out=w_a[:, 0:32], in_=w_a[:, 0:32], func=AF.Exp), inc="sc")
        P.w(V, "sc")
        P.r(V, lambda e: e.tensor_scalar_add(w_a[:, 0:32], w_a[:, 0:32], 1.0))
        P.r(V, lambda e: e.reciprocal(w_a[:, 0:32], w_a[:, 0:32]), inc="v")
        b0c = w_a[:, 0:16]; b1c = w_a[:, 16:32]
        def dot(dst_cols, aa, bb):
            P.r(V, lambda e, aa=aa, bb=bb: e.tensor_tensor(w_b[:, 0:1024], aa, bb, OP.mult), inc="v")
            a3 = w_b[:, 0:1024].rearrange("p (h d) -> p h d", d=64)
            P.r(V, lambda e, a3=a3, dst_cols=dst_cols: e.tensor_reduce(p_a[:, dst_cols:dst_cols + 16], a3, AX.X, OP.add), inc="v")
        q0n = c_c[:, 0:1024]; k0n = c_c[:, 1024:2048]; q1n = c_c[:, 2048:3072]; k1n = c_b[:, 0:1024]
        dot(32, q0n, k0n)
        dot(48, q1n, k0n)
        dot(16, q1n, k1n)
        dot(0, k1n, k0n)
        v0 = c_a[:, 2048:3072]; v1 = c_b[:, 2048:3072]
        b0b = b0c[:, :, None].broadcast_to([128, 16, 64])
        P.r(V, lambda e, b0b=b0b, v0=v0: e.tensor_tensor(
            c_a[:, 0:1024].rearrange("p (h d) -> p h d", d=64), v0.rearrange("p (h d) -> p h d", d=64), b0b, OP.mult), inc="v")
        vi0 = c_a[:, 0:1024]
        d_q0k0 = p_a[:, 32:48][:, :, None].broadcast_to([128, 16, 64])
        P.r(V, lambda e, d_q0k0=d_q0k0, vi0=vi0: e.tensor_tensor(
            c_a[:, 1024:2048].rearrange("p (h d) -> p h d", d=64), vi0.rearrange("p (h d) -> p h d", d=64), d_q0k0, OP.mult), inc="v")
        d_k1k0 = p_a[:, 0:16][:, :, None].broadcast_to([128, 16, 64])
        P.r(V, lambda e, d_k1k0=d_k1k0, vi0=vi0: e.tensor_tensor(
            w_b[:, 0:1024].rearrange("p (h d) -> p h d", d=64), vi0.rearrange("p (h d) -> p h d", d=64), d_k1k0, OP.mult), inc="v")
        P.r(V, lambda e, v1=v1: e.tensor_tensor(w_b[:, 0:1024], v1, w_b[:, 0:1024], OP.subtract), inc="v")
        b1b = b1c[:, :, None].broadcast_to([128, 16, 64])
        P.r(V, lambda e, b1b=b1b: e.tensor_tensor(
            w_b[:, 0:1024].rearrange("p (h d) -> p h d", d=64), w_b[:, 0:1024].rearrange("p (h d) -> p h d", d=64), b1b, OP.mult), inc="v")
        vi1 = w_b[:, 0:1024]
        d_q1k0 = p_a[:, 48:64][:, :, None].broadcast_to([128, 16, 64])
        d_q1k1 = p_a[:, 16:32][:, :, None].broadcast_to([128, 16, 64])
        P.r(V, lambda e, d_q1k0=d_q1k0, vi0=vi0: e.tensor_tensor(
            w_b[:, 1024:2048].rearrange("p (h d) -> p h d", d=64), vi0.rearrange("p (h d) -> p h d", d=64), d_q1k0, OP.mult), inc="v")
        P.r(V, lambda e, d_q1k1=d_q1k1, vi1=vi1: e.tensor_tensor(
            w_b[:, 2048:3072].rearrange("p (h d) -> p h d", d=64), vi1.rearrange("p (h d) -> p h d", d=64), d_q1k1, OP.mult), inc="v")
        P.r(V, lambda e: e.tensor_tensor(w_b[:, 1024:2048], w_b[:, 1024:2048], w_b[:, 2048:3072], OP.add), inc="v")
        P.r(V, lambda e: e.tensor_scalar(c_a[:, 1024:2048], c_a[:, 1024:2048], 0.125, None, OP.mult))
        P.r(V, lambda e: e.tensor_scalar(c_b[:, 1024:2048], w_b[:, 1024:2048], 0.125, None, OP.mult), inc="v")
        P.w(S, "v")
        P.d(S, ap["s_lay"][ts_(j, 128), :], c_a[:, 1024:2048], "q1")
        P.d(S, ap["s_lay"][ts_(8 + j, 128), :], c_b[:, 1024:2048], "q1")
        P.w(SY, "q1"); P.w(V, "q1"); P.w(S, "q1")
    P.barrier()

    layfm = big1[:, 0:16384]
    P.d(G, lngrow[:, :], ap["ln_g"][None, :].broadcast_to([128, D]), "q2")
    P.d(G, lnbrow[:, :], ap["ln_b"][None, :].broadcast_to([128, D]), "q2")
    for k in range(8):
        P.d(SY, layfm[:, ts_(k, T)], ap["s_lay"][:, ts_(k, 128)], "q0", transpose=True)
    for k in range(8):
        P.d(SY, wmov[:, k * 512:(k + 1) * 512], ap["s_wo"][0:512, ts_(k, 128)], "q0", transpose=True)
        P.d(SY, wmov2[:, k * 512:(k + 1) * 512], ap["s_wo"][512:1024, ts_(k, 128)], "q0", transpose=True)
    P.barrier()
    P.wall(TE)
    for i in range(NT):
        for mc in range(2):
            ps = PS[mc]
            wv = wmov if mc == 0 else wmov2
            for k in range(8):
                st = layfm[:, off_(k * T, i, 128)]
                P.r(TE, lambda e, ps=ps, st=st, wv=wv, k=k:
                    e.matmul(ps[:, 0:512], st, wv[:, ts_(k, 512)], start=(k == 0), stop=(k == 7)),
                    inc=("t" if k == 7 else None))
        if i == 1:
            P.barrier()
        P.d(SY, c_a[:, 0:D], ap["s_xn1"][ts_(i, 128), :], "q0")
        P.d(SY, w_b[:, 0:D], ap["s_x1"][ts_(i, 128), :], "q0")
        P.w(V, "t"); P.wall(V)
        P.r(V, lambda e: e.tensor_tensor(w_a[:, 0:512], PS[0][:, 0:512], c_a[:, 0:512], OP.add))
        P.r(V, lambda e: e.tensor_tensor(w_a[:, 512:1024], PS[1][:, 0:512], c_a[:, 512:1024], OP.add), inc="v")
        P.w(TE, "v")
        mu = p_a[:, 0:1]; vv = p_a[:, 1:2]; rr = p_a[:, 2:3]
        P.r(V, lambda e, mu=mu: e.reduce_sum(mu, w_a[:, 0:D], AX.X))
        P.r(V, lambda e, mu=mu: e.tensor_scalar_mul(mu, mu, 1.0 / D), inc="v")
        P.w(S, "v")
        P.r(S, lambda e: e.activation(out=c_b[:, 0:D], in_=w_a[:, 0:D], func=AF.Square), inc="sc")
        P.w(V, "sc")
        P.r(V, lambda e, vv=vv: e.reduce_sum(vv, c_b[:, 0:D], AX.X))
        P.r(V, lambda e, vv=vv: e.tensor_scalar_mul(vv, vv, 1.0 / D))
        P.r(V, lambda e, mu=mu, rr=rr: e.tensor_tensor(rr, mu, mu, OP.mult))
        P.r(V, lambda e, vv=vv, rr=rr: e.tensor_tensor(rr, vv, rr, OP.subtract))
        P.r(V, lambda e, rr=rr: e.tensor_scalar_add(rr, rr, 1e-5), inc="v")
        P.w(S, "v")
        P.r(S, lambda e, rr=rr: e.activation(out=rr, in_=rr, func=AF.Sqrt), inc="sc")
        P.w(V, "sc")
        P.r(V, lambda e, rr=rr: e.reciprocal(rr, rr))
        P.r(V, lambda e, mu=mu, rr=rr: e.tensor_scalar(w_a[:, 0:D], w_a[:, 0:D], mu, rr, OP.subtract, OP.mult))
        P.r(V, lambda e: e.tensor_tensor(w_a[:, 0:D], w_a[:, 0:D], lngrow[:, :], OP.mult))
        P.r(V, lambda e: e.tensor_tensor(w_a[:, 0:D], w_a[:, 0:D], lnbrow[:, :], OP.add), inc="v")
        P.w(SY, "v")
        P.d(SY, c_b[:, 0:D], ap["s_g1"][ts_(i, 128), :], "q0")
        P.wall(V)
        P.r(V, lambda e: e.tensor_tensor(w_a[:, 0:D], w_a[:, 0:D], c_b[:, 0:D], OP.mult), inc="v")
        P.r(V, lambda e: e.tensor_tensor(w_a[:, 0:D], w_a[:, 0:D], w_b[:, 0:D], OP.add), inc="v")
        P.w(S, "v")
        b = i // 8; j = i % 8
        P.d(S, ap["out"][b, ts_(j, 128), :], w_a[:, 0:D], "q1")
        P.w(SY, "q1"); P.w(V, "q1"); P.w(S, "q1"); P.w(G, "q1")
        if i == 0:
            P.barrier()
    P.wall(SY); P.wall(G); P.wall(V)

    import os as _os
    _cut = int(_os.environ.get("MK_CUT", "-1"))
    if _cut >= 0 and hasattr(P, "marks") and _cut < len(P.marks):
        for e in ENGINES:
            P.stream[e] = P.stream[e][:P.marks[_cut][e]]
    blkc = nc.Block(); blk = blkc.__enter__()

    def emit(eng_name):
        def _f(e):
            for fn in P.stream[eng_name]:
                fn(e)
        return _f
    blk.sync(emit("sync"))
    blk.vector(emit("vector"))
    blk.scalar(emit("scalar"))
    blk.gpsimd(emit("gpsimd"))
    blk.tensor(emit("tensor"))
    blkc.__exit__(None, None, None)
    for t in reversed(ctx):
        t.__exit__(None, None, None)
    return nc


_NC_CACHE = None


def kernel(L0_attn_x, L0_attn_W_in, L0_attn_conv_w, L0_attn_conv_b, L0_attn_W_xproj,
           L0_attn_A_log, L0_attn_D_param, L0_attn_W_out, L0_ffn_w_gate, L0_ffn_w_up,
           L0_ffn_w_down, L1_attn_W_qkvb, L1_attn_W_o, L1_attn_ln_g, L1_attn_ln_b,
           L1_ffn_w_gate, L1_ffn_w_up, L1_ffn_w_down):
    global _NC_CACHE
    if _NC_CACHE is None:
        _NC_CACHE = build_program()
    nc_ = _NC_CACHE
    x = np.ascontiguousarray(L0_attn_x, dtype=np.float32)
    Wq = np.ascontiguousarray(L1_attn_W_qkvb, dtype=np.float32).reshape(16, 193, 1024)
    Wq_perm = np.concatenate([
        Wq[:, 0:64].reshape(1024, 1024),
        Wq[:, 64:128].reshape(1024, 1024),
        Wq[:, 128:192].reshape(1024, 1024),
        Wq[:, 192:193].reshape(16, 1024),
        np.zeros((16, 1024), np.float32)], axis=0)
    maskc = (np.arange(128)[:, None] % TC <= np.arange(TC)[None, :]).astype(ml_dtypes.bfloat16)
    identc = np.eye(128, dtype=ml_dtypes.bfloat16)
    in_maps = []
    for c in range(NCORES):
        lo = c * LLOC
        halo = x[:, lo - 3:lo].copy() if c > 0 else np.zeros((B, 3, D), np.float32)
        rm = (np.arange(1, NCORES + 1) <= c).astype(np.float32)
        rm = np.zeros(NCORES, np.float32)
        rm[1:] = (np.arange(0, NCORES - 1) < c).astype(np.float32)
        in_maps.append({
            "x": x[:, lo:lo + LLOC].copy(),
            "xh": halo,
            "W_in": L0_attn_W_in, "conv_w": L0_attn_conv_w, "conv_b": L0_attn_conv_b,
            "W_xproj": L0_attn_W_xproj, "A_log": L0_attn_A_log, "D_param": L0_attn_D_param,
            "W_out": L0_attn_W_out,
            "f0_wg": L0_ffn_w_gate, "f0_wu": L0_ffn_w_up, "f0_wd": L0_ffn_w_down,
            "Wqkvb": Wq_perm, "W_o": L1_attn_W_o, "ln_g": L1_attn_ln_g, "ln_b": L1_attn_ln_b,
            "f1_wg": L1_ffn_w_gate, "f1_wu": L1_ffn_w_up, "f1_wd": L1_ffn_w_down,
            "mask": maskc, "ident": identc, "identf": identc.astype(np.float32), "rmask": rm,
        })
    res = run_bass_kernel_spmd(nc_, in_maps, core_ids=list(range(NCORES)))
    out = np.concatenate([np.asarray(r["out"], np.float32) for r in res.results], axis=1)
    return out




# version 07
# speedup vs baseline: 1.1024x; 1.0088x over previous
import numpy as np
import ml_dtypes
import concourse.bass as bass
import concourse.mybir as mybir
from concourse.bass_utils import run_bass_kernel_spmd

F32 = mybir.dt.float32
BF16 = mybir.dt.bfloat16
I16 = mybir.dt.int16
OP = mybir.AluOpType
AF = mybir.ActivationFunctionType
AX = mybir.AxisListType

B = 2
LLOC = 1024
T = 2048
NT = 16
D = 1024
DI = 2048
TC = 32
NCH = 32
NG = 8
NCORES = 8

ENGINES = ("sync", "vector", "scalar", "gpsimd", "tensor")
SEMS = {}
V, S, G, TE, SY = "vector", "scalar", "gpsimd", "tensor", "sync"


def ts_(i, n=128):
    return slice(i * n, (i + 1) * n)


def off_(base, i, n=128):
    return slice(base + i * n, base + (i + 1) * n)


class Prog:
    NW = 6

    def __init__(self):
        self.stream = {e: [] for e in ENGINES}
        self.cnt = {s: 0 for s in ["v", "sc", "g", "t", "cc", "bs"]}
        self.qs = {}
        for q in ("q0", "q1", "q2"):
            self.qs[q] = {"idx": 0, "tot": [0] * self.NW, "open": False}

    def _qsem(self, q, i):
        return q + chr(ord("a") + i)

    def w(self, eng, sem, val=None):
        if sem in self.qs:
            st = self.qs[sem]
            for i in range(self.NW):
                if st["tot"][i] > 0:
                    self.stream[eng].append(
                        lambda e, s=self._qsem(sem, i), vv=st["tot"][i]: e.wait_ge(SEMS[s], vv))
            st["open"] = False
            return
        v = self.cnt[sem] if val is None else val
        if v > 0:
            self.stream[eng].append(lambda e, s=sem, vv=v: e.wait_ge(SEMS[s], vv))

    OWN = {"vector": "v", "scalar": "sc", "gpsimd": "g", "tensor": "t"}

    def r(self, eng, fn, inc=None, k=1):
        own = self.OWN.get(eng)
        if own is not None:
            v = self.cnt[own]
            if v > 0:
                self.stream[eng].append(lambda e, s=own, vv=v: e.wait_ge(SEMS[s], vv))
            if inc is None or inc == own:
                self.cnt[own] += 1
                self.stream[eng].append(lambda e, f=fn, s=own: f(e).then_inc(SEMS[s], 1))
            else:
                self.cnt[inc] += k
                self.stream[eng].append(lambda e, f=fn, s=inc, kk=k: f(e).then_inc(SEMS[s], kk))
        else:
            if inc is not None:
                self.cnt[inc] += k
                self.stream[eng].append(lambda e, f=fn, s=inc, kk=k: f(e).then_inc(SEMS[s], kk))
            else:
                self.stream[eng].append(lambda e, f=fn: f(e)) 

    def d(self, eng, out, in_, q, transpose=False, slow=False):
        st = self.qs[q]
        if not st["open"]:
            st["idx"] = (st["idx"] + 1) % self.NW
            i = st["idx"]
            if st["tot"][i] > 0:
                self.stream[eng].append(
                    lambda e, s=self._qsem(q, i), vv=st["tot"][i]: e.wait_ge(SEMS[s], vv))
            st["open"] = True
        i = st["idx"]
        st["tot"][i] += 16
        sname = self._qsem(q, i)
        if transpose and not slow:
            self.stream[eng].append(
                lambda e, o=out, i_=in_, s=sname: e.dma_start_transpose(o, i_).then_inc(SEMS[s], 16))
        elif transpose and slow:
            def _f(e, o=out, i_=in_, s=sname):
                with e.bass.allow_non_contiguous_dma(reason="tiny transpose"):
                    return e.dma_start_transpose(o, i_).then_inc(SEMS[s], 16)
            self.stream[eng].append(_f)
        else:
            self.stream[eng].append(
                lambda e, o=out, i_=in_, s=sname: e.dma_start(out=o, in_=i_, allow_slow_non_contiguous=slow).then_inc(SEMS[s], 16))

    def snap(self):
        return dict(self.cnt)

    def wall(self, eng, snap=None):
        for q in ("q0", "q1", "q2"):
            self.w(eng, q)
        sn = self.cnt if snap is None else snap
        for s, v in sn.items():
            if v > 0:
                self.w(eng, s, v)

    def barrier(self):
        m = {}
        for eng, sem in [("sync", "bs"), ("vector", "v"), ("scalar", "sc"),
                         ("gpsimd", "g"), ("tensor", "t")]:
            self.cnt[sem] += 1
            m[sem] = self.cnt[sem]
            self.stream[eng].append(lambda e, s=sem: e.sem_inc(SEMS[s], 1))
        for eng in ENGINES:
            for s, v in m.items():
                if s != {"sync": "bs", "vector": "v", "scalar": "sc",
                         "gpsimd": "g", "tensor": "t"}[eng] or True:
                    self.w(eng, s, v)
            for q in ("q0", "q1", "q2"):
                self.w(eng, q)
        if not hasattr(self, "marks"):
            self.marks = []
        self.marks.append({e: len(self.stream[e]) for e in ENGINES})


def build_program():
    nc = bass.Bass("TRN2", target_bir_lowering=False)
    P = Prog()
    ap = {}

    def IN(name, shape, dt=F32):
        ap[name] = nc.dram_tensor(name, list(shape), dt, kind="ExternalInput").ap()

    def OUT(name, shape, dt=F32):
        ap[name] = nc.dram_tensor(name, list(shape), dt, kind="ExternalOutput").ap()

    import os as _os2
    _dbg = set((_os2.environ.get("MK_DBG", "")).split(",")) - {""}

    def INT(name, shape, dt=F32, **kw):
        kind = "ExternalOutput" if name in _dbg else "Internal"
        if kw.get("addr_space"):
            kind = "Internal"
        ap[name] = nc.dram_tensor(name, list(shape), dt, kind=kind, **kw).ap()

    IN("x", (B, LLOC, D)); IN("xh", (B, 3, D))
    IN("W_in", (2 * DI, D)); IN("conv_w", (DI, 4)); IN("conv_b", (DI,))
    IN("W_xproj", (33, DI)); IN("A_log", (16,)); IN("D_param", (DI,))
    IN("W_out", (D, DI))
    IN("f0_wg", (4096, D)); IN("f0_wu", (4096, D)); IN("f0_wd", (D, 4096))
    IN("Wqkvb", (3104, D))
    IN("W_o", (D, D)); IN("ln_g", (D,)); IN("ln_b", (D,))
    IN("f1_wg", (2816, D)); IN("f1_wu", (2816, D)); IN("f1_wd", (D, 2816))
    IN("mask", (128, TC), BF16)
    IN("ident", (128, 128), BF16)
    IN("identf", (128, 128))
    IN("rmask", (NCORES,))
    OUT("out", (B, LLOC, D))

    INT("s_win", (2 * DI, D), BF16)
    INT("s_wxp", (48, DI), BF16)
    INT("s_wout", (D, DI), BF16)
    INT("s_f0wg", (4096, D), BF16); INT("s_f0wu", (4096, D), BF16); INT("s_f0wd", (D, 4096), BF16)
    INT("s_qkvb", (3104, D), BF16); INT("s_wo", (D, D), BF16)
    INT("s_f1wg", (2816, D), BF16); INT("s_f1wu", (2816, D), BF16); INT("s_f1wd", (D, 2816), BF16)
    INT("s_xn", (T, D), BF16)
    INT("s_xq", (T, D), BF16)
    INT("s_xcf", (DI, T), BF16)
    INT("s_sz", (T, DI), BF16)
    INT("s_y", (T, DI), BF16)
    INT("s_yg", (T, DI), BF16)
    INT("s_h2", (T, 4096), F32)
    INT("s_xq2", (T, 4096), BF16)
    INT("s_x1", (T, D), F32)
    INT("s_xn1", (T, D), BF16)
    INT("s_xq1", (T, D), BF16)
    INT("s_qkv", (T, 3104), BF16)
    INT("s_xq3", (T, 2816), BF16)
    INT("s_lay", (T, D), BF16)
    INT("s_s", (T, D), BF16)
    INT("s_g", (T, D), BF16)
    INT("s_g1", (T, D), BF16)
    INT("s_fb", (16, B * LLOC), BF16)
    INT("s_dbg", (128, B * LLOC), F32)
    INT("s_dbg2", (128, 64), F32)
    INT("s_dbg3", (128, 2560), BF16)
    INT("cc_in", (32, DI + 1), F32)
    INT("cc_out", (NCORES, 32, DI + 1), F32, addr_space="Shared")

    ctx = []
    SB = {}

    def sb(name, p, f, dt=F32):
        t = nc.sbuf_tensor(name, [p, f], dt)
        h = t.__enter__(); ctx.append(t); SB[name] = h
        return h

    msk = sb("msk", 128, TC, BF16)
    idn = sb("idn", 128, 128, BF16)
    idnf = sb("idnf", 128, 128)
    rmk = sb("rmk", 32, NCORES)
    Acol = sb("Acol", 16, 1)
    cbw = sb("cbw", 128, 16 * 4)
    cbb = sb("cbb", 128, 16)
    onec = sb("onec", 128, 2)
    sxinv = sb("sxinv", 128, NT)
    sx2i = sb("sx2i", 128, NT)
    sxg = sb("sxg", 128, NT); sxu = sb("sxu", 128, NT); sxd = sb("sxd", 128, NT)
    wsc = sb("wsc", 4, 8)
    mgr = sb("mgr", 128, 4)
    p_a = sb("p_a", 128, 64)
    p_b = sb("p_b", 128, 128)
    w_a = sb("w_a", 128, 4096)
    w_b = sb("w_b", 128, 4096)
    c_a = sb("c_a", 128, 4096, BF16)
    c_b = sb("c_b", 128, 4096, BF16)
    i_a = sb("i_a", 128, 4096, I16)
    xhn = sb("xhn", 8, D, BF16)
    xhh = sb("xhh", 8, D)
    xhf = sb("xhf", 128, 8 * 8, BF16)
    A16 = sb("A16", 128, 49152, BF16)
    A32 = sb("A32", 128, 11264)

    def a16(off, n):
        return A16[:, off:off + n]

    def a32(off, n):
        return A32[:, off:off + n]

    dprow = A16[:, 47104:49152]
    lngrow = A16[:, 16384:17408]
    lnbrow = A16[:, 17408:18432]

    xnfm = a16(0, 16384)
    xpfc = a16(16384, 8256)
    xcfc = a16(24640, 8192)
    winS = a16(32832, 2048)
    zwv = a16(34880, 4096)
    wxpT = a16(41024, 768)
    tails = a16(41856, 64)
    c_c = a16(43008, 4096)
    fC = a16(0, 2048); fB = a16(2048, 2048); fCc = a16(4096, 2048)
    btT = a16(6144, 2048)
    gsb = a16(7168, 16384)
    hstg = a16(23552, 2048)
    hinb = a16(25600, 2048)
    hinb2 = a16(6144, 2048)
    xtm = a16(27648, 4096)
    big1 = A16[:, 0:32768]
    bufA = A16[:, 32768:49152]
    wmov = bufA[:, 0:8192]
    wmov2 = bufA[:, 8192:16384]
    fL = A32[:, 0:2048]; fsc = A32[:, 2048:4096]; f0z = A32[:, 4096:5120]
    xdb = A32[:, 5120:7168]
    hcurb = [A32[:, 7168:9216], A32[:, 9216:11264]]; hin = A32[:, 0:2048]
    xdbB = A32[:, 5120:7168]
    xdbC = A32[:, 7168:9216]
    xdbV = A32[:, 9216:11264]
    fDst = p_b[:, 0:64]; fDch = p_b[:, 64:128]
    fzv = w_a[:, 2048:4096]

    PSG_t = nc.psum_tensor("psg", [128, 2048], F32)
    PSG = PSG_t.__enter__(); ctx.append(PSG_t)
    PSY_t = nc.psum_tensor("psy", [128, 2048], F32)
    PSY = PSY_t.__enter__(); ctx.append(PSY_t)
    PS = [PSG[:, 0:512], PSG[:, 512:1024], PSG[:, 1024:1536], PSG[:, 1536:2048],
          PSY[:, 0:512], PSY[:, 512:1024], PSY[:, 1024:1536], PSY[:, 1536:2048]]

    semnames = ["v", "sc", "g", "t", "cc", "bs"]
    for q in ("q0", "q1", "q2"):
        for i in range(Prog.NW):
            semnames.append(q + chr(ord("a") + i))
    for s in semnames:
        h = nc.semaphore(s)
        SEMS[s] = h.__enter__(); ctx.append(h)

    P.d(SY, msk[:, :], ap["mask"][:, :], "q0")
    P.d(SY, idn[:, :], ap["ident"][:, :], "q0")
    P.d(SY, idnf[:, :], ap["identf"][:, :], "q0")
    P.d(SY, rmk[:, :], ap["rmask"][None, :].broadcast_to([32, NCORES]), "q0")
    P.d(G, dprow[:, :], ap["D_param"][None, :].broadcast_to([128, DI]), "q2")
    P.d(SY, Acol[:, 0:1], ap["A_log"][:, None], "q0")
    P.d(SY, cbw[:, :].rearrange("p (k j) -> p k j", j=4),
        ap["conv_w"].rearrange("(k p) j -> p k j", p=128), "q0", slow=True)
    P.d(SY, cbb[:, :], ap["conv_b"].rearrange("(k p) -> p k", p=128), "q0", slow=True)
    P.wall(V); P.wall(S)
    P.r(V, lambda e: e.memset(onec[:, :], 1.0), inc="v")
    P.r(S, lambda e: e.activation(out=Acol[0:16, 0:1], in_=Acol[0:16, 0:1], func=AF.Exp), inc="sc")
    P.w(V, "sc")
    P.r(V, lambda e: e.tensor_scalar_mul(Acol[0:16, 0:1], Acol[0:16, 0:1], -1.0), inc="v")

    xr = ap["x"].rearrange("b (i p) d -> (b i) p d", p=128)
    scmark = {}
    for i in range(NT):
        o = (i % 2) * D
        sq = "q1" if i % 2 == 0 else "q2"
        pb = 4 * (i % 2)
        if i >= 2:
            P.w(SY, "sc", scmark[i - 2])
        else:
            P.w(SY, "v"); P.w(SY, "sc")
        P.d(SY, w_a[:, o:o + D], xr[i], "q0")
        mu = p_a[:, pb:pb + 1]; vv = p_a[:, pb + 1:pb + 2]; rr = p_a[:, pb + 2:pb + 3]; rmx = p_a[:, pb + 3:pb + 4]
        P.w(V, "q0"); P.w(V, sq)
        P.r(V, lambda e, mu=mu, o=o: e.reduce_sum(mu, w_a[:, o:o + D], AX.X))
        P.r(V, lambda e, mu=mu: e.tensor_scalar_mul(mu, mu, 1.0 / D), inc="v")
        P.w(S, "v")
        P.r(S, lambda e, o=o: e.activation(out=w_b[:, o:o + D], in_=w_a[:, o:o + D], func=AF.Square), inc="sc")
        P.w(V, "sc")
        P.r(V, lambda e, vv=vv, o=o: e.reduce_sum(vv, w_b[:, o:o + D], AX.X))
        P.r(V, lambda e, vv=vv: e.tensor_scalar_mul(vv, vv, 1.0 / D))
        P.r(V, lambda e, mu=mu, rr=rr: e.tensor_tensor(rr, mu, mu, OP.mult))
        P.r(V, lambda e, vv=vv, rr=rr: e.tensor_tensor(rr, vv, rr, OP.subtract))
        P.r(V, lambda e, rr=rr: e.tensor_scalar_add(rr, rr, 1e-5), inc="v")
        P.w(S, "v")
        P.r(S, lambda e, rr=rr: e.activation(out=rr, in_=rr, func=AF.Sqrt), inc="sc")
        P.w(V, "sc")
        P.r(V, lambda e, rr=rr: e.reciprocal(rr, rr))
        P.r(V, lambda e, mu=mu, rr=rr, o=o: e.tensor_scalar(w_a[:, o:o + D], w_a[:, o:o + D], mu, rr, OP.subtract, OP.mult))
        P.r(V, lambda e, o=o: e.tensor_copy(c_a[:, o:o + D], w_a[:, o:o + D]))
        P.r(V, lambda e, rmx=rmx, o=o: e.reduce_max(rmx, w_a[:, o:o + D], AX.X, apply_absolute_value=True))
        P.r(V, lambda e, rmx=rmx: e.tensor_scalar_max(rmx, rmx, 1e-5))
        P.r(V, lambda e, i=i, rmx=rmx: e.tensor_scalar_mul(sxinv[:, i:i + 1], rmx, 1.0 / 127.0))
        P.r(V, lambda e, i=i, rr=rr: e.reciprocal(rr, sxinv[:, i:i + 1]), inc="v")
        P.w(S, "v")
        P.r(S, lambda e, rr=rr, o=o: e.activation(out=w_b[:, o:o + D], in_=w_a[:, o:o + D], func=AF.Identity, scale=rr, bias=0.0), inc="sc")
        scmark[i] = P.cnt["sc"]
        P.w(V, "sc")
        P.r(V, lambda e, o=o: e.tensor_copy(i_a[:, o:o + D], w_b[:, o:o + D]))
        P.r(V, lambda e, o=o: e.tensor_copy(c_b[:, o:o + D], i_a[:, o:o + D]), inc="v")
        P.w(S, "v")
        P.d(S, ap["s_xn"][ts_(i, 128), :], c_a[:, o:o + D], sq)
        P.d(S, ap["s_xq"][ts_(i, 128), :], c_b[:, o:o + D], sq)
    P.w(SY, "q1"); P.w(V, "q1"); P.w(S, "q1")
    P.w(SY, "q2"); P.w(V, "q2"); P.w(S, "q2")

    P.r(V, lambda e: e.memset(xhh[:, :], 0.0), inc="v")
    P.wall(SY)
    P.d(SY, xhh[0:3, :], ap["xh"][0], "q0")
    P.d(SY, xhh[3:6, :], ap["xh"][1], "q0")
    P.wall(V); P.wall(S)
    hmu = p_a[0:8, 4:5]; hvv = p_a[0:8, 5:6]; hm2 = p_a[0:8, 6:7]
    P.r(V, lambda e: e.reduce_sum(hmu, xhh[:, :], AX.X))
    P.r(V, lambda e: e.tensor_scalar_mul(hmu, hmu, 1.0 / D), inc="v")
    P.w(S, "v")
    P.r(S, lambda e: e.activation(out=w_b[0:8, 0:D], in_=xhh[:, :], func=AF.Square), inc="sc")
    P.w(V, "sc")
    P.r(V, lambda e: e.reduce_sum(hvv, w_b[0:8, 0:D], AX.X))
    P.r(V, lambda e: e.tensor_scalar_mul(hvv, hvv, 1.0 / D))
    P.r(V, lambda e: e.tensor_tensor(hm2, hmu, hmu, OP.mult))
    P.r(V, lambda e: e.tensor_tensor(hvv, hvv, hm2, OP.subtract))
    P.r(V, lambda e: e.tensor_scalar_add(hvv, hvv, 1e-5), inc="v")
    P.w(S, "v")
    P.r(S, lambda e: e.activation(out=hvv, in_=hvv, func=AF.Sqrt), inc="sc")
    P.w(V, "sc")
    P.r(V, lambda e: e.reciprocal(hvv, hvv))
    P.r(V, lambda e: e.tensor_scalar(xhh[:, :], xhh[:, :], hmu, hvv, OP.subtract, OP.mult))
    P.r(V, lambda e: e.tensor_copy(xhn[:, :], xhh[:, :]), inc="v")
    P.wall(TE)
    for k in range(8):
        P.r(TE, lambda e, k=k: e.transpose(PSY[0:128, k * 8:k * 8 + 8], xhh[0:8, ts_(k, 128)], idnf[0:8, 0:8]), inc="t")
    P.w(V, "t")
    P.r(V, lambda e: e.tensor_copy(xhf[:, 0:64], PSY[:, 0:64]), inc="v")
    P.w(TE, "v")

    def cast_rows(src_ap, dst_ap, nrows, ncols):
        nt = (nrows + 127) // 128
        vmark = {}
        for r in range(nt):
            p = min(128, nrows - r * 128)
            wsrc = w_a if r % 2 == 0 else w_b
            cdst = c_a if r % 2 == 0 else c_b
            sq = "q1" if r % 2 == 0 else "q2"
            if r >= 2:
                P.w(SY, "v", vmark[r - 2])
            else:
                P.w(SY, "v"); P.w(SY, "sc")
            P.d(SY, wsrc[0:p, 0:ncols], src_ap[r * 128:r * 128 + p, :], "q0")
            P.w(V, "q0")
            if r < 2:
                P.w(V, "q1")
            P.w(V, sq)
            P.r(V, lambda e, p=p, n=ncols, a=wsrc, b=cdst: e.tensor_copy(b[0:p, 0:n], a[0:p, 0:n]), inc="v")
            vmark[r] = P.cnt["v"]
            P.w(S, "v", vmark[r])
            P.d(S, dst_ap[r * 128:r * 128 + p, :], cdst[0:p, 0:ncols], sq)
        P.w(SY, "q1"); P.w(V, "q1"); P.w(S, "q1")
        P.w(SY, "q2"); P.w(V, "q2"); P.w(S, "q2")
    cast_rows(ap["W_in"], ap["s_win"], 2 * DI, D)
    cast_rows(ap["W_xproj"], ap["s_wxp"], 33, DI)
    P.r(V, lambda e: e.memset(c_a[0:15, 0:DI], 0.0), inc="v")
    P.w(S, "v")
    P.d(S, ap["s_wxp"][33:48, :], c_a[0:15, 0:DI], "q1")
    P.w(SY, "q1"); P.w(V, "q1"); P.w(S, "q1")
    cast_rows(ap["W_out"], ap["s_wout"], D, DI)
    cast_rows(ap["Wqkvb"], ap["s_qkvb"], 3104, D)
    cast_rows(ap["W_o"], ap["s_wo"], D, D)

    def f0_quant(src, dst, nrows, ncols, scidx):
        nt = nrows // 128
        P.r(V, lambda e: e.memset(p_a[:, 8:9], 0.0), inc="v")
        amark = {}
        for r in range(nt):
            wsrc = w_a if r % 2 == 0 else w_b
            if r >= 2:
                P.w(SY, "v", amark[r - 2])
            else:
                P.w(SY, "v"); P.w(SY, "sc")
            P.d(SY, wsrc[:, 0:ncols], src[ts_(r, 128), :], "q0")
            P.w(V, "q0")
            P.r(V, lambda e, n=ncols, a=wsrc: e.reduce_sum(p_a[:, 9:10], a[:, 0:n], AX.X, apply_absolute_value=True))
            P.r(V, lambda e: e.tensor_tensor(p_a[:, 8:9], p_a[:, 8:9], p_a[:, 9:10], OP.add), inc="v")
            amark[r] = P.cnt["v"]
        P.wall(TE)
        P.r(TE, lambda e: e.matmul(PS[7][0:1, 0:1], p_a[:, 8:9], onec[:, 0:1], start=True, stop=True), inc="t")
        P.w(V, "t")
        P.r(V, lambda e, n=nrows * ncols: e.tensor_scalar(wsc[0:1, 0:1], PS[7][0:1, 0:1], 1.0 / n, 1e-5, OP.mult, OP.max))
        P.r(V, lambda e: e.tensor_scalar_mul(wsc[0:1, 1:2], wsc[0:1, 0:1], 0.5), inc="v")
        P.r(V, lambda e: e.memset(w_b[0:1, 0:128], 1.0), inc="v")
        P.w(TE, "v")
        P.r(TE, lambda e: e.matmul(PS[7][:, 4:5], w_b[0:1, 0:128], wsc[0:1, 1:2], start=True, stop=True), inc="t")
        P.r(TE, lambda e, scidx=scidx: e.matmul(PS[7][:, 8:9], w_b[0:1, 0:128], wsc[0:1, 0:1], start=True, stop=True), inc="t")
        P.w(V, "t")
        P.r(V, lambda e: e.tensor_copy(p_a[:, 10:11], PS[7][:, 4:5]))
        P.r(V, lambda e: e.tensor_scalar_mul(p_a[:, 11:12], p_a[:, 10:11], -1.0))
        P.r(V, lambda e, scidx=scidx: e.tensor_copy(mgr[:, scidx:scidx + 1], PS[7][:, 8:9]), inc="v")
        vmark = {}
        gq0 = A16[:, 7168:11264]; gq1 = A16[:, 11264:15360]
        for r in range(nt):
            wsrc = w_a if r % 2 == 0 else w_b
            ga = c_a if r % 2 == 0 else gq0
            gb = c_b if r % 2 == 0 else gq1
            sq = "q1" if r % 2 == 0 else "q2"
            if r >= 2:
                P.w(SY, "v", vmark[r - 2])
            else:
                P.w(SY, "v")
            P.d(SY, wsrc[:, 0:ncols], src[ts_(r, 128), :], "q0")
            P.w(V, "q0"); P.w(V, sq)
            P.r(V, lambda e, n=ncols, a=wsrc, g=ga: e.tensor_scalar(g[:, 0:n], a[:, 0:n], p_a[:, 10:11], None, OP.is_gt))
            P.r(V, lambda e, n=ncols, a=wsrc, g=gb: e.tensor_scalar(g[:, 0:n], a[:, 0:n], p_a[:, 11:12], None, OP.is_lt))
            P.r(V, lambda e, n=ncols, g=ga, h=gb: e.tensor_tensor(g[:, 0:n], g[:, 0:n], h[:, 0:n], OP.subtract), inc="v")
            vmark[r] = P.cnt["v"]
            P.w(S, "v", vmark[r])
            P.d(S, dst[ts_(r, 128), :], ga[:, 0:ncols], sq)
        P.w(SY, "q1"); P.w(V, "q1"); P.w(S, "q1")
        P.w(SY, "q2"); P.w(V, "q2"); P.w(S, "q2")
    f0_quant(ap["f0_wg"], ap["s_f0wg"], 4096, D, 0)
    f0_quant(ap["f0_wu"], ap["s_f0wu"], 4096, D, 1)
    f0_quant(ap["f0_wd"], ap["s_f0wd"], D, 4096, 2)

    def fp4_on_tile(p, ncols):
        nb = ncols // 16
        a3 = w_a[0:p, 0:ncols].rearrange("p (n s) -> p n s", s=16)
        bm = w_b[0:p, 3500:3500 + nb]
        rs = w_b[0:p, 3200:3200 + nb]
        P.r(V, lambda e, a3=a3, bm=bm: e.tensor_reduce(bm, a3, AX.X, OP.max, apply_absolute_value=True))
        P.r(V, lambda e, bm=bm: e.tensor_scalar(bm, bm, 1.0 / 6.0, 1e-12, OP.mult, OP.max))
        P.r(V, lambda e, rs=rs, bm=bm: e.reciprocal(rs, bm), inc="v")
        rsb = rs[:, :, None].broadcast_to([p, nb, 16])
        P.r(V, lambda e, a3=a3, rsb=rsb: e.tensor_tensor(a3, a3, rsb, OP.mult), inc="v")
        P.w(S, "v")
        P.r(S, lambda e, p=p, n=ncols: e.activation(out=w_b[0:p, 0:n], in_=w_a[0:p, 0:n], func=AF.Abs, scale=2.0), inc="sc")
        P.w(V, "sc")
        P.r(V, lambda e, p=p, n=ncols: e.tensor_copy(i_a[0:p, 0:n], w_b[0:p, 0:n]))
        P.r(V, lambda e, p=p, n=ncols: e.tensor_copy(c_a[0:p, 0:n], i_a[0:p, 0:n]))
        P.r(V, lambda e, p=p, n=ncols: e.tensor_scalar(c_a[0:p, 0:n], c_a[0:p, 0:n], 4.0, 0.5, OP.min, OP.mult), inc="v")
        for th in (5.0, 7.0):
            P.r(V, lambda e, p=p, n=ncols, th=th: e.tensor_scalar(c_b[0:p, 0:n], w_b[0:p, 0:n], th, None, OP.is_gt))
            P.r(V, lambda e, p=p, n=ncols: e.tensor_tensor(c_a[0:p, 0:n], c_a[0:p, 0:n], c_b[0:p, 0:n], OP.add))
        P.r(V, lambda e, p=p, n=ncols: e.tensor_scalar(c_b[0:p, 0:n], w_b[0:p, 0:n], 10.0, 2.0, OP.is_gt, OP.mult))
        P.r(V, lambda e, p=p, n=ncols: e.tensor_tensor(c_a[0:p, 0:n], c_a[0:p, 0:n], c_b[0:p, 0:n], OP.add), inc="v")
        P.w(S, "v")
        P.r(S, lambda e, p=p, n=ncols: e.activation(out=c_b[0:p, 0:n], in_=w_a[0:p, 0:n], func=AF.Sign), inc="sc")
        P.w(V, "sc")
        P.r(V, lambda e, p=p, n=ncols: e.tensor_tensor(c_a[0:p, 0:n], c_a[0:p, 0:n], c_b[0:p, 0:n], OP.mult))
        scb = bm[:, :, None].broadcast_to([p, nb, 16])
        ca3 = c_a[0:p, 0:ncols].rearrange("p (n s) -> p n s", s=16)
        P.r(V, lambda e, ca3=ca3, scb=scb: e.tensor_tensor(ca3, ca3, scb, OP.mult), inc="v")

    def f1_quant(src, dst, nrows, ncols):
        nt = (nrows + 127) // 128
        for r in range(nt):
            p = min(128, nrows - r * 128)
            if r >= 1:
                P.w(SY, "v")
            P.d(SY, w_a[0:p, 0:ncols], src[r * 128:r * 128 + p, :], "q0")
            P.w(V, "q0"); P.w(V, "q1")
            fp4_on_tile(p, ncols)
            P.w(S, "v")
            P.d(S, dst[r * 128:r * 128 + p, :], c_a[0:p, 0:ncols], "q1")
            P.w(SY, "q1"); P.w(V, "q1"); P.w(S, "q1")
    f1_quant(ap["f1_wg"], ap["s_f1wg"], 2816, D)
    f1_quant(ap["f1_wu"], ap["s_f1wu"], 2816, D)
    f1_quant(ap["f1_wd"], ap["s_f1wd"], D, 2816)

    P.r(V, lambda e: e.tensor_scalar(sxg[:, :], sxinv[:, :], mgr[:, 0:1], None, OP.mult))
    P.r(V, lambda e: e.tensor_scalar(sxu[:, :], sxinv[:, :], mgr[:, 1:2], None, OP.mult), inc="v")
    if "s_dbg2" in _dbg:
        P.w(S, "v")
        P.d(S, ap["s_dbg2"][:, 0:NT], sxg[:, :], "q1")
        P.d(S, ap["s_dbg2"][:, 16:16 + NT], sxu[:, :], "q1")
        P.d(S, ap["s_dbg2"][:, 32:32 + NT], sxinv[:, :], "q1")
        P.d(S, ap["s_dbg2"][:, 48:52], mgr[:, 0:4], "q1")
        P.w(SY, "q1"); P.w(V, "q1"); P.w(S, "q1")
    P.barrier()

    for k in range(8):
        P.d(SY, xnfm[:, ts_(k, T)], ap["s_xn"][:, ts_(k, 128)], "q0", transpose=True)
    for k in range(16):
        P.d(SY, wxpT[:, k * 48:k * 48 + 48], ap["s_wxp"][:, ts_(k, 128)], "q0", transpose=True)
    for b in range(B):
        for lc in range(2):
            t0 = b * LLOC + lc * 512
            for ot in range(16):
                wbuf = winS[:, (ot % 2) * 1024:(ot % 2) * 1024 + 1024]
                P.w(SY, "t")
                for k in range(8):
                    P.d(SY, wbuf[:, k * 128:(k + 1) * 128], ap["s_win"][ts_(ot, 128), ts_(k, 128)], "q0", transpose=True)
                P.wall(TE)
                for k in range(8):
                    st = wbuf[:, k * 128:(k + 1) * 128]
                    mv = xnfm[:, k * T + t0: k * T + t0 + 512]
                    P.r(TE, lambda e, st=st, mv=mv, k=k:
                        e.matmul(PS[0][:, 0:512], st, mv, start=(k == 0), stop=(k == 7)),
                        inc=("t" if k == 7 else None))
                if lc == 0:
                    for k in range(8):
                        st = wbuf[:, k * 128:(k + 1) * 128]
                        mv = xhf[:, k * 8 + b * 3: k * 8 + b * 3 + 3]
                        P.r(TE, lambda e, st=st, mv=mv, k=k:
                            e.matmul(PS[1][:, 0:3], st, mv, start=(k == 0), stop=(k == 7)),
                            inc=("t" if k == 7 else None))
                P.w(V, "t")
                col0 = ot * 516
                if lc == 0:
                    P.r(V, lambda e, col0=col0: e.tensor_copy(xpfc[:, col0:col0 + 3], PS[1][:, 0:3]))
                else:
                    P.r(V, lambda e, col0=col0, ot=ot: e.tensor_copy(xpfc[:, col0:col0 + 3], tails[:, ot * 4:ot * 4 + 3]))
                P.r(V, lambda e, col0=col0: e.tensor_copy(xpfc[:, col0 + 3:col0 + 515], PS[0][:, 0:512]))
                P.r(V, lambda e, ot=ot: e.tensor_copy(tails[:, ot * 4:ot * 4 + 3], PS[0][:, 509:512]), inc="v")
                P.w(TE, "v")
            for ot in range(16):
                col0 = ot * 516
                P.r(V, lambda e, col0=col0, ot=ot:
                    e.tensor_scalar(w_a[:, 0:512], xpfc[:, col0:col0 + 512], cbw[:, 4 * ot:4 * ot + 1], cbb[:, ot:ot + 1], OP.mult, OP.add))
                for j in (1, 2, 3):
                    P.r(V, lambda e, col0=col0, ot=ot, j=j:
                        e.scalar_tensor_tensor(w_a[:, 0:512], xpfc[:, col0 + j:col0 + j + 512], cbw[:, 4 * ot + j:4 * ot + j + 1], w_a[:, 0:512], OP.mult, OP.add), inc="v")
                P.w(S, "v")
                P.r(S, lambda e, ot=ot: e.activation(out=xcfc[:, ts_(ot, 512)], in_=w_a[:, 0:512], func=AF.Silu), inc="sc")
                P.w(V, "sc")
            P.wall(S)
            for ot in range(16):
                P.d(S, ap["s_xcf"][ts_(ot, 128), t0:t0 + 512], xcfc[:, ts_(ot, 512)], "q1")
            P.w(S, "q1"); P.w(V, "q1")
            P.wall(TE)
            for k in range(16):
                stB = wxpT[:, k * 48:k * 48 + 16]
                stC = wxpT[:, k * 48 + 16:k * 48 + 32]
                stV = wxpT[:, k * 48 + 32:k * 48 + 33]
                mv = xcfc[:, ts_(k, 512)]
                P.r(TE, lambda e, st=stB, mv=mv, k=k:
                    e.matmul(PSY[0:16, 0:512], st, mv, start=(k == 0), stop=(k == 15)))
                P.r(TE, lambda e, st=stC, mv=mv, k=k:
                    e.matmul(PSY[0:16, 512:1024], st, mv, start=(k == 0), stop=(k == 15)))
                P.r(TE, lambda e, st=stV, mv=mv, k=k:
                    e.matmul(PSY[0:1, 1024:1536], st, mv, start=(k == 0), stop=(k == 15)),
                    inc=("t" if k == 15 else None))
            P.w(V, "t")
            P.r(V, lambda e, t0=t0: e.tensor_copy(xdbB[0:16, t0:t0 + 512], PSY[0:16, 0:512]))
            P.r(V, lambda e, t0=t0: e.tensor_copy(xdbC[0:16, t0:t0 + 512], PSY[0:16, 512:1024]))
            P.r(V, lambda e, t0=t0: e.tensor_copy(xdbV[0:1, t0:t0 + 512], PSY[0:1, 1024:1536]), inc="v")
            P.w(TE, "v")
    for zc in range(4):
        P.w(SY, "t")
        for k in range(8):
            P.d(SY, zwv[:, k * 512:(k + 1) * 512], ap["s_win"][2048 + zc * 512:2048 + (zc + 1) * 512, ts_(k, 128)], "q0", transpose=True)
        P.wall(TE)
        for i in range(NT):
            ps = PS[4 + (i % 2)]
            for k in range(8):
                st = xnfm[:, off_(k * T, i, 128)]
                mv = zwv[:, k * 512:(k + 1) * 512]
                P.r(TE, lambda e, ps=ps, st=st, mv=mv, k=k:
                    e.matmul(ps[:, 0:512], st, mv, start=(k == 0), stop=(k == 7)),
                    inc=("t" if k == 7 else None))
            P.w(S, "t")
            P.w(S, "q1")
            P.r(S, lambda e, ps=ps: e.activation(out=c_a[:, 0:512], in_=ps[:, 0:512], func=AF.Silu), inc="sc")
            P.w(S, "sc")
            P.d(S, ap["s_sz"][ts_(i, 128), zc * 512:(zc + 1) * 512], c_a[:, 0:512], "q1")
            P.w(TE, "sc")
    P.barrier()

    P.r(S, lambda e: e.activation(out=fsc[0:1, :], in_=xdbV[0:1, 0:T], func=AF.Exp), inc="sc")
    P.w(S, "sc")
    P.r(S, lambda e: e.activation(out=fsc[0:1, :], in_=fsc[0:1, :], func=AF.Ln, bias=1.0), inc="sc")
    P.w(V, "sc")
    P.r(V, lambda e: e.tensor_scalar(fsc[0:1, :], fsc[0:1, :], 0.001, 0.1, OP.max, OP.min), inc="v")
    P.r(V, lambda e: e.memset(w_b[0:1, 100:116], 1.0), inc="v")
    P.w(TE, "v")
    for half in range(4):
        P.r(TE, lambda e, half=half: e.matmul(PS[0][0:16, 0:512], w_b[0:1, 100:116],
                                              fsc[0:1, half * 512:(half + 1) * 512], start=True, stop=True), inc="t")
        P.w(V, "t")
        P.r(V, lambda e, half=half: e.tensor_copy(fzv[0:16, half * 512:(half + 1) * 512], PS[0][0:16, 0:512]), inc="v")
        P.w(TE, "v")
    P.r(V, lambda e: e.tensor_scalar(fsc[0:16, :], fzv[0:16, :], Acol[0:16, 0:1], None, OP.mult))
    P.r(V, lambda e: e.tensor_scalar(fsc[0:16, :], fsc[0:16, :], -20.0, 0.0, OP.max, OP.min), inc="v")
    P.r(V, lambda e: e.tensor_tensor(fL[0:16, :], fzv[0:16, :], xdbB[0:16, 0:T], OP.mult))
    P.r(V, lambda e: e.tensor_scalar(fL[0:16, :], fL[0:16, :], -10.0, 10.0, OP.max, OP.min))
    P.r(V, lambda e: e.tensor_copy(fzv[0:16, :], fL[0:16, :]), inc="v")
    P.r(V, lambda e: e.memset(f0z[:, :], 0.0), inc="v")
    for b in range(B):
        P.r(V, lambda e, b=b: e.tensor_tensor_scan(
            fL[0:16, b * LLOC:(b + 1) * LLOC], fsc[0:16, b * LLOC:(b + 1) * LLOC],
            f0z[0:16, :], 0.0, OP.add, OP.add), inc="v")
    P.barrier()
    P.r(V, lambda e: e.memset(fDst[:, :], 0.0), inc="v")
    for b in range(B):
        src3 = fL[0:16, b * LLOC:(b + 1) * LLOC].rearrange("p (n c) -> p n c", c=TC)[:, 0:NCH - 1, TC - 1:TC]
        dst3 = fDst[0:16, b * NCH + 1:(b + 1) * NCH][:, :, None]
        P.r(V, lambda e, s=src3, d=dst3: e.tensor_copy(d, s), inc="v")
    for b in range(B):
        P.r(V, lambda e, b=b: e.tensor_tensor(
            fsc[0:16, b * LLOC:(b + 1) * LLOC].rearrange("p (n c) -> p n c", c=TC),
            fL[0:16, b * LLOC:(b + 1) * LLOC].rearrange("p (n c) -> p n c", c=TC),
            fDst[0:16, b * NCH:(b + 1) * NCH][:, :, None].broadcast_to([16, NCH, TC]),
            OP.subtract), inc="v")
    P.w(S, "v")
    P.r(S, lambda e: e.activation(out=w_a[0:16, 0:T], in_=fsc[0:16, :], func=AF.Exp), inc="sc")
    P.r(S, lambda e: e.activation(out=w_b[0:16, 0:T], in_=fsc[0:16, :], func=AF.Exp, scale=-1.0), inc="sc")
    P.w(V, "sc")
    P.r(V, lambda e: e.tensor_tensor(fC[0:16, :], xdbC[0:16, 0:T], w_a[0:16, 0:T], OP.mult))
    P.r(V, lambda e: e.tensor_tensor(fB[0:16, :], fzv[0:16, :], w_b[0:16, 0:T], OP.mult), inc="v")
    for b in range(B):
        src3 = w_a[0:16, b * LLOC:(b + 1) * LLOC].rearrange("p (n c) -> p n c", c=TC)[:, :, TC - 1:TC]
        dst3 = fDch[0:16, b * NCH:(b + 1) * NCH][:, :, None]
        P.r(V, lambda e, s=src3, d=dst3: e.tensor_copy(d, s), inc="v")
    P.w(S, "v")
    P.r(S, lambda e: e.activation(out=w_b[0:16, 3000:3000 + B * NCH], in_=fDst[0:16, :], func=AF.Exp), inc="sc")
    P.w(V, "sc")
    for b in range(B):
        P.r(V, lambda e, b=b: e.tensor_tensor(
            fCc[0:16, b * LLOC:(b + 1) * LLOC].rearrange("p (n c) -> p n c", c=TC),
            fC[0:16, b * LLOC:(b + 1) * LLOC].rearrange("p (n c) -> p n c", c=TC),
            w_b[0:16, 3000 + b * NCH:3000 + (b + 1) * NCH][:, :, None].broadcast_to([16, NCH, TC]),
            OP.mult), inc="v")
    P.barrier()
    if "s_dbg" in _dbg:
        P.w(S, "v"); P.w(S, "sc")
        P.d(S, ap["s_dbg"][0:16, :], fL[0:16, :], "q1")
        P.d(S, ap["s_dbg"][16:32, :], fsc[0:16, :], "q1")
        P.d(S, ap["s_dbg"][32:48, 0:B * NCH], fDst[0:16, :], "q1")
        P.d(S, ap["s_dbg"][48:64, 0:B * NCH], fDch[0:16, :], "q1")
        P.d(S, ap["s_dbg"][64:80, :], xdbB[0:16, 0:T], "q1")
        P.d(S, ap["s_dbg"][80:96, :], xdbC[0:16, 0:T], "q1")
        P.d(S, ap["s_dbg"][96:112, :], fzv[0:16, 0:T], "q1")
        P.d(S, ap["s_dbg"][112:128, :], w_b[0:16, 0:T], "q1")
        P.w(SY, "q1"); P.w(V, "q1"); P.w(S, "q1")
    P.w(S, "v")
    P.d(S, ap["s_fb"][:, :], fB[0:16, :], "q1")
    P.w(SY, "q1"); P.w(V, "q1"); P.w(S, "q1")
    P.barrier()
    for b in range(B):
        for gid in range(NG):
            g_ = b * NG + gid
            dst = btT[:, g_ * 128:g_ * 128 + 16]
            src = ap["s_fb"][0:16, b * LLOC + gid * 128:(b * LLOC) + (gid + 1) * 128]
            P.d(SY, dst, src, "q0", transpose=True)
            P.w(SY, "q0")
            if g_ == 0:
                P.barrier()
    P.barrier()

    P.r(V, lambda e: e.memset(hcurb[0][0:16, :], 0.0))
    P.r(V, lambda e: e.memset(hcurb[1][0:16, :], 0.0), inc="v")
    for b in range(B):
        for gid in range(NG):
            xbuf = xtm[:, (gid % 2) * DI:((gid % 2) + 1) * DI]
            P.w(SY, "t")
            P.d(SY, xbuf, ap["s_xcf"][:, b * LLOC + gid * 128: b * LLOC + (gid + 1) * 128], "q0", transpose=True)
            P.wall(TE)
            for c in range(4):
                n = gid * 4 + c
                blk = (b * NG + gid) * 4 + c
                stm = fB[0:16, b * LLOC + n * TC: b * LLOC + (n + 1) * TC]
                mvm = fC[0:16, b * LLOC + n * TC: b * LLOC + (n + 1) * TC]
                P.r(TE, lambda e, stm=stm, mvm=mvm, c=c:
                    e.matmul(PSG[32 * c:32 * c + 32, 0:32], stm, mvm, start=True, stop=True,
                             tile_position=(0, 32 * c)), inc="t")
                P.w(V, "t")
                mcb = c_b[32 * c:32 * c + 32, (n % 2) * TC:((n % 2) + 1) * TC]
                P.r(V, lambda e, mcb=mcb, c=c: e.tensor_tensor(mcb, PSG[32 * c:32 * c + 32, 0:32], msk[32 * c:32 * c + 32, :], OP.mult), inc="v")
                P.w(TE, "v")
                for dc in range(4):
                    stg = btT[32 * c:32 * c + 32, (b * NG + gid) * 128:(b * NG + gid) * 128 + 16]
                    mvx = xbuf[32 * c:32 * c + 32, dc * 512:(dc + 1) * 512]
                    P.r(TE, lambda e, stg=stg, mvx=mvx, c=c, dc=dc:
                        e.matmul(PSG[0:16, dc * 512:(dc + 1) * 512], stg, mvx, start=True, stop=True,
                                 tile_position=(32 * c, 0)),
                        inc=("t" if dc == 3 else None))
                P.w(V, "t")
                P.r(V, lambda e, b=b: e.tensor_copy(hstg[0:16, :], hcurb[b][0:16, :]), inc="v")
                P.r(V, lambda e, b=b: e.tensor_tensor(w_a[0:16, 0:DI], hcurb[b][0:16, :], PSG[0:16, 0:DI], OP.add))
                P.r(V, lambda e, b=b, n=n: e.tensor_scalar(hcurb[b][0:16, :], w_a[0:16, 0:DI], fDch[0:16, b * NCH + n:b * NCH + n + 1], None, OP.mult), inc="v")
                P.w(TE, "v")
                for dc in range(4):
                    mvx = xbuf[32 * c:32 * c + 32, dc * 512:(dc + 1) * 512]
                    stc = fC[0:16, b * LLOC + n * TC: b * LLOC + (n + 1) * TC]
                    mvh = hstg[0:16, dc * 512:(dc + 1) * 512]
                    P.r(TE, lambda e, mcb=mcb, mvx=mvx, c=c, dc=dc:
                        e.matmul(PSY[32 * c:32 * c + 32, dc * 512:(dc + 1) * 512], mcb, mvx, start=True, stop=False,
                                 tile_position=(32 * c, 32 * c)))
                    P.r(TE, lambda e, stc=stc, mvh=mvh, c=c, dc=dc:
                        e.matmul(PSY[32 * c:32 * c + 32, dc * 512:(dc + 1) * 512], stc, mvh, start=False, stop=True,
                                 tile_position=(0, 32 * c)),
                        inc=("t" if dc == 3 else None))
            P.w(V, "t"); P.w(V, "q1")
            P.r(V, lambda e: e.tensor_copy(c_a[:, 0:DI], PSY[:, 0:DI]), inc="v")
            P.w(S, "v")
            P.d(S, ap["s_y"][ts_(b * NG + gid, 128), :], c_a[:, 0:DI], "q1")
            P.w(TE, "q1"); P.w(V, "q1"); P.w(S, "q1")
    P.w(S, "v")
    P.r(S, lambda e: e.activation(out=p_a[0:16, 16:17], in_=fL[0:16, LLOC - 1:LLOC], func=AF.Exp), inc="sc")
    P.r(S, lambda e: e.activation(out=p_a[0:16, 17:18], in_=fL[0:16, 2 * LLOC - 1:2 * LLOC], func=AF.Exp), inc="sc")
    P.wall(SY)
    P.d(SY, ap["cc_in"][0:16, 0:DI], hcurb[0][0:16, :], "q0")
    P.d(SY, ap["cc_in"][16:32, 0:DI], hcurb[1][0:16, :], "q0")
    P.d(SY, ap["cc_in"][0:16, DI:DI + 1], p_a[0:16, 16:17], "q0", slow=True)
    P.d(SY, ap["cc_in"][16:32, DI:DI + 1], p_a[0:16, 17:18], "q0", slow=True)
    P.w(G, "q0")
    P.r(G, lambda e: e.collective_compute("AllGather", OP.bypass, replica_groups=[list(range(NCORES))],
                                          ins=[ap["cc_in"][:]], outs=[ap["cc_out"][:]]), inc="cc")
    P.w(SY, "cc")
    P.wall(V)
    P.r(V, lambda e: e.memset(hin[0:32, :], 0.0), inc="v")
    for r in range(NCORES - 1):
        P.w(V, "q1")
        P.wall(SY)
        P.d(SY, w_b[0:32, 0:DI + 1], ap["cc_out"][r], "q0")
        P.wall(V)
        Dr = w_b[0:32, DI:DI + 1]
        Gr = w_b[0:32, 0:DI]
        mr = rmk[:, r + 1:r + 2]
        P.r(V, lambda e, Dr=Dr: e.tensor_scalar(p_a[0:32, 20:21], Dr, -1.0, None, OP.add))
        P.r(V, lambda e, mr=mr: e.scalar_tensor_tensor(p_a[0:32, 21:22], p_a[0:32, 20:21], mr, onec[0:32, 0:1], OP.mult, OP.add), inc="v")
        P.r(V, lambda e: e.tensor_scalar(hin[0:32, :], hin[0:32, :], p_a[0:32, 21:22], None, OP.mult), inc="v")
        P.r(V, lambda e, Gr=Gr, mr=mr: e.scalar_tensor_tensor(hin[0:32, :], Gr, mr, hin[0:32, :], OP.mult, OP.add), inc="v")
    P.r(V, lambda e: e.tensor_copy(hinb[0:16, 0:DI], hin[0:16, :]), inc="v")
    P.wall(SY)
    P.d(SY, ap["cc_in"][0:16, 0:DI], hin[16:32, :], "q0")
    P.w(G, "q0")
    P.d(G, hinb2[0:16, 0:DI], ap["cc_in"][0:16, 0:DI], "q2")
    P.barrier()

    for i in range(NT):
        b = i // 8; j = i % 8
        P.wall(TE)
        for dc in range(4):
            stc = fCc[0:16, b * LLOC + j * 128:(b * LLOC + j * 128) + 128]
            hb = hinb if b == 0 else hinb2
            mvh = hb[0:16, dc * 512:(dc + 1) * 512]
            P.r(TE, lambda e, stc=stc, mvh=mvh, dc=dc:
                e.matmul(PS[dc][:, 0:512], stc, mvh, start=True, stop=True), inc="t")
        P.d(SY, c_a[:, 0:DI], ap["s_y"][ts_(i, 128), :], "q0")
        P.d(SY, c_b[:, 0:DI], ap["s_sz"][ts_(i, 128), :], "q0")
        P.d(SY, c_c[:, 0:DI], ap["s_xcf"][:, 0:0 + 0] if False else c_c[:, 0:DI], "q0") if False else None
        P.d(SY, c_c[:, 0:DI], ap["s_xcf"][:, b * LLOC + j * 128: b * LLOC + (j + 1) * 128], "q0", transpose=True)
        P.wall(V)
        for dc in range(4):
            P.r(V, lambda e, dc=dc: e.tensor_tensor(w_a[:, dc * 512:(dc + 1) * 512], c_a[:, dc * 512:(dc + 1) * 512], PS[dc][:, 0:512], OP.add), inc="v")
        P.w(TE, "v")
        P.r(V, lambda e: e.tensor_tensor(w_a[:, 0:DI], w_a[:, 0:DI], c_b[:, 0:DI], OP.mult), inc="v")
        P.r(V, lambda e: e.tensor_tensor(c_c[:, 0:DI], c_c[:, 0:DI], dprow[:, :], OP.mult))
        P.r(V, lambda e: e.tensor_tensor(c_a[:, 0:DI], w_a[:, 0:DI], c_c[:, 0:DI], OP.add), inc="v")
        P.w(S, "v")
        P.d(S, ap["s_yg"][ts_(i, 128), :], c_a[:, 0:DI], "q1")
        P.w(SY, "q1"); P.w(V, "q1"); P.w(S, "q1")
    P.barrier()

    ygfm = big1
    woT = bufA
    for k in range(16):
        P.d(SY, ygfm[:, ts_(k, T)], ap["s_yg"][:, ts_(k, 128)], "q0", transpose=True)
        P.d(SY, woT[:, ts_(k, D)], ap["s_wout"][:, ts_(k, 128)], "q0", transpose=True)
    P.wall(TE)
    for i in range(NT):
        for mc in range(2):
            ps = PS[mc]
            for k in range(16):
                st = ygfm[:, off_(k * T, i, 128)]
                mv = woT[:, k * D + mc * 512: k * D + mc * 512 + 512]
                P.r(TE, lambda e, ps=ps, st=st, mv=mv, k=k:
                    e.matmul(ps[:, 0:512], st, mv, start=(k == 0), stop=(k == 15)),
                    inc=("t" if k == 15 else None))
        P.w(V, "t"); P.w(V, "q1")
        P.r(V, lambda e: e.tensor_copy(c_a[:, 0:512], PS[0][:, 0:512]))
        P.r(V, lambda e: e.tensor_copy(c_a[:, 512:1024], PS[1][:, 0:512]), inc="v")
        P.w(TE, "v"); P.w(S, "v")
        P.d(S, ap["s_s"][ts_(i, 128), :], c_a[:, 0:D], "q1")
    P.barrier()

    xqfm = A16[:, 0:16384]
    for k in range(8):
        P.d(SY, xqfm[:, ts_(k, T)], ap["s_xq"][:, ts_(k, 128)], "q0", transpose=True)
    P.r(V, lambda e: e.memset(p_a[:, 24:25], 0.0), inc="v")
    P.r(V, lambda e: e.memset(sx2i[:, :], 0.0), inc="v")
    wmovB = A16[:, 16384:24576]
    wmov2B = A16[:, 24576:32768]
    tmark = {}
    for oc in range(8):
        wg_ = wmov if oc % 2 == 0 else wmovB
        wu_ = wmov2 if oc % 2 == 0 else wmov2B
        if oc >= 2:
            P.w(SY, "t", tmark[oc - 2])
        for k in range(8):
            P.d(SY, wg_[:, ts_(k, 512)], ap["s_f0wg"][oc * 512:(oc + 1) * 512, ts_(k, 128)], "q0", transpose=True)
            P.d(SY, wu_[:, ts_(k, 512)], ap["s_f0wu"][oc * 512:(oc + 1) * 512, ts_(k, 128)], "q0", transpose=True)
        P.w(TE, "q0")
        for i in range(NT):
            for k in range(8):
                st = xqfm[:, off_(k * T, i, 128)]
                P.r(TE, lambda e, st=st, k=k, w=wg_: e.matmul(PS[0][:, 0:512], st, w[:, ts_(k, 512)], start=(k == 0), stop=(k == 7)), inc=("t" if k == 7 else None))
            for k in range(8):
                st = xqfm[:, off_(k * T, i, 128)]
                P.r(TE, lambda e, st=st, k=k, w=wu_: e.matmul(PS[1][:, 0:512], st, w[:, ts_(k, 512)], start=(k == 0), stop=(k == 7)), inc=("t" if k == 7 else None))
            if i == NT - 1:
                tmark[oc] = P.cnt["t"]
            P.w(S, "t")
            P.r(S, lambda e, i=i: e.activation(out=w_a[:, 0:512], in_=PS[0][:, 0:512], func=AF.Silu, scale=sxg[:, i:i + 1]), inc="sc")
            P.r(S, lambda e, i=i: e.activation(out=w_b[:, 0:512], in_=PS[1][:, 0:512], func=AF.Identity, scale=sxu[:, i:i + 1], bias=0.0), inc="sc")
            P.w(V, "sc")
            P.w(TE, "sc")
            P.r(V, lambda e: e.tensor_tensor(w_a[:, 0:512], w_a[:, 0:512], w_b[:, 0:512], OP.mult))
            P.r(V, lambda e, i=i: e.reduce_max(p_a[:, 25:26], w_a[:, 0:512], AX.X, apply_absolute_value=True))
            P.r(V, lambda e, i=i: e.tensor_tensor(sx2i[:, i:i + 1], sx2i[:, i:i + 1], p_a[:, 25:26], OP.max), inc="v")
            P.w(S, "v")
            P.d(S, ap["s_h2"][ts_(i, 128), oc * 512:(oc + 1) * 512], w_a[:, 0:512], "q1")
            P.w(V, "q1"); P.w(S, "q1")
    P.r(V, lambda e: e.tensor_scalar(sx2i[:, :], sx2i[:, :], 1e-5, 1.0 / 127.0, OP.max, OP.mult))
    P.r(V, lambda e: e.tensor_scalar(sxd[:, :], sx2i[:, :], mgr[:, 2:3], None, OP.mult), inc="v")
    P.barrier()
    q8mark = {}
    for i in range(NT):
        lw = w_a[:, 0:4096] if i % 2 == 0 else A32[:, 0:4096]
        sw = w_b[:, 0:4096] if i % 2 == 0 else A32[:, 4096:8192]
        cq = c_a if i % 2 == 0 else c_b
        sq = "q1" if i % 2 == 0 else "q2"
        pcol = 26 if i % 2 == 0 else 25
        if i >= 2:
            P.w(SY, "sc", q8mark[i - 2])
        P.d(SY, lw, ap["s_h2"][ts_(i, 128), :], "q0")
        P.w(V, "q0"); P.w(V, sq); P.w(V, "sc")
        P.r(V, lambda e, i=i, pcol=pcol: e.reciprocal(p_a[:, pcol:pcol + 1], sx2i[:, i:i + 1]), inc="v")
        P.w(S, "v")
        P.r(S, lambda e, lw=lw, sw=sw, pcol=pcol: e.activation(out=sw, in_=lw, func=AF.Identity, scale=p_a[:, pcol:pcol + 1], bias=0.0), inc="sc")
        q8mark[i] = P.cnt["sc"]
        P.w(V, "sc")
        P.r(V, lambda e, sw=sw: e.tensor_copy(i_a[:, 0:4096], sw))
        P.r(V, lambda e, cq=cq: e.tensor_copy(cq[:, 0:4096], i_a[:, 0:4096]), inc="v")
        P.w(S, "v")
        P.d(S, ap["s_xq2"][ts_(i, 128), :], cq[:, 0:4096], sq)
    P.w(SY, "q1"); P.w(V, "q1"); P.w(S, "q1")
    P.w(SY, "q2"); P.w(V, "q2"); P.w(S, "q2")
    xq2fm = big1
    for half in range(2):
        P.w(SY, "t")
        for k in range(16):
            kk = half * 16 + k
            P.d(SY, xq2fm[:, ts_(k, T)], ap["s_xq2"][:, ts_(kk, 128)], "q0", transpose=True)
            P.d(SY, wmov[:, ts_(k, 512)], ap["s_f0wd"][0:512, ts_(kk, 128)], "q0", transpose=True)
            P.d(SY, wmov2[:, ts_(k, 512)], ap["s_f0wd"][512:1024, ts_(kk, 128)], "q0", transpose=True)
        if half == 0 and "s_dbg3" in _dbg:
            P.w(S, "q0")
            P.d(S, ap["s_dbg3"][:, 0:T], xq2fm[:, 0:T], "q1")
            P.d(S, ap["s_dbg3"][:, T:T + 512], wmov[:, 0:512], "q1")
            P.w(SY, "q1"); P.w(V, "q1"); P.w(S, "q1")
        P.wall(TE)
        for i in range(NT):
            for mc in range(2):
                ps = PS[mc]
                wv = wmov if mc == 0 else wmov2
                for k in range(16):
                    st = xq2fm[:, off_(k * T, i, 128)]
                    P.r(TE, lambda e, ps=ps, st=st, wv=wv, k=k:
                        e.matmul(ps[:, 0:512], st, wv[:, ts_(k, 512)], start=(k == 0), stop=(k == 15)),
                        inc=("t" if k == 15 else None))
            if half == 0:
                P.w(V, "t")
                P.r(V, lambda e: e.tensor_copy(w_a[:, 0:512], PS[0][:, 0:512]))
                P.r(V, lambda e: e.tensor_copy(w_a[:, 512:1024], PS[1][:, 0:512]), inc="v")
                P.w(TE, "v"); P.w(S, "v")
                P.d(S, ap["s_h2"][ts_(i, 128), 0:1024], w_a[:, 0:1024], "q1")
                P.w(V, "q1")
            else:
                P.w(SY, "v")
                P.d(SY, w_b[:, 0:1024], ap["s_h2"][ts_(i, 128), 0:1024], "q0")
                P.w(V, "t"); P.wall(V)
                P.r(V, lambda e: e.tensor_tensor(w_a[:, 0:512], PS[0][:, 0:512], w_b[:, 0:512], OP.add))
                P.r(V, lambda e: e.tensor_tensor(w_a[:, 512:1024], PS[1][:, 0:512], w_b[:, 512:1024], OP.add), inc="v")
                P.w(TE, "v"); P.w(S, "v"); P.w(S, "q1")
                P.r(S, lambda e, i=i: e.activation(out=c_b[:, 0:D], in_=w_a[:, 0:D], func=AF.Silu, scale=sxd[:, i:i + 1]), inc="sc")
                P.w(S, "sc")
                P.d(S, ap["s_g"][ts_(i, 128), :], c_b[:, 0:D], "q1")
                P.w(V, "sc")
    P.barrier()

    for i in range(NT):
        P.d(SY, w_b[:, 0:D], xr[i], "q0")
        P.d(SY, c_a[:, 0:D], ap["s_s"][ts_(i, 128), :], "q0")
        P.d(SY, c_b[:, 0:D], ap["s_g"][ts_(i, 128), :], "q0")
        P.wall(V)
        P.r(V, lambda e: e.tensor_tensor(c_a[:, 0:D], c_a[:, 0:D], c_b[:, 0:D], OP.mult), inc="v")
        P.r(V, lambda e: e.tensor_tensor(w_a[:, 0:D], w_b[:, 0:D], c_a[:, 0:D], OP.add), inc="v")
        P.w(S, "v")
        P.d(S, ap["s_x1"][ts_(i, 128), :], w_a[:, 0:D], "q1")
        P.w(V, "q1"); P.w(S, "q1")
        mu = p_a[:, 0:1]; vv = p_a[:, 1:2]; rr = p_a[:, 2:3]
        P.r(V, lambda e, mu=mu: e.reduce_sum(mu, w_a[:, 0:D], AX.X))
        P.r(V, lambda e, mu=mu: e.tensor_scalar_mul(mu, mu, 1.0 / D), inc="v")
        P.w(S, "v")
        P.r(S, lambda e: e.activation(out=w_b[:, 0:D], in_=w_a[:, 0:D], func=AF.Square), inc="sc")
        P.w(V, "sc")
        P.r(V, lambda e, vv=vv: e.reduce_sum(vv, w_b[:, 0:D], AX.X))
        P.r(V, lambda e, vv=vv: e.tensor_scalar_mul(vv, vv, 1.0 / D))
        P.r(V, lambda e, mu=mu, rr=rr: e.tensor_tensor(rr, mu, mu, OP.mult))
        P.r(V, lambda e, vv=vv, rr=rr: e.tensor_tensor(rr, vv, rr, OP.subtract))
        P.r(V, lambda e, rr=rr: e.tensor_scalar_add(rr, rr, 1e-5), inc="v")
        P.w(S, "v")
        P.r(S, lambda e, rr=rr: e.activation(out=rr, in_=rr, func=AF.Sqrt), inc="sc")
        P.w(V, "sc")
        P.r(V, lambda e, rr=rr: e.reciprocal(rr, rr))
        P.r(V, lambda e, mu=mu, rr=rr: e.tensor_scalar(w_a[:, 0:D], w_a[:, 0:D], mu, rr, OP.subtract, OP.mult))
        P.r(V, lambda e: e.tensor_copy(c_b[:, 0:D], w_a[:, 0:D]), inc="v")
        P.w(S, "v")
        P.d(S, ap["s_xn1"][ts_(i, 128), :], c_b[:, 0:D], "q1")
        P.w(V, "q1"); P.w(S, "q1")
        fp4_on_tile(128, D)
        P.w(S, "v")
        P.d(S, ap["s_xq1"][ts_(i, 128), :], c_a[:, 0:D], "q1")
        P.w(SY, "q1"); P.w(V, "q1"); P.w(S, "q1")
    P.barrier()

    xq1fm = A16[:, 0:16384]
    for k in range(8):
        P.d(SY, xq1fm[:, ts_(k, T)], ap["s_xq1"][:, ts_(k, 128)], "q0", transpose=True)
    OCW = [512, 512, 512, 512, 512, 256]
    ocs = [0, 512, 1024, 1536, 2048, 2560]
    wmovB = A16[:, 16384:24576]
    wmov2B = A16[:, 24576:32768]
    tmark = {}
    for oc in range(6):
        ow = OCW[oc]; o0 = ocs[oc]
        wg_ = wmov if oc % 2 == 0 else wmovB
        wu_ = wmov2 if oc % 2 == 0 else wmov2B
        if oc >= 2:
            P.w(SY, "t", tmark[oc - 2])
        for k in range(8):
            P.d(SY, wg_[:, k * 512:k * 512 + ow], ap["s_f1wg"][o0:o0 + ow, ts_(k, 128)], "q0", transpose=True)
            P.d(SY, wu_[:, k * 512:k * 512 + ow], ap["s_f1wu"][o0:o0 + ow, ts_(k, 128)], "q0", transpose=True)
        P.w(TE, "q0")
        for i in range(NT):
            for k in range(8):
                st = xq1fm[:, off_(k * T, i, 128)]
                P.r(TE, lambda e, st=st, k=k, ow=ow, w=wg_: e.matmul(PS[0][:, 0:ow], st, w[:, k * 512:k * 512 + ow], start=(k == 0), stop=(k == 7)), inc=("t" if k == 7 else None))
            for k in range(8):
                st = xq1fm[:, off_(k * T, i, 128)]
                P.r(TE, lambda e, st=st, k=k, ow=ow, w=wu_: e.matmul(PS[1][:, 0:ow], st, w[:, k * 512:k * 512 + ow], start=(k == 0), stop=(k == 7)), inc=("t" if k == 7 else None))
            if i == NT - 1:
                tmark[oc] = P.cnt["t"]
            P.w(S, "t")
            P.r(S, lambda e, ow=ow: e.activation(out=w_a[:, 0:ow], in_=PS[0][:, 0:ow], func=AF.Silu), inc="sc")
            P.r(S, lambda e, ow=ow: e.activation(out=w_b[:, 3000:3000 + ow] if False else w_b[:, 0:ow], in_=PS[1][:, 0:ow], func=AF.Identity, bias=0.0), inc="sc")
            P.w(V, "sc"); P.w(TE, "sc")
            P.r(V, lambda e, ow=ow: e.tensor_tensor(w_a[:, 0:ow], w_a[:, 0:ow], w_b[:, 0:ow], OP.mult), inc="v")
            fp4_on_tile(128, ow)
            P.w(S, "v")
            P.d(S, ap["s_xq3"][ts_(i, 128), o0:o0 + ow], c_a[:, 0:ow], "q1")
            P.w(V, "q1"); P.w(S, "q1")
    P.barrier()

    xq3fm = big1
    for half in range(2):
        P.w(SY, "t")
        for k in range(11):
            kk = half * 11 + k
            P.d(SY, xq3fm[:, ts_(k, T)], ap["s_xq3"][:, ts_(kk, 128)], "q0", transpose=True)
            P.d(SY, wmov[:, ts_(k, 512)], ap["s_f1wd"][0:512, ts_(kk, 128)], "q0", transpose=True)
            P.d(SY, wmov2[:, ts_(k, 512)], ap["s_f1wd"][512:1024, ts_(kk, 128)], "q0", transpose=True)
        P.wall(TE)
        for i in range(NT):
            for mc in range(2):
                ps = PS[mc]
                wv = wmov if mc == 0 else wmov2
                for k in range(11):
                    st = xq3fm[:, off_(k * T, i, 128)]
                    P.r(TE, lambda e, ps=ps, st=st, wv=wv, k=k:
                        e.matmul(ps[:, 0:512], st, wv[:, ts_(k, 512)], start=(k == 0), stop=(k == 10)),
                        inc=("t" if k == 10 else None))
            if half == 0:
                P.w(V, "t")
                P.r(V, lambda e: e.tensor_copy(w_a[:, 0:512], PS[0][:, 0:512]))
                P.r(V, lambda e: e.tensor_copy(w_a[:, 512:1024], PS[1][:, 0:512]), inc="v")
                P.w(TE, "v"); P.w(S, "v")
                P.d(S, ap["s_h2"][ts_(i, 128), 0:1024], w_a[:, 0:1024], "q1")
                P.w(V, "q1")
            else:
                P.w(SY, "v")
                P.d(SY, w_b[:, 0:1024], ap["s_h2"][ts_(i, 128), 0:1024], "q0")
                P.w(V, "t"); P.wall(V)
                P.r(V, lambda e: e.tensor_tensor(w_a[:, 0:512], PS[0][:, 0:512], w_b[:, 0:512], OP.add))
                P.r(V, lambda e: e.tensor_tensor(w_a[:, 512:1024], PS[1][:, 0:512], w_b[:, 512:1024], OP.add), inc="v")
                P.w(TE, "v"); P.w(S, "v"); P.w(S, "q1")
                P.r(S, lambda e: e.activation(out=c_b[:, 0:D], in_=w_a[:, 0:D], func=AF.Silu), inc="sc")
                P.w(S, "sc")
                P.d(S, ap["s_g1"][ts_(i, 128), :], c_b[:, 0:D], "q1")
                P.w(V, "sc")
    P.barrier()

    xn1fm = big1[:, 0:16384]
    for k in range(8):
        P.d(SY, xn1fm[:, ts_(k, T)], ap["s_xn1"][:, ts_(k, 128)], "q0", transpose=True)
    QCW = [512, 512, 512, 512, 512, 512, 32]
    qcs = [0, 512, 1024, 1536, 2048, 2560, 3072]
    wmovB = A16[:, 16384:24576]
    tmark = {}
    for oc in range(7):
        tmark[oc] = P.cnt["t"]
        ow = QCW[oc]; o0 = qcs[oc]
        wg_ = wmov if oc % 2 == 0 else wmovB
        if oc >= 2:
            P.w(SY, "t", tmark[oc - 1])
        for k in range(8):
            P.d(SY, wg_[:, k * 512:k * 512 + ow], ap["s_qkvb"][o0:o0 + ow, ts_(k, 128)], "q0", transpose=True)
        P.w(TE, "q0")
        for i in range(NT):
            for k in range(8):
                st = xn1fm[:, off_(k * T, i, 128)]
                P.r(TE, lambda e, st=st, k=k, ow=ow, w=wg_: e.matmul(PS[0][:, 0:ow], st, w[:, k * 512:k * 512 + ow], start=(k == 0), stop=(k == 7)), inc=("t" if k == 7 else None))
            P.w(V, "t")
            P.r(V, lambda e, ow=ow: e.tensor_copy(c_a[:, 0:ow], PS[0][:, 0:ow]), inc="v")
            P.w(TE, "v"); P.w(S, "v")
            P.d(S, ap["s_qkv"][ts_(i, 128), o0:o0 + ow], c_a[:, 0:ow], "q1")
            P.w(V, "q1"); P.w(S, "q1")
    P.barrier()

    def elu1_norm(dst_bf, src_bf, tmpf):
        P.r(V, lambda e: e.tensor_scalar(w_a[:, 0:1024], src_bf, 0.0, None, OP.min), inc="v")
        P.w(S, "v")
        P.r(S, lambda e: e.activation(out=w_a[:, 1024:2048], in_=w_a[:, 0:1024], func=AF.Exp), inc="sc")
        P.w(V, "sc")
        P.r(V, lambda e: e.tensor_scalar(w_a[:, 0:1024], src_bf, 0.0, None, OP.max))
        P.r(V, lambda e: e.tensor_tensor(w_a[:, 0:1024], w_a[:, 0:1024], w_a[:, 1024:2048], OP.add), inc="v")
        a3 = w_a[:, 0:1024].rearrange("p (h d) -> p h d", d=64)
        P.r(V, lambda e, a3=a3: e.tensor_reduce(p_a[:, 32:48], a3, AX.X, OP.add))
        P.r(V, lambda e: e.reciprocal(p_a[:, 32:48], p_a[:, 32:48]), inc="v")
        rsb = p_a[:, 32:48][:, :, None].broadcast_to([128, 16, 64])
        P.r(V, lambda e, a3=a3, rsb=rsb, dst_bf=dst_bf: e.tensor_tensor(
            dst_bf.rearrange("p (h d) -> p h d", d=64), a3, rsb, OP.mult), inc="v")

    for j in range(8):
        P.wall(V)
        P.d(SY, c_a[:, 0:3104], ap["s_qkv"][ts_(j, 128), :], "q0")
        P.d(SY, c_b[:, 0:3104], ap["s_qkv"][ts_(8 + j, 128), :], "q0")
        P.wall(V)
        elu1_norm(c_c[:, 0:1024], c_a[:, 0:1024], 0)
        elu1_norm(c_c[:, 1024:2048], c_a[:, 1024:2048], 0)
        elu1_norm(c_c[:, 2048:3072], c_b[:, 0:1024], 0)
        elu1_norm(c_b[:, 0:1024], c_b[:, 1024:2048], 0)
        P.r(V, lambda e: e.tensor_scalar(w_a[:, 0:16], c_a[:, 3072:3088], -1.0, None, OP.mult))
        P.r(V, lambda e: e.tensor_scalar(w_a[:, 16:32], c_b[:, 3072:3088], -1.0, None, OP.mult), inc="v")
        P.w(S, "v")
        P.r(S, lambda e: e.activation(out=w_a[:, 0:32], in_=w_a[:, 0:32], func=AF.Exp), inc="sc")
        P.w(V, "sc")
        P.r(V, lambda e: e.tensor_scalar_add(w_a[:, 0:32], w_a[:, 0:32], 1.0))
        P.r(V, lambda e: e.reciprocal(w_a[:, 0:32], w_a[:, 0:32]), inc="v")
        b0c = w_a[:, 0:16]; b1c = w_a[:, 16:32]
        def dot(dst_cols, aa, bb):
            P.r(V, lambda e, aa=aa, bb=bb: e.tensor_tensor(w_b[:, 0:1024], aa, bb, OP.mult), inc="v")
            a3 = w_b[:, 0:1024].rearrange("p (h d) -> p h d", d=64)
            P.r(V, lambda e, a3=a3, dst_cols=dst_cols: e.tensor_reduce(p_a[:, dst_cols:dst_cols + 16], a3, AX.X, OP.add), inc="v")
        q0n = c_c[:, 0:1024]; k0n = c_c[:, 1024:2048]; q1n = c_c[:, 2048:3072]; k1n = c_b[:, 0:1024]
        dot(32, q0n, k0n)
        dot(48, q1n, k0n)
        dot(16, q1n, k1n)
        dot(0, k1n, k0n)
        v0 = c_a[:, 2048:3072]; v1 = c_b[:, 2048:3072]
        b0b = b0c[:, :, None].broadcast_to([128, 16, 64])
        P.r(V, lambda e, b0b=b0b, v0=v0: e.tensor_tensor(
            c_a[:, 0:1024].rearrange("p (h d) -> p h d", d=64), v0.rearrange("p (h d) -> p h d", d=64), b0b, OP.mult), inc="v")
        vi0 = c_a[:, 0:1024]
        d_q0k0 = p_a[:, 32:48][:, :, None].broadcast_to([128, 16, 64])
        P.r(V, lambda e, d_q0k0=d_q0k0, vi0=vi0: e.tensor_tensor(
            c_a[:, 1024:2048].rearrange("p (h d) -> p h d", d=64), vi0.rearrange("p (h d) -> p h d", d=64), d_q0k0, OP.mult), inc="v")
        d_k1k0 = p_a[:, 0:16][:, :, None].broadcast_to([128, 16, 64])
        P.r(V, lambda e, d_k1k0=d_k1k0, vi0=vi0: e.tensor_tensor(
            w_b[:, 0:1024].rearrange("p (h d) -> p h d", d=64), vi0.rearrange("p (h d) -> p h d", d=64), d_k1k0, OP.mult), inc="v")
        P.r(V, lambda e, v1=v1: e.tensor_tensor(w_b[:, 0:1024], v1, w_b[:, 0:1024], OP.subtract), inc="v")
        b1b = b1c[:, :, None].broadcast_to([128, 16, 64])
        P.r(V, lambda e, b1b=b1b: e.tensor_tensor(
            w_b[:, 0:1024].rearrange("p (h d) -> p h d", d=64), w_b[:, 0:1024].rearrange("p (h d) -> p h d", d=64), b1b, OP.mult), inc="v")
        vi1 = w_b[:, 0:1024]
        d_q1k0 = p_a[:, 48:64][:, :, None].broadcast_to([128, 16, 64])
        d_q1k1 = p_a[:, 16:32][:, :, None].broadcast_to([128, 16, 64])
        P.r(V, lambda e, d_q1k0=d_q1k0, vi0=vi0: e.tensor_tensor(
            w_b[:, 1024:2048].rearrange("p (h d) -> p h d", d=64), vi0.rearrange("p (h d) -> p h d", d=64), d_q1k0, OP.mult), inc="v")
        P.r(V, lambda e, d_q1k1=d_q1k1, vi1=vi1: e.tensor_tensor(
            w_b[:, 2048:3072].rearrange("p (h d) -> p h d", d=64), vi1.rearrange("p (h d) -> p h d", d=64), d_q1k1, OP.mult), inc="v")
        P.r(V, lambda e: e.tensor_tensor(w_b[:, 1024:2048], w_b[:, 1024:2048], w_b[:, 2048:3072], OP.add), inc="v")
        P.r(V, lambda e: e.tensor_scalar(c_a[:, 1024:2048], c_a[:, 1024:2048], 0.125, None, OP.mult))
        P.r(V, lambda e: e.tensor_scalar(c_b[:, 1024:2048], w_b[:, 1024:2048], 0.125, None, OP.mult), inc="v")
        P.w(S, "v")
        P.d(S, ap["s_lay"][ts_(j, 128), :], c_a[:, 1024:2048], "q1")
        P.d(S, ap["s_lay"][ts_(8 + j, 128), :], c_b[:, 1024:2048], "q1")
        P.w(SY, "q1"); P.w(V, "q1"); P.w(S, "q1")
    P.barrier()

    layfm = big1[:, 0:16384]
    P.d(G, lngrow[:, :], ap["ln_g"][None, :].broadcast_to([128, D]), "q2")
    P.d(G, lnbrow[:, :], ap["ln_b"][None, :].broadcast_to([128, D]), "q2")
    for k in range(8):
        P.d(SY, layfm[:, ts_(k, T)], ap["s_lay"][:, ts_(k, 128)], "q0", transpose=True)
    for k in range(8):
        P.d(SY, wmov[:, k * 512:(k + 1) * 512], ap["s_wo"][0:512, ts_(k, 128)], "q0", transpose=True)
        P.d(SY, wmov2[:, k * 512:(k + 1) * 512], ap["s_wo"][512:1024, ts_(k, 128)], "q0", transpose=True)
    P.barrier()
    P.wall(TE)
    for i in range(NT):
        for mc in range(2):
            ps = PS[mc]
            wv = wmov if mc == 0 else wmov2
            for k in range(8):
                st = layfm[:, off_(k * T, i, 128)]
                P.r(TE, lambda e, ps=ps, st=st, wv=wv, k=k:
                    e.matmul(ps[:, 0:512], st, wv[:, ts_(k, 512)], start=(k == 0), stop=(k == 7)),
                    inc=("t" if k == 7 else None))
        if i == 1:
            P.barrier()
        P.d(SY, c_a[:, 0:D], ap["s_xn1"][ts_(i, 128), :], "q0")
        P.d(SY, w_b[:, 0:D], ap["s_x1"][ts_(i, 128), :], "q0")
        P.w(V, "t"); P.wall(V)
        P.r(V, lambda e: e.tensor_tensor(w_a[:, 0:512], PS[0][:, 0:512], c_a[:, 0:512], OP.add))
        P.r(V, lambda e: e.tensor_tensor(w_a[:, 512:1024], PS[1][:, 0:512], c_a[:, 512:1024], OP.add), inc="v")
        P.w(TE, "v")
        mu = p_a[:, 0:1]; vv = p_a[:, 1:2]; rr = p_a[:, 2:3]
        P.r(V, lambda e, mu=mu: e.reduce_sum(mu, w_a[:, 0:D], AX.X))
        P.r(V, lambda e, mu=mu: e.tensor_scalar_mul(mu, mu, 1.0 / D), inc="v")
        P.w(S, "v")
        P.r(S, lambda e: e.activation(out=c_b[:, 0:D], in_=w_a[:, 0:D], func=AF.Square), inc="sc")
        P.w(V, "sc")
        P.r(V, lambda e, vv=vv: e.reduce_sum(vv, c_b[:, 0:D], AX.X))
        P.r(V, lambda e, vv=vv: e.tensor_scalar_mul(vv, vv, 1.0 / D))
        P.r(V, lambda e, mu=mu, rr=rr: e.tensor_tensor(rr, mu, mu, OP.mult))
        P.r(V, lambda e, vv=vv, rr=rr: e.tensor_tensor(rr, vv, rr, OP.subtract))
        P.r(V, lambda e, rr=rr: e.tensor_scalar_add(rr, rr, 1e-5), inc="v")
        P.w(S, "v")
        P.r(S, lambda e, rr=rr: e.activation(out=rr, in_=rr, func=AF.Sqrt), inc="sc")
        P.w(V, "sc")
        P.r(V, lambda e, rr=rr: e.reciprocal(rr, rr))
        P.r(V, lambda e, mu=mu, rr=rr: e.tensor_scalar(w_a[:, 0:D], w_a[:, 0:D], mu, rr, OP.subtract, OP.mult))
        P.r(V, lambda e: e.tensor_tensor(w_a[:, 0:D], w_a[:, 0:D], lngrow[:, :], OP.mult))
        P.r(V, lambda e: e.tensor_tensor(w_a[:, 0:D], w_a[:, 0:D], lnbrow[:, :], OP.add), inc="v")
        P.w(SY, "v")
        P.d(SY, c_b[:, 0:D], ap["s_g1"][ts_(i, 128), :], "q0")
        P.wall(V)
        P.r(V, lambda e: e.tensor_tensor(w_a[:, 0:D], w_a[:, 0:D], c_b[:, 0:D], OP.mult), inc="v")
        P.r(V, lambda e: e.tensor_tensor(w_a[:, 0:D], w_a[:, 0:D], w_b[:, 0:D], OP.add), inc="v")
        P.w(S, "v")
        b = i // 8; j = i % 8
        P.d(S, ap["out"][b, ts_(j, 128), :], w_a[:, 0:D], "q1")
        P.w(SY, "q1"); P.w(V, "q1"); P.w(S, "q1"); P.w(G, "q1")
        if i == 0:
            P.barrier()
    P.wall(SY); P.wall(G); P.wall(V)

    import os as _os
    _cut = int(_os.environ.get("MK_CUT", "-1"))
    if _cut >= 0 and hasattr(P, "marks") and _cut < len(P.marks):
        for e in ENGINES:
            P.stream[e] = P.stream[e][:P.marks[_cut][e]]
    blkc = nc.Block(); blk = blkc.__enter__()

    def emit(eng_name):
        def _f(e):
            for fn in P.stream[eng_name]:
                fn(e)
        return _f
    blk.sync(emit("sync"))
    blk.vector(emit("vector"))
    blk.scalar(emit("scalar"))
    blk.gpsimd(emit("gpsimd"))
    blk.tensor(emit("tensor"))
    blkc.__exit__(None, None, None)
    for t in reversed(ctx):
        t.__exit__(None, None, None)
    return nc


_NC_CACHE = None


def kernel(L0_attn_x, L0_attn_W_in, L0_attn_conv_w, L0_attn_conv_b, L0_attn_W_xproj,
           L0_attn_A_log, L0_attn_D_param, L0_attn_W_out, L0_ffn_w_gate, L0_ffn_w_up,
           L0_ffn_w_down, L1_attn_W_qkvb, L1_attn_W_o, L1_attn_ln_g, L1_attn_ln_b,
           L1_ffn_w_gate, L1_ffn_w_up, L1_ffn_w_down):
    global _NC_CACHE
    if _NC_CACHE is None:
        _NC_CACHE = build_program()
    nc_ = _NC_CACHE
    x = np.ascontiguousarray(L0_attn_x, dtype=np.float32)
    Wq = np.ascontiguousarray(L1_attn_W_qkvb, dtype=np.float32).reshape(16, 193, 1024)
    Wq_perm = np.concatenate([
        Wq[:, 0:64].reshape(1024, 1024),
        Wq[:, 64:128].reshape(1024, 1024),
        Wq[:, 128:192].reshape(1024, 1024),
        Wq[:, 192:193].reshape(16, 1024),
        np.zeros((16, 1024), np.float32)], axis=0)
    maskc = (np.arange(128)[:, None] % TC <= np.arange(TC)[None, :]).astype(ml_dtypes.bfloat16)
    identc = np.eye(128, dtype=ml_dtypes.bfloat16)
    in_maps = []
    for c in range(NCORES):
        lo = c * LLOC
        halo = x[:, lo - 3:lo].copy() if c > 0 else np.zeros((B, 3, D), np.float32)
        rm = (np.arange(1, NCORES + 1) <= c).astype(np.float32)
        rm = np.zeros(NCORES, np.float32)
        rm[1:] = (np.arange(0, NCORES - 1) < c).astype(np.float32)
        in_maps.append({
            "x": x[:, lo:lo + LLOC].copy(),
            "xh": halo,
            "W_in": L0_attn_W_in, "conv_w": L0_attn_conv_w, "conv_b": L0_attn_conv_b,
            "W_xproj": L0_attn_W_xproj, "A_log": L0_attn_A_log, "D_param": L0_attn_D_param,
            "W_out": L0_attn_W_out,
            "f0_wg": L0_ffn_w_gate, "f0_wu": L0_ffn_w_up, "f0_wd": L0_ffn_w_down,
            "Wqkvb": Wq_perm, "W_o": L1_attn_W_o, "ln_g": L1_attn_ln_g, "ln_b": L1_attn_ln_b,
            "f1_wg": L1_ffn_w_gate, "f1_wu": L1_ffn_w_up, "f1_wd": L1_ffn_w_down,
            "mask": maskc, "ident": identc, "identf": identc.astype(np.float32), "rmask": rm,
        })
    res = run_bass_kernel_spmd(nc_, in_maps, core_ids=list(range(NCORES)))
    out = np.concatenate([np.asarray(r["out"], np.float32) for r in res.results], axis=1)
    return out




# version 08 (final)
# speedup vs baseline: 1.1335x; 1.0283x over previous
import numpy as np
import ml_dtypes
import concourse.bass as bass
import concourse.mybir as mybir
from concourse.bass_utils import run_bass_kernel_spmd

F32 = mybir.dt.float32
BF16 = mybir.dt.bfloat16
I16 = mybir.dt.int16
OP = mybir.AluOpType
AF = mybir.ActivationFunctionType
AX = mybir.AxisListType

B = 2
LLOC = 1024
T = 2048
NT = 16
D = 1024
DI = 2048
TC = 32
NCH = 32
NG = 8
NCORES = 8

ENGINES = ("sync", "vector", "scalar", "gpsimd", "tensor")
SEMS = {}
V, S, G, TE, SY = "vector", "scalar", "gpsimd", "tensor", "sync"


def ts_(i, n=128):
    return slice(i * n, (i + 1) * n)


def off_(base, i, n=128):
    return slice(base + i * n, base + (i + 1) * n)


class Prog:
    NW = 6

    def __init__(self):
        self.stream = {e: [] for e in ENGINES}
        self.cnt = {s: 0 for s in ["v", "sc", "g", "t", "cc", "bs"]}
        self.qs = {}
        for q in ("q0", "q1", "q2"):
            self.qs[q] = {"idx": 0, "tot": [0] * self.NW, "open": False}

    def _qsem(self, q, i):
        return q + chr(ord("a") + i)

    def w(self, eng, sem, val=None):
        if sem in self.qs:
            st = self.qs[sem]
            for i in range(self.NW):
                if st["tot"][i] > 0:
                    self.stream[eng].append(
                        lambda e, s=self._qsem(sem, i), vv=st["tot"][i]: e.wait_ge(SEMS[s], vv))
            st["open"] = False
            return
        v = self.cnt[sem] if val is None else val
        if v > 0:
            self.stream[eng].append(lambda e, s=sem, vv=v: e.wait_ge(SEMS[s], vv))

    OWN = {"vector": "v", "scalar": "sc", "gpsimd": "g", "tensor": "t"}

    def r(self, eng, fn, inc=None, k=1):
        own = self.OWN.get(eng)
        if own is not None:
            v = self.cnt[own]
            if v > 0:
                self.stream[eng].append(lambda e, s=own, vv=v: e.wait_ge(SEMS[s], vv))
            if inc is None or inc == own:
                self.cnt[own] += 1
                self.stream[eng].append(lambda e, f=fn, s=own: f(e).then_inc(SEMS[s], 1))
            else:
                self.cnt[inc] += k
                self.stream[eng].append(lambda e, f=fn, s=inc, kk=k: f(e).then_inc(SEMS[s], kk))
        else:
            if inc is not None:
                self.cnt[inc] += k
                self.stream[eng].append(lambda e, f=fn, s=inc, kk=k: f(e).then_inc(SEMS[s], kk))
            else:
                self.stream[eng].append(lambda e, f=fn: f(e)) 

    def d(self, eng, out, in_, q, transpose=False, slow=False):
        st = self.qs[q]
        if not st["open"]:
            st["idx"] = (st["idx"] + 1) % self.NW
            i = st["idx"]
            if st["tot"][i] > 0:
                self.stream[eng].append(
                    lambda e, s=self._qsem(q, i), vv=st["tot"][i]: e.wait_ge(SEMS[s], vv))
            st["open"] = True
        i = st["idx"]
        st["tot"][i] += 16
        sname = self._qsem(q, i)
        if transpose and not slow:
            self.stream[eng].append(
                lambda e, o=out, i_=in_, s=sname: e.dma_start_transpose(o, i_).then_inc(SEMS[s], 16))
        elif transpose and slow:
            def _f(e, o=out, i_=in_, s=sname):
                with e.bass.allow_non_contiguous_dma(reason="tiny transpose"):
                    return e.dma_start_transpose(o, i_).then_inc(SEMS[s], 16)
            self.stream[eng].append(_f)
        else:
            self.stream[eng].append(
                lambda e, o=out, i_=in_, s=sname: e.dma_start(out=o, in_=i_, allow_slow_non_contiguous=slow).then_inc(SEMS[s], 16))

    def snap(self):
        return dict(self.cnt)

    def wall(self, eng, snap=None):
        for q in ("q0", "q1", "q2"):
            self.w(eng, q)
        sn = self.cnt if snap is None else snap
        for s, v in sn.items():
            if v > 0:
                self.w(eng, s, v)

    def barrier(self):
        m = {}
        for eng, sem in [("sync", "bs"), ("vector", "v"), ("scalar", "sc"),
                         ("gpsimd", "g"), ("tensor", "t")]:
            self.cnt[sem] += 1
            m[sem] = self.cnt[sem]
            self.stream[eng].append(lambda e, s=sem: e.sem_inc(SEMS[s], 1))
        for eng in ENGINES:
            for s, v in m.items():
                if s != {"sync": "bs", "vector": "v", "scalar": "sc",
                         "gpsimd": "g", "tensor": "t"}[eng] or True:
                    self.w(eng, s, v)
            for q in ("q0", "q1", "q2"):
                self.w(eng, q)
        if not hasattr(self, "marks"):
            self.marks = []
        self.marks.append({e: len(self.stream[e]) for e in ENGINES})


def build_program():
    nc = bass.Bass("TRN2", target_bir_lowering=False)
    P = Prog()
    ap = {}

    def IN(name, shape, dt=F32):
        ap[name] = nc.dram_tensor(name, list(shape), dt, kind="ExternalInput").ap()

    def OUT(name, shape, dt=F32):
        ap[name] = nc.dram_tensor(name, list(shape), dt, kind="ExternalOutput").ap()

    import os as _os2
    _dbg = set((_os2.environ.get("MK_DBG", "")).split(",")) - {""}

    def INT(name, shape, dt=F32, **kw):
        kind = "ExternalOutput" if name in _dbg else "Internal"
        if kw.get("addr_space"):
            kind = "Internal"
        ap[name] = nc.dram_tensor(name, list(shape), dt, kind=kind, **kw).ap()

    IN("x", (B, LLOC, D)); IN("xh", (B, 3, D))
    IN("W_in", (2 * DI, D)); IN("conv_w", (DI, 4)); IN("conv_b", (DI,))
    IN("W_xproj", (33, DI)); IN("A_log", (16,)); IN("D_param", (DI,))
    IN("W_out", (D, DI))
    IN("f0_wg", (4096, D)); IN("f0_wu", (4096, D)); IN("f0_wd", (D, 4096))
    IN("Wqkvb", (3104, D))
    IN("W_o", (D, D)); IN("ln_g", (D,)); IN("ln_b", (D,))
    IN("f1_wg", (2816, D)); IN("f1_wu", (2816, D)); IN("f1_wd", (D, 2816))
    IN("mask", (128, TC), BF16)
    IN("ident", (128, 128), BF16)
    IN("identf", (128, 128))
    IN("rmask", (NCORES,))
    OUT("out", (B, LLOC, D))

    INT("s_win", (2 * DI, D), BF16)
    INT("s_wxp", (48, DI), BF16)
    INT("s_wout", (D, DI), BF16)
    INT("s_f0wg", (4096, D), BF16); INT("s_f0wu", (4096, D), BF16); INT("s_f0wd", (D, 4096), BF16)
    INT("s_qkvb", (3104, D), BF16); INT("s_wo", (D, D), BF16)
    INT("s_f1wg", (2816, D), BF16); INT("s_f1wu", (2816, D), BF16); INT("s_f1wd", (D, 2816), BF16)
    INT("s_xn", (T, D), BF16)
    INT("s_xq", (T, D), BF16)
    INT("s_xcf", (DI, T), BF16)
    INT("s_sz", (T, DI), BF16)
    INT("s_y", (T, DI), BF16)
    INT("s_yg", (T, DI), BF16)
    INT("s_h2", (T, 4096), F32)
    INT("s_xq2", (T, 4096), BF16)
    INT("s_x1", (T, D), F32)
    INT("s_xn1", (T, D), BF16)
    INT("s_xq1", (T, D), BF16)
    INT("s_qkv", (T, 3104), BF16)
    INT("s_xq3", (T, 2816), BF16)
    INT("s_lay", (T, D), BF16)
    INT("s_s", (T, D), BF16)
    INT("s_g", (T, D), BF16)
    INT("s_g1", (T, D), BF16)
    INT("s_fb", (16, B * LLOC), BF16)
    INT("s_dbg", (128, B * LLOC), F32)
    INT("s_dbg2", (128, 64), F32)
    INT("s_dbg3", (128, 2560), BF16)
    INT("cc_in", (32, DI + 1), F32)
    INT("cc_out", (NCORES, 32, DI + 1), F32, addr_space="Shared")

    ctx = []
    SB = {}

    def sb(name, p, f, dt=F32):
        t = nc.sbuf_tensor(name, [p, f], dt)
        h = t.__enter__(); ctx.append(t); SB[name] = h
        return h

    msk = sb("msk", 128, TC, BF16)
    idn = sb("idn", 128, 128, BF16)
    idnf = sb("idnf", 128, 128)
    rmk = sb("rmk", 32, NCORES)
    Acol = sb("Acol", 16, 1)
    cbw = sb("cbw", 128, 16 * 4)
    cbb = sb("cbb", 128, 16)
    onec = sb("onec", 128, 2)
    sxinv = sb("sxinv", 128, NT)
    sx2i = sb("sx2i", 128, NT)
    sxg = sb("sxg", 128, NT); sxu = sb("sxu", 128, NT); sxd = sb("sxd", 128, NT)
    wsc = sb("wsc", 4, 8)
    mgr = sb("mgr", 128, 4)
    p_a = sb("p_a", 128, 64)
    p_b = sb("p_b", 128, 128)
    w_a = sb("w_a", 128, 4096)
    w_b = sb("w_b", 128, 4096)
    c_a = sb("c_a", 128, 4096, BF16)
    c_b = sb("c_b", 128, 4096, BF16)
    i_a = sb("i_a", 128, 4096, I16)
    xhn = sb("xhn", 8, D, BF16)
    xhh = sb("xhh", 8, D)
    xhf = sb("xhf", 128, 8 * 8, BF16)
    A16 = sb("A16", 128, 49152, BF16)
    A32 = sb("A32", 128, 11264)

    def a16(off, n):
        return A16[:, off:off + n]

    def a32(off, n):
        return A32[:, off:off + n]

    dprow = A16[:, 47104:49152]
    lngrow = A16[:, 16384:17408]
    lnbrow = A16[:, 17408:18432]

    xnfm = a16(0, 16384)
    xpfc = a16(16384, 8256)
    xcfc = a16(24640, 8192)
    winS = a16(32832, 2048)
    zwv = a16(34880, 4096)
    wxpT = a16(41024, 768)
    tails = a16(41856, 64)
    c_c = a16(43008, 4096)
    fC = a16(0, 2048); fB = a16(2048, 2048); fCc = a16(4096, 2048)
    btT = a16(6144, 2048)
    gsb = a16(7168, 16384)
    hstg = a16(23552, 2048)
    hinb = a16(25600, 2048)
    hinb2 = a16(6144, 2048)
    xtm = a16(27648, 4096)
    big1 = A16[:, 0:32768]
    bufA = A16[:, 32768:49152]
    wmov = bufA[:, 0:8192]
    wmov2 = bufA[:, 8192:16384]
    fL = A32[:, 0:2048]; fsc = A32[:, 2048:4096]; f0z = A32[:, 4096:5120]
    xdb = A32[:, 5120:7168]
    hcurb = [A32[:, 7168:9216], A32[:, 9216:11264]]; hin = A32[:, 0:2048]
    xdbB = A32[:, 5120:7168]
    xdbC = A32[:, 7168:9216]
    xdbV = A32[:, 9216:11264]
    fDst = p_b[:, 0:64]; fDch = p_b[:, 64:128]
    fzv = w_a[:, 2048:4096]

    PSG_t = nc.psum_tensor("psg", [128, 2048], F32)
    PSG = PSG_t.__enter__(); ctx.append(PSG_t)
    PSY_t = nc.psum_tensor("psy", [128, 2048], F32)
    PSY = PSY_t.__enter__(); ctx.append(PSY_t)
    PS = [PSG[:, 0:512], PSG[:, 512:1024], PSG[:, 1024:1536], PSG[:, 1536:2048],
          PSY[:, 0:512], PSY[:, 512:1024], PSY[:, 1024:1536], PSY[:, 1536:2048]]

    semnames = ["v", "sc", "g", "t", "cc", "bs"]
    for q in ("q0", "q1", "q2"):
        for i in range(Prog.NW):
            semnames.append(q + chr(ord("a") + i))
    for s in semnames:
        h = nc.semaphore(s)
        SEMS[s] = h.__enter__(); ctx.append(h)

    P.d(SY, msk[:, :], ap["mask"][:, :], "q0")
    P.d(SY, idn[:, :], ap["ident"][:, :], "q0")
    P.d(SY, idnf[:, :], ap["identf"][:, :], "q0")
    P.d(SY, rmk[:, :], ap["rmask"][None, :].broadcast_to([32, NCORES]), "q0")
    P.d(G, dprow[:, :], ap["D_param"][None, :].broadcast_to([128, DI]), "q2")
    P.d(SY, Acol[:, 0:1], ap["A_log"][:, None], "q0")
    P.d(SY, cbw[:, :].rearrange("p (k j) -> p k j", j=4),
        ap["conv_w"].rearrange("(k p) j -> p k j", p=128), "q0", slow=True)
    P.d(SY, cbb[:, :], ap["conv_b"].rearrange("(k p) -> p k", p=128), "q0", slow=True)
    P.wall(V); P.wall(S)
    P.r(V, lambda e: e.memset(onec[:, :], 1.0), inc="v")
    P.r(S, lambda e: e.activation(out=Acol[0:16, 0:1], in_=Acol[0:16, 0:1], func=AF.Exp), inc="sc")
    P.w(V, "sc")
    P.r(V, lambda e: e.tensor_scalar_mul(Acol[0:16, 0:1], Acol[0:16, 0:1], -1.0), inc="v")

    xr = ap["x"].rearrange("b (i p) d -> (b i) p d", p=128)
    scmark = {}
    for i in range(NT):
        o = (i % 2) * D
        sq = "q1" if i % 2 == 0 else "q2"
        pb = 4 * (i % 2)
        if i >= 2:
            P.w(SY, "sc", scmark[i - 2])
        else:
            P.w(SY, "v"); P.w(SY, "sc")
        P.d(SY, w_a[:, o:o + D], xr[i], "q0")
        mu = p_a[:, pb:pb + 1]; vv = p_a[:, pb + 1:pb + 2]; rr = p_a[:, pb + 2:pb + 3]; rmx = p_a[:, pb + 3:pb + 4]
        P.w(V, "q0"); P.w(V, sq)
        P.r(V, lambda e, mu=mu, o=o: e.reduce_sum(mu, w_a[:, o:o + D], AX.X))
        P.r(V, lambda e, mu=mu: e.tensor_scalar_mul(mu, mu, 1.0 / D), inc="v")
        P.w(S, "v")
        P.r(S, lambda e, o=o: e.activation(out=w_b[:, o:o + D], in_=w_a[:, o:o + D], func=AF.Square), inc="sc")
        P.w(V, "sc")
        P.r(V, lambda e, vv=vv, o=o: e.reduce_sum(vv, w_b[:, o:o + D], AX.X))
        P.r(V, lambda e, vv=vv: e.tensor_scalar_mul(vv, vv, 1.0 / D))
        P.r(V, lambda e, mu=mu, rr=rr: e.tensor_tensor(rr, mu, mu, OP.mult))
        P.r(V, lambda e, vv=vv, rr=rr: e.tensor_tensor(rr, vv, rr, OP.subtract))
        P.r(V, lambda e, rr=rr: e.tensor_scalar_add(rr, rr, 1e-5), inc="v")
        P.w(S, "v")
        P.r(S, lambda e, rr=rr: e.activation(out=rr, in_=rr, func=AF.Sqrt), inc="sc")
        P.w(V, "sc")
        P.r(V, lambda e, rr=rr: e.reciprocal(rr, rr))
        P.r(V, lambda e, mu=mu, rr=rr, o=o: e.tensor_scalar(w_a[:, o:o + D], w_a[:, o:o + D], mu, rr, OP.subtract, OP.mult))
        P.r(V, lambda e, o=o: e.tensor_copy(c_a[:, o:o + D], w_a[:, o:o + D]))
        P.r(V, lambda e, rmx=rmx, o=o: e.reduce_max(rmx, w_a[:, o:o + D], AX.X, apply_absolute_value=True))
        P.r(V, lambda e, rmx=rmx: e.tensor_scalar_max(rmx, rmx, 1e-5))
        P.r(V, lambda e, i=i, rmx=rmx: e.tensor_scalar_mul(sxinv[:, i:i + 1], rmx, 1.0 / 127.0))
        P.r(V, lambda e, i=i, rr=rr: e.reciprocal(rr, sxinv[:, i:i + 1]), inc="v")
        P.w(S, "v")
        P.r(S, lambda e, rr=rr, o=o: e.activation(out=w_b[:, o:o + D], in_=w_a[:, o:o + D], func=AF.Identity, scale=rr, bias=0.0), inc="sc")
        scmark[i] = P.cnt["sc"]
        P.w(V, "sc")
        P.r(V, lambda e, o=o: e.tensor_copy(i_a[:, o:o + D], w_b[:, o:o + D]))
        P.r(V, lambda e, o=o: e.tensor_copy(c_b[:, o:o + D], i_a[:, o:o + D]), inc="v")
        P.w(S, "v")
        P.d(S, ap["s_xn"][ts_(i, 128), :], c_a[:, o:o + D], sq)
        P.d(S, ap["s_xq"][ts_(i, 128), :], c_b[:, o:o + D], sq)
    P.w(SY, "q1"); P.w(V, "q1"); P.w(S, "q1")
    P.w(SY, "q2"); P.w(V, "q2"); P.w(S, "q2")

    P.r(V, lambda e: e.memset(xhh[:, :], 0.0), inc="v")
    P.wall(SY)
    P.d(SY, xhh[0:3, :], ap["xh"][0], "q0")
    P.d(SY, xhh[3:6, :], ap["xh"][1], "q0")
    P.wall(V); P.wall(S)
    hmu = p_a[0:8, 4:5]; hvv = p_a[0:8, 5:6]; hm2 = p_a[0:8, 6:7]
    P.r(V, lambda e: e.reduce_sum(hmu, xhh[:, :], AX.X))
    P.r(V, lambda e: e.tensor_scalar_mul(hmu, hmu, 1.0 / D), inc="v")
    P.w(S, "v")
    P.r(S, lambda e: e.activation(out=w_b[0:8, 0:D], in_=xhh[:, :], func=AF.Square), inc="sc")
    P.w(V, "sc")
    P.r(V, lambda e: e.reduce_sum(hvv, w_b[0:8, 0:D], AX.X))
    P.r(V, lambda e: e.tensor_scalar_mul(hvv, hvv, 1.0 / D))
    P.r(V, lambda e: e.tensor_tensor(hm2, hmu, hmu, OP.mult))
    P.r(V, lambda e: e.tensor_tensor(hvv, hvv, hm2, OP.subtract))
    P.r(V, lambda e: e.tensor_scalar_add(hvv, hvv, 1e-5), inc="v")
    P.w(S, "v")
    P.r(S, lambda e: e.activation(out=hvv, in_=hvv, func=AF.Sqrt), inc="sc")
    P.w(V, "sc")
    P.r(V, lambda e: e.reciprocal(hvv, hvv))
    P.r(V, lambda e: e.tensor_scalar(xhh[:, :], xhh[:, :], hmu, hvv, OP.subtract, OP.mult))
    P.r(V, lambda e: e.tensor_copy(xhn[:, :], xhh[:, :]), inc="v")
    P.wall(TE)
    for k in range(8):
        P.r(TE, lambda e, k=k: e.transpose(PSY[0:128, k * 8:k * 8 + 8], xhh[0:8, ts_(k, 128)], idnf[0:8, 0:8]), inc="t")
    P.w(V, "t")
    P.r(V, lambda e: e.tensor_copy(xhf[:, 0:64], PSY[:, 0:64]), inc="v")
    P.w(TE, "v")

    def cast_rows(src_ap, dst_ap, nrows, ncols):
        nt = (nrows + 127) // 128
        vmark = {}
        for r in range(nt):
            p = min(128, nrows - r * 128)
            wsrc = w_a if r % 2 == 0 else w_b
            cdst = c_a if r % 2 == 0 else c_b
            sq = "q1" if r % 2 == 0 else "q2"
            if r >= 2:
                P.w(SY, "v", vmark[r - 2])
            else:
                P.w(SY, "v"); P.w(SY, "sc")
            P.d(SY, wsrc[0:p, 0:ncols], src_ap[r * 128:r * 128 + p, :], "q0")
            P.w(V, "q0")
            if r < 2:
                P.w(V, "q1")
            P.w(V, sq)
            P.r(V, lambda e, p=p, n=ncols, a=wsrc, b=cdst: e.tensor_copy(b[0:p, 0:n], a[0:p, 0:n]), inc="v")
            vmark[r] = P.cnt["v"]
            P.w(S, "v", vmark[r])
            P.d(S, dst_ap[r * 128:r * 128 + p, :], cdst[0:p, 0:ncols], sq)
        P.w(SY, "q1"); P.w(V, "q1"); P.w(S, "q1")
        P.w(SY, "q2"); P.w(V, "q2"); P.w(S, "q2")
    cast_rows(ap["W_in"], ap["s_win"], 2 * DI, D)
    cast_rows(ap["W_xproj"], ap["s_wxp"], 33, DI)
    P.r(V, lambda e: e.memset(c_a[0:15, 0:DI], 0.0), inc="v")
    P.w(S, "v")
    P.d(S, ap["s_wxp"][33:48, :], c_a[0:15, 0:DI], "q1")
    P.w(SY, "q1"); P.w(V, "q1"); P.w(S, "q1")
    cast_rows(ap["W_out"], ap["s_wout"], D, DI)
    cast_rows(ap["Wqkvb"], ap["s_qkvb"], 3104, D)
    cast_rows(ap["W_o"], ap["s_wo"], D, D)

    def f0_quant(src, dst, nrows, ncols, scidx):
        nt = nrows // 128
        P.r(V, lambda e: e.memset(p_a[:, 8:9], 0.0), inc="v")
        amark = {}
        for r in range(nt):
            wsrc = w_a if r % 2 == 0 else w_b
            if r >= 2:
                P.w(SY, "v", amark[r - 2])
            else:
                P.w(SY, "v"); P.w(SY, "sc")
            P.d(SY, wsrc[:, 0:ncols], src[ts_(r, 128), :], "q0")
            P.w(V, "q0")
            P.r(V, lambda e, n=ncols, a=wsrc: e.reduce_sum(p_a[:, 9:10], a[:, 0:n], AX.X, apply_absolute_value=True))
            P.r(V, lambda e: e.tensor_tensor(p_a[:, 8:9], p_a[:, 8:9], p_a[:, 9:10], OP.add), inc="v")
            amark[r] = P.cnt["v"]
        P.wall(TE)
        P.r(TE, lambda e: e.matmul(PS[7][0:1, 0:1], p_a[:, 8:9], onec[:, 0:1], start=True, stop=True), inc="t")
        P.w(V, "t")
        P.r(V, lambda e, n=nrows * ncols: e.tensor_scalar(wsc[0:1, 0:1], PS[7][0:1, 0:1], 1.0 / n, 1e-5, OP.mult, OP.max))
        P.r(V, lambda e: e.tensor_scalar_mul(wsc[0:1, 1:2], wsc[0:1, 0:1], 0.5), inc="v")
        P.r(V, lambda e: e.memset(w_b[0:1, 0:128], 1.0), inc="v")
        P.w(TE, "v")
        P.r(TE, lambda e: e.matmul(PS[7][:, 4:5], w_b[0:1, 0:128], wsc[0:1, 1:2], start=True, stop=True), inc="t")
        P.r(TE, lambda e, scidx=scidx: e.matmul(PS[7][:, 8:9], w_b[0:1, 0:128], wsc[0:1, 0:1], start=True, stop=True), inc="t")
        P.w(V, "t")
        P.r(V, lambda e: e.tensor_copy(p_a[:, 10:11], PS[7][:, 4:5]))
        P.r(V, lambda e: e.tensor_scalar_mul(p_a[:, 11:12], p_a[:, 10:11], -1.0))
        P.r(V, lambda e, scidx=scidx: e.tensor_copy(mgr[:, scidx:scidx + 1], PS[7][:, 8:9]), inc="v")
        vmark = {}
        gq0 = A16[:, 7168:11264]; gq1 = A16[:, 11264:15360]
        for r in range(nt):
            wsrc = w_a if r % 2 == 0 else w_b
            ga = c_a if r % 2 == 0 else gq0
            gb = c_b if r % 2 == 0 else gq1
            sq = "q1" if r % 2 == 0 else "q2"
            if r >= 2:
                P.w(SY, "v", vmark[r - 2])
            else:
                P.w(SY, "v")
            P.d(SY, wsrc[:, 0:ncols], src[ts_(r, 128), :], "q0")
            P.w(V, "q0"); P.w(V, sq)
            P.r(V, lambda e, n=ncols, a=wsrc, g=ga: e.tensor_scalar(g[:, 0:n], a[:, 0:n], p_a[:, 10:11], None, OP.is_gt))
            P.r(V, lambda e, n=ncols, a=wsrc, g=gb: e.tensor_scalar(g[:, 0:n], a[:, 0:n], p_a[:, 11:12], None, OP.is_lt))
            P.r(V, lambda e, n=ncols, g=ga, h=gb: e.tensor_tensor(g[:, 0:n], g[:, 0:n], h[:, 0:n], OP.subtract), inc="v")
            vmark[r] = P.cnt["v"]
            P.w(S, "v", vmark[r])
            P.d(S, dst[ts_(r, 128), :], ga[:, 0:ncols], sq)
        P.w(SY, "q1"); P.w(V, "q1"); P.w(S, "q1")
        P.w(SY, "q2"); P.w(V, "q2"); P.w(S, "q2")
    f0_quant(ap["f0_wg"], ap["s_f0wg"], 4096, D, 0)
    f0_quant(ap["f0_wu"], ap["s_f0wu"], 4096, D, 1)
    f0_quant(ap["f0_wd"], ap["s_f0wd"], D, 4096, 2)

    def fp4_on_tile(p, ncols, W=None, U=None, CA=None, CB=None):
        W = w_a if W is None else W
        U = w_b if U is None else U
        CA = c_a if CA is None else CA
        CB = c_b if CB is None else CB
        nb = ncols // 16
        a3 = W[0:p, 0:ncols].rearrange("p (n s) -> p n s", s=16)
        bm = U[0:p, 3500:3500 + nb]
        rs = U[0:p, 3200:3200 + nb]
        P.r(V, lambda e, a3=a3, bm=bm: e.tensor_reduce(bm, a3, AX.X, OP.max, apply_absolute_value=True))
        P.r(V, lambda e, bm=bm: e.tensor_scalar(bm, bm, 1.0 / 6.0, 1e-12, OP.mult, OP.max))
        P.r(V, lambda e, rs=rs, bm=bm: e.reciprocal(rs, bm), inc="v")
        rsb = rs[:, :, None].broadcast_to([p, nb, 16])
        P.r(V, lambda e, a3=a3, rsb=rsb: e.tensor_tensor(a3, a3, rsb, OP.mult), inc="v")
        P.w(S, "v")
        P.r(S, lambda e, p=p, n=ncols, W=W, U=U: e.activation(out=U[0:p, 0:n], in_=W[0:p, 0:n], func=AF.Abs, scale=2.0), inc="sc")
        P.w(V, "sc")
        P.r(V, lambda e, p=p, n=ncols, U=U: e.tensor_copy(i_a[0:p, 0:n], U[0:p, 0:n]))
        P.r(V, lambda e, p=p, n=ncols, CA=CA: e.tensor_copy(CA[0:p, 0:n], i_a[0:p, 0:n]))
        P.r(V, lambda e, p=p, n=ncols, CA=CA: e.tensor_scalar(CA[0:p, 0:n], CA[0:p, 0:n], 4.0, 0.5, OP.min, OP.mult), inc="v")
        for th in (5.0, 7.0):
            P.r(V, lambda e, p=p, n=ncols, th=th, U=U, CB=CB: e.tensor_scalar(CB[0:p, 0:n], U[0:p, 0:n], th, None, OP.is_gt))
            P.r(V, lambda e, p=p, n=ncols, CA=CA, CB=CB: e.tensor_tensor(CA[0:p, 0:n], CA[0:p, 0:n], CB[0:p, 0:n], OP.add))
        P.r(V, lambda e, p=p, n=ncols, U=U, CB=CB: e.tensor_scalar(CB[0:p, 0:n], U[0:p, 0:n], 10.0, 2.0, OP.is_gt, OP.mult))
        P.r(V, lambda e, p=p, n=ncols, CA=CA, CB=CB: e.tensor_tensor(CA[0:p, 0:n], CA[0:p, 0:n], CB[0:p, 0:n], OP.add), inc="v")
        P.w(S, "v")
        P.r(S, lambda e, p=p, n=ncols, W=W, CB=CB: e.activation(out=CB[0:p, 0:n], in_=W[0:p, 0:n], func=AF.Sign), inc="sc")
        P.w(V, "sc")
        P.r(V, lambda e, p=p, n=ncols, CA=CA, CB=CB: e.tensor_tensor(CA[0:p, 0:n], CA[0:p, 0:n], CB[0:p, 0:n], OP.mult))
        scb = bm[:, :, None].broadcast_to([p, nb, 16])
        ca3 = CA[0:p, 0:ncols].rearrange("p (n s) -> p n s", s=16)
        P.r(V, lambda e, ca3=ca3, scb=scb: e.tensor_tensor(ca3, ca3, scb, OP.mult), inc="v")

    def f1_quant(src, dst, nrows, ncols):
        nt = (nrows + 127) // 128
        fmark = {}
        for r in range(nt):
            p = min(128, nrows - r * 128)
            Wb = w_a if r % 2 == 0 else A32[:, 0:4096]
            Ub = w_b if r % 2 == 0 else A32[:, 4096:8192]
            CAb = c_a if r % 2 == 0 else A16[:, 7168:11264]
            CBb = c_b if r % 2 == 0 else A16[:, 11264:15360]
            sq = "q1" if r % 2 == 0 else "q2"
            if r >= 2:
                P.w(SY, "sc", fmark[r - 2])
            elif r == 0:
                P.w(SY, "v"); P.w(SY, "sc")
            P.d(SY, Wb[0:p, 0:ncols], src[r * 128:r * 128 + p, :], "q0")
            P.w(V, "q0"); P.w(V, sq)
            fp4_on_tile(p, ncols, W=Wb, U=Ub, CA=CAb, CB=CBb)
            fmark[r] = P.cnt["sc"]
            P.w(S, "v")
            P.d(S, dst[r * 128:r * 128 + p, :], CAb[0:p, 0:ncols], sq)
        P.w(SY, "q1"); P.w(V, "q1"); P.w(S, "q1")
        P.w(SY, "q2"); P.w(V, "q2"); P.w(S, "q2")
    f1_quant(ap["f1_wg"], ap["s_f1wg"], 2816, D)
    f1_quant(ap["f1_wu"], ap["s_f1wu"], 2816, D)
    f1_quant(ap["f1_wd"], ap["s_f1wd"], D, 2816)

    P.r(V, lambda e: e.tensor_scalar(sxg[:, :], sxinv[:, :], mgr[:, 0:1], None, OP.mult))
    P.r(V, lambda e: e.tensor_scalar(sxu[:, :], sxinv[:, :], mgr[:, 1:2], None, OP.mult), inc="v")
    if "s_dbg2" in _dbg:
        P.w(S, "v")
        P.d(S, ap["s_dbg2"][:, 0:NT], sxg[:, :], "q1")
        P.d(S, ap["s_dbg2"][:, 16:16 + NT], sxu[:, :], "q1")
        P.d(S, ap["s_dbg2"][:, 32:32 + NT], sxinv[:, :], "q1")
        P.d(S, ap["s_dbg2"][:, 48:52], mgr[:, 0:4], "q1")
        P.w(SY, "q1"); P.w(V, "q1"); P.w(S, "q1")
    P.barrier()

    for k in range(8):
        P.d(SY, xnfm[:, ts_(k, T)], ap["s_xn"][:, ts_(k, 128)], "q0", transpose=True)
    for k in range(16):
        P.d(SY, wxpT[:, k * 48:k * 48 + 48], ap["s_wxp"][:, ts_(k, 128)], "q0", transpose=True)
    for b in range(B):
        for lc in range(2):
            t0 = b * LLOC + lc * 512
            for ot in range(16):
                wbuf = winS[:, (ot % 2) * 1024:(ot % 2) * 1024 + 1024]
                P.w(SY, "t")
                for k in range(8):
                    P.d(SY, wbuf[:, k * 128:(k + 1) * 128], ap["s_win"][ts_(ot, 128), ts_(k, 128)], "q0", transpose=True)
                P.wall(TE)
                for k in range(8):
                    st = wbuf[:, k * 128:(k + 1) * 128]
                    mv = xnfm[:, k * T + t0: k * T + t0 + 512]
                    P.r(TE, lambda e, st=st, mv=mv, k=k:
                        e.matmul(PS[0][:, 0:512], st, mv, start=(k == 0), stop=(k == 7)),
                        inc=("t" if k == 7 else None))
                if lc == 0:
                    for k in range(8):
                        st = wbuf[:, k * 128:(k + 1) * 128]
                        mv = xhf[:, k * 8 + b * 3: k * 8 + b * 3 + 3]
                        P.r(TE, lambda e, st=st, mv=mv, k=k:
                            e.matmul(PS[1][:, 0:3], st, mv, start=(k == 0), stop=(k == 7)),
                            inc=("t" if k == 7 else None))
                P.w(V, "t")
                col0 = ot * 516
                if lc == 0:
                    P.r(V, lambda e, col0=col0: e.tensor_copy(xpfc[:, col0:col0 + 3], PS[1][:, 0:3]))
                else:
                    P.r(V, lambda e, col0=col0, ot=ot: e.tensor_copy(xpfc[:, col0:col0 + 3], tails[:, ot * 4:ot * 4 + 3]))
                P.r(V, lambda e, col0=col0: e.tensor_copy(xpfc[:, col0 + 3:col0 + 515], PS[0][:, 0:512]))
                P.r(V, lambda e, ot=ot: e.tensor_copy(tails[:, ot * 4:ot * 4 + 3], PS[0][:, 509:512]), inc="v")
                P.w(TE, "v")
            for ot in range(16):
                col0 = ot * 516
                P.r(V, lambda e, col0=col0, ot=ot:
                    e.tensor_scalar(w_a[:, 0:512], xpfc[:, col0:col0 + 512], cbw[:, 4 * ot:4 * ot + 1], cbb[:, ot:ot + 1], OP.mult, OP.add))
                for j in (1, 2, 3):
                    P.r(V, lambda e, col0=col0, ot=ot, j=j:
                        e.scalar_tensor_tensor(w_a[:, 0:512], xpfc[:, col0 + j:col0 + j + 512], cbw[:, 4 * ot + j:4 * ot + j + 1], w_a[:, 0:512], OP.mult, OP.add), inc="v")
                P.w(S, "v")
                P.r(S, lambda e, ot=ot: e.activation(out=xcfc[:, ts_(ot, 512)], in_=w_a[:, 0:512], func=AF.Silu), inc="sc")
                P.w(V, "sc")
            P.wall(S)
            for ot in range(16):
                P.d(S, ap["s_xcf"][ts_(ot, 128), t0:t0 + 512], xcfc[:, ts_(ot, 512)], "q1")
            P.w(S, "q1"); P.w(V, "q1")
            P.wall(TE)
            for k in range(16):
                stB = wxpT[:, k * 48:k * 48 + 16]
                stC = wxpT[:, k * 48 + 16:k * 48 + 32]
                stV = wxpT[:, k * 48 + 32:k * 48 + 33]
                mv = xcfc[:, ts_(k, 512)]
                P.r(TE, lambda e, st=stB, mv=mv, k=k:
                    e.matmul(PSY[0:16, 0:512], st, mv, start=(k == 0), stop=(k == 15)))
                P.r(TE, lambda e, st=stC, mv=mv, k=k:
                    e.matmul(PSY[0:16, 512:1024], st, mv, start=(k == 0), stop=(k == 15)))
                P.r(TE, lambda e, st=stV, mv=mv, k=k:
                    e.matmul(PSY[0:1, 1024:1536], st, mv, start=(k == 0), stop=(k == 15)),
                    inc=("t" if k == 15 else None))
            P.w(V, "t")
            P.r(V, lambda e, t0=t0: e.tensor_copy(xdbB[0:16, t0:t0 + 512], PSY[0:16, 0:512]))
            P.r(V, lambda e, t0=t0: e.tensor_copy(xdbC[0:16, t0:t0 + 512], PSY[0:16, 512:1024]))
            P.r(V, lambda e, t0=t0: e.tensor_copy(xdbV[0:1, t0:t0 + 512], PSY[0:1, 1024:1536]), inc="v")
            P.w(TE, "v")
    for zc in range(4):
        P.w(SY, "t")
        for k in range(8):
            P.d(SY, zwv[:, k * 512:(k + 1) * 512], ap["s_win"][2048 + zc * 512:2048 + (zc + 1) * 512, ts_(k, 128)], "q0", transpose=True)
        P.wall(TE)
        for i in range(NT):
            ps = PS[4 + (i % 2)]
            for k in range(8):
                st = xnfm[:, off_(k * T, i, 128)]
                mv = zwv[:, k * 512:(k + 1) * 512]
                P.r(TE, lambda e, ps=ps, st=st, mv=mv, k=k:
                    e.matmul(ps[:, 0:512], st, mv, start=(k == 0), stop=(k == 7)),
                    inc=("t" if k == 7 else None))
            P.w(S, "t")
            P.w(S, "q1")
            P.r(S, lambda e, ps=ps: e.activation(out=c_a[:, 0:512], in_=ps[:, 0:512], func=AF.Silu), inc="sc")
            P.w(S, "sc")
            P.d(S, ap["s_sz"][ts_(i, 128), zc * 512:(zc + 1) * 512], c_a[:, 0:512], "q1")
            P.w(TE, "sc")
    P.barrier()

    P.r(S, lambda e: e.activation(out=fsc[0:1, :], in_=xdbV[0:1, 0:T], func=AF.Exp), inc="sc")
    P.w(S, "sc")
    P.r(S, lambda e: e.activation(out=fsc[0:1, :], in_=fsc[0:1, :], func=AF.Ln, bias=1.0), inc="sc")
    P.w(V, "sc")
    P.r(V, lambda e: e.tensor_scalar(fsc[0:1, :], fsc[0:1, :], 0.001, 0.1, OP.max, OP.min), inc="v")
    P.r(V, lambda e: e.memset(w_b[0:1, 100:116], 1.0), inc="v")
    P.w(TE, "v")
    for half in range(4):
        P.r(TE, lambda e, half=half: e.matmul(PS[0][0:16, 0:512], w_b[0:1, 100:116],
                                              fsc[0:1, half * 512:(half + 1) * 512], start=True, stop=True), inc="t")
        P.w(V, "t")
        P.r(V, lambda e, half=half: e.tensor_copy(fzv[0:16, half * 512:(half + 1) * 512], PS[0][0:16, 0:512]), inc="v")
        P.w(TE, "v")
    P.r(V, lambda e: e.tensor_scalar(fsc[0:16, :], fzv[0:16, :], Acol[0:16, 0:1], None, OP.mult))
    P.r(V, lambda e: e.tensor_scalar(fsc[0:16, :], fsc[0:16, :], -20.0, 0.0, OP.max, OP.min), inc="v")
    P.r(V, lambda e: e.tensor_tensor(fL[0:16, :], fzv[0:16, :], xdbB[0:16, 0:T], OP.mult))
    P.r(V, lambda e: e.tensor_scalar(fL[0:16, :], fL[0:16, :], -10.0, 10.0, OP.max, OP.min))
    P.r(V, lambda e: e.tensor_copy(fzv[0:16, :], fL[0:16, :]), inc="v")
    P.r(V, lambda e: e.memset(f0z[:, :], 0.0), inc="v")
    for b in range(B):
        P.r(V, lambda e, b=b: e.tensor_tensor_scan(
            fL[0:16, b * LLOC:(b + 1) * LLOC], fsc[0:16, b * LLOC:(b + 1) * LLOC],
            f0z[0:16, :], 0.0, OP.add, OP.add), inc="v")
    P.barrier()
    P.r(V, lambda e: e.memset(fDst[:, :], 0.0), inc="v")
    for b in range(B):
        src3 = fL[0:16, b * LLOC:(b + 1) * LLOC].rearrange("p (n c) -> p n c", c=TC)[:, 0:NCH - 1, TC - 1:TC]
        dst3 = fDst[0:16, b * NCH + 1:(b + 1) * NCH][:, :, None]
        P.r(V, lambda e, s=src3, d=dst3: e.tensor_copy(d, s), inc="v")
    for b in range(B):
        P.r(V, lambda e, b=b: e.tensor_tensor(
            fsc[0:16, b * LLOC:(b + 1) * LLOC].rearrange("p (n c) -> p n c", c=TC),
            fL[0:16, b * LLOC:(b + 1) * LLOC].rearrange("p (n c) -> p n c", c=TC),
            fDst[0:16, b * NCH:(b + 1) * NCH][:, :, None].broadcast_to([16, NCH, TC]),
            OP.subtract), inc="v")
    P.w(S, "v")
    P.r(S, lambda e: e.activation(out=w_a[0:16, 0:T], in_=fsc[0:16, :], func=AF.Exp), inc="sc")
    P.r(S, lambda e: e.activation(out=w_b[0:16, 0:T], in_=fsc[0:16, :], func=AF.Exp, scale=-1.0), inc="sc")
    P.w(V, "sc")
    P.r(V, lambda e: e.tensor_tensor(fC[0:16, :], xdbC[0:16, 0:T], w_a[0:16, 0:T], OP.mult))
    P.r(V, lambda e: e.tensor_tensor(fB[0:16, :], fzv[0:16, :], w_b[0:16, 0:T], OP.mult), inc="v")
    for b in range(B):
        src3 = w_a[0:16, b * LLOC:(b + 1) * LLOC].rearrange("p (n c) -> p n c", c=TC)[:, :, TC - 1:TC]
        dst3 = fDch[0:16, b * NCH:(b + 1) * NCH][:, :, None]
        P.r(V, lambda e, s=src3, d=dst3: e.tensor_copy(d, s), inc="v")
    P.w(S, "v")
    P.r(S, lambda e: e.activation(out=w_b[0:16, 3000:3000 + B * NCH], in_=fDst[0:16, :], func=AF.Exp), inc="sc")
    P.w(V, "sc")
    for b in range(B):
        P.r(V, lambda e, b=b: e.tensor_tensor(
            fCc[0:16, b * LLOC:(b + 1) * LLOC].rearrange("p (n c) -> p n c", c=TC),
            fC[0:16, b * LLOC:(b + 1) * LLOC].rearrange("p (n c) -> p n c", c=TC),
            w_b[0:16, 3000 + b * NCH:3000 + (b + 1) * NCH][:, :, None].broadcast_to([16, NCH, TC]),
            OP.mult), inc="v")
    P.barrier()
    if "s_dbg" in _dbg:
        P.w(S, "v"); P.w(S, "sc")
        P.d(S, ap["s_dbg"][0:16, :], fL[0:16, :], "q1")
        P.d(S, ap["s_dbg"][16:32, :], fsc[0:16, :], "q1")
        P.d(S, ap["s_dbg"][32:48, 0:B * NCH], fDst[0:16, :], "q1")
        P.d(S, ap["s_dbg"][48:64, 0:B * NCH], fDch[0:16, :], "q1")
        P.d(S, ap["s_dbg"][64:80, :], xdbB[0:16, 0:T], "q1")
        P.d(S, ap["s_dbg"][80:96, :], xdbC[0:16, 0:T], "q1")
        P.d(S, ap["s_dbg"][96:112, :], fzv[0:16, 0:T], "q1")
        P.d(S, ap["s_dbg"][112:128, :], w_b[0:16, 0:T], "q1")
        P.w(SY, "q1"); P.w(V, "q1"); P.w(S, "q1")
    P.w(S, "v")
    P.d(S, ap["s_fb"][:, :], fB[0:16, :], "q1")
    P.w(SY, "q1"); P.w(V, "q1"); P.w(S, "q1")
    P.barrier()
    for b in range(B):
        for gid in range(NG):
            g_ = b * NG + gid
            dst = btT[:, g_ * 128:g_ * 128 + 16]
            src = ap["s_fb"][0:16, b * LLOC + gid * 128:(b * LLOC) + (gid + 1) * 128]
            P.d(SY, dst, src, "q0", transpose=True)
            P.w(SY, "q0")
            if g_ == 0:
                P.barrier()
    P.barrier()

    P.r(V, lambda e: e.memset(hcurb[0][0:16, :], 0.0))
    P.r(V, lambda e: e.memset(hcurb[1][0:16, :], 0.0), inc="v")
    for b in range(B):
        for gid in range(NG):
            xbuf = xtm[:, (gid % 2) * DI:((gid % 2) + 1) * DI]
            P.w(SY, "t")
            P.d(SY, xbuf, ap["s_xcf"][:, b * LLOC + gid * 128: b * LLOC + (gid + 1) * 128], "q0", transpose=True)
            P.wall(TE)
            for c in range(4):
                n = gid * 4 + c
                blk = (b * NG + gid) * 4 + c
                stm = fB[0:16, b * LLOC + n * TC: b * LLOC + (n + 1) * TC]
                mvm = fC[0:16, b * LLOC + n * TC: b * LLOC + (n + 1) * TC]
                P.r(TE, lambda e, stm=stm, mvm=mvm, c=c:
                    e.matmul(PSG[32 * c:32 * c + 32, 0:32], stm, mvm, start=True, stop=True,
                             tile_position=(0, 32 * c)), inc="t")
                P.w(V, "t")
                mcb = c_b[32 * c:32 * c + 32, (n % 2) * TC:((n % 2) + 1) * TC]
                P.r(V, lambda e, mcb=mcb, c=c: e.tensor_tensor(mcb, PSG[32 * c:32 * c + 32, 0:32], msk[32 * c:32 * c + 32, :], OP.mult), inc="v")
                P.w(TE, "v")
                for dc in range(4):
                    stg = btT[32 * c:32 * c + 32, (b * NG + gid) * 128:(b * NG + gid) * 128 + 16]
                    mvx = xbuf[32 * c:32 * c + 32, dc * 512:(dc + 1) * 512]
                    P.r(TE, lambda e, stg=stg, mvx=mvx, c=c, dc=dc:
                        e.matmul(PSG[0:16, dc * 512:(dc + 1) * 512], stg, mvx, start=True, stop=True,
                                 tile_position=(32 * c, 0)),
                        inc=("t" if dc == 3 else None))
                P.w(V, "t")
                P.r(V, lambda e, b=b: e.tensor_copy(hstg[0:16, :], hcurb[b][0:16, :]), inc="v")
                P.r(V, lambda e, b=b: e.tensor_tensor(w_a[0:16, 0:DI], hcurb[b][0:16, :], PSG[0:16, 0:DI], OP.add))
                P.r(V, lambda e, b=b, n=n: e.tensor_scalar(hcurb[b][0:16, :], w_a[0:16, 0:DI], fDch[0:16, b * NCH + n:b * NCH + n + 1], None, OP.mult), inc="v")
                P.w(TE, "v")
                for dc in range(4):
                    mvx = xbuf[32 * c:32 * c + 32, dc * 512:(dc + 1) * 512]
                    stc = fC[0:16, b * LLOC + n * TC: b * LLOC + (n + 1) * TC]
                    mvh = hstg[0:16, dc * 512:(dc + 1) * 512]
                    P.r(TE, lambda e, mcb=mcb, mvx=mvx, c=c, dc=dc:
                        e.matmul(PSY[32 * c:32 * c + 32, dc * 512:(dc + 1) * 512], mcb, mvx, start=True, stop=False,
                                 tile_position=(32 * c, 32 * c)))
                    P.r(TE, lambda e, stc=stc, mvh=mvh, c=c, dc=dc:
                        e.matmul(PSY[32 * c:32 * c + 32, dc * 512:(dc + 1) * 512], stc, mvh, start=False, stop=True,
                                 tile_position=(0, 32 * c)),
                        inc=("t" if dc == 3 else None))
            P.w(V, "t"); P.w(V, "q1")
            P.r(V, lambda e: e.tensor_copy(c_a[:, 0:DI], PSY[:, 0:DI]), inc="v")
            P.w(S, "v")
            P.d(S, ap["s_y"][ts_(b * NG + gid, 128), :], c_a[:, 0:DI], "q1")
            P.w(TE, "q1"); P.w(V, "q1"); P.w(S, "q1")
    P.w(S, "v")
    P.r(S, lambda e: e.activation(out=p_a[0:16, 16:17], in_=fL[0:16, LLOC - 1:LLOC], func=AF.Exp), inc="sc")
    P.r(S, lambda e: e.activation(out=p_a[0:16, 17:18], in_=fL[0:16, 2 * LLOC - 1:2 * LLOC], func=AF.Exp), inc="sc")
    P.wall(SY)
    P.d(SY, ap["cc_in"][0:16, 0:DI], hcurb[0][0:16, :], "q0")
    P.d(SY, ap["cc_in"][16:32, 0:DI], hcurb[1][0:16, :], "q0")
    P.d(SY, ap["cc_in"][0:16, DI:DI + 1], p_a[0:16, 16:17], "q0", slow=True)
    P.d(SY, ap["cc_in"][16:32, DI:DI + 1], p_a[0:16, 17:18], "q0", slow=True)
    P.w(G, "q0")
    P.r(G, lambda e: e.collective_compute("AllGather", OP.bypass, replica_groups=[list(range(NCORES))],
                                          ins=[ap["cc_in"][:]], outs=[ap["cc_out"][:]]), inc="cc")
    P.w(SY, "cc")
    P.wall(V)
    P.r(V, lambda e: e.memset(hin[0:32, :], 0.0), inc="v")
    for r in range(NCORES - 1):
        P.w(V, "q1")
        P.wall(SY)
        P.d(SY, w_b[0:32, 0:DI + 1], ap["cc_out"][r], "q0")
        P.wall(V)
        Dr = w_b[0:32, DI:DI + 1]
        Gr = w_b[0:32, 0:DI]
        mr = rmk[:, r + 1:r + 2]
        P.r(V, lambda e, Dr=Dr: e.tensor_scalar(p_a[0:32, 20:21], Dr, -1.0, None, OP.add))
        P.r(V, lambda e, mr=mr: e.scalar_tensor_tensor(p_a[0:32, 21:22], p_a[0:32, 20:21], mr, onec[0:32, 0:1], OP.mult, OP.add), inc="v")
        P.r(V, lambda e: e.tensor_scalar(hin[0:32, :], hin[0:32, :], p_a[0:32, 21:22], None, OP.mult), inc="v")
        P.r(V, lambda e, Gr=Gr, mr=mr: e.scalar_tensor_tensor(hin[0:32, :], Gr, mr, hin[0:32, :], OP.mult, OP.add), inc="v")
    P.r(V, lambda e: e.tensor_copy(hinb[0:16, 0:DI], hin[0:16, :]), inc="v")
    P.wall(SY)
    P.d(SY, ap["cc_in"][0:16, 0:DI], hin[16:32, :], "q0")
    P.w(G, "q0")
    P.d(G, hinb2[0:16, 0:DI], ap["cc_in"][0:16, 0:DI], "q2")
    P.barrier()

    for i in range(NT):
        b = i // 8; j = i % 8
        P.wall(TE)
        for dc in range(4):
            stc = fCc[0:16, b * LLOC + j * 128:(b * LLOC + j * 128) + 128]
            hb = hinb if b == 0 else hinb2
            mvh = hb[0:16, dc * 512:(dc + 1) * 512]
            P.r(TE, lambda e, stc=stc, mvh=mvh, dc=dc:
                e.matmul(PS[dc][:, 0:512], stc, mvh, start=True, stop=True), inc="t")
        P.d(SY, c_a[:, 0:DI], ap["s_y"][ts_(i, 128), :], "q0")
        P.d(SY, c_b[:, 0:DI], ap["s_sz"][ts_(i, 128), :], "q0")
        P.d(SY, c_c[:, 0:DI], ap["s_xcf"][:, 0:0 + 0] if False else c_c[:, 0:DI], "q0") if False else None
        P.d(SY, c_c[:, 0:DI], ap["s_xcf"][:, b * LLOC + j * 128: b * LLOC + (j + 1) * 128], "q0", transpose=True)
        P.wall(V)
        for dc in range(4):
            P.r(V, lambda e, dc=dc: e.tensor_tensor(w_a[:, dc * 512:(dc + 1) * 512], c_a[:, dc * 512:(dc + 1) * 512], PS[dc][:, 0:512], OP.add), inc="v")
        P.w(TE, "v")
        P.r(V, lambda e: e.tensor_tensor(w_a[:, 0:DI], w_a[:, 0:DI], c_b[:, 0:DI], OP.mult), inc="v")
        P.r(V, lambda e: e.tensor_tensor(c_c[:, 0:DI], c_c[:, 0:DI], dprow[:, :], OP.mult))
        P.r(V, lambda e: e.tensor_tensor(c_a[:, 0:DI], w_a[:, 0:DI], c_c[:, 0:DI], OP.add), inc="v")
        P.w(S, "v")
        P.d(S, ap["s_yg"][ts_(i, 128), :], c_a[:, 0:DI], "q1")
        P.w(SY, "q1"); P.w(V, "q1"); P.w(S, "q1")
    P.barrier()

    ygfm = big1
    woT = bufA
    for k in range(16):
        P.d(SY, ygfm[:, ts_(k, T)], ap["s_yg"][:, ts_(k, 128)], "q0", transpose=True)
        P.d(SY, woT[:, ts_(k, D)], ap["s_wout"][:, ts_(k, 128)], "q0", transpose=True)
    P.wall(TE)
    for i in range(NT):
        for mc in range(2):
            ps = PS[mc]
            for k in range(16):
                st = ygfm[:, off_(k * T, i, 128)]
                mv = woT[:, k * D + mc * 512: k * D + mc * 512 + 512]
                P.r(TE, lambda e, ps=ps, st=st, mv=mv, k=k:
                    e.matmul(ps[:, 0:512], st, mv, start=(k == 0), stop=(k == 15)),
                    inc=("t" if k == 15 else None))
        P.w(V, "t"); P.w(V, "q1")
        P.r(V, lambda e: e.tensor_copy(c_a[:, 0:512], PS[0][:, 0:512]))
        P.r(V, lambda e: e.tensor_copy(c_a[:, 512:1024], PS[1][:, 0:512]), inc="v")
        P.w(TE, "v"); P.w(S, "v")
        P.d(S, ap["s_s"][ts_(i, 128), :], c_a[:, 0:D], "q1")
    P.barrier()

    xqfm = A16[:, 0:16384]
    for k in range(8):
        P.d(SY, xqfm[:, ts_(k, T)], ap["s_xq"][:, ts_(k, 128)], "q0", transpose=True)
    P.r(V, lambda e: e.memset(p_a[:, 24:25], 0.0), inc="v")
    P.r(V, lambda e: e.memset(sx2i[:, :], 0.0), inc="v")
    wmovB = A16[:, 16384:24576]
    wmov2B = A16[:, 24576:32768]
    tmark = {}
    for oc in range(8):
        wg_ = wmov if oc % 2 == 0 else wmovB
        wu_ = wmov2 if oc % 2 == 0 else wmov2B
        if oc >= 2:
            P.w(SY, "t", tmark[oc - 2])
        for k in range(8):
            P.d(SY, wg_[:, ts_(k, 512)], ap["s_f0wg"][oc * 512:(oc + 1) * 512, ts_(k, 128)], "q0", transpose=True)
            P.d(SY, wu_[:, ts_(k, 512)], ap["s_f0wu"][oc * 512:(oc + 1) * 512, ts_(k, 128)], "q0", transpose=True)
        P.w(TE, "q0")
        for i in range(NT):
            for k in range(8):
                st = xqfm[:, off_(k * T, i, 128)]
                P.r(TE, lambda e, st=st, k=k, w=wg_: e.matmul(PS[0][:, 0:512], st, w[:, ts_(k, 512)], start=(k == 0), stop=(k == 7)), inc=("t" if k == 7 else None))
            for k in range(8):
                st = xqfm[:, off_(k * T, i, 128)]
                P.r(TE, lambda e, st=st, k=k, w=wu_: e.matmul(PS[1][:, 0:512], st, w[:, ts_(k, 512)], start=(k == 0), stop=(k == 7)), inc=("t" if k == 7 else None))
            if i == NT - 1:
                tmark[oc] = P.cnt["t"]
            P.w(S, "t")
            P.r(S, lambda e, i=i: e.activation(out=w_a[:, 0:512], in_=PS[0][:, 0:512], func=AF.Silu, scale=sxg[:, i:i + 1]), inc="sc")
            P.r(S, lambda e, i=i: e.activation(out=w_b[:, 0:512], in_=PS[1][:, 0:512], func=AF.Identity, scale=sxu[:, i:i + 1], bias=0.0), inc="sc")
            P.w(V, "sc")
            P.w(TE, "sc")
            P.r(V, lambda e: e.tensor_tensor(w_a[:, 0:512], w_a[:, 0:512], w_b[:, 0:512], OP.mult))
            P.r(V, lambda e, i=i: e.reduce_max(p_a[:, 25:26], w_a[:, 0:512], AX.X, apply_absolute_value=True))
            P.r(V, lambda e, i=i: e.tensor_tensor(sx2i[:, i:i + 1], sx2i[:, i:i + 1], p_a[:, 25:26], OP.max), inc="v")
            P.w(S, "v")
            P.d(S, ap["s_h2"][ts_(i, 128), oc * 512:(oc + 1) * 512], w_a[:, 0:512], "q1")
            P.w(V, "q1"); P.w(S, "q1")
    P.r(V, lambda e: e.tensor_scalar(sx2i[:, :], sx2i[:, :], 1e-5, 1.0 / 127.0, OP.max, OP.mult))
    P.r(V, lambda e: e.tensor_scalar(sxd[:, :], sx2i[:, :], mgr[:, 2:3], None, OP.mult), inc="v")
    P.barrier()
    q8mark = {}
    for i in range(NT):
        lw = w_a[:, 0:4096] if i % 2 == 0 else A32[:, 0:4096]
        sw = w_b[:, 0:4096] if i % 2 == 0 else A32[:, 4096:8192]
        cq = c_a if i % 2 == 0 else c_b
        sq = "q1" if i % 2 == 0 else "q2"
        pcol = 26 if i % 2 == 0 else 25
        if i >= 2:
            P.w(SY, "sc", q8mark[i - 2])
        P.d(SY, lw, ap["s_h2"][ts_(i, 128), :], "q0")
        P.w(V, "q0"); P.w(V, sq); P.w(V, "sc")
        P.r(V, lambda e, i=i, pcol=pcol: e.reciprocal(p_a[:, pcol:pcol + 1], sx2i[:, i:i + 1]), inc="v")
        P.w(S, "v")
        P.r(S, lambda e, lw=lw, sw=sw, pcol=pcol: e.activation(out=sw, in_=lw, func=AF.Identity, scale=p_a[:, pcol:pcol + 1], bias=0.0), inc="sc")
        q8mark[i] = P.cnt["sc"]
        P.w(V, "sc")
        P.r(V, lambda e, sw=sw: e.tensor_copy(i_a[:, 0:4096], sw))
        P.r(V, lambda e, cq=cq: e.tensor_copy(cq[:, 0:4096], i_a[:, 0:4096]), inc="v")
        P.w(S, "v")
        P.d(S, ap["s_xq2"][ts_(i, 128), :], cq[:, 0:4096], sq)
    P.w(SY, "q1"); P.w(V, "q1"); P.w(S, "q1")
    P.w(SY, "q2"); P.w(V, "q2"); P.w(S, "q2")
    xq2fm = big1
    for half in range(2):
        P.w(SY, "t")
        for k in range(16):
            kk = half * 16 + k
            P.d(SY, xq2fm[:, ts_(k, T)], ap["s_xq2"][:, ts_(kk, 128)], "q0", transpose=True)
            P.d(SY, wmov[:, ts_(k, 512)], ap["s_f0wd"][0:512, ts_(kk, 128)], "q0", transpose=True)
            P.d(SY, wmov2[:, ts_(k, 512)], ap["s_f0wd"][512:1024, ts_(kk, 128)], "q0", transpose=True)
        if half == 0 and "s_dbg3" in _dbg:
            P.w(S, "q0")
            P.d(S, ap["s_dbg3"][:, 0:T], xq2fm[:, 0:T], "q1")
            P.d(S, ap["s_dbg3"][:, T:T + 512], wmov[:, 0:512], "q1")
            P.w(SY, "q1"); P.w(V, "q1"); P.w(S, "q1")
        P.wall(TE)
        for i in range(NT):
            for mc in range(2):
                ps = PS[mc]
                wv = wmov if mc == 0 else wmov2
                for k in range(16):
                    st = xq2fm[:, off_(k * T, i, 128)]
                    P.r(TE, lambda e, ps=ps, st=st, wv=wv, k=k:
                        e.matmul(ps[:, 0:512], st, wv[:, ts_(k, 512)], start=(k == 0), stop=(k == 15)),
                        inc=("t" if k == 15 else None))
            if half == 0:
                P.w(V, "t")
                P.r(V, lambda e: e.tensor_copy(w_a[:, 0:512], PS[0][:, 0:512]))
                P.r(V, lambda e: e.tensor_copy(w_a[:, 512:1024], PS[1][:, 0:512]), inc="v")
                P.w(TE, "v"); P.w(S, "v")
                P.d(S, ap["s_h2"][ts_(i, 128), 0:1024], w_a[:, 0:1024], "q1")
                P.w(V, "q1")
            else:
                P.w(SY, "v")
                P.d(SY, w_b[:, 0:1024], ap["s_h2"][ts_(i, 128), 0:1024], "q0")
                P.w(V, "t"); P.wall(V)
                P.r(V, lambda e: e.tensor_tensor(w_a[:, 0:512], PS[0][:, 0:512], w_b[:, 0:512], OP.add))
                P.r(V, lambda e: e.tensor_tensor(w_a[:, 512:1024], PS[1][:, 0:512], w_b[:, 512:1024], OP.add), inc="v")
                P.w(TE, "v"); P.w(S, "v"); P.w(S, "q1")
                P.r(S, lambda e, i=i: e.activation(out=c_b[:, 0:D], in_=w_a[:, 0:D], func=AF.Silu, scale=sxd[:, i:i + 1]), inc="sc")
                P.w(S, "sc")
                P.d(S, ap["s_g"][ts_(i, 128), :], c_b[:, 0:D], "q1")
                P.w(V, "sc")
    P.barrier()

    for i in range(NT):
        P.d(SY, w_b[:, 0:D], xr[i], "q0")
        P.d(SY, c_a[:, 0:D], ap["s_s"][ts_(i, 128), :], "q0")
        P.d(SY, c_b[:, 0:D], ap["s_g"][ts_(i, 128), :], "q0")
        P.wall(V)
        P.r(V, lambda e: e.tensor_tensor(c_a[:, 0:D], c_a[:, 0:D], c_b[:, 0:D], OP.mult), inc="v")
        P.r(V, lambda e: e.tensor_tensor(w_a[:, 0:D], w_b[:, 0:D], c_a[:, 0:D], OP.add), inc="v")
        P.w(S, "v")
        P.d(S, ap["s_x1"][ts_(i, 128), :], w_a[:, 0:D], "q1")
        P.w(V, "q1"); P.w(S, "q1")
        mu = p_a[:, 0:1]; vv = p_a[:, 1:2]; rr = p_a[:, 2:3]
        P.r(V, lambda e, mu=mu: e.reduce_sum(mu, w_a[:, 0:D], AX.X))
        P.r(V, lambda e, mu=mu: e.tensor_scalar_mul(mu, mu, 1.0 / D), inc="v")
        P.w(S, "v")
        P.r(S, lambda e: e.activation(out=w_b[:, 0:D], in_=w_a[:, 0:D], func=AF.Square), inc="sc")
        P.w(V, "sc")
        P.r(V, lambda e, vv=vv: e.reduce_sum(vv, w_b[:, 0:D], AX.X))
        P.r(V, lambda e, vv=vv: e.tensor_scalar_mul(vv, vv, 1.0 / D))
        P.r(V, lambda e, mu=mu, rr=rr: e.tensor_tensor(rr, mu, mu, OP.mult))
        P.r(V, lambda e, vv=vv, rr=rr: e.tensor_tensor(rr, vv, rr, OP.subtract))
        P.r(V, lambda e, rr=rr: e.tensor_scalar_add(rr, rr, 1e-5), inc="v")
        P.w(S, "v")
        P.r(S, lambda e, rr=rr: e.activation(out=rr, in_=rr, func=AF.Sqrt), inc="sc")
        P.w(V, "sc")
        P.r(V, lambda e, rr=rr: e.reciprocal(rr, rr))
        P.r(V, lambda e, mu=mu, rr=rr: e.tensor_scalar(w_a[:, 0:D], w_a[:, 0:D], mu, rr, OP.subtract, OP.mult))
        P.r(V, lambda e: e.tensor_copy(c_b[:, 0:D], w_a[:, 0:D]), inc="v")
        P.w(S, "v")
        P.d(S, ap["s_xn1"][ts_(i, 128), :], c_b[:, 0:D], "q1")
        P.w(V, "q1"); P.w(S, "q1")
        fp4_on_tile(128, D)
        P.w(S, "v")
        P.d(S, ap["s_xq1"][ts_(i, 128), :], c_a[:, 0:D], "q1")
        P.w(SY, "q1"); P.w(V, "q1"); P.w(S, "q1")
    P.barrier()

    xq1fm = A16[:, 0:16384]
    for k in range(8):
        P.d(SY, xq1fm[:, ts_(k, T)], ap["s_xq1"][:, ts_(k, 128)], "q0", transpose=True)
    OCW = [512, 512, 512, 512, 512, 256]
    ocs = [0, 512, 1024, 1536, 2048, 2560]
    wmovB = A16[:, 16384:24576]
    wmov2B = A16[:, 24576:32768]
    tmark = {}
    for oc in range(6):
        ow = OCW[oc]; o0 = ocs[oc]
        wg_ = wmov if oc % 2 == 0 else wmovB
        wu_ = wmov2 if oc % 2 == 0 else wmov2B
        if oc >= 2:
            P.w(SY, "t", tmark[oc - 2])
        for k in range(8):
            P.d(SY, wg_[:, k * 512:k * 512 + ow], ap["s_f1wg"][o0:o0 + ow, ts_(k, 128)], "q0", transpose=True)
            P.d(SY, wu_[:, k * 512:k * 512 + ow], ap["s_f1wu"][o0:o0 + ow, ts_(k, 128)], "q0", transpose=True)
        P.w(TE, "q0")
        for i in range(NT):
            for k in range(8):
                st = xq1fm[:, off_(k * T, i, 128)]
                P.r(TE, lambda e, st=st, k=k, ow=ow, w=wg_: e.matmul(PS[0][:, 0:ow], st, w[:, k * 512:k * 512 + ow], start=(k == 0), stop=(k == 7)), inc=("t" if k == 7 else None))
            for k in range(8):
                st = xq1fm[:, off_(k * T, i, 128)]
                P.r(TE, lambda e, st=st, k=k, ow=ow, w=wu_: e.matmul(PS[1][:, 0:ow], st, w[:, k * 512:k * 512 + ow], start=(k == 0), stop=(k == 7)), inc=("t" if k == 7 else None))
            if i == NT - 1:
                tmark[oc] = P.cnt["t"]
            P.w(S, "t")
            P.r(S, lambda e, ow=ow: e.activation(out=w_a[:, 0:ow], in_=PS[0][:, 0:ow], func=AF.Silu), inc="sc")
            P.r(S, lambda e, ow=ow: e.activation(out=w_b[:, 3000:3000 + ow] if False else w_b[:, 0:ow], in_=PS[1][:, 0:ow], func=AF.Identity, bias=0.0), inc="sc")
            P.w(V, "sc"); P.w(TE, "sc")
            P.r(V, lambda e, ow=ow: e.tensor_tensor(w_a[:, 0:ow], w_a[:, 0:ow], w_b[:, 0:ow], OP.mult), inc="v")
            fp4_on_tile(128, ow)
            P.w(S, "v")
            P.d(S, ap["s_xq3"][ts_(i, 128), o0:o0 + ow], c_a[:, 0:ow], "q1")
            P.w(V, "q1"); P.w(S, "q1")
    P.barrier()

    xq3fm = big1
    for half in range(2):
        P.w(SY, "t")
        for k in range(11):
            kk = half * 11 + k
            P.d(SY, xq3fm[:, ts_(k, T)], ap["s_xq3"][:, ts_(kk, 128)], "q0", transpose=True)
            P.d(SY, wmov[:, ts_(k, 512)], ap["s_f1wd"][0:512, ts_(kk, 128)], "q0", transpose=True)
            P.d(SY, wmov2[:, ts_(k, 512)], ap["s_f1wd"][512:1024, ts_(kk, 128)], "q0", transpose=True)
        P.wall(TE)
        for i in range(NT):
            for mc in range(2):
                ps = PS[mc]
                wv = wmov if mc == 0 else wmov2
                for k in range(11):
                    st = xq3fm[:, off_(k * T, i, 128)]
                    P.r(TE, lambda e, ps=ps, st=st, wv=wv, k=k:
                        e.matmul(ps[:, 0:512], st, wv[:, ts_(k, 512)], start=(k == 0), stop=(k == 10)),
                        inc=("t" if k == 10 else None))
            if half == 0:
                P.w(V, "t")
                P.r(V, lambda e: e.tensor_copy(w_a[:, 0:512], PS[0][:, 0:512]))
                P.r(V, lambda e: e.tensor_copy(w_a[:, 512:1024], PS[1][:, 0:512]), inc="v")
                P.w(TE, "v"); P.w(S, "v")
                P.d(S, ap["s_h2"][ts_(i, 128), 0:1024], w_a[:, 0:1024], "q1")
                P.w(V, "q1")
            else:
                P.w(SY, "v")
                P.d(SY, w_b[:, 0:1024], ap["s_h2"][ts_(i, 128), 0:1024], "q0")
                P.w(V, "t"); P.wall(V)
                P.r(V, lambda e: e.tensor_tensor(w_a[:, 0:512], PS[0][:, 0:512], w_b[:, 0:512], OP.add))
                P.r(V, lambda e: e.tensor_tensor(w_a[:, 512:1024], PS[1][:, 0:512], w_b[:, 512:1024], OP.add), inc="v")
                P.w(TE, "v"); P.w(S, "v"); P.w(S, "q1")
                P.r(S, lambda e: e.activation(out=c_b[:, 0:D], in_=w_a[:, 0:D], func=AF.Silu), inc="sc")
                P.w(S, "sc")
                P.d(S, ap["s_g1"][ts_(i, 128), :], c_b[:, 0:D], "q1")
                P.w(V, "sc")
    P.barrier()

    xn1fm = big1[:, 0:16384]
    for k in range(8):
        P.d(SY, xn1fm[:, ts_(k, T)], ap["s_xn1"][:, ts_(k, 128)], "q0", transpose=True)
    QCW = [512, 512, 512, 512, 512, 512, 32]
    qcs = [0, 512, 1024, 1536, 2048, 2560, 3072]
    wmovB = A16[:, 16384:24576]
    tmark = {}
    for oc in range(7):
        tmark[oc] = P.cnt["t"]
        ow = QCW[oc]; o0 = qcs[oc]
        wg_ = wmov if oc % 2 == 0 else wmovB
        if oc >= 2:
            P.w(SY, "t", tmark[oc - 1])
        for k in range(8):
            P.d(SY, wg_[:, k * 512:k * 512 + ow], ap["s_qkvb"][o0:o0 + ow, ts_(k, 128)], "q0", transpose=True)
        P.w(TE, "q0")
        for i in range(NT):
            for k in range(8):
                st = xn1fm[:, off_(k * T, i, 128)]
                P.r(TE, lambda e, st=st, k=k, ow=ow, w=wg_: e.matmul(PS[0][:, 0:ow], st, w[:, k * 512:k * 512 + ow], start=(k == 0), stop=(k == 7)), inc=("t" if k == 7 else None))
            P.w(V, "t")
            P.r(V, lambda e, ow=ow: e.tensor_copy(c_a[:, 0:ow], PS[0][:, 0:ow]), inc="v")
            P.w(TE, "v"); P.w(S, "v")
            P.d(S, ap["s_qkv"][ts_(i, 128), o0:o0 + ow], c_a[:, 0:ow], "q1")
            P.w(V, "q1"); P.w(S, "q1")
    P.barrier()

    def elu1_norm(dst_bf, src_bf, tmpf):
        P.r(V, lambda e: e.tensor_scalar(w_a[:, 0:1024], src_bf, 0.0, None, OP.min), inc="v")
        P.w(S, "v")
        P.r(S, lambda e: e.activation(out=w_a[:, 1024:2048], in_=w_a[:, 0:1024], func=AF.Exp), inc="sc")
        P.w(V, "sc")
        P.r(V, lambda e: e.tensor_scalar(w_a[:, 0:1024], src_bf, 0.0, None, OP.max))
        P.r(V, lambda e: e.tensor_tensor(w_a[:, 0:1024], w_a[:, 0:1024], w_a[:, 1024:2048], OP.add), inc="v")
        a3 = w_a[:, 0:1024].rearrange("p (h d) -> p h d", d=64)
        P.r(V, lambda e, a3=a3: e.tensor_reduce(p_a[:, 32:48], a3, AX.X, OP.add))
        P.r(V, lambda e: e.reciprocal(p_a[:, 32:48], p_a[:, 32:48]), inc="v")
        rsb = p_a[:, 32:48][:, :, None].broadcast_to([128, 16, 64])
        P.r(V, lambda e, a3=a3, rsb=rsb, dst_bf=dst_bf: e.tensor_tensor(
            dst_bf.rearrange("p (h d) -> p h d", d=64), a3, rsb, OP.mult), inc="v")

    for j in range(8):
        P.wall(V)
        P.d(SY, c_a[:, 0:3104], ap["s_qkv"][ts_(j, 128), :], "q0")
        P.d(SY, c_b[:, 0:3104], ap["s_qkv"][ts_(8 + j, 128), :], "q0")
        P.wall(V)
        elu1_norm(c_c[:, 0:1024], c_a[:, 0:1024], 0)
        elu1_norm(c_c[:, 1024:2048], c_a[:, 1024:2048], 0)
        elu1_norm(c_c[:, 2048:3072], c_b[:, 0:1024], 0)
        elu1_norm(c_b[:, 0:1024], c_b[:, 1024:2048], 0)
        P.r(V, lambda e: e.tensor_scalar(w_a[:, 0:16], c_a[:, 3072:3088], -1.0, None, OP.mult))
        P.r(V, lambda e: e.tensor_scalar(w_a[:, 16:32], c_b[:, 3072:3088], -1.0, None, OP.mult), inc="v")
        P.w(S, "v")
        P.r(S, lambda e: e.activation(out=w_a[:, 0:32], in_=w_a[:, 0:32], func=AF.Exp), inc="sc")
        P.w(V, "sc")
        P.r(V, lambda e: e.tensor_scalar_add(w_a[:, 0:32], w_a[:, 0:32], 1.0))
        P.r(V, lambda e: e.reciprocal(w_a[:, 0:32], w_a[:, 0:32]), inc="v")
        b0c = w_a[:, 0:16]; b1c = w_a[:, 16:32]
        def dot(dst_cols, aa, bb):
            P.r(V, lambda e, aa=aa, bb=bb: e.tensor_tensor(w_b[:, 0:1024], aa, bb, OP.mult), inc="v")
            a3 = w_b[:, 0:1024].rearrange("p (h d) -> p h d", d=64)
            P.r(V, lambda e, a3=a3, dst_cols=dst_cols: e.tensor_reduce(p_a[:, dst_cols:dst_cols + 16], a3, AX.X, OP.add), inc="v")
        q0n = c_c[:, 0:1024]; k0n = c_c[:, 1024:2048]; q1n = c_c[:, 2048:3072]; k1n = c_b[:, 0:1024]
        dot(32, q0n, k0n)
        dot(48, q1n, k0n)
        dot(16, q1n, k1n)
        dot(0, k1n, k0n)
        v0 = c_a[:, 2048:3072]; v1 = c_b[:, 2048:3072]
        b0b = b0c[:, :, None].broadcast_to([128, 16, 64])
        P.r(V, lambda e, b0b=b0b, v0=v0: e.tensor_tensor(
            c_a[:, 0:1024].rearrange("p (h d) -> p h d", d=64), v0.rearrange("p (h d) -> p h d", d=64), b0b, OP.mult), inc="v")
        vi0 = c_a[:, 0:1024]
        d_q0k0 = p_a[:, 32:48][:, :, None].broadcast_to([128, 16, 64])
        P.r(V, lambda e, d_q0k0=d_q0k0, vi0=vi0: e.tensor_tensor(
            c_a[:, 1024:2048].rearrange("p (h d) -> p h d", d=64), vi0.rearrange("p (h d) -> p h d", d=64), d_q0k0, OP.mult), inc="v")
        d_k1k0 = p_a[:, 0:16][:, :, None].broadcast_to([128, 16, 64])
        P.r(V, lambda e, d_k1k0=d_k1k0, vi0=vi0: e.tensor_tensor(
            w_b[:, 0:1024].rearrange("p (h d) -> p h d", d=64), vi0.rearrange("p (h d) -> p h d", d=64), d_k1k0, OP.mult), inc="v")
        P.r(V, lambda e, v1=v1: e.tensor_tensor(w_b[:, 0:1024], v1, w_b[:, 0:1024], OP.subtract), inc="v")
        b1b = b1c[:, :, None].broadcast_to([128, 16, 64])
        P.r(V, lambda e, b1b=b1b: e.tensor_tensor(
            w_b[:, 0:1024].rearrange("p (h d) -> p h d", d=64), w_b[:, 0:1024].rearrange("p (h d) -> p h d", d=64), b1b, OP.mult), inc="v")
        vi1 = w_b[:, 0:1024]
        d_q1k0 = p_a[:, 48:64][:, :, None].broadcast_to([128, 16, 64])
        d_q1k1 = p_a[:, 16:32][:, :, None].broadcast_to([128, 16, 64])
        P.r(V, lambda e, d_q1k0=d_q1k0, vi0=vi0: e.tensor_tensor(
            w_b[:, 1024:2048].rearrange("p (h d) -> p h d", d=64), vi0.rearrange("p (h d) -> p h d", d=64), d_q1k0, OP.mult), inc="v")
        P.r(V, lambda e, d_q1k1=d_q1k1, vi1=vi1: e.tensor_tensor(
            w_b[:, 2048:3072].rearrange("p (h d) -> p h d", d=64), vi1.rearrange("p (h d) -> p h d", d=64), d_q1k1, OP.mult), inc="v")
        P.r(V, lambda e: e.tensor_tensor(w_b[:, 1024:2048], w_b[:, 1024:2048], w_b[:, 2048:3072], OP.add), inc="v")
        P.r(V, lambda e: e.tensor_scalar(c_a[:, 1024:2048], c_a[:, 1024:2048], 0.125, None, OP.mult))
        P.r(V, lambda e: e.tensor_scalar(c_b[:, 1024:2048], w_b[:, 1024:2048], 0.125, None, OP.mult), inc="v")
        P.w(S, "v")
        P.d(S, ap["s_lay"][ts_(j, 128), :], c_a[:, 1024:2048], "q1")
        P.d(S, ap["s_lay"][ts_(8 + j, 128), :], c_b[:, 1024:2048], "q1")
        P.w(SY, "q1"); P.w(V, "q1"); P.w(S, "q1")
    P.barrier()

    layfm = big1[:, 0:16384]
    P.d(G, lngrow[:, :], ap["ln_g"][None, :].broadcast_to([128, D]), "q2")
    P.d(G, lnbrow[:, :], ap["ln_b"][None, :].broadcast_to([128, D]), "q2")
    for k in range(8):
        P.d(SY, layfm[:, ts_(k, T)], ap["s_lay"][:, ts_(k, 128)], "q0", transpose=True)
    for k in range(8):
        P.d(SY, wmov[:, k * 512:(k + 1) * 512], ap["s_wo"][0:512, ts_(k, 128)], "q0", transpose=True)
        P.d(SY, wmov2[:, k * 512:(k + 1) * 512], ap["s_wo"][512:1024, ts_(k, 128)], "q0", transpose=True)
    P.barrier()
    P.wall(TE)
    for i in range(NT):
        for mc in range(2):
            ps = PS[mc]
            wv = wmov if mc == 0 else wmov2
            for k in range(8):
                st = layfm[:, off_(k * T, i, 128)]
                P.r(TE, lambda e, ps=ps, st=st, wv=wv, k=k:
                    e.matmul(ps[:, 0:512], st, wv[:, ts_(k, 512)], start=(k == 0), stop=(k == 7)),
                    inc=("t" if k == 7 else None))
        if i == 1:
            P.barrier()
        P.d(SY, c_a[:, 0:D], ap["s_xn1"][ts_(i, 128), :], "q0")
        P.d(SY, w_b[:, 0:D], ap["s_x1"][ts_(i, 128), :], "q0")
        P.w(V, "t"); P.wall(V)
        P.r(V, lambda e: e.tensor_tensor(w_a[:, 0:512], PS[0][:, 0:512], c_a[:, 0:512], OP.add))
        P.r(V, lambda e: e.tensor_tensor(w_a[:, 512:1024], PS[1][:, 0:512], c_a[:, 512:1024], OP.add), inc="v")
        P.w(TE, "v")
        mu = p_a[:, 0:1]; vv = p_a[:, 1:2]; rr = p_a[:, 2:3]
        P.r(V, lambda e, mu=mu: e.reduce_sum(mu, w_a[:, 0:D], AX.X))
        P.r(V, lambda e, mu=mu: e.tensor_scalar_mul(mu, mu, 1.0 / D), inc="v")
        P.w(S, "v")
        P.r(S, lambda e: e.activation(out=c_b[:, 0:D], in_=w_a[:, 0:D], func=AF.Square), inc="sc")
        P.w(V, "sc")
        P.r(V, lambda e, vv=vv: e.reduce_sum(vv, c_b[:, 0:D], AX.X))
        P.r(V, lambda e, vv=vv: e.tensor_scalar_mul(vv, vv, 1.0 / D))
        P.r(V, lambda e, mu=mu, rr=rr: e.tensor_tensor(rr, mu, mu, OP.mult))
        P.r(V, lambda e, vv=vv, rr=rr: e.tensor_tensor(rr, vv, rr, OP.subtract))
        P.r(V, lambda e, rr=rr: e.tensor_scalar_add(rr, rr, 1e-5), inc="v")
        P.w(S, "v")
        P.r(S, lambda e, rr=rr: e.activation(out=rr, in_=rr, func=AF.Sqrt), inc="sc")
        P.w(V, "sc")
        P.r(V, lambda e, rr=rr: e.reciprocal(rr, rr))
        P.r(V, lambda e, mu=mu, rr=rr: e.tensor_scalar(w_a[:, 0:D], w_a[:, 0:D], mu, rr, OP.subtract, OP.mult))
        P.r(V, lambda e: e.tensor_tensor(w_a[:, 0:D], w_a[:, 0:D], lngrow[:, :], OP.mult))
        P.r(V, lambda e: e.tensor_tensor(w_a[:, 0:D], w_a[:, 0:D], lnbrow[:, :], OP.add), inc="v")
        P.w(SY, "v")
        P.d(SY, c_b[:, 0:D], ap["s_g1"][ts_(i, 128), :], "q0")
        P.wall(V)
        P.r(V, lambda e: e.tensor_tensor(w_a[:, 0:D], w_a[:, 0:D], c_b[:, 0:D], OP.mult), inc="v")
        P.r(V, lambda e: e.tensor_tensor(w_a[:, 0:D], w_a[:, 0:D], w_b[:, 0:D], OP.add), inc="v")
        P.w(S, "v")
        b = i // 8; j = i % 8
        P.d(S, ap["out"][b, ts_(j, 128), :], w_a[:, 0:D], "q1")
        P.w(SY, "q1"); P.w(V, "q1"); P.w(S, "q1"); P.w(G, "q1")
        if i == 0:
            P.barrier()
    P.wall(SY); P.wall(G); P.wall(V)

    import os as _os
    _cut = int(_os.environ.get("MK_CUT", "-1"))
    if _cut >= 0 and hasattr(P, "marks") and _cut < len(P.marks):
        for e in ENGINES:
            P.stream[e] = P.stream[e][:P.marks[_cut][e]]
    blkc = nc.Block(); blk = blkc.__enter__()

    def emit(eng_name):
        def _f(e):
            for fn in P.stream[eng_name]:
                fn(e)
        return _f
    blk.sync(emit("sync"))
    blk.vector(emit("vector"))
    blk.scalar(emit("scalar"))
    blk.gpsimd(emit("gpsimd"))
    blk.tensor(emit("tensor"))
    blkc.__exit__(None, None, None)
    for t in reversed(ctx):
        t.__exit__(None, None, None)
    return nc


_NC_CACHE = None


def kernel(L0_attn_x, L0_attn_W_in, L0_attn_conv_w, L0_attn_conv_b, L0_attn_W_xproj,
           L0_attn_A_log, L0_attn_D_param, L0_attn_W_out, L0_ffn_w_gate, L0_ffn_w_up,
           L0_ffn_w_down, L1_attn_W_qkvb, L1_attn_W_o, L1_attn_ln_g, L1_attn_ln_b,
           L1_ffn_w_gate, L1_ffn_w_up, L1_ffn_w_down):
    global _NC_CACHE
    if _NC_CACHE is None:
        _NC_CACHE = build_program()
    nc_ = _NC_CACHE
    x = np.ascontiguousarray(L0_attn_x, dtype=np.float32)
    Wq = np.ascontiguousarray(L1_attn_W_qkvb, dtype=np.float32).reshape(16, 193, 1024)
    Wq_perm = np.concatenate([
        Wq[:, 0:64].reshape(1024, 1024),
        Wq[:, 64:128].reshape(1024, 1024),
        Wq[:, 128:192].reshape(1024, 1024),
        Wq[:, 192:193].reshape(16, 1024),
        np.zeros((16, 1024), np.float32)], axis=0)
    maskc = (np.arange(128)[:, None] % TC <= np.arange(TC)[None, :]).astype(ml_dtypes.bfloat16)
    identc = np.eye(128, dtype=ml_dtypes.bfloat16)
    in_maps = []
    for c in range(NCORES):
        lo = c * LLOC
        halo = x[:, lo - 3:lo].copy() if c > 0 else np.zeros((B, 3, D), np.float32)
        rm = (np.arange(1, NCORES + 1) <= c).astype(np.float32)
        rm = np.zeros(NCORES, np.float32)
        rm[1:] = (np.arange(0, NCORES - 1) < c).astype(np.float32)
        in_maps.append({
            "x": x[:, lo:lo + LLOC].copy(),
            "xh": halo,
            "W_in": L0_attn_W_in, "conv_w": L0_attn_conv_w, "conv_b": L0_attn_conv_b,
            "W_xproj": L0_attn_W_xproj, "A_log": L0_attn_A_log, "D_param": L0_attn_D_param,
            "W_out": L0_attn_W_out,
            "f0_wg": L0_ffn_w_gate, "f0_wu": L0_ffn_w_up, "f0_wd": L0_ffn_w_down,
            "Wqkvb": Wq_perm, "W_o": L1_attn_W_o, "ln_g": L1_attn_ln_g, "ln_b": L1_attn_ln_b,
            "f1_wg": L1_ffn_w_gate, "f1_wu": L1_ffn_w_up, "f1_wd": L1_ffn_w_down,
            "mask": maskc, "ident": identc, "identf": identc.astype(np.float32), "rmask": rm,
        })
    res = run_bass_kernel_spmd(nc_, in_maps, core_ids=list(range(NCORES)))
    out = np.concatenate([np.asarray(r["out"], np.float32) for r in res.results], axis=1)
    return out


